# version 03
# baseline (speedup 1.0000x reference)
import math
from contextlib import ExitStack
from types import SimpleNamespace as NS

import numpy as np
import ml_dtypes
import concourse.bass as bass
import concourse.mybir as mybir
from concourse.bass_utils import run_bass_kernel_spmd

F32 = mybir.dt.float32
BF16 = mybir.dt.bfloat16
I32 = mybir.dt.int32
AF = mybir.ActivationFunctionType
ALU = mybir.AluOpType
AX = mybir.AxisListType

N_CORES = 8
D = 1024
B = 2
S = 8192
NTOK = B * S
TPC = NTOK // N_CORES

BFD = np.dtype(ml_dtypes.bfloat16)


class Sem:

    def __init__(self, h):
        self.h = h
        self.n = 0

    def inc(self, instr, k=1):
        instr.then_inc(self.h, k)
        self.n += k
        return instr

    def wait(self, eng, n=None):
        eng.wait_ge(self.h, self.n if n is None else n)


_REGC = [0]
def _rn(tag):
    _REGC[0] += 1
    return f"{tag}{_REGC[0]}"


class Ctx:

    def __init__(self, name):
        self.nc = bass.Bass("TRN2", target_bir_lowering=False, debug=False)
        self.st = ExitStack()
        self.name = name

    def sb(self, name, shape, dtype=F32, stack=None):
        name = getattr(self, "prefix", "") + name
        st = stack or getattr(self, "cur_stack", None) or self.st
        return st.enter_context(self.nc.sbuf_tensor(name, shape, dtype))

    def ps(self, name, shape, dtype=F32, stack=None):
        name = getattr(self, "prefix", "") + name
        st = stack or getattr(self, "cur_stack", None) or self.st
        return st.enter_context(self.nc.psum_tensor(name, shape, dtype))

    def sem(self, name):
        reg = getattr(self, "_semreg", None)
        if reg is None:
            reg = self._semreg = {}
        if name not in reg:
            reg[name] = Sem(self.st.enter_context(self.nc.semaphore(name)))
        return reg[name]

    def din(self, name, shape, dtype=F32):
        return self.nc.dram_tensor(name, list(shape), dtype, kind="ExternalInput").ap()

    def dout(self, name, shape, dtype=F32):
        return self.nc.dram_tensor(name, list(shape), dtype, kind="ExternalOutput").ap()

    def begin(self):
        self.block = self.st.enter_context(self.nc.Block())
        self.tc = self.sem("ch_t")
        self.vc = self.sem("ch_v")
        self.sc = self.sem("ch_s")
        self.gc = self.sem("ch_g")
        self.yc = self.sem("ch_y")

    def end(self):
        self.st.close()

    def V(self, method, *a, **k):
        self.nc.vector.wait_ge(self.vc.h, self.vc.n)
        i = getattr(self.nc.vector, method)(*a, **k)
        self.vc.inc(i)
        return i

    def S(self, method, *a, **k):
        self.nc.scalar.wait_ge(self.sc.h, self.sc.n)
        i = getattr(self.nc.scalar, method)(*a, **k)
        self.sc.inc(i)
        return i

    def T(self, method, *a, **k):
        return getattr(self.nc.tensor, method)(*a, **k)

    def tmark(self, instr=None):
        self.tc.inc(instr if instr is not None else self.nc.tensor.nop())
        return self.tc.n

    def gdma(self, out, in_, sem=None, **k):
        i = self.nc.gpsimd.dma_start(out=out, in_=in_, **k)
        sem = sem if sem is not None else self.gc
        sem.inc(i, 16)
        return sem.n

    def ydma(self, out, in_, sem=None, **k):
        i = self.nc.sync.dma_start(out=out, in_=in_, **k)
        sem = sem if sem is not None else self.yc
        sem.inc(i, 16)
        return sem.n

    def pid(self):
        return self.pide("sync")

    def pide(self, engname):
        d = getattr(self, "_pids", None)
        if d is None:
            d = self._pids = {}
        if engname not in d:
            d[engname] = getattr(self.nc, engname).partition_id()
        return d[engname]

    def make_consts(self, values):
        done = getattr(self, "_consts_done", set())
        self._consts_done = done
        vals = sorted(set(float(v) for v in values) - done)
        if not vals:
            return
        done.update(vals)
        tile = self.sb(f"c_constpool{len(done)}", [128, max(1, len(vals))])
        for j, v in enumerate(vals):
            self.V("memset", tile[:, j:j + 1], v)
            self.nc.const_aps.register(F32, v, tile[:, j:j + 1])
        n = self.vc.n
        for eng in (self.nc.scalar, self.nc.tensor, self.nc.gpsimd, self.nc.sync):
            eng.wait_ge(self.vc.h, n)



def make_ln(c, ps_sums, ps_bc):
    key = getattr(c, "prefix", "")
    if getattr(c, "_ln_key", None) != key:
        c._ln_key = key
        c.ln_sq = [c.sb("ln_sq0", [128, 512]), c.sb("ln_sq1", [128, 512])]
        c.ln_rows = c.sb("ln_rows", [1, 4 * 512])
        c.ln_ones_col = c.sb("ln_ones_col", [128, 1])
        c.ln_ones_row = c.sb("ln_ones_row", [1, 128])
        c.ln_br = c.sb("ln_br", [128, 512])
        c.ln_bm = c.sb("ln_bm", [128, 512])
        c.ln_init_done = False
    c.ln_ps_sums = ps_sums
    c.ln_ps_bc = ps_bc


def ln_fm_512(c, x_views, h_views, eps=1e-5, h32_views=None, rms2_eps=None,
              colmul=None, coladd=None):
    nc = c.nc
    nch = len(x_views)
    ndim = nch * 128
    if not c.ln_init_done:
        c.V("memset", c.ln_ones_col[:], 1.0)
        c.V("memset", c.ln_ones_row[:], 1.0)
        c.ln_init_done = True
    c.vc.wait(nc.tensor)
    for icc in range(nch):
        c.T("matmul", c.ln_ps_sums[0:1, :], c.ln_ones_col[:, 0:1],
            x_views[icc], start=(icc == 0), stop=(icc == nch - 1))
    c.vc.wait(nc.scalar)
    sqmm_at = {}
    for icc in range(nch):
        if icc >= 2:
            nc.scalar.wait_ge(c.tc.h, sqmm_at[icc - 2])
        c.S("activation", c.ln_sq[icc % 2][:, :], x_views[icc], AF.Square)
        c.sc.wait(nc.tensor)
        i = c.T("matmul", c.ln_ps_bc[0:1, :], c.ln_ones_col[:, 0:1],
                c.ln_sq[icc % 2][:, :], start=(icc == 0), stop=(icc == nch - 1))
        sqmm_at[icc] = c.tmark(i)
    sums_done = c.tc.n
    r = c.ln_rows
    rm, rv, rs = r[0:1, 0:512], r[0:1, 512:1024], r[0:1, 1024:1536]
    nc.vector.wait_ge(c.tc.h, sums_done)
    c.V("tensor_scalar", rm, c.ln_ps_sums[0:1, :], 1.0 / ndim, None, ALU.mult)
    c.V("tensor_scalar", rv, c.ln_ps_bc[0:1, :], 1.0 / ndim, None, ALU.mult)
    c.V("tensor_mul", rs, rm, rm)
    c.V("tensor_sub", rv, rv, rs)
    rw = r[0:1, 1536:2048]
    if rms2_eps is not None:
        c.V("tensor_copy", rw, rv)
    c.vc.wait(nc.scalar)
    c.S("activation", rs, rv, AF.Sqrt, bias=eps)
    c.sc.wait(nc.vector)
    c.V("reciprocal", rv, rs)
    if rms2_eps is not None:
        c.V("tensor_mul", rs, rv, rv)
        c.V("tensor_mul", rs, rs, rw)
        c.vc.wait(nc.scalar)
        c.S("activation", rs, rs, AF.Sqrt, bias=rms2_eps)
        c.sc.wait(nc.vector)
        c.V("reciprocal", rw, rs)
        c.V("tensor_mul", rv, rv, rw)
    c.V("scalar_tensor_tensor", rm, rm, -1.0, rv, ALU.mult, ALU.mult)
    rows_done = c.vc.n
    nc.tensor.wait_ge(c.vc.h, rows_done)
    i = c.T("matmul", c.ln_ps_bc[:, :], c.ln_ones_row[0:1, :], rv, start=True, stop=True)
    bc1 = c.tmark(i)
    nc.vector.wait_ge(c.tc.h, bc1)
    c.V("tensor_copy", c.ln_br[:, :], c.ln_ps_bc[:, :])
    nc.tensor.wait_ge(c.vc.h, c.vc.n)
    i = c.T("matmul", c.ln_ps_bc[:, :], c.ln_ones_row[0:1, :], rm, start=True, stop=True)
    bc2 = c.tmark(i)
    nc.vector.wait_ge(c.tc.h, bc2)
    c.V("tensor_copy", c.ln_bm[:, :], c.ln_ps_bc[:, :])
    for icc in range(nch):
        c.V("tensor_mul", c.ln_sq[icc % 2][:, :], x_views[icc], c.ln_br[:, :])
        if colmul is not None:
            c.V("tensor_add", c.ln_sq[icc % 2][:, :], c.ln_sq[icc % 2][:, :],
                c.ln_bm[:, :])
            if coladd is not None:
                c.V("tensor_scalar", h_views[icc], c.ln_sq[icc % 2][:, :],
                    colmul[icc], coladd[icc], ALU.mult, ALU.add)
            elif h32_views is not None:
                c.V("tensor_scalar", h32_views[icc], c.ln_sq[icc % 2][:, :],
                    colmul[icc], None, ALU.mult)
                c.V("tensor_copy", h_views[icc], h32_views[icc])
            else:
                c.V("tensor_scalar", h_views[icc], c.ln_sq[icc % 2][:, :],
                    colmul[icc], None, ALU.mult)
        elif h32_views is not None:
            c.V("tensor_add", h32_views[icc], c.ln_sq[icc % 2][:, :], c.ln_bm[:, :])
            c.V("tensor_copy", h_views[icc], h32_views[icc])
        else:
            c.V("tensor_add", h_views[icc], c.ln_sq[icc % 2][:, :], c.ln_bm[:, :])



def build_ffn0(T=TPC, c=None, io=None):
    own = c is None
    if own:
        c = Ctx("ffn0")
    nc = c.nc
    blk = c.st if own else ExitStack()
    if own:
        io = NS(xT=c.din("xT", (D, T)), wgT=c.din("wgT", (D, 3072), I32),
                wuT=c.din("wuT", (D, 3072), I32), wdT=c.din("wdT", (3072, D), I32),
                yT=c.dout("yT", (D, T)))
    xT, wgT, wuT, wdT, yT = io.xT, io.wgT, io.wuT, io.wdT, io.yT
    H = 3072
    NOC = H // 128
    TT = 1024
    NTT = T // TT
    s127 = 1.0 / 127.0
    c.cur_stack = blk

    x = [c.sb(f"x{i}", [128, T]) for i in range(8)]
    h = [c.sb(f"hln{i}", [128, TT], BF16) for i in range(8)]
    h2 = c.sb("hid2", [128, NOC, TT], BF16)
    silu_t = c.sb("silu", [128, 512])
    wbuf = [c.sb(f"wb{i}", [128, 2, 8, 128], BF16) for i in range(2)]
    dbuf = [c.sb(f"db{i}", [128, 24, 128], BF16) for i in range(2)]

    ps_g = [c.ps(f"psg{i}", [128, 512]) for i in range(2)]
    ps_u = [c.ps(f"psu{i}", [128, 512]) for i in range(2)]
    ps_o = [c.ps(f"pso{i}", [128, 512]) for i in range(2)]
    ps_sums = c.ps("ps_sums", [1, 512])
    ps_bc = c.ps("ps_bc", [128, 512])

    wls = None
    dls = None
    if own:
        c.begin()
    wls = [c.sem("wls0"), c.sem("wls1")]
    dls = [c.sem("dls0"), c.sem("dls1")]
    make_ln(c, ps_sums, ps_bc)
    c.make_consts([1e-5, s127])

    for i in range(8):
        c.ydma(x[i][:], xT[i * 128:(i + 1) * 128, :])
    c.yc.wait(nc.scalar)
    c.yc.wait(nc.vector)
    c.yc.wait(nc.tensor)

    wgT_r = wgT.rearrange("(ic p) o -> p ic o", p=128)
    wuT_r = wuT.rearrange("(ic p) o -> p ic o", p=128)
    wdT_r = wdT.rearrange("(hc p) d -> p hc d", p=128)

    for tt in range(NTT):
        t0 = tt * TT
        if tt > 0:
            nc.vector.wait_ge(c.tc.h, c.tc.n)
            nc.scalar.wait_ge(c.tc.h, c.tc.n)
        for hh in range(2):
            a, b_ = t0 + hh * 512, t0 + hh * 512 + 512
            ln_fm_512(c, [xi[:, a:b_] for xi in x],
                      [hi[:, hh * 512:hh * 512 + 512] for hi in h])
        h_ready = c.vc.n
        nc.tensor.wait_ge(c.vc.h, h_ready)
        nc.gpsimd.wait_ge(c.vc.h, h_ready)

        wl_at, mm_at, evs_at, evv_at = {}, {}, {}, {}

        def prefetch(oc):
            if oc >= 2:
                nc.gpsimd.wait_ge(c.tc.h, mm_at[oc - 2])
            o0 = oc * 128
            nbuf = wbuf[oc % 2]
            sm = wls[oc % 2]
            c.gdma(nbuf[:, 0], wgT_r[:, :, o0:o0 + 128], sem=sm)
            wl_at[oc] = (sm, c.gdma(nbuf[:, 1], wuT_r[:, :, o0:o0 + 128], sem=sm))

        prefetch(0)
        prefetch(1)
        for oc in range(NOC):
            buf = wbuf[oc % 2]
            nc.tensor.wait_ge(wl_at[oc][0].h, wl_at[oc][1])
            pg, pu = ps_g[oc % 2], ps_u[oc % 2]
            if oc >= 2:
                nc.tensor.wait_ge(c.sc.h, evs_at[oc - 2])
                nc.tensor.wait_ge(c.vc.h, evv_at[oc - 2])
            for hh in range(2):
                sl = slice(hh * 512, hh * 512 + 512)
                if hh == 1:
                    nc.tensor.wait_ge(c.sc.h, c.sc.n)
                    nc.tensor.wait_ge(c.vc.h, c.vc.n)
                for icc in range(8):
                    i = c.T("matmul", pg[:, :], buf[:, 0, icc], h[icc][:, sl],
                            start=(icc == 0), stop=(icc == 7))
                g_done = c.tmark(i)
                for icc in range(8):
                    i = c.T("matmul", pu[:, :], buf[:, 1, icc], h[icc][:, sl],
                            start=(icc == 0), stop=(icc == 7))
                u_done = c.tmark(i)
                nc.scalar.wait_ge(c.tc.h, g_done)
                nc.scalar.wait_ge(c.vc.h, c.vc.n)
                c.S("activation", silu_t[:, :], pg[:, :], AF.Silu, scale=s127)
                silu_done = c.sc.n
                nc.vector.wait_ge(c.tc.h, u_done)
                nc.vector.wait_ge(c.sc.h, silu_done)
                c.V("scalar_tensor_tensor", h2[:, oc, sl], pu[:, :], s127,
                    silu_t[:, :], ALU.mult, ALU.mult)
            mm_at[oc] = c.tc.n
            evs_at[oc] = c.sc.n
            evv_at[oc] = c.vc.n
            if oc + 2 < NOC:
                prefetch(oc + 2)

        h2_all = c.vc.n
        dwl_at, dmm_at, devv_at = {}, {}, {}

        def dprefetch(dc):
            if dc >= 2:
                nc.gpsimd.wait_ge(c.tc.h, dmm_at[dc - 2])
            d0 = dc * 128
            sm = dls[dc % 2]
            dwl_at[dc] = (sm, c.gdma(dbuf[dc % 2][:], wdT_r[:, :, d0:d0 + 128], sem=sm))

        dprefetch(0)
        dprefetch(1)
        nc.tensor.wait_ge(c.vc.h, h2_all)
        for dc in range(8):
            buf = dbuf[dc % 2]
            nc.tensor.wait_ge(dwl_at[dc][0].h, dwl_at[dc][1])
            po = ps_o[dc % 2]
            if dc >= 2:
                nc.tensor.wait_ge(c.vc.h, devv_at[dc - 2])
            for hh in range(2):
                sl = slice(hh * 512, hh * 512 + 512)
                if hh == 1:
                    nc.tensor.wait_ge(c.vc.h, c.vc.n)
                for hc in range(NOC):
                    i = c.T("matmul", po[:, :], buf[:, hc], h2[:, hc, sl],
                            start=(hc == 0), stop=(hc == NOC - 1))
                od = c.tmark(i)
                nc.vector.wait_ge(c.tc.h, od)
                c.V("scalar_tensor_tensor",
                    x[dc][:, t0 + hh * 512:t0 + hh * 512 + 512], po[:, :], 0.5 * s127,
                    x[dc][:, t0 + hh * 512:t0 + hh * 512 + 512], ALU.mult, ALU.add)
            dmm_at[dc] = c.tc.n
            devv_at[dc] = c.vc.n
            if dc + 2 < 8:
                dprefetch(dc + 2)
        c.vc.wait(nc.gpsimd)
        for i in range(8):
            c.gdma(yT[i * 128:(i + 1) * 128, t0:t0 + TT], x[i][:, t0:t0 + TT])
    c.gc.wait(nc.gpsimd)
    c.cur_stack = None
    if own:
        c.end()
        return c.nc
    barrier(c)
    blk.close()
    return None



_cache = {}


def _get(name, builder):
    if name not in _cache:
        _cache[name] = builder()
    return _cache[name]


def run_ffn0(x, wg, wu, wd):
    nc = _get("ffn0", build_ffn0)
    wgT = np.ascontiguousarray(wg.T)
    wuT = np.ascontiguousarray(wu.T)
    wdT = np.ascontiguousarray(wd.T)
    in_maps = []
    for cid in range(N_CORES):
        t0 = cid * TPC
        in_maps.append({"xT": np.ascontiguousarray(x[:, t0:t0 + TPC]),
                        "wgT": wgT, "wuT": wuT, "wdT": wdT})
    res = run_bass_kernel_spmd(nc, in_maps, core_ids=list(range(N_CORES)))
    return np.concatenate([r["yT"] for r in res.results], axis=1)


def kernel(**inp):
    return kernel_fused(**inp)


def kernel_multilaunch(**inp):
    _fp4_cache.clear()
    x = np.ascontiguousarray(inp["L0_attn_x"].reshape(NTOK, D).T)

    x = run_ffn0(x, inp["L0_ffn_w_gate"], inp["L0_ffn_w_up"], inp["L0_ffn_w_down"])
    x = run_deltanet(x, inp)
    x = run_ffn0(x, inp["L0_ffn_w_gate"], inp["L0_ffn_w_up"], inp["L0_ffn_w_down"])
    x = run_fp4_app(x, inp)
    x = run_l1a(x, inp)
    x = run_fp4_app(x, inp)
    x = run_l2f(x, inp)
    x = run_l2a(x, inp)
    x = run_l2f(x, inp)
    x = run_l3f(x, inp)
    x = run_l3a(x, inp)
    x = run_l3f(x, inp)

    return np.ascontiguousarray(x.T).reshape(B, S, D).astype(np.float32)



def build_l3f_route(T=TPC, c=None, io=None):
    own = c is None
    if own:
        c = Ctx("l3f_route")
    nc = c.nc
    blk = c.st if own else ExitStack()
    if own:
        io = NS(xT=c.din("xT", (D, T)), Wr=c.din("Wr", (D, 8)),
                brv=c.din("brv", (8,)), identf=c.din("identf", (128, 128)),
                h_out=c.dout("h_out", (D, T), BF16), oh_out=c.dout("oh_out", (8, T)),
                csum_out=c.dout("csum_out", (8, T)), gate_out=c.dout("gate_out", (T,)),
                counts_out=c.dout("counts_out", (8,)))
    xT, Wr, brv, identf = io.xT, io.Wr, io.brv, io.identf
    h_out, oh_out, csum_out = io.h_out, io.oh_out, io.csum_out
    gate_out, counts_out = io.gate_out, io.counts_out
    c.cur_stack = blk

    xb = [c.sb(f"xb{i}", [128, 8, 512]) for i in range(2)]
    h = c.sb("hb", [128, 8, 512], BF16)
    h32 = c.sb("hb32", [128, 8, 512])
    idf = c.sb("idf", [128, 128])
    wr_sb = c.sb("wr_sb", [128, 8, 8])
    br_row = c.sb("br_row", [1, 8])
    lg_tm = c.sb("lg_tm", [128, 4, 8])
    rowsc = c.sb("rowsc", [128, 8])
    mx = c.sb("mx", [128, 2])
    zs = c.sb("zs", [128, 2])
    oh_fm = c.sb("oh_fm", [8, T])
    cs_fm = c.sb("cs_fm", [8, T])
    ones8 = c.sb("ones8", [8, T])
    cnt = c.sb("cnt", [8, 1])

    ps_tr = c.ps("ps_tr", [128, 8])
    ps_tr2 = c.ps("ps_tr2", [8, 128])
    ps_sums = c.ps("ps_sums", [1, 512])
    ps_bc = c.ps("ps_bc", [128, 512])

    if own:
        c.begin()
    make_ln(c, ps_sums, ps_bc)
    c.make_consts([1e-5])
    xls = [c.sem("xl0"), c.sem("xl1")]
    xl_at = {}

    c.ydma(idf[:], identf[:, :])
    c.ydma(wr_sb[:], Wr.rearrange("(ic p) e -> p ic e", p=128))
    c.ydma(br_row[:], brv.unsqueeze(0))
    c.yc.wait(nc.tensor)
    c.V("memset", ones8[:], 1.0)
    nst = T // 512
    xl_at[0] = c.gdma(xb[0][:], xT.rearrange("(ic p) t -> p ic t", p=128)[:, :, 0:512],
                      sem=xls[0])
    for st in range(nst):
        t0 = st * 512
        if st + 1 < nst:
            if st >= 1:
                nc.gpsimd.wait_ge(c.vc.h, c.vc.n)
            xl_at[st + 1] = c.gdma(xb[(st + 1) % 2][:],
                                   xT.rearrange("(ic p) t -> p ic t", p=128)
                                   [:, :, t0 + 512:t0 + 1024], sem=xls[(st + 1) % 2])
        xv = xb[st % 2]
        nc.tensor.wait_ge(xls[st % 2].h, xl_at[st])
        nc.scalar.wait_ge(xls[st % 2].h, xl_at[st])
        nc.vector.wait_ge(xls[st % 2].h, xl_at[st])
        if st > 0:
            nc.vector.wait_ge(c.tc.h, c.tc.n)
            nc.vector.wait_ge(c.gc.h, c.gc.n)
        ln_fm_512(c, [xv[:, i] for i in range(8)], [h[:, i] for i in range(8)],
                  h32_views=[h32[:, i] for i in range(8)])
        h_ready = c.vc.n
        nc.gpsimd.wait_ge(c.vc.h, h_ready)
        c.gdma(h_out.rearrange("(ic p) t -> p ic t", p=128)[:, :, t0:t0 + 512], h[:])
        nc.tensor.wait_ge(c.vc.h, h_ready)
        for bb in range(4):
            sl = slice(bb * 128, bb * 128 + 128)
            nc.vector.wait_ge(c.gc.h, c.gc.n)
            nc.vector.wait_ge(c.tc.h, c.tc.n)
            nc.scalar.wait_ge(c.tc.h, c.tc.n)
            for icc in range(8):
                c.T("matmul", ps_tr[:, 0:8], h32[:, icc, sl], wr_sb[:, icc],
                    start=(icc == 0), stop=False)
            i = c.T("matmul", ps_tr[:, 0:8], c.ln_ones_row[0:1, sl.start:sl.start + 128]
                    if False else c.ln_ones_row[0:1, 0:128], br_row[0:1, :],
                    start=False, stop=True)
            lgd = c.tmark(i)
            nc.vector.wait_ge(c.tc.h, lgd)
            c.V("tensor_copy", lg_tm[:, bb], ps_tr[:, 0:8])
            c.V("reduce_max", mx[:, 0:1], lg_tm[:, bb], AX.X)
            c.V("tensor_scalar", mx[:, 1:2], mx[:, 0:1], -1.0, None, ALU.mult)
            c.vc.wait(nc.scalar)
            c.S("activation", rowsc[:, :], lg_tm[:, bb], AF.Exp, bias=mx[:, 1:2])
            c.sc.wait(nc.vector)
            c.V("reduce_sum", zs[:, 0:1], rowsc[:, :], AX.X)
            c.V("reciprocal", zs[:, 1:2], zs[:, 0:1])
            c.V("tensor_scalar", rowsc[:, 0:8], lg_tm[:, bb], mx[:, 0:1], None,
                ALU.is_equal)
            oh_ready = c.vc.n
            nc.gpsimd.wait_ge(c.vc.h, oh_ready)
            c.gdma(gate_out.unsqueeze(1)[t0 + bb * 128:t0 + bb * 128 + 128, :],
                   zs[:, 1:2])
            nc.tensor.wait_ge(c.vc.h, oh_ready)
            i = c.T("transpose", ps_tr2[0:8, :], rowsc[:, 0:8], idf[:])
            trd2 = c.tmark(i)
            nc.vector.wait_ge(c.tc.h, trd2)
            c.V("tensor_copy", oh_fm[0:8, t0 + bb * 128:t0 + bb * 128 + 128],
                ps_tr2[0:8, 0:128])
    c.V("tensor_tensor_scan", cs_fm[:, :], ones8[:, :], oh_fm[:, :], 0.0,
        ALU.mult, ALU.add)
    c.V("reduce_sum", cnt[:, :], oh_fm[:, :], AX.X)
    done = c.vc.n
    nc.gpsimd.wait_ge(c.vc.h, done)
    c.gdma(oh_out[:, :], oh_fm[:, :])
    c.gdma(csum_out[:, :], cs_fm[:, :])
    c.gdma(counts_out.unsqueeze(1), cnt[:, :])
    c.gc.wait(nc.gpsimd)
    c.cur_stack = None
    if own:
        c.end()
        return c.nc
    barrier(c)
    blk.close()
    return None


def build_l3f_experts(T=TPC, c=None, io=None):
    own = c is None
    if own:
        c = Ctx("l3f_experts")
    nc = c.nc
    blk = c.st if own else ExitStack()
    if own:
        io = NS(xT=c.din("xT", (D, T)), hT=c.din("hT", (D, T), BF16),
                oh_in=c.din("oh_in", (8, T)), csum_in=c.din("csum_in", (8, T)),
                gate_in=c.din("gate_in", (T,)), counts_ce=c.din("counts_ce", (8, 8)),
                wrow=c.din("wrow", (8,)), W1=c.din("W1", (8, D, D)),
                b1r=c.din("b1r", (128, 8, 8)), W2=c.din("W2", (8, D, D)),
                b2r=c.din("b2r", (128, 8, 8)), selmat=c.din("selmat", (8, 8 * 128)),
                yT=c.dout("yT", (D, T)))
    xT, hT, oh_in, csum_in, gate_in = io.xT, io.hT, io.oh_in, io.csum_in, io.gate_in
    counts_ce, wrow, W1, b1r, W2, b2r = (io.counts_ce, io.wrow, io.W1, io.b1r,
                                         io.W2, io.b2r)
    selmat, yT = io.selmat, io.yT
    c.cur_stack = blk

    TT = 1024
    NTT = T // TT
    x = [c.sb(f"x{i}", [128, T]) for i in range(8)]
    h = [c.sb(f"h{i}", [128, T], BF16) for i in range(8)]
    mid = c.sb("mid", [128, 8, TT], BF16)
    cw = c.sb("cw", [8, T])
    scr8 = c.sb("scr8", [8, T])
    offc = c.sb("offc", [8, 1])
    cnt_sb = c.sb("cnt_sb", [8, 8])
    wrow_sb = c.sb("wrow_sb", [8, 1])
    gate_sb = c.sb("gate_sb", [1, T])
    b1_sb = c.sb("b1_sb", [128, 8, 8])
    b2_sb = c.sb("b2_sb", [128, 8, 8])
    cwb = c.sb("cwb", [128, TT])
    etmp = c.sb("etmp", [128, 512])
    wbuf = [c.sb(f"wb{i}", [128, 8, 128], BF16) for i in range(2)]
    dbuf = [c.sb(f"db{i}", [128, 8, 128], BF16) for i in range(2)]

    ps_m = [c.ps(f"psm{i}", [128, 512]) for i in range(2)]
    ps_y = [c.ps(f"psy{i}", [128, 512]) for i in range(2)]
    ps_off = c.ps("ps_off", [8, 1])
    ps_cwb = c.ps("ps_cwb", [128, 512])
    ps_g8 = c.ps("ps_g8", [8, 512])

    if own:
        c.begin()
    wls = [c.sem("wls0"), c.sem("wls1")]
    dls = [c.sem("dls0"), c.sem("dls1")]
    sel_sb = c.sb("sel_sb", [8, 8 * 128])
    ones18 = c.sb("ones18", [1, 8])
    gate8 = c.sb("gate8", [8, T])

    c.ydma(sel_sb[:], selmat[:, :])
    for i in range(8):
        c.ydma(x[i][:], xT[i * 128:(i + 1) * 128, :])
        c.ydma(h[i][:], hT[i * 128:(i + 1) * 128, :])
    c.ydma(cw[:, :], oh_in[:, :])
    c.ydma(scr8[:, :], csum_in[:, :])
    c.ydma(cnt_sb[:], counts_ce[:, :])
    c.ydma(wrow_sb[:], wrow.unsqueeze(1))
    c.ydma(gate_sb[:], gate_in.unsqueeze(0))
    c.ydma(b1_sb[:], b1r[:, :, :])
    c.ydma(b2_sb[:], b2r[:, :, :])
    c.yc.wait(nc.tensor)
    c.yc.wait(nc.vector)
    c.yc.wait(nc.scalar)

    i = c.T("matmul", ps_off[:, :], cnt_sb[:, 0:8], wrow_sb[:, 0:1],
            start=True, stop=True)
    offd = c.tmark(i)
    nc.vector.wait_ge(c.tc.h, offd)
    c.V("tensor_copy", offc[:, :], ps_off[:, :])
    c.V("memset", ones18[:], 1.0)
    c.vc.wait(nc.tensor)
    for stp in range(T // 512):
        i = c.T("matmul", ps_g8[:, :], ones18[0:1, :],
                gate_sb[0:1, stp * 512:stp * 512 + 512], start=True, stop=True)
        gd = c.tmark(i)
        nc.vector.wait_ge(c.tc.h, gd)
        c.V("tensor_copy", gate8[:, stp * 512:stp * 512 + 512], ps_g8[:, :])
        nc.tensor.wait_ge(c.vc.h, c.vc.n)
    c.V("tensor_scalar", scr8[:, :], scr8[:, :], offc[:, 0:1], None, ALU.add)
    c.V("tensor_single_scalar", scr8[:, :], scr8[:, :], 2048.0, ALU.is_lt)
    c.V("tensor_mul", cw[:, :], cw[:, :], scr8[:, :])
    c.V("tensor_mul", cw[:, :], cw[:, :], gate8[:, :])
    cw_ready = c.vc.n

    W1r = [W1[e].rearrange("(ic p) o -> p ic o", p=128) for e in range(8)]
    W2r = [W2[e].rearrange("(hc p) o -> p hc o", p=128) for e in range(8)]
    first = True
    for e in range(8):
        for tt in range(NTT):
            t0 = tt * TT
            nc.tensor.wait_ge(c.vc.h, cw_ready if first else c.vc.n)
            for hhb in range(2):
                if hhb == 1:
                    nc.tensor.wait_ge(c.vc.h, c.vc.n)
                i = c.T("matmul", ps_cwb[:, :], sel_sb[:, e * 128:(e + 1) * 128],
                        cw[:, t0 + hhb * 512:t0 + hhb * 512 + 512],
                        start=True, stop=True)
                cbd = c.tmark(i)
                nc.vector.wait_ge(c.tc.h, cbd)
                c.V("tensor_copy", cwb[:, hhb * 512:hhb * 512 + 512], ps_cwb[:, :])
            cwb_ready = c.vc.n
            first = False
            wl_at, mm_at, ev_at = {}, {}, {}

            def pre1(oc):
                if oc >= 2:
                    nc.gpsimd.wait_ge(c.tc.h, mm_at[oc - 2])
                sm = wls[oc % 2]
                wl_at[oc] = (sm, c.gdma(wbuf[oc % 2][:], W1r[e][:, :, oc * 128:oc * 128 + 128], sem=sm))

            pre1(0)
            pre1(1)
            for oc in range(8):
                nc.tensor.wait_ge(wl_at[oc][0].h, wl_at[oc][1])
                pm = ps_m[oc % 2]
                if oc >= 2:
                    nc.tensor.wait_ge(c.sc.h, ev_at[oc - 2])
                for hhb in range(2):
                    sl = slice(hhb * 512, hhb * 512 + 512)
                    if hhb == 1:
                        nc.tensor.wait_ge(c.sc.h, c.sc.n)
                    for icc in range(8):
                        i = c.T("matmul", pm[:, :], wbuf[oc % 2][:, icc],
                                h[icc][:, t0 + hhb * 512:t0 + hhb * 512 + 512],
                                start=(icc == 0), stop=(icc == 7))
                    md = c.tmark(i)
                    nc.scalar.wait_ge(c.tc.h, md)
                    c.S("activation", mid[:, oc, sl], pm[:, :], AF.Relu,
                        bias=b1_sb[:, e, oc:oc + 1])
                mm_at[oc] = c.tc.n
                ev_at[oc] = c.sc.n
                if oc + 2 < 8:
                    pre1(oc + 2)
            mid_all = c.sc.n
            dwl_at, dmm_at, dev_at = {}, {}, {}

            def pre2(dc):
                if dc >= 2:
                    nc.gpsimd.wait_ge(c.tc.h, dmm_at[dc - 2])
                sm = dls[dc % 2]
                dwl_at[dc] = (sm, c.gdma(dbuf[dc % 2][:], W2r[e][:, :, dc * 128:dc * 128 + 128], sem=sm))

            pre2(0)
            pre2(1)
            nc.tensor.wait_ge(c.sc.h, mid_all)
            nc.tensor.wait_ge(c.vc.h, cwb_ready)
            for dc in range(8):
                nc.tensor.wait_ge(dwl_at[dc][0].h, dwl_at[dc][1])
                py = ps_y[dc % 2]
                if dc >= 2:
                    nc.tensor.wait_ge(c.vc.h, dev_at[dc - 2])
                for hhb in range(2):
                    sl = slice(hhb * 512, hhb * 512 + 512)
                    if hhb == 1:
                        nc.tensor.wait_ge(c.vc.h, c.vc.n)
                    for hc in range(8):
                        i = c.T("matmul", py[:, :], dbuf[dc % 2][:, hc],
                                mid[:, hc, sl], start=(hc == 0), stop=(hc == 7))
                    yd = c.tmark(i)
                    nc.vector.wait_ge(c.tc.h, yd)
                    c.V("scalar_tensor_tensor", etmp[:, :], py[:, :],
                        b2_sb[:, e, dc:dc + 1], cwb[:, sl], ALU.add, ALU.mult)
                    c.V("scalar_tensor_tensor",
                        x[dc][:, t0 + hhb * 512:t0 + hhb * 512 + 512], etmp[:, :], 0.5,
                        x[dc][:, t0 + hhb * 512:t0 + hhb * 512 + 512], ALU.mult, ALU.add)
                dmm_at[dc] = c.tc.n
                dev_at[dc] = c.vc.n
                if dc + 2 < 8:
                    pre2(dc + 2)
    c.vc.wait(nc.gpsimd)
    for i in range(8):
        c.gdma(yT[i * 128:(i + 1) * 128, :], x[i][:, :])
    c.gc.wait(nc.gpsimd)
    c.cur_stack = None
    if own:
        c.end()
        return c.nc
    barrier(c)
    blk.close()
    return None


def run_l3f(x, inp):
    nc_a = _get("l3f_route", build_l3f_route)
    nc_b = _get("l3f_experts", build_l3f_experts)
    identf = np.eye(128, dtype=np.float32)
    Wr = np.ascontiguousarray(inp["L3_ffn_Wr"])
    brv = np.ascontiguousarray(inp["L3_ffn_br"])
    in_maps = []
    for cid in range(N_CORES):
        t0 = cid * TPC
        in_maps.append({"xT": np.ascontiguousarray(x[:, t0:t0 + TPC]),
                        "Wr": Wr, "brv": brv, "identf": identf})
    res = run_bass_kernel_spmd(nc_a, in_maps, core_ids=list(range(N_CORES)))
    r = res.results
    counts_ce = np.stack([r[cid]["counts_out"] for cid in range(N_CORES)])
    b1r = np.ascontiguousarray(inp["L3_ffn_b1"].reshape(8, 8, 128).transpose(2, 0, 1))
    b2r = np.ascontiguousarray(inp["L3_ffn_b2"].reshape(8, 8, 128).transpose(2, 0, 1))
    W1 = np.ascontiguousarray(inp["L3_ffn_W1"])
    W2 = np.ascontiguousarray(inp["L3_ffn_W2"])
    in_maps = []
    for cid in range(N_CORES):
        t0 = cid * TPC
        wrow = (np.arange(8) < cid).astype(np.float32)
        selmat = np.repeat(np.eye(8, dtype=np.float32), 128, axis=1)
        in_maps.append({"selmat": selmat,
                        "xT": np.ascontiguousarray(x[:, t0:t0 + TPC]),
                        "hT": r[cid]["h_out"],
                        "oh_in": r[cid]["oh_out"],
                        "csum_in": r[cid]["csum_out"],
                        "gate_in": r[cid]["gate_out"],
                        "counts_ce": counts_ce, "wrow": wrow,
                        "W1": W1, "b1r": b1r, "W2": W2, "b2r": b2r})
    res = run_bass_kernel_spmd(nc_b, in_maps, core_ids=list(range(N_CORES)))
    return np.concatenate([rr["yT"] for rr in res.results], axis=1)



def build_l2f(T=TPC, c=None, io=None):
    own = c is None
    if own:
        c = Ctx("l2f")
    nc = c.nc
    blk = c.st if own else ExitStack()
    HID = 4096
    NOC = HID // 128
    if own:
        io = NS(xT=c.din("xT", (D, T)), nwr=c.din("nwr", (128, 8)),
                rwT=c.din("rwT", (D, 8)), sw1=c.din("sw1", (D, HID)),
                sb1r=c.din("sb1r", (128, 32)), sw2=c.din("sw2", (HID, D)),
                sb2r=c.din("sb2r", (128, 8)), ew1=c.din("ew1", (8, D, HID)),
                eb1r=c.din("eb1r", (128, 8, 32)), ew2=c.din("ew2", (8, HID, D)),
                eb2r=c.din("eb2r", (128, 8, 8)), identf=c.din("identf", (128, 128)),
                selmat=c.din("selmat", (8, 8 * 128)), yT=c.dout("yT", (D, T)))
    xT, nwr, rwT, sw1, sb1r, sw2, sb2r = (io.xT, io.nwr, io.rwT, io.sw1, io.sb1r,
                                          io.sw2, io.sb2r)
    ew1, eb1r, ew2, eb2r, identf, selmat, yT = (io.ew1, io.eb1r, io.ew2, io.eb2r,
                                                io.identf, io.selmat, io.yT)
    c.cur_stack = blk

    TT = 1024
    NTT = T // TT
    xb = c.sb("xb", [128, 8, TT])
    h = c.sb("hb", [128, 8, TT], BF16)
    h32 = c.sb("h32b", [128, 8, 512])
    mid = c.sb("mid", [128, NOC, TT], BF16)
    cw = c.sb("cwf", [8, TT])
    cwb = c.sb("cwbb", [128, TT])
    etmp = c.sb("etmp", [128, 512])
    idf = c.sb("idf", [128, 128])
    nw_sb = c.sb("nw_sb", [128, 8])
    rw_sb = c.sb("rw_sb", [128, 8, 8])
    sb1_sb = c.sb("sb1_sb", [128, 32])
    sb2_sb = c.sb("sb2_sb", [128, 8])
    eb1_sb = c.sb("eb1_sb", [128, 8, 32])
    eb2_sb = c.sb("eb2_sb", [128, 8, 8])
    lg_tm = c.sb("lg_tm", [128, 8])
    l2m = c.sb("l2m", [128, 8])
    mx2 = c.sb("mx2", [128, 6])
    cw_tm = c.sb("cw_tm", [128, 8])
    sel_sb2 = c.sb("sel_sb2", [8, 8 * 128])
    wbuf = [c.sb(f"wb{i}", [128, 8, 128], BF16) for i in range(2)]
    dbuf = [c.sb(f"db{i}", [128, 32, 128], BF16) for i in range(2)]

    ps_m = [c.ps(f"psm{i}", [128, 512]) for i in range(2)]
    ps_y = [c.ps(f"psy{i}", [128, 512]) for i in range(2)]
    ps_r = c.ps("ps_r", [128, 8])
    ps_t2 = c.ps("ps_t2", [8, 128])
    ps_sums = c.ps("ps_sums", [1, 512])
    ps_bc = c.ps("ps_bc", [128, 512])

    if own:
        c.begin()
    wls = [c.sem("wls0"), c.sem("wls1")]
    dls = [c.sem("dls0"), c.sem("dls1")]
    xls = c.sem("xls")
    make_ln(c, ps_sums, ps_bc)
    c.make_consts([1e-5, 1e-8])

    c.ydma(idf[:], identf[:, :])
    c.ydma(sel_sb2[:], selmat[:, :])
    c.ydma(nw_sb[:], nwr[:, :])
    c.ydma(rw_sb[:], rwT.rearrange("(ic p) e -> p ic e", p=128))
    c.ydma(sb1_sb[:], sb1r[:, :])
    c.ydma(sb2_sb[:], sb2r[:, :])
    c.ydma(eb1_sb[:], eb1r[:, :, :])
    c.ydma(eb2_sb[:], eb2r[:, :, :])
    c.yc.wait(nc.tensor)
    c.yc.wait(nc.vector)
    c.yc.wait(nc.scalar)

    xT_r = xT.rearrange("(ic p) t -> p ic t", p=128)
    sw1_r = sw1.rearrange("(ic p) o -> p ic o", p=128)
    sw2_r = sw2.rearrange("(hc p) o -> p hc o", p=128)
    ew1_r = [ew1[e].rearrange("(ic p) o -> p ic o", p=128) for e in range(8)]
    ew2_r = [ew2[e].rearrange("(hc p) o -> p hc o", p=128) for e in range(8)]

    for tt in range(NTT):
        t0 = tt * TT
        nc.gpsimd.wait_ge(c.vc.h, c.vc.n)
        nc.gpsimd.wait_ge(c.gc.h, c.gc.n)
        xld = c.gdma(xb[:], xT_r[:, :, t0:t0 + TT], sem=xls)
        for eng in (nc.tensor, nc.vector, nc.scalar):
            eng.wait_ge(xls.h, xld)
        for st in range(2):
            s0 = st * 512
            if st > 0 or tt > 0:
                nc.vector.wait_ge(c.tc.h, c.tc.n)
                nc.scalar.wait_ge(c.tc.h, c.tc.n)
            ln_fm_512(c, [xb[:, i, s0:s0 + 512] for i in range(8)],
                      [h[:, i, s0:s0 + 512] for i in range(8)],
                      h32_views=[h32[:, i] for i in range(8)],
                      rms2_eps=1e-8, colmul=[nw_sb[:, i:i + 1] for i in range(8)])
            h_ready = c.vc.n
            nc.tensor.wait_ge(c.vc.h, h_ready)
            for bb in range(4):
                sl = slice(bb * 128, bb * 128 + 128)
                nc.vector.wait_ge(c.tc.h, c.tc.n)
                for icc in range(8):
                    i = c.T("matmul", ps_r[:, :], h32[:, icc, sl], rw_sb[:, icc],
                            start=(icc == 0), stop=(icc == 7))
                lgd = c.tmark(i)
                nc.vector.wait_ge(c.tc.h, lgd)
                c.V("tensor_copy", lg_tm[:, :], ps_r[:, :])
                c.V("reduce_max", mx2[:, 0:1], lg_tm[:, :], AX.X)
                c.V("tensor_scalar", cw_tm[:, :], lg_tm[:, :], mx2[:, 0:1], None,
                    ALU.is_equal)
                c.V("scalar_tensor_tensor", l2m[:, :], cw_tm[:, :], -1e30,
                    lg_tm[:, :], ALU.mult, ALU.add)
                c.V("reduce_max", mx2[:, 1:2], l2m[:, :], AX.X)
                c.vc.wait(nc.scalar)
                c.S("activation", mx2[:, 2:4], mx2[:, 0:2], AF.Sigmoid)
                c.sc.wait(nc.vector)
                c.V("reduce_sum", mx2[:, 4:5], mx2[:, 2:4], AX.X)
                c.V("tensor_scalar", mx2[:, 4:5], mx2[:, 4:5], 1e-6, None, ALU.add)
                c.V("reciprocal", mx2[:, 5:6], mx2[:, 4:5])
                c.V("tensor_mul", mx2[:, 2:4], mx2[:, 2:4],
                    mx2[:, 5:6].broadcast_to([128, 2]))
                c.V("tensor_scalar", cw_tm[:, :], cw_tm[:, :], mx2[:, 2:3], None,
                    ALU.mult)
                c.V("tensor_scalar", l2m[:, :], l2m[:, :], mx2[:, 1:2], None,
                    ALU.is_equal)
                c.V("tensor_scalar", l2m[:, :], l2m[:, :], mx2[:, 3:4], None,
                    ALU.mult)
                c.V("tensor_add", cw_tm[:, :], cw_tm[:, :], l2m[:, :])
                cwd = c.vc.n
                nc.tensor.wait_ge(c.vc.h, cwd)
                i = c.T("transpose", ps_t2[0:8, :], cw_tm[:, :], idf[:])
                trd = c.tmark(i)
                nc.vector.wait_ge(c.tc.h, trd)
                c.V("tensor_copy", cw[0:8, s0 + bb * 128:s0 + bb * 128 + 128],
                    ps_t2[0:8, 0:128])
        cw_ready = c.vc.n
        nc.gpsimd.wait_ge(c.vc.h, cw_ready)

        for xx in range(-1, 8):
            w1r = sw1_r if xx < 0 else ew1_r[xx]
            w2r = sw2_r if xx < 0 else ew2_r[xx]
            b1c = sb1_sb if xx < 0 else eb1_sb[:, xx]
            b2c = sb2_sb if xx < 0 else eb2_sb[:, xx]
            if xx >= 0:
                nc.tensor.wait_ge(c.vc.h, c.vc.n)
                for hhb in range(2):
                    i = c.T("matmul", ps_bc[:, :], sel_sb2[:, xx * 128:(xx + 1) * 128],
                            cw[:, hhb * 512:hhb * 512 + 512], start=True, stop=True)
                    cbd = c.tmark(i)
                    nc.vector.wait_ge(c.tc.h, cbd)
                    c.V("tensor_copy", cwb[:, hhb * 512:hhb * 512 + 512], ps_bc[:, :])
                    nc.tensor.wait_ge(c.vc.h, c.vc.n)
            cwb_ready = c.vc.n
            wl_at, mm_at, ev_at = {}, {}, {}

            def pre1(oc):
                if oc >= 2:
                    nc.gpsimd.wait_ge(c.tc.h, mm_at[oc - 2])
                sm = wls[oc % 2]
                wl_at[oc] = (sm, c.gdma(wbuf[oc % 2][:], w1r[:, :, oc * 128:oc * 128 + 128], sem=sm))

            pre1(0)
            pre1(1)
            for oc in range(NOC):
                nc.tensor.wait_ge(wl_at[oc][0].h, wl_at[oc][1])
                pm = ps_m[oc % 2]
                if oc >= 2:
                    nc.tensor.wait_ge(c.sc.h, ev_at[oc - 2])
                for hhb in range(2):
                    sl = slice(hhb * 512, hhb * 512 + 512)
                    if hhb == 1:
                        nc.tensor.wait_ge(c.sc.h, c.sc.n)
                    for icc in range(8):
                        i = c.T("matmul", pm[:, :], wbuf[oc % 2][:, icc],
                                h[:, icc, t0 % TT + sl.start:t0 % TT + sl.stop]
                                if False else h[:, icc, sl],
                                start=(icc == 0), stop=(icc == 7))
                    md = c.tmark(i)
                    nc.scalar.wait_ge(c.tc.h, md)
                    c.S("activation", mid[:, oc, sl], pm[:, :], AF.Relu,
                        bias=b1c[:, oc:oc + 1])
                mm_at[oc] = c.tc.n
                ev_at[oc] = c.sc.n
                if oc + 2 < NOC:
                    pre1(oc + 2)
            mid_all = c.sc.n
            dwl_at, dmm_at, dev_at = {}, {}, {}

            def pre2(dc):
                if dc >= 2:
                    nc.gpsimd.wait_ge(c.tc.h, dmm_at[dc - 2])
                sm = dls[dc % 2]
                dwl_at[dc] = (sm, c.gdma(dbuf[dc % 2][:], w2r[:, :, dc * 128:dc * 128 + 128], sem=sm))

            pre2(0)
            pre2(1)
            nc.tensor.wait_ge(c.sc.h, mid_all)
            nc.tensor.wait_ge(c.vc.h, cwb_ready)
            for dc in range(8):
                nc.tensor.wait_ge(dwl_at[dc][0].h, dwl_at[dc][1])
                py = ps_y[dc % 2]
                if dc >= 2:
                    nc.tensor.wait_ge(c.vc.h, dev_at[dc - 2])
                for hhb in range(2):
                    sl = slice(hhb * 512, hhb * 512 + 512)
                    if hhb == 1:
                        nc.tensor.wait_ge(c.vc.h, c.vc.n)
                    for hc in range(NOC):
                        i = c.T("matmul", py[:, :], dbuf[dc % 2][:, hc],
                                mid[:, hc, sl], start=(hc == 0), stop=(hc == NOC - 1))
                    yd = c.tmark(i)
                    nc.vector.wait_ge(c.tc.h, yd)
                    if xx < 0:
                        c.V("tensor_scalar", etmp[:, :], py[:, :], b2c[:, dc:dc + 1],
                            None, ALU.add)
                    else:
                        c.V("scalar_tensor_tensor", etmp[:, :], py[:, :],
                            b2c[:, dc:dc + 1], cwb[:, sl], ALU.add, ALU.mult)
                    c.V("scalar_tensor_tensor", xb[:, dc, sl], etmp[:, :], 0.5,
                        xb[:, dc, sl], ALU.mult, ALU.add)
                dmm_at[dc] = c.tc.n
                dev_at[dc] = c.vc.n
                if dc + 2 < 8:
                    pre2(dc + 2)
        c.vc.wait(nc.gpsimd)
        c.gdma(yT.rearrange("(ic p) t -> p ic t", p=128)[:, :, t0:t0 + TT], xb[:])
    c.gc.wait(nc.gpsimd)
    c.cur_stack = None
    if own:
        c.end()
        return c.nc
    barrier(c)
    blk.close()
    return None


def run_l2f(x, inp):
    nc = _get("l2f", build_l2f)
    identf = np.eye(128, dtype=np.float32)
    nwr = np.ascontiguousarray(inp["L2_ffn_norm_weight"].reshape(8, 128).T)
    rwT = np.ascontiguousarray(inp["L2_ffn_router_w"].T)
    sb1r = np.ascontiguousarray(inp["L2_ffn_sb1"].reshape(32, 128).T)
    sb2r = np.ascontiguousarray(inp["L2_ffn_sb2"].reshape(8, 128).T)
    eb1r = np.ascontiguousarray(inp["L2_ffn_eb1"].reshape(8, 32, 128).transpose(2, 0, 1))
    eb2r = np.ascontiguousarray(inp["L2_ffn_eb2"].reshape(8, 8, 128).transpose(2, 0, 1))
    in_maps = []
    for cid in range(N_CORES):
        t0 = cid * TPC
        in_maps.append({"xT": np.ascontiguousarray(x[:, t0:t0 + TPC]),
                        "selmat": np.repeat(np.eye(8, dtype=np.float32), 128, axis=1),
                        "nwr": nwr, "rwT": rwT,
                        "sw1": inp["L2_ffn_sw1"], "sb1r": sb1r,
                        "sw2": inp["L2_ffn_sw2"], "sb2r": sb2r,
                        "ew1": inp["L2_ffn_ew1"], "eb1r": eb1r,
                        "ew2": inp["L2_ffn_ew2"], "eb2r": eb2r,
                        "identf": identf})
    res = run_bass_kernel_spmd(nc, in_maps, core_ids=list(range(N_CORES)))
    return np.concatenate([r["yT"] for r in res.results], axis=1)



_FP4_MIDS = [0.25, 0.75, 1.25, 1.75, 2.5, 3.5, 5.0]
_FP4_STEPS = [0.5, 0.5, 0.5, 0.5, 1.0, 1.0, 2.0]


def make_quant(c, F_max):
    key = (getattr(c, "prefix", ""), F_max)
    if getattr(c, "_q_key", None) == key:
        return
    c._q_key = key
    c.q_Fmax = F_max
    c.q_xa = c.sb("q_xa", [128, F_max])
    c.q_sc = c.sb("q_sc", [128, F_max // 16])
    c.q_acc = c.sb("q_acc", [128, F_max])
    c.q_ind = c.sb("q_ind", [128, F_max])
    c.q_sgn = c.sb("q_sgn", [128, F_max])


def quant_fp4(c, out_view, in_view, F):
    nc = c.nc
    nb = F // 16
    xa, sc = c.q_xa[:, 0:F], c.q_sc[:, 0:nb]
    acc, ind, sgn = c.q_acc[:, 0:F], c.q_ind[:, 0:F], c.q_sgn[:, 0:F]
    in3 = in_view.rearrange("p (a b) -> p a b", b=16)
    c.vc.wait(nc.scalar)
    c.S("sign", sgn, in_view)
    c.V("tensor_reduce", sc.unsqueeze(2), in3, AX.X, ALU.max,
        apply_absolute_value=True)
    c.V("tensor_scalar_mul", sc, sc, 1.0 / 6.0)
    c.V("tensor_scalar_max", sc, sc, 1e-12)
    c.V("reciprocal", sc, sc)
    c.vc.wait(nc.scalar)
    c.S("activation", xa, in_view, AF.Abs)
    c.sc.wait(nc.vector)
    c.V("tensor_tensor", xa.rearrange("p (a b) -> p a b", b=16),
        xa.rearrange("p (a b) -> p a b", b=16),
        sc.unsqueeze(2).broadcast_to([128, nb, 16]), ALU.mult)
    c.V("memset", acc, 0.0)
    for mid, step in zip(_FP4_MIDS, _FP4_STEPS):
        c.V("tensor_single_scalar", ind, xa, mid, ALU.is_gt)
        c.V("scalar_tensor_tensor", acc, ind, step, acc, ALU.mult, ALU.add)
    c.V("reciprocal", sc, sc)
    c.V("tensor_tensor", acc.rearrange("p (a b) -> p a b", b=16),
        acc.rearrange("p (a b) -> p a b", b=16),
        sc.unsqueeze(2).broadcast_to([128, nb, 16]), ALU.mult)
    c.sc.wait(nc.vector)
    c.V("tensor_mul", out_view, acc, sgn)



N_UNITS = 33


def build_fp4_prep(c=None, io=None):
    own = c is None
    if own:
        c = Ctx("fp4_prep")
    nc = c.nc
    blk = c.st if own else ExitStack()
    if own:
        io = NS(win=c.din("win", (N_UNITS, 128, 256)),
                identb=c.din("identb", (128, 128), BF16),
                wout=c.dout("wout", (N_UNITS, 2, 128, 128), BF16))
    win, identb, wout = io.win, io.identb, io.wout
    c.cur_stack = blk
    u_in = [c.sb(f"u_in{i}", [128, 256]) for i in range(2)]
    u_q = c.sb("u_q", [128, 256], BF16)
    u_tr = c.sb("u_tr", [128, 2, 128], BF16)
    idb = c.sb("idb", [128, 128], BF16)
    ps_tr = c.ps("ps_tr", [128, 128], BF16)
    if own:
        c.begin()
    make_quant(c, 256)
    uls = [c.sem("uls0"), c.sem("uls1")]
    c.ydma(idb[:], identb[:, :])
    c.yc.wait(nc.tensor)
    ul_at = {}
    ul_at[0] = c.gdma(u_in[0][:], win[0], sem=uls[0])
    for u in range(N_UNITS):
        if u + 1 < N_UNITS:
            if u >= 1:
                nc.gpsimd.wait_ge(c.vc.h, c.vc.n)
            ul_at[u + 1] = c.gdma(u_in[(u + 1) % 2][:], win[u + 1],
                                  sem=uls[(u + 1) % 2])
        nc.vector.wait_ge(uls[u % 2].h, ul_at[u])
        nc.scalar.wait_ge(uls[u % 2].h, ul_at[u])
        nc.vector.wait_ge(c.tc.h, c.tc.n)
        nc.vector.wait_ge(c.gc.h, c.gc.n)
        quant_fp4(c, u_q[:, :], u_in[u % 2][:, :], 256)
        qd = c.vc.n
        nc.tensor.wait_ge(c.vc.h, qd)
        for b2 in range(2):
            if u >= 1 or b2 == 1:
                nc.tensor.wait_ge(c.vc.h, c.vc.n)
            i = c.T("transpose", ps_tr[:, :], u_q[:, b2 * 128:b2 * 128 + 128], idb[:])
            trd = c.tmark(i)
            nc.vector.wait_ge(c.tc.h, trd)
            c.V("tensor_copy", u_tr[:, b2], ps_tr[:, :])
        trc = c.vc.n
        nc.gpsimd.wait_ge(c.vc.h, trc)
        c.gdma(wout[u].rearrange("b p o -> p b o"), u_tr[:])
    c.gc.wait(nc.gpsimd)
    c.cur_stack = None
    if own:
        c.end()
        return c.nc
    barrier(c)
    blk.close()
    return None



def build_fp4_app(T=TPC, c=None, io=None, fm_io=False):
    own = c is None
    if own:
        c = Ctx("fp4_app")
    nc = c.nc
    blk = c.st if own else ExitStack()
    HID = 2816
    NHC = HID // 128
    NTB = 2
    STEP = NTB * 128
    if own:
        io = NS(x_in=c.din("x_in", (T, D)), wgqT=c.din("wgqT", (D, HID), BF16),
                wuqT=c.din("wuqT", (D, HID), BF16), wdqT=c.din("wdqT", (HID, D), BF16),
                identb=c.din("identb", (128, 128), BF16), y_out=c.dout("y_out", (T, D)))
    x_in, wgqT, wuqT, wdqT, identb, y_out = (io.x_in, io.wgqT, io.wuqT, io.wdqT,
                                             io.identb, io.y_out)
    c.cur_stack = blk

    x = c.sb("x", [128, NTB, D])
    xfm = c.sb("xfm", [128, 8, NTB * 128]) if fm_io else None
    yst = c.sb("yst", [128, 128], BF16) if fm_io else None
    hq = c.sb("hq", [128, D], BF16)
    hqT = c.sb("hqT", [128, 8, STEP], BF16)
    h2 = c.sb("h2f", [128, NTB, HID])
    h2q = c.sb("h2q", [128, NTB, HID], BF16)
    h2qT = c.sb("h2qT", [128, NHC, STEP], BF16)
    lnt = c.sb("lnt", [128, D])
    lnrows = c.sb("lnrows", [128, 4])
    silu2 = c.sb("silu2", [128, 512])
    idb = c.sb("idb", [128, 128], BF16)
    wb = [c.sb(f"wb{i}", [128, 2, 8, 512], BF16) for i in range(2)]
    db = [c.sb(f"dbz{i}", [128, NHC, 512], BF16) for i in range(2)]

    ps_tr = c.ps("ps_tr", [128, 128], BF16)
    ps_g = c.ps("ps_g", [128, 512])
    ps_u = c.ps("ps_u", [128, 512])
    ps_o = [c.ps(f"ps_o{i}", [128, 512]) for i in range(2)]

    if own:
        c.begin()
    make_quant(c, HID)
    c.make_consts([1e-5])
    xls = c.sem("xls")
    wls = [c.sem("wls0"), c.sem("wls1")]
    dls = [c.sem("dls0"), c.sem("dls1")]
    c.ydma(idb[:], identb[:, :])
    if fm_io:
        idf32 = c.sb("idf32fp4", [128, 128])
        c.ydma(idf32[:], io.identf32[:, :])
    c.yc.wait(nc.tensor)

    NT = T // STEP
    OBS = [(i * 512, min(HID, i * 512 + 512)) for i in range((HID + 511) // 512)]
    wgq_r = wgqT.rearrange("(ic p) o -> p ic o", p=128)
    wuq_r = wuqT.rearrange("(ic p) o -> p ic o", p=128)
    wdq_r = wdqT.rearrange("(hc p) o -> p hc o", p=128)
    xl_at = {}

    def xload(st):
        if not fm_io:
            xl_at[st] = c.gdma(x[:], x_in.rearrange("(n p) d -> p n d", p=128)
                               [:, st * NTB:st * NTB + NTB, :], sem=xls)
        else:
            xl_at[st] = c.gdma(xfm[:], x_in.rearrange("(ic p) t -> p ic t", p=128)
                               [:, :, st * STEP:st * STEP + STEP], sem=xls)

    for st in range(NT):
        if st > 0:
            nc.gpsimd.wait_ge(c.gc.h, c.gc.n)
        xload(st)
        for eng in (nc.tensor, nc.vector, nc.scalar):
            eng.wait_ge(xls.h, xl_at[st])
        if fm_io:
            for tb in range(NTB):
                for icc in range(8):
                    nc.tensor.wait_ge(c.vc.h, c.vc.n)
                    i = c.T("transpose", ps_g[:, 0:128].bitcast(F32)
                            if False else ps_g[:, 0:128],
                            xfm[:, icc, tb * 128:tb * 128 + 128], idf32[:])
                    trd = c.tmark(i)
                    nc.vector.wait_ge(c.tc.h, trd)
                    c.V("tensor_copy", x[:, tb, icc * 128:icc * 128 + 128],
                        ps_g[:, 0:128])
        for tb in range(NTB):
            nc.vector.wait_ge(c.tc.h, c.tc.n)
            xv = x[:, tb, :]
            c.V("reduce_sum", lnrows[:, 0:1], xv, AX.X)
            c.V("tensor_scalar", lnrows[:, 0:1], lnrows[:, 0:1], 1.0 / D, None,
                ALU.mult)
            c.V("tensor_scalar", lnt[:, :], xv, lnrows[:, 0:1], None, ALU.subtract)
            c.vc.wait(nc.scalar)
            c.S("activation", c.q_xa[:, 0:D], lnt[:, :], AF.Square)
            c.sc.wait(nc.vector)
            c.V("reduce_sum", lnrows[:, 1:2], c.q_xa[:, 0:D], AX.X)
            c.V("tensor_scalar", lnrows[:, 1:2], lnrows[:, 1:2], 1.0 / D, None,
                ALU.mult)
            c.vc.wait(nc.scalar)
            c.S("activation", lnrows[:, 2:3], lnrows[:, 1:2], AF.Sqrt, bias=1e-5)
            c.sc.wait(nc.vector)
            c.V("reciprocal", lnrows[:, 3:4], lnrows[:, 2:3])
            c.V("tensor_scalar", lnt[:, :], lnt[:, :], lnrows[:, 3:4], None,
                ALU.mult)
            quant_fp4(c, hq[:, :], lnt[:, :], D)
            qd_ = c.vc.n
            nc.tensor.wait_ge(c.vc.h, qd_)
            for icc in range(8):
                if icc > 0:
                    nc.tensor.wait_ge(c.vc.h, c.vc.n)
                i = c.T("transpose", ps_tr[:, :], hq[:, icc * 128:icc * 128 + 128],
                        idb[:])
                trd = c.tmark(i)
                nc.vector.wait_ge(c.tc.h, trd)
                c.V("tensor_copy", hqT[:, icc, tb * 128:tb * 128 + 128], ps_tr[:, :])
        hqT_all = c.vc.n
        wl_at, mm_at, ev_at = {}, {}, {}

        def pre1(ob):
            if ob >= 2:
                nc.gpsimd.wait_ge(c.tc.h, mm_at[ob - 2])
            o0, o1 = OBS[ob]
            sm = wls[ob % 2]
            c.gdma(wb[ob % 2][:, 0, :, 0:o1 - o0], wgq_r[:, :, o0:o1], sem=sm)
            wl_at[ob] = (sm, c.gdma(wb[ob % 2][:, 1, :, 0:o1 - o0],
                                    wuq_r[:, :, o0:o1], sem=sm))

        pre1(0)
        pre1(1)
        nc.tensor.wait_ge(c.vc.h, hqT_all)
        for ob in range(len(OBS)):
            o0, o1 = OBS[ob]
            ow = o1 - o0
            nc.tensor.wait_ge(wl_at[ob][0].h, wl_at[ob][1])
            for tb in range(NTB):
                if tb > 0 or ob > 0:
                    nc.tensor.wait_ge(c.vc.h, c.vc.n)
                    nc.tensor.wait_ge(c.sc.h, c.sc.n)
                for icc in range(8):
                    i = c.T("matmul", ps_g[:, 0:ow],
                            hqT[:, icc, tb * 128:tb * 128 + 128],
                            wb[ob % 2][:, 0, icc, 0:ow],
                            start=(icc == 0), stop=(icc == 7))
                gd = c.tmark(i)
                for icc in range(8):
                    i = c.T("matmul", ps_u[:, 0:ow], hqT[:, icc, tb * 128:tb * 128 + 128],
                            wb[ob % 2][:, 1, icc, 0:ow],
                            start=(icc == 0), stop=(icc == 7))
                ud = c.tmark(i)
                nc.scalar.wait_ge(c.tc.h, gd)
                nc.scalar.wait_ge(c.vc.h, c.vc.n)
                c.S("activation", silu2[:, 0:ow], ps_g[:, 0:ow], AF.Silu)
                sd = c.sc.n
                nc.vector.wait_ge(c.tc.h, ud)
                nc.vector.wait_ge(c.sc.h, sd)
                c.V("tensor_mul", h2[:, tb, o0:o1], ps_u[:, 0:ow], silu2[:, 0:ow])
            mm_at[ob] = c.tc.n
            ev_at[ob] = c.vc.n
            if ob + 2 < len(OBS):
                pre1(ob + 2)
        for tb in range(NTB):
            nc.vector.wait_ge(c.tc.h, c.tc.n)
            quant_fp4(c, h2q[:, tb, :], h2[:, tb, :], HID)
            qd_ = c.vc.n
            nc.tensor.wait_ge(c.vc.h, qd_)
            for hc in range(NHC):
                if hc > 0:
                    nc.tensor.wait_ge(c.vc.h, c.vc.n)
                i = c.T("transpose", ps_tr[:, :], h2q[:, tb, hc * 128:hc * 128 + 128],
                        idb[:])
                trd = c.tmark(i)
                nc.vector.wait_ge(c.tc.h, trd)
                c.V("tensor_copy", h2qT[:, hc, tb * 128:tb * 128 + 128], ps_tr[:, :])
        h2qT_all = c.vc.n
        dl_at, dmm_at, dev_at = {}, {}, {}

        def pre2(dbk):
            if dbk >= 2:
                nc.gpsimd.wait_ge(c.tc.h, dmm_at[dbk - 2])
            sm = dls[dbk % 2]
            dl_at[dbk] = (sm, c.gdma(db[dbk % 2][:], wdq_r[:, :, dbk * 512:dbk * 512 + 512], sem=sm))

        pre2(0)
        pre2(1)
        nc.tensor.wait_ge(c.vc.h, h2qT_all)
        for dbk in range(2):
            nc.tensor.wait_ge(dl_at[dbk][0].h, dl_at[dbk][1])
            po = ps_o[dbk % 2]
            for tb in range(NTB):
                if tb > 0 or dbk > 0:
                    nc.tensor.wait_ge(c.vc.h, c.vc.n)
                for hc in range(NHC):
                    i = c.T("matmul", po[:, :], h2qT[:, hc, tb * 128:tb * 128 + 128],
                            db[dbk % 2][:, hc, :], start=(hc == 0), stop=(hc == NHC - 1))
                od = c.tmark(i)
                nc.vector.wait_ge(c.tc.h, od)
                c.V("scalar_tensor_tensor", x[:, tb, dbk * 512:dbk * 512 + 512],
                    po[:, :], 0.5, x[:, tb, dbk * 512:dbk * 512 + 512],
                    ALU.mult, ALU.add)
            dmm_at[dbk] = c.tc.n
            dev_at[dbk] = c.vc.n
        yd2 = c.vc.n
        if not fm_io:
            nc.gpsimd.wait_ge(c.vc.h, yd2)
            c.gdma(y_out.rearrange("(n p) d -> p n d", p=128)
                   [:, st * NTB:st * NTB + NTB, :], x[:])
        else:
            nc.tensor.wait_ge(c.vc.h, yd2)
            for tb in range(NTB):
                for icc in range(8):
                    if tb > 0 or icc > 0:
                        nc.tensor.wait_ge(c.vc.h, c.vc.n)
                    i = c.T("transpose", ps_g[:, 0:128],
                            x[:, tb, icc * 128:icc * 128 + 128], idf32[:])
                    trd = c.tmark(i)
                    nc.vector.wait_ge(c.tc.h, trd)
                    c.V("tensor_copy", xfm[:, icc, tb * 128:tb * 128 + 128],
                        ps_g[:, 0:128])
            ydf = c.vc.n
            nc.gpsimd.wait_ge(c.vc.h, ydf)
            c.gdma(y_out.rearrange("(ic p) t -> p ic t", p=128)
                   [:, :, st * STEP:st * STEP + STEP], xfm[:])
    c.gc.wait(nc.gpsimd)
    c.cur_stack = None
    if own:
        c.end()
        return c.nc
    barrier(c)
    blk.close()
    return None


_fp4_cache = {}


def run_fp4_prep(inp):
    if "wq" in _fp4_cache:
        return _fp4_cache["wq"]
    nc = _get("fp4_prep", build_fp4_prep)
    wg, wu, wd = inp["L1_ffn_w_gate"], inp["L1_ffn_w_up"], inp["L1_ffn_w_down"]
    units = []
    for w in (wg, wu, wd):
        r, cdim = w.shape
        u = w.reshape(r // 128, 128, cdim // 256, 256).transpose(0, 2, 1, 3)
        units.append(u.reshape(-1, 128, 256))
    units = np.concatenate(units, axis=0)
    identb = np.eye(128, dtype=BFD)
    in_maps = []
    for cid in range(N_CORES):
        in_maps.append({"win": np.ascontiguousarray(units[cid * N_UNITS:(cid + 1) * N_UNITS]),
                        "identb": identb})
    res = run_bass_kernel_spmd(nc, in_maps, core_ids=list(range(N_CORES)))
    out = np.concatenate([r["wout"] for r in res.results], axis=0)
    def asm(block, r, cdim):
        u = block.reshape(r // 128, cdim // 256, 2, 128, 128)
        u = u.transpose(1, 2, 3, 0, 4)
        return np.ascontiguousarray(u.reshape(cdim, r))
    n1 = (2816 // 128) * (1024 // 256)
    wgqT = asm(out[0:n1], 2816, 1024)
    wuqT = asm(out[n1:2 * n1], 2816, 1024)
    wdqT = asm(out[2 * n1:3 * n1], 1024, 2816)
    _fp4_cache["wq"] = (wgqT, wuqT, wdqT)
    return _fp4_cache["wq"]


def run_fp4_app(x, inp):
    wgqT, wuqT, wdqT = run_fp4_prep(inp)
    nc = _get("fp4_app", build_fp4_app)
    x_tm = np.ascontiguousarray(x.T)
    identb = np.eye(128, dtype=BFD)
    in_maps = []
    for cid in range(N_CORES):
        t0 = cid * TPC
        in_maps.append({"x_in": np.ascontiguousarray(x_tm[t0:t0 + TPC]),
                        "wgqT": wgqT, "wuqT": wuqT, "wdqT": wdqT,
                        "identb": identb})
    res = run_bass_kernel_spmd(nc, in_maps, core_ids=list(range(N_CORES)))
    y = np.concatenate([r["y_out"] for r in res.results], axis=0)
    return np.ascontiguousarray(y.T)



def build_l3a_qkv(T=TPC, c=None, io=None):
    own = c is None
    if own:
        c = Ctx("l3a_qkv")
    nc = c.nc
    blk = c.st if own else ExitStack()
    DL = 256
    if own:
        io = NS(xT=c.din("xT", (D, T)), Wq=c.din("Wq", (D, D)),
                bqr=c.din("bqr", (128, 8)), Wdn=c.din("Wdn", (D, DL)),
                bdnr=c.din("bdnr", (128, 2)), Wupk=c.din("Wupk", (DL, D)),
                bupkr=c.din("bupkr", (128, 8)), Wupv=c.din("Wupv", (DL, D)),
                bupv=c.din("bupv", (D,)), cosF=c.din("cosF", (128, T)),
                sinPM=c.din("sinPM", (128, T)), Pswap=c.din("Pswap", (128, 128)),
                q_out=c.dout("q_out", (D, T), BF16),
                k_out=c.dout("k_out", (D, T), BF16),
                v_out=c.dout("v_out", (T, D), BF16))
    xT, Wq, bqr, Wdn, bdnr = io.xT, io.Wq, io.bqr, io.Wdn, io.bdnr
    Wupk, bupk, Wupv, bupv = io.Wupk, io.bupkr, io.Wupv, io.bupv
    cosF, sinPM, Pswap = io.cosF, io.sinPM, io.Pswap
    q_out, k_out, v_out = io.q_out, io.k_out, io.v_out
    c.cur_stack = blk

    xb = [c.sb(f"xb{i}", [128, 8, 512]) for i in range(2)]
    h = c.sb("hb", [128, 8, 512], BF16)
    wq_sb = c.sb("wq_sb", [128, 8, D], BF16)
    wdn_sb = c.sb("wdn_sb", [128, 8, DL], BF16)
    wupk_sb = c.sb("wupk_sb", [128, 2, D], BF16)
    wupv_sb = c.sb("wupv_sb", [128, 2, D], BF16)
    bq_sb = c.sb("bq_sb", [128, 8])
    bdn_sb = c.sb("bdn_sb", [128, 2])
    bupk_sb = c.sb("bupk_sb", [128, 8])
    bupv_row = c.sb("bupv_row", [1, D], BF16)
    cos_sb = c.sb("cos_sb", [128, T])
    sin_sb = c.sb("sin_sb", [128, T])
    psw_sb = c.sb("psw_sb", [128, 128])
    lat = c.sb("lat", [128, 2, 512], BF16)
    rsc1 = c.sb("rsc1", [128, 512])
    rsc2 = c.sb("rsc2", [128, 512])
    qstore = c.sb("qstore", [128, 512], BF16)
    ones1 = c.sb("ones1b", [1, 128], BF16)

    ps_a = [c.ps(f"ps_a{i}", [128, 512]) for i in range(2)]
    ps_sw = c.ps("ps_sw", [128, 512])
    ps_v = c.ps("ps_v", [128, 512])
    ps_sums = c.ps("ps_sums", [1, 512])
    ps_bc = c.ps("ps_bc", [128, 512])

    if own:
        c.begin()
    make_ln(c, ps_sums, ps_bc)
    c.make_consts([1e-5])
    xls = [c.sem("xls0"), c.sem("xls1")]

    c.gdma(wq_sb[:], Wq.rearrange("(ic p) o -> p ic o", p=128))
    c.gdma(wdn_sb[:], Wdn.rearrange("(ic p) o -> p ic o", p=128))
    c.gdma(wupk_sb[:], Wupk.rearrange("(ic p) o -> p ic o", p=128))
    c.gdma(wupv_sb[:], Wupv.rearrange("(ic p) o -> p ic o", p=128))
    c.gdma(bupv_row[:], bupv.unsqueeze(0))
    c.ydma(bq_sb[:], bqr[:, :])
    c.ydma(bdn_sb[:], bdnr[:, :])
    c.ydma(bupk_sb[:], bupk[:, :])
    c.ydma(cos_sb[:], cosF[:, :])
    c.ydma(sin_sb[:], sinPM[:, :])
    c.ydma(psw_sb[:], Pswap[:, :])
    c.V("memset", ones1[:], 1.0)
    c.gc.wait(nc.tensor)
    c.yc.wait(nc.tensor)
    c.yc.wait(nc.vector)

    xT_r = xT.rearrange("(ic p) t -> p ic t", p=128)
    nst = T // 512
    xl_at = {0: c.gdma(xb[0][:], xT_r[:, :, 0:512], sem=xls[0])}

    def rope_pair(src, dst, t0):
        vv = c.vc.n
        nc.tensor.wait_ge(c.vc.h, vv)
        i = c.T("matmul", ps_sw[:, :], psw_sb[:, :], src, start=True, stop=True)
        swd = c.tmark(i)
        c.V("tensor_mul", rsc1[:, :], src, cos_sb[:, t0:t0 + 512])
        nc.vector.wait_ge(c.tc.h, swd)
        c.V("tensor_mul", rsc2[:, :], ps_sw[:, :], sin_sb[:, t0:t0 + 512])
        c.V("tensor_add", dst, rsc1[:, :], rsc2[:, :])

    for st in range(nst):
        t0 = st * 512
        if st + 1 < nst:
            if st >= 1:
                nc.gpsimd.wait_ge(c.vc.h, c.vc.n)
            xl_at[st + 1] = c.gdma(xb[(st + 1) % 2][:], xT_r[:, :, t0 + 512:t0 + 1024],
                                   sem=xls[(st + 1) % 2])
        xv = xb[st % 2]
        for eng in (nc.tensor, nc.vector, nc.scalar):
            eng.wait_ge(xls[st % 2].h, xl_at[st])
        if st > 0:
            nc.vector.wait_ge(c.tc.h, c.tc.n)
            nc.scalar.wait_ge(c.tc.h, c.tc.n)
        ln_fm_512(c, [xv[:, i] for i in range(8)], [h[:, i] for i in range(8)])
        h_ready = c.vc.n
        nc.tensor.wait_ge(c.vc.h, h_ready)
        for oc in range(8):
            pa = ps_a[oc % 2]
            if oc >= 2:
                nc.tensor.wait_ge(c.vc.h, c.vc.n)
            for icc in range(8):
                i = c.T("matmul", pa[:, :], wq_sb[:, icc, oc * 128:oc * 128 + 128],
                        h[:, icc], start=(icc == 0), stop=(icc == 7))
            qd_ = c.tmark(i)
            nc.vector.wait_ge(c.tc.h, qd_)
            nc.vector.wait_ge(c.gc.h, c.gc.n)
            c.V("tensor_scalar", c.ln_sq[1][:, :], pa[:, :], bq_sb[:, oc:oc + 1],
                None, ALU.add)
            rope_pair(c.ln_sq[1][:, 0:512], qstore[:, 0:512], t0)
            wr = c.vc.n
            nc.gpsimd.wait_ge(c.vc.h, wr)
            c.gdma(q_out[oc * 128:oc * 128 + 128, t0:t0 + 512], qstore[:])
        for oc in range(2):
            pa = ps_a[oc % 2]
            nc.tensor.wait_ge(c.vc.h, c.vc.n)
            for icc in range(8):
                i = c.T("matmul", pa[:, :], wdn_sb[:, icc, oc * 128:oc * 128 + 128],
                        h[:, icc], start=(icc == 0), stop=(icc == 7))
            ld_ = c.tmark(i)
            nc.vector.wait_ge(c.tc.h, ld_)
            c.V("tensor_scalar", lat[:, oc, :], pa[:, :], bdn_sb[:, oc:oc + 1],
                None, ALU.add)
        lat_ready = c.vc.n
        nc.tensor.wait_ge(c.vc.h, lat_ready)
        for oc in range(8):
            pa = ps_a[oc % 2]
            if oc >= 2:
                nc.tensor.wait_ge(c.vc.h, c.vc.n)
            for icc in range(2):
                i = c.T("matmul", pa[:, :], wupk_sb[:, icc, oc * 128:oc * 128 + 128],
                        lat[:, icc], start=(icc == 0), stop=(icc == 1))
            kd = c.tmark(i)
            nc.vector.wait_ge(c.tc.h, kd)
            nc.vector.wait_ge(c.gc.h, c.gc.n)
            c.V("tensor_scalar", c.ln_sq[1][:, :], pa[:, :], bupk_sb[:, oc:oc + 1],
                None, ALU.add)
            rope_pair(c.ln_sq[1][:, 0:512], qstore[:, 0:512], t0)
            wr = c.vc.n
            nc.gpsimd.wait_ge(c.vc.h, wr)
            c.gdma(k_out[oc * 128:oc * 128 + 128, t0:t0 + 512], qstore[:])
        for tb in range(4):
            for ob in range(2):
                nc.tensor.wait_ge(c.vc.h, c.vc.n)
                for icc in range(2):
                    c.T("matmul", ps_v[:, :],
                        lat[:, icc, tb * 128:tb * 128 + 128],
                        wupv_sb[:, icc, ob * 512:ob * 512 + 512],
                        start=(icc == 0), stop=False)
                i = c.T("matmul", ps_v[:, :], ones1[0:1, 0:128],
                        bupv_row[0:1, ob * 512:ob * 512 + 512],
                        start=False, stop=True)
                vd = c.tmark(i)
                nc.vector.wait_ge(c.tc.h, vd)
                nc.vector.wait_ge(c.gc.h, c.gc.n)
                c.V("tensor_copy", qstore[:, :], ps_v[:, :])
                wr = c.vc.n
                nc.gpsimd.wait_ge(c.vc.h, wr)
                c.gdma(v_out[t0 + tb * 128:t0 + tb * 128 + 128,
                             ob * 512:ob * 512 + 512], qstore[:])
    c.gc.wait(nc.gpsimd)
    c.cur_stack = None
    if own:
        c.end()
        return c.nc
    barrier(c)
    blk.close()
    return None


def build_l3a_attn(T=TPC, SQ=S, c=None, io=None, kv_ranked=False):
    own = c is None
    if own:
        c = Ctx("l3a_attn")
    nc = c.nc
    blk = c.st if own else ExitStack()
    if own:
        io = NS(xT=c.din("xT", (D, T)), q_in=c.din("q_in", (D, T), BF16),
                k_in=c.din("k_in", (D, SQ), BF16), v_in=c.din("v_in", (SQ, D), BF16),
                Wout=c.din("Wout", (D, D)), boutr=c.din("boutr", (128, 8)),
                yT=c.dout("yT", (D, T)))
    xT, q_in, k_in, v_in, Wout, boutr, yT = (io.xT, io.q_in, io.k_in, io.v_in,
                                             io.Wout, io.boutr, io.yT)
    c.cur_stack = blk

    NQB = T // 512
    NKB = SQ // 128
    q = c.sb("q", [128, 8, T], BF16)
    kp = c.sb("kp", [128, SQ], BF16)
    vp = c.sb("vp", [128, NKB, 128], BF16)
    pt = [c.sb(f"pt{i}", [128, 512], BF16) for i in range(2)]
    ctx = c.sb("ctx", [128, 8, T], BF16)
    den = c.sb("den", [1, 512])
    deni = c.sb("deni", [1, 512])
    ctf = c.sb("ctf", [128, 512])
    wout_sb = c.sb("wout_sb", [128, 8, D], BF16)
    bout_sb = c.sb("bout_sb", [128, 8])
    xc = [c.sb(f"xc{i}", [128, 512]) for i in range(2)]
    onesb = c.sb("onesb", [128, 1], BF16)
    ones128f = c.sb("ones128f", [1, 128])

    ps_s = [c.ps(f"ps_s{i}", [128, 512]) for i in range(2)]
    ps_acc = c.ps("ps_acc", [128, 512])
    ps_den = c.ps("ps_den", [1, 512])
    ps_bc = c.ps("ps_bc", [128, 512])
    ps_y = [c.ps(f"ps_y{i}", [128, 512]) for i in range(2)]

    if own:
        c.begin()
    c.make_consts([0.125])
    kvl = c.sem("kvl")
    xls = [c.sem("xls0"), c.sem("xls1")]
    sts = [c.sem("sts0"), c.sem("sts1")]
    c.gdma(wout_sb[:], Wout.rearrange("(ic p) o -> p ic o", p=128))
    c.ydma(bout_sb[:], boutr[:, :])
    for i in range(8):
        c.ydma(q[:, i, :], q_in[i * 128:(i + 1) * 128, :])
    c.V("memset", onesb[:], 1.0)
    c.V("memset", ones128f[:], 1.0)
    c.yc.wait(nc.tensor)
    c.gc.wait(nc.tensor)
    c.vc.wait(nc.tensor)

    TLOC = T

    def kv_load(hp):
        if not kv_ranked:
            c.gdma(kp[:], k_in[hp * 128:(hp + 1) * 128, :], sem=kvl)
            return c.gdma(vp[:], v_in.rearrange("(kb p) d -> p kb d", p=128)
                          [:, :, hp * 128:(hp + 1) * 128], sem=kvl)
        for sl_ in range(4):
            pid = c.pid()
            with nc.sync.register(_rn("kvo")) as rg:
                nc.sync.reg_div(rg, pid, 4)
                nc.sync.reg_mul(rg, rg, 4)
                nc.sync.reg_add(rg, rg, sl_)
                nc.sync.reg_mul(rg, rg, D * TLOC)
                nc.sync.reg_add(rg, rg, hp * 128 * TLOC)
                dynk = bass.AP(k_in.tensor, rg, [[TLOC, 128], [1, TLOC]])
                c.ydma(kp[:, sl_ * TLOC:(sl_ + 1) * TLOC], dynk, sem=kvl)
            with nc.sync.register(_rn("kvo2")) as rg2:
                nc.sync.reg_div(rg2, pid, 4)
                nc.sync.reg_mul(rg2, rg2, 4)
                nc.sync.reg_add(rg2, rg2, sl_)
                nc.sync.reg_mul(rg2, rg2, D * TLOC)
                nc.sync.reg_add(rg2, rg2, hp * 128)
                dynv = bass.AP(v_in.tensor, rg2, [[D, 128], [128 * D, TLOC // 128],
                                                  [1, 128]])
                last = c.ydma(vp[:, sl_ * (TLOC // 128):(sl_ + 1) * (TLOC // 128), :],
                              dynv, sem=kvl)
        return last

    kv_at = {}
    kv_at[0] = kv_load(0)

    for hp in range(8):
        nc.tensor.wait_ge(kvl.h, kv_at[hp])
        for h2 in range(2):
            hb = h2 * 64
            for qb in range(NQB):
                q0 = qb * 512
                nc.tensor.wait_ge(c.vc.h, c.vc.n)
                for kvb in range(NKB):
                    pss = ps_s[kvb % 2]
                    if kvb >= 2:
                        nc.tensor.wait_ge(c.sc.h, c.sc.n)
                    i = c.T("matmul", pss[:, :],
                            kp[hb:hb + 64, kvb * 128:kvb * 128 + 128],
                            q[hb:hb + 64, hp, q0:q0 + 512],
                            start=True, stop=True)
                    sd = c.tmark(i)
                    nc.scalar.wait_ge(c.tc.h, sd)
                    c.S("activation", pt[kvb % 2][:, :], pss[:, :], AF.Exp,
                        scale=0.125)
                    pd = c.sc.n
                    nc.tensor.wait_ge(c.sc.h, pd)
                    c.T("matmul", ps_den[:, :], onesb[:, 0:1], pt[kvb % 2][:, :],
                        start=(kvb == 0), stop=(kvb == NKB - 1))
                    i = c.T("matmul", ps_acc[hb:hb + 64, :],
                            vp[:, kvb, hb:hb + 64], pt[kvb % 2][:, :],
                            start=(kvb == 0), stop=(kvb == NKB - 1))
                ad = c.tmark(i)
                nc.vector.wait_ge(c.tc.h, ad)
                c.V("reciprocal", deni[:, :], ps_den[:, :])
                rd = c.vc.n
                nc.tensor.wait_ge(c.vc.h, rd)
                i = c.T("matmul", ps_bc[:, :], ones128f[0:1, :], deni[0:1, :],
                        start=True, stop=True)
                bd = c.tmark(i)
                nc.vector.wait_ge(c.tc.h, bd)
                c.V("tensor_copy", ctf[:, :], ps_bc[:, :])
                c.V("tensor_mul", ctx[hb:hb + 64, hp, q0:q0 + 512],
                    ps_acc[hb:hb + 64, :], ctf[hb:hb + 64, :])
        if hp + 1 < 8:
            nc.gpsimd.wait_ge(c.tc.h, c.tc.n)
            nc.sync.wait_ge(c.tc.h, c.tc.n)
            kv_at[hp + 1] = kv_load(hp + 1)
    ctx_all = c.vc.n
    nc.tensor.wait_ge(c.vc.h, ctx_all)
    done_ev = {}
    st_at = {}
    xl_at2 = {}

    def xload(k_):
        oc, qb = divmod(k_, NQB)
        if k_ >= 2:
            nc.sync.wait_ge(sts[k_ % 2].h, st_at.get(k_ - 2, 0))
        xl_at2[k_] = c.ydma(xc[k_ % 2][:],
                            xT[oc * 128:(oc + 1) * 128, qb * 512:qb * 512 + 512],
                            sem=xls[k_ % 2])

    xload(0)
    xload(1)
    for k_ in range(8 * NQB):
        oc, qb = divmod(k_, NQB)
        py = ps_y[k_ % 2]
        if k_ >= 2:
            nc.tensor.wait_ge(c.vc.h, done_ev[k_ - 2])
        for icc in range(8):
            i = c.T("matmul", py[:, :], wout_sb[:, icc, oc * 128:oc * 128 + 128],
                    ctx[:, icc, qb * 512:qb * 512 + 512],
                    start=(icc == 0), stop=(icc == 7))
        yd = c.tmark(i)
        nc.vector.wait_ge(c.tc.h, yd)
        nc.vector.wait_ge(xls[k_ % 2].h, xl_at2[k_])
        c.V("scalar_tensor_tensor", xc[k_ % 2][:, :], py[:, :],
            bout_sb[:, oc:oc + 1], xc[k_ % 2][:, :], ALU.add, ALU.add)
        done_ev[k_] = c.vc.n
        nc.gpsimd.wait_ge(c.vc.h, done_ev[k_])
        st_at[k_] = c.gdma(yT[oc * 128:(oc + 1) * 128, qb * 512:qb * 512 + 512],
                           xc[k_ % 2][:], sem=sts[k_ % 2])
        if k_ + 2 < 8 * NQB:
            xload(k_ + 2)
    c.gc.wait(nc.gpsimd)
    c.cur_stack = None
    if own:
        c.end()
        return c.nc
    barrier(c)
    blk.close()
    return None


def run_l3a(x, inp):
    nc_a = _get("l3a_qkv", build_l3a_qkv)
    nc_b = _get("l3a_attn", build_l3a_attn)
    bqr = np.ascontiguousarray(inp["L3_attn_bq"].reshape(8, 128).T)
    bdnr = np.ascontiguousarray(inp["L3_attn_bdown"].reshape(2, 128).T)
    Wup = inp["L3_attn_Wup"].reshape(256, 16, 2, 64)
    Wupk = np.ascontiguousarray(Wup[:, :, 0, :].reshape(256, 1024))
    Wupv = np.ascontiguousarray(Wup[:, :, 1, :].reshape(256, 1024))
    bup = inp["L3_attn_bup"].reshape(16, 2, 64)
    bupk = np.ascontiguousarray(bup[:, 0, :].reshape(1024))
    bupv = np.ascontiguousarray(bup[:, 1, :].reshape(1024))
    inv_freq = 1.0 / 10000.0 ** (np.arange(0, 64, 2, dtype=np.float64) / 64.0)
    Pswap = np.zeros((128, 128), np.float32)
    for hh in range(2):
        for j in range(32):
            Pswap[hh * 64 + 32 + j, hh * 64 + j] = 1.0
            Pswap[hh * 64 + j, hh * 64 + 32 + j] = 1.0
    in_maps = []
    for cid in range(N_CORES):
        t0 = cid * TPC
        pos = np.arange(TPC, dtype=np.float64) + (t0 % S)
        fr = pos[:, None] * inv_freq[None, :]
        cos32 = np.cos(fr).astype(np.float32).T
        sin32 = np.sin(fr).astype(np.float32).T
        cosF = np.tile(np.concatenate([cos32, cos32], 0), (2, 1))
        sinPM = np.tile(np.concatenate([-sin32, sin32], 0), (2, 1))
        in_maps.append({"xT": np.ascontiguousarray(x[:, t0:t0 + TPC]),
                        "Wq": inp["L3_attn_Wq"], "bqr": bqr,
                        "Wdn": inp["L3_attn_Wdown"], "bdnr": bdnr,
                        "Wupk": Wupk, "bupkr": np.ascontiguousarray(bupk.reshape(8, 128).T),
                        "Wupv": Wupv, "bupv": bupv,
                        "cosF": np.ascontiguousarray(cosF),
                        "sinPM": np.ascontiguousarray(sinPM), "Pswap": Pswap})
    res = run_bass_kernel_spmd(nc_a, in_maps, core_ids=list(range(N_CORES)))
    r = res.results
    q_all = np.concatenate([rr["q_out"] for rr in r], axis=1)
    k_all = np.concatenate([rr["k_out"] for rr in r], axis=1)
    v_all = np.concatenate([rr["v_out"] for rr in r], axis=0)
    boutr = np.ascontiguousarray(inp["L3_attn_bout"].reshape(8, 128).T)
    in_maps = []
    for cid in range(N_CORES):
        t0 = cid * TPC
        b = t0 // S
        in_maps.append({"xT": np.ascontiguousarray(x[:, t0:t0 + TPC]),
                        "q_in": np.ascontiguousarray(q_all[:, t0:t0 + TPC]),
                        "k_in": np.ascontiguousarray(k_all[:, b * S:(b + 1) * S]),
                        "v_in": np.ascontiguousarray(v_all[b * S:(b + 1) * S]),
                        "Wout": inp["L3_attn_Wout"], "boutr": boutr})
    res = run_bass_kernel_spmd(nc_b, in_maps, core_ids=list(range(N_CORES)))
    return np.concatenate([rr["yT"] for rr in res.results], axis=1)



def build_l1a_qkv(T=TPC, c=None, io=None):
    own = c is None
    if own:
        c = Ctx("l1a_qkv")
    nc = c.nc
    blk = c.st if own else ExitStack()
    if own:
        io = NS(xT=c.din("xT", (D, T)), WqT=c.din("WqT", (D, D)),
                bqc=c.din("bqc", (128, 8)), WkT=c.din("WkT", (D, D)),
                bkc=c.din("bkc", (128, 8)), WvT=c.din("WvT", (D, D)),
                bvr=c.din("bvr", (D,)), W0in=c.din("W0in", (128, 16, 64)),
                identf=c.din("identf", (128, 128)),
                q_out=c.dout("q_out", (D, T)), v_out=c.dout("v_out", (T, D), BF16),
                lg_out=c.dout("lg_out", (T, D)), lmax_out=c.dout("lmax_out", (16,)))
    xT, WqT, bqc, WkT, bkc, WvT, bvr, W0in = (io.xT, io.WqT, io.bqc, io.WkT,
                                              io.bkc, io.WvT, io.bvr, io.W0in)
    q_out, v_out, lg_out, lmax_out = io.q_out, io.v_out, io.lg_out, io.lmax_out
    identf = io.identf
    c.cur_stack = blk

    xb = [c.sb(f"xb{i}", [128, 8, 512]) for i in range(2)]
    h = c.sb("hb", [128, 8, 512], BF16)
    wq_sb = c.sb("wq_sb", [128, 8, D], BF16)
    wk_sb = c.sb("wk_sb", [128, 8, D], BF16)
    wv_sb = c.sb("wv_sb", [128, 8, D], BF16)
    bq_sb = c.sb("bq_sb", [128, 8])
    bk_sb = c.sb("bk_sb", [128, 8])
    bv_row = c.sb("bv_row", [1, D], BF16)
    w0_sb = c.sb("w0_sb", [128, 16, 64])
    nhalf = c.sb("nhalf", [128, 64])
    kf = c.sb("kf", [128, 8, 512])
    k2 = c.sb("k2", [128, 8, 512])
    lgt = c.sb("lgt", [128, 1024])
    lmax = c.sb("lmax", [128, 16])
    lrow = c.sb("lrow", [16, 128])
    evb = c.sb("evb", [128, 512])
    ones1 = c.sb("ones1b", [1, 128], BF16)
    idf = c.sb("idf2", [128, 128])

    ps_a = [c.ps(f"ps_a{i}", [128, 512]) for i in range(2)]
    ps_l = c.ps("ps_l", [128, 64])
    ps_t = c.ps("ps_t", [16, 128])
    ps_sums = c.ps("ps_sums", [1, 512])
    ps_bc = c.ps("ps_bc", [128, 512])

    if own:
        c.begin()
    make_ln(c, ps_sums, ps_bc)
    c.make_consts([1e-5])
    xls = [c.sem("xls0"), c.sem("xls1")]

    c.gdma(wq_sb[:], WqT.rearrange("(ic p) o -> p ic o", p=128))
    c.gdma(wk_sb[:], WkT.rearrange("(ic p) o -> p ic o", p=128))
    c.gdma(wv_sb[:], WvT.rearrange("(ic p) o -> p ic o", p=128))
    c.gdma(bv_row[:], bvr.unsqueeze(0))
    c.ydma(bq_sb[:], bqc[:, :])
    c.ydma(bk_sb[:], bkc[:, :])
    c.ydma(w0_sb[:], W0in[:, :, :])
    c.ydma(idf[:], identf[:, :])
    c.V("memset", ones1[:], 1.0)
    c.V("memset", nhalf[:], -0.5)
    c.V("memset", lmax[:], -1e30)
    c.gc.wait(nc.tensor)
    c.yc.wait(nc.tensor)
    c.yc.wait(nc.vector)

    xT_r = xT.rearrange("(ic p) t -> p ic t", p=128)
    nst = T // 512
    xl_at = {0: c.gdma(xb[0][:], xT_r[:, :, 0:512], sem=xls[0])}
    for st in range(nst):
        t0 = st * 512
        if st + 1 < nst:
            if st >= 1:
                nc.gpsimd.wait_ge(c.vc.h, c.vc.n)
            xl_at[st + 1] = c.gdma(xb[(st + 1) % 2][:], xT_r[:, :, t0 + 512:t0 + 1024],
                                   sem=xls[(st + 1) % 2])
        xv = xb[st % 2]
        for eng in (nc.tensor, nc.vector, nc.scalar):
            eng.wait_ge(xls[st % 2].h, xl_at[st])
        if st > 0:
            nc.vector.wait_ge(c.tc.h, c.tc.n)
            nc.scalar.wait_ge(c.tc.h, c.tc.n)
        ln_fm_512(c, [xv[:, i] for i in range(8)], [h[:, i] for i in range(8)])
        h_ready = c.vc.n
        nc.tensor.wait_ge(c.vc.h, h_ready)
        for oc in range(8):
            pa = ps_a[oc % 2]
            if oc >= 2:
                nc.tensor.wait_ge(c.vc.h, c.vc.n)
            for icc in range(8):
                i = c.T("matmul", pa[:, :], wq_sb[:, icc, oc * 128:oc * 128 + 128],
                        h[:, icc], start=(icc == 0), stop=(icc == 7))
            qd_ = c.tmark(i)
            nc.vector.wait_ge(c.tc.h, qd_)
            nc.vector.wait_ge(c.gc.h, c.gc.n)
            c.V("tensor_scalar", evb[:, :], pa[:, :], bq_sb[:, oc:oc + 1], None, ALU.add)
            wr = c.vc.n
            nc.gpsimd.wait_ge(c.vc.h, wr)
            c.gdma(q_out[oc * 128:oc * 128 + 128, t0:t0 + 512], evb[:])
        for tb in range(4):
            for ob in range(2):
                nc.tensor.wait_ge(c.vc.h, c.vc.n)
                for icc in range(8):
                    c.T("matmul", ps_a[0][:, :], h[:, icc, tb * 128:tb * 128 + 128],
                        wv_sb[:, icc, ob * 512:ob * 512 + 512],
                        start=(icc == 0), stop=False)
                i = c.T("matmul", ps_a[0][:, :], ones1[0:1, 0:128],
                        bv_row[0:1, ob * 512:ob * 512 + 512], start=False, stop=True)
                vd = c.tmark(i)
                nc.vector.wait_ge(c.tc.h, vd)
                nc.vector.wait_ge(c.gc.h, c.gc.n)
                c.V("tensor_copy", evb[:, :].bitcast(BF16)[:, 0:512], ps_a[0][:, :])
                wr = c.vc.n
                nc.gpsimd.wait_ge(c.vc.h, wr)
                c.gdma(v_out[t0 + tb * 128:t0 + tb * 128 + 128,
                             ob * 512:ob * 512 + 512], evb[:, :].bitcast(BF16)[:, 0:512])
        for oc in range(8):
            pa = ps_a[oc % 2]
            nc.tensor.wait_ge(c.vc.h, c.vc.n)
            nc.tensor.wait_ge(c.sc.h, c.sc.n)
            for icc in range(8):
                i = c.T("matmul", pa[:, :], wk_sb[:, icc, oc * 128:oc * 128 + 128],
                        h[:, icc], start=(icc == 0), stop=(icc == 7))
            kd = c.tmark(i)
            nc.vector.wait_ge(c.tc.h, kd)
            c.V("tensor_scalar", kf[:, oc, :], pa[:, :], bk_sb[:, oc:oc + 1], None, ALU.add)
            c.vc.wait(nc.scalar)
            c.S("activation", k2[:, oc, :], kf[:, oc, :], AF.Square)
        kk_ready = c.sc.n
        nc.tensor.wait_ge(c.sc.h, kk_ready)
        for tb in range(4):
            nc.vector.wait_ge(c.gc.h, c.gc.n)
            for hh in range(16):
                ch, hb = hh // 2, (hh % 2) * 64
                sl = slice(tb * 128, tb * 128 + 128)
                if tb > 0 or hh > 0:
                    nc.tensor.wait_ge(c.vc.h, c.vc.n)
                c.T("matmul", ps_l[:, :],
                    kf[hb:hb + 64, ch, sl], w0_sb[hb:hb + 64, hh, :],
                    start=True, stop=False)
                i = c.T("matmul", ps_l[:, :],
                        k2[hb:hb + 64, ch, sl], nhalf[hb:hb + 64, :],
                        start=False, stop=True)
                lgd = c.tmark(i)
                nc.vector.wait_ge(c.tc.h, lgd)
                c.V("tensor_copy", lgt[:, hh * 64:hh * 64 + 64], ps_l[:, :])
            c.V("tensor_reduce", evb[:, 0:16].unsqueeze(2),
                lgt[:, :].rearrange("p (hh f) -> p hh f", f=64), AX.X, ALU.max)
            c.V("tensor_max", lmax[:, :], lmax[:, :], evb[:, 0:16])
            lg_ready = c.vc.n
            nc.gpsimd.wait_ge(c.vc.h, lg_ready)
            c.gdma(lg_out[t0 + tb * 128:t0 + tb * 128 + 128, :], lgt[:])
    lm_ready = c.vc.n
    nc.tensor.wait_ge(c.vc.h, lm_ready)
    i = c.T("transpose", ps_t[:, :], lmax[:, :], idf[:])
    td = c.tmark(i)
    nc.vector.wait_ge(c.tc.h, td)
    c.V("tensor_copy", lrow[:, :], ps_t[:, :])
    c.V("tensor_reduce", lmax[0:16, 0:1], lrow[:, :], AX.X, ALU.max)
    fm_ = c.vc.n
    nc.gpsimd.wait_ge(c.vc.h, fm_)
    c.gdma(lmax_out.unsqueeze(1), lmax[0:16, 0:1])
    c.gc.wait(nc.gpsimd)
    c.cur_stack = None
    if own:
        c.end()
        return c.nc
    barrier(c)
    blk.close()
    return None


def build_l1a_nd(T=TPC, c=None, io=None):
    own = c is None
    if own:
        c = Ctx("l1a_nd")
    nc = c.nc
    blk = c.st if own else ExitStack()
    if own:
        io = NS(lg_in=c.din("lg_in", (T, D)), v_in=c.din("v_in", (T, D), BF16),
                lmax8=c.din("lmax8", (16, 4)), np_out=c.dout("np_out", (64, D)),
                dp_out=c.dout("dp_out", (64, 16)), bmask=None)
    lg_in, v_in, lmax8 = io.lg_in, io.v_in, io.lmax8
    np_out, dp_out = io.np_out, io.dp_out
    bmask = io.bmask
    c.cur_stack = blk

    NTB = T // 128
    alpha = c.sb("alpha", [128, NTB, D], BF16)
    vt = c.sb("vt", [128, NTB, D], BF16)
    lgt = [c.sb(f"lgt{i}", [128, D]) for i in range(2)]
    NRK = 4 if bmask is None else 8
    lm_sb = c.sb("lm_sb", [1, NRK, 16] if bmask is not None else [1, 16, 4])
    bm_sb = c.sb("bm_sb", [1, 8])
    gmx = c.sb("gmx", [1, 16])
    gmaxr = c.sb("gmaxr", [1, D])
    gmb = c.sb("gmb", [128, D])
    sub = c.sb("sub", [128, D])
    n_sb = c.sb("n_sb", [64, D])
    d_sb = c.sb("d_sb", [64, 16])
    onesc = c.sb("onesc", [128, 1], BF16)
    ones1f = c.sb("ones1f", [1, 128])

    ps_g = c.ps("ps_g", [128, 512])
    ps_n = c.ps("ps_n", [64, 64])
    ps_d = c.ps("ps_d", [64, 1])

    if own:
        c.begin()
    lls = [c.sem("lls0"), c.sem("lls1")]
    vl = c.sem("vl")
    c.ydma(lm_sb[:], lmax8.unsqueeze(0))
    if bmask is not None:
        c.ydma(bm_sb[:], bmask)
    c.V("memset", onesc[:], 1.0)
    c.V("memset", ones1f[:], 1.0)
    c.yc.wait(nc.vector)
    if bmask is None:
        c.V("tensor_reduce", gmx[0:1, :].unsqueeze(2), lm_sb[:, :, :], AX.X, ALU.max)
    else:
        c.V("tensor_scalar", bm_sb[0:1, :], bm_sb[0:1, :], 1e30, -1e30,
            ALU.mult, ALU.add)
        c.V("tensor_tensor", lm_sb[0:1, :, :], lm_sb[0:1, :, :],
            bm_sb[0:1, :].unsqueeze(2).broadcast_to([1, 8, 16]), ALU.add)
        c.V("tensor_reduce", gmx[0:1, :].unsqueeze(2),
            lm_sb[0:1, 0:8, :].transpose([0, 2, 1]), AX.X, ALU.max)
    c.V("tensor_copy", gmaxr[0:1, :].rearrange("a (h f) -> a h f", f=64),
        gmx[0:1, :].unsqueeze(2).broadcast_to([1, 16, 64]))
    gm_ready = c.vc.n
    nc.tensor.wait_ge(c.vc.h, gm_ready)
    for bb in range(2):
        if bb == 1:
            nc.tensor.wait_ge(c.vc.h, c.vc.n)
        i = c.T("matmul", ps_g[:, :], ones1f[0:1, :],
                gmaxr[0:1, bb * 512:bb * 512 + 512], start=True, stop=True)
        gd = c.tmark(i)
        nc.vector.wait_ge(c.tc.h, gd)
        c.V("tensor_copy", gmb[:, bb * 512:bb * 512 + 512], ps_g[:, :])
    lg_r = lg_in.rearrange("(n p) d -> p n d", p=128)
    vld = c.gdma(vt[:], v_in.rearrange("(n p) d -> p n d", p=128), sem=vl)
    ll_at = {0: c.gdma(lgt[0][:], lg_r[:, 0, :], sem=lls[0])}
    for tb in range(NTB):
        if tb + 1 < NTB:
            if tb >= 1:
                nc.gpsimd.wait_ge(c.sc.h, c.sc.n)
            ll_at[tb + 1] = c.gdma(lgt[(tb + 1) % 2][:], lg_r[:, tb + 1, :],
                                   sem=lls[(tb + 1) % 2])
        nc.vector.wait_ge(lls[tb % 2].h, ll_at[tb])
        nc.vector.wait_ge(c.sc.h, c.sc.n)
        c.V("tensor_sub", sub[:, :], lgt[tb % 2][:, :], gmb[:, :])
        c.vc.wait(nc.scalar)
        c.S("activation", alpha[:, tb, :], sub[:, :], AF.Exp)
    al_ready = c.sc.n
    nc.tensor.wait_ge(c.sc.h, al_ready)
    nc.tensor.wait_ge(vl.h, vld)
    for hh in range(16):
        for tb in range(NTB):
            c.T("matmul", ps_n[:, 0:64],
                alpha[:, tb, hh * 64:hh * 64 + 64], vt[:, tb, hh * 64:hh * 64 + 64],
                start=(tb == 0), stop=(tb == NTB - 1))
        for tb in range(NTB):
            i = c.T("matmul", ps_d[:, 0:1],
                    alpha[:, tb, hh * 64:hh * 64 + 64], onesc[:, 0:1],
                    start=(tb == 0), stop=(tb == NTB - 1))
        nd = c.tmark(i)
        nc.vector.wait_ge(c.tc.h, nd)
        c.V("tensor_copy", n_sb[:, hh * 64:hh * 64 + 64], ps_n[:, 0:64])
        c.V("tensor_copy", d_sb[:, hh:hh + 1], ps_d[:, 0:1])
        nc.tensor.wait_ge(c.vc.h, c.vc.n)
    done = c.vc.n
    nc.gpsimd.wait_ge(c.vc.h, done)
    c.gdma(np_out[:, :], n_sb[:, :])
    c.gdma(dp_out[:, :], d_sb[:, :])
    c.gc.wait(nc.gpsimd)
    c.cur_stack = None
    if own:
        c.end()
        return c.nc
    barrier(c)
    blk.close()
    return None


def build_l1a_out(T=TPC, c=None, io=None):
    own = c is None
    if own:
        c = Ctx("l1a_out")
    nc = c.nc
    blk = c.st if own else ExitStack()
    if own:
        io = NS(xT=c.din("xT", (D, T)), q_in=c.din("q_in", (D, T)),
                np_in=c.din("np_in", (4, 64, D)), dp_in=c.din("dp_in", (4, 64, 16)),
                W0in=c.din("W0in", (128, 16, 64)), WoT=c.din("WoT", (D, D)),
                boc=c.din("boc", (128, 8)), yT=c.dout("yT", (D, T)), bmaskc=None)
    xT, q_in, np_in, dp_in, W0in, WoT, boc, yT = (io.xT, io.q_in, io.np_in,
                                                  io.dp_in, io.W0in, io.WoT,
                                                  io.boc, io.yT)
    bmaskc = io.bmaskc
    NRK = 4 if bmaskc is None else 8
    c.cur_stack = blk

    q = c.sb("q", [128, 8, T])
    ctx = c.sb("ctx", [128, 8, T], BF16)
    ng = c.sb("ng", [64, NRK, D])
    dg = c.sb("dg", [64, NRK, 16])
    bmc_sb = c.sb("bmc_sb", [128, 8])
    ngb = c.sb("ngb", [64, D], BF16)
    dgb = c.sb("dgb", [64, 16], BF16)
    rz = c.sb("rz", [1, 16])
    rzrow = c.sb("rzrow", [1, D])
    w0_sb = c.sb("w0_sb", [128, 16, 64])
    wo_sb = c.sb("wo_sb", [128, 8, D], BF16)
    bo_sb = c.sb("bo_sb", [128, 8])
    phl = c.sb("phl", [128, 512])
    phm = c.sb("phm", [1, 512])
    phb = c.sb("phb", [128, 512])
    phi = c.sb("phi", [128, 512], BF16)
    ones64b = c.sb("ones64b", [64, 1], BF16)
    sphi = c.sb("sphi", [1, 512])
    dnr = c.sb("dnr", [1, 512])
    dnb = c.sb("dnb", [128, 512])
    xc = [c.sb(f"xc{i}", [128, 512]) for i in range(2)]
    ones64 = c.sb("ones64", [64, 1])
    ones1f = c.sb("ones1f", [1, 128])

    ps_ph = c.ps("ps_ph", [128, 512])
    ps_m = c.ps("ps_m", [1, 512])
    ps_b = c.ps("ps_b", [128, 512])
    ps_n2 = c.ps("ps_n2", [128, 512])
    ps_d2 = c.ps("ps_d2", [1, 512])
    ps_z = c.ps("ps_z", [1, 16])
    ps_y = [c.ps(f"ps_y{i}", [128, 512]) for i in range(2)]

    if own:
        c.begin()
    xls = [c.sem("xls0"), c.sem("xls1")]
    sts = [c.sem("sts0"), c.sem("sts1")]
    for i in range(8):
        c.ydma(q[:, i, :], q_in[i * 128:(i + 1) * 128, :])
    for cc in range(NRK):
        c.ydma(ng[:, cc, :], np_in[cc])
    for cc in range(NRK):
        c.ydma(dg[:, cc, :], dp_in[cc])
    if bmaskc is not None:
        c.ydma(bmc_sb[:], bmaskc)
    c.ydma(w0_sb[:], W0in[:, :, :])
    c.gdma(wo_sb[:], WoT.rearrange("(ic p) o -> p ic o", p=128))
    c.ydma(bo_sb[:], boc[:, :])
    c.V("memset", ones64[:], 1.0)
    c.V("memset", ones64b[:], 1.0)
    c.V("memset", ones1f[:], 1.0)
    c.yc.wait(nc.vector)
    c.gc.wait(nc.tensor)
    if bmaskc is None:
        c.V("tensor_add", ng[:, 0, :], ng[:, 0, :], ng[:, 1, :])
        c.V("tensor_add", ng[:, 2, :], ng[:, 2, :], ng[:, 3, :])
        c.V("tensor_add", ng[:, 0, :], ng[:, 0, :], ng[:, 2, :])
        c.V("tensor_add", dg[:, 0, :], dg[:, 0, :], dg[:, 1, :])
        c.V("tensor_add", dg[:, 2, :], dg[:, 2, :], dg[:, 3, :])
        c.V("tensor_add", dg[:, 0, :], dg[:, 0, :], dg[:, 2, :])
    else:
        c.V("tensor_scalar", ng[:, 0, :], ng[:, 0, :], bmc_sb[0:64, 0:1],
            None, ALU.mult)
        c.V("tensor_scalar", dg[:, 0, :], dg[:, 0, :], bmc_sb[0:64, 0:1],
            None, ALU.mult)
        for cc in range(1, 8):
            c.V("scalar_tensor_tensor", ng[:, 0, :], ng[:, cc, :],
                bmc_sb[0:64, cc:cc + 1], ng[:, 0, :], ALU.mult, ALU.add)
            c.V("scalar_tensor_tensor", dg[:, 0, :], dg[:, cc, :],
                bmc_sb[0:64, cc:cc + 1], dg[:, 0, :], ALU.mult, ALU.add)
    nd_ready = c.vc.n
    nc.tensor.wait_ge(c.vc.h, nd_ready)
    i = c.T("matmul", ps_z[0:1, :], ones64[:, 0:1], dg[:, 0, :], start=True, stop=True)
    zd = c.tmark(i)
    nc.vector.wait_ge(c.tc.h, zd)
    c.V("reciprocal", rz[0:1, :], ps_z[0:1, :])
    c.V("tensor_copy", rzrow[0:1, :].rearrange("a (h f) -> a h f", f=64),
        rz[0:1, :].unsqueeze(2).broadcast_to([1, 16, 64]))
    rz_ready = c.vc.n
    nc.tensor.wait_ge(c.vc.h, rz_ready)
    for bb in range(2):
        if bb == 1:
            nc.tensor.wait_ge(c.vc.h, c.vc.n)
        i = c.T("matmul", ps_b[:, :], ones1f[0:1, :],
                rzrow[0:1, bb * 512:bb * 512 + 512], start=True, stop=True)
        bd = c.tmark(i)
        nc.vector.wait_ge(c.tc.h, bd)
        c.V("tensor_tensor", ngb[:, bb * 512:bb * 512 + 512], ng[:, 0, bb * 512:bb * 512 + 512],
            ps_b[0:64, :], ALU.mult)
    nc.tensor.wait_ge(c.vc.h, c.vc.n)
    i = c.T("matmul", ps_b[0:64, 0:16], ones1f[0:1, 0:64], rz[0:1, 0:16],
            start=True, stop=True)
    bd = c.tmark(i)
    nc.vector.wait_ge(c.tc.h, bd)
    c.V("tensor_tensor", dgb[:, :], dg[:, 0, :], ps_b[0:64, 0:16], ALU.mult)
    nd2 = c.vc.n
    nc.tensor.wait_ge(c.vc.h, nd2)

    NTB = T // 128
    for tb in range(NTB):
        for hh in range(16):
            ch, hb = hh // 2, (hh % 2) * 64
            if not (tb == 0 and hh == 0):
                nc.tensor.wait_ge(c.vc.h, c.vc.n)
                nc.tensor.wait_ge(c.sc.h, c.sc.n)
            i = c.T("matmul", ps_ph[0:64, 0:128], w0_sb[hb:hb + 64, hh, :],
                    q[hb:hb + 64, ch, tb * 128:tb * 128 + 128], start=True, stop=True)
            pld = c.tmark(i)
            nc.vector.wait_ge(c.tc.h, pld)
            c.V("tensor_copy", phl[0:64, 0:128], ps_ph[0:64, 0:128])
            pev = c.vc.n
            nc.tensor.wait_ge(c.vc.h, pev)
            i = c.T("matmul", ps_m[0:1, 0:128], ones64[:, 0:1], phl[0:64, 0:128],
                    start=True, stop=True)
            md = c.tmark(i)
            nc.vector.wait_ge(c.tc.h, md)
            c.V("tensor_scalar", phm[0:1, 0:128], ps_m[0:1, 0:128], -1.0 / 64, None,
                ALU.mult)
            mr = c.vc.n
            nc.tensor.wait_ge(c.vc.h, mr)
            i = c.T("matmul", ps_b[0:64, 0:128], ones1f[0:1, 0:64], phm[0:1, 0:128],
                    start=True, stop=True)
            bd2 = c.tmark(i)
            nc.vector.wait_ge(c.tc.h, bd2)
            c.V("tensor_tensor", phl[0:64, 0:128], phl[0:64, 0:128], ps_b[0:64, 0:128],
                ALU.add)
            c.vc.wait(nc.scalar)
            c.S("activation", phi[0:64, 0:128], phl[0:64, 0:128], AF.Exp)
            phid = c.sc.n
            nc.tensor.wait_ge(c.sc.h, phid)
            c.T("matmul", ps_n2[hb:hb + 64, 0:128], ngb[:, hh * 64:hh * 64 + 64],
                phi[0:64, 0:128], start=True, stop=True)
            c.T("matmul", ps_d2[0:1, 0:128], dgb[:, hh:hh + 1], phi[0:64, 0:128],
                start=True, stop=True)
            i = c.T("matmul", ps_m[0:1, 0:128], ones64b[:, 0:1], phi[0:64, 0:128],
                    start=True, stop=True)
            numd = c.tmark(i)
            nc.vector.wait_ge(c.tc.h, numd)
            c.V("tensor_copy", sphi[0:1, 0:128], ps_m[0:1, 0:128])
            c.V("scalar_tensor_tensor", dnr[0:1, 0:128], sphi[0:1, 0:128], 1e-6,
                ps_d2[0:1, 0:128], ALU.mult, ALU.add)
            c.V("reciprocal", dnr[0:1, 0:128], dnr[0:1, 0:128])
            dnd = c.vc.n
            nc.tensor.wait_ge(c.vc.h, dnd)
            i = c.T("matmul", ps_b[:, 0:128], ones1f[0:1, :], dnr[0:1, 0:128],
                    start=True, stop=True)
            bd3 = c.tmark(i)
            nc.vector.wait_ge(c.tc.h, bd3)
            c.V("tensor_copy", dnb[:, 0:128], ps_b[:, 0:128])
            c.V("tensor_mul", ctx[hb:hb + 64, ch, tb * 128:tb * 128 + 128],
                ps_n2[hb:hb + 64, 0:128], dnb[hb:hb + 64, 0:128])
    ctx_all = c.vc.n
    nc.tensor.wait_ge(c.vc.h, ctx_all)
    NQB = T // 512
    done_ev, st_at, xl_at2 = {}, {}, {}

    def xload(k_):
        oc, qb = divmod(k_, NQB)
        if k_ >= 2:
            nc.sync.wait_ge(sts[k_ % 2].h, st_at.get(k_ - 2, 0))
        xl_at2[k_] = c.ydma(xc[k_ % 2][:],
                            xT[oc * 128:(oc + 1) * 128, qb * 512:qb * 512 + 512],
                            sem=xls[k_ % 2])

    xload(0)
    xload(1)
    for k_ in range(8 * NQB):
        oc, qb = divmod(k_, NQB)
        py = ps_y[k_ % 2]
        if k_ >= 2:
            nc.tensor.wait_ge(c.vc.h, done_ev[k_ - 2])
        for icc in range(8):
            i = c.T("matmul", py[:, :], wo_sb[:, icc, oc * 128:oc * 128 + 128],
                    ctx[:, icc, qb * 512:qb * 512 + 512],
                    start=(icc == 0), stop=(icc == 7))
        yd = c.tmark(i)
        nc.vector.wait_ge(c.tc.h, yd)
        nc.vector.wait_ge(xls[k_ % 2].h, xl_at2[k_])
        c.V("scalar_tensor_tensor", xc[k_ % 2][:, :], py[:, :],
            bo_sb[:, oc:oc + 1], xc[k_ % 2][:, :], ALU.add, ALU.add)
        done_ev[k_] = c.vc.n
        nc.gpsimd.wait_ge(c.vc.h, done_ev[k_])
        st_at[k_] = c.gdma(yT[oc * 128:(oc + 1) * 128, qb * 512:qb * 512 + 512],
                           xc[k_ % 2][:], sem=sts[k_ % 2])
        if k_ + 2 < 8 * NQB:
            xload(k_ + 2)
    c.gc.wait(nc.gpsimd)
    c.cur_stack = None
    if own:
        c.end()
        return c.nc
    barrier(c)
    blk.close()
    return None


def run_l1a(x, inp):
    nc_a = _get("l1a_qkv", build_l1a_qkv)
    nc_b = _get("l1a_nd", build_l1a_nd)
    nc_c = _get("l1a_out", build_l1a_out)
    identf = np.eye(128, dtype=np.float32)
    W0h = inp["L1_attn_Wrf"][0].transpose(1, 0, 2)
    W0 = np.ascontiguousarray(np.concatenate([W0h, W0h], axis=0))
    in_maps = []
    for cid in range(N_CORES):
        t0 = cid * TPC
        in_maps.append({"xT": np.ascontiguousarray(x[:, t0:t0 + TPC]),
                        "WqT": np.ascontiguousarray(inp["L1_attn_Wq"].T),
                        "bqc": np.ascontiguousarray(inp["L1_attn_bq"].reshape(8, 128).T),
                        "WkT": np.ascontiguousarray(inp["L1_attn_Wk"].T),
                        "bkc": np.ascontiguousarray(inp["L1_attn_bk"].reshape(8, 128).T),
                        "WvT": np.ascontiguousarray(inp["L1_attn_Wv"].T),
                        "bvr": inp["L1_attn_bv"],
                        "W0in": W0, "identf": identf})
    res = run_bass_kernel_spmd(nc_a, in_maps, core_ids=list(range(N_CORES)))
    r = res.results
    lmax_all = np.stack([rr["lmax_out"] for rr in r], axis=1)
    in_maps = []
    for cid in range(N_CORES):
        b = cid // 4
        in_maps.append({"lg_in": r[cid]["lg_out"], "v_in": r[cid]["v_out"],
                        "lmax8": np.ascontiguousarray(lmax_all[:, b * 4:b * 4 + 4])})
    res = run_bass_kernel_spmd(nc_b, in_maps, core_ids=list(range(N_CORES)))
    r2 = res.results
    in_maps = []
    for cid in range(N_CORES):
        t0 = cid * TPC
        b = cid // 4
        np4 = np.stack([r2[cc]["np_out"] for cc in range(b * 4, b * 4 + 4)])
        dp4 = np.stack([r2[cc]["dp_out"] for cc in range(b * 4, b * 4 + 4)])
        in_maps.append({"xT": np.ascontiguousarray(x[:, t0:t0 + TPC]),
                        "q_in": r[cid]["q_out"],
                        "np_in": np4, "dp_in": dp4, "W0in": W0,
                        "WoT": np.ascontiguousarray(inp["L1_attn_Wo"].T),
                        "boc": np.ascontiguousarray(inp["L1_attn_bo"].reshape(8, 128).T)})
    res = run_bass_kernel_spmd(nc_c, in_maps, core_ids=list(range(N_CORES)))
    return np.concatenate([rr["yT"] for rr in res.results], axis=1)



def barrier(c):
    nc = c.nc
    for eng in (nc.tensor, nc.vector, nc.scalar, nc.gpsimd, nc.sync):
        eng.wait_ge(c.tc.h, c.tc.n)
        eng.wait_ge(c.vc.h, c.vc.n)
        eng.wait_ge(c.sc.h, c.sc.n)
        eng.wait_ge(c.gc.h, c.gc.n)
        eng.wait_ge(c.yc.h, c.yc.n)


def build_l2a(T=TPC, c=None, io=None):
    own = c is None
    if own:
        c = Ctx("l2a")
    nc = c.nc
    blk = c.st if own else ExitStack()
    HALO = 512
    TE = T + HALO
    if own:
        io = NS(xT=c.din("xT", (D, TE)), WqT=c.din("WqT", (D, D)),
                WkT=c.din("WkT", (D, D)), WvT=c.din("WvT", (D, D)),
                WkpT=c.din("WkpT", (D, D)), WoT=c.din("WoT", (D, D)),
                gmr=c.din("gmr", (128, 8)), btr=c.din("btr", (128, 8)),
                maskb=c.din("maskb", (128, 256)),
                identb=c.din("identb", (128, 128), BF16), yT=c.dout("yT", (D, T)))
    xT, WqT, WkT, WvT, WkpT, WoT = io.xT, io.WqT, io.WkT, io.WvT, io.WkpT, io.WoT
    gmr, btr, maskb, identb, yT = io.gmr, io.btr, io.maskb, io.identb, io.yT
    c.cur_stack = blk

    qsc = c.nc.dram_tensor("qsc", [TE, D], BF16, kind="Internal").ap()
    ksc = c.nc.dram_tensor("ksc", [TE, D], BF16, kind="Internal").ap()
    kpsc = c.nc.dram_tensor("kpsc", [TE, D], BF16, kind="Internal").ap()
    vsc = c.nc.dram_tensor("vsc", [TE, D], BF16, kind="Internal").ap()
    hsc = c.nc.dram_tensor("hsc", [D, T], F32, kind="Internal").ap()

    idb = c.sb("idb", [128, 128], BF16)
    mk_sb = c.sb("mk_sb", [128, 256])
    gm_sb = c.sb("gm_sb", [128, 8])
    bt_sb = c.sb("bt_sb", [128, 8])
    ctxf = c.sb("ctxf", [128, 8, T], BF16)
    ps_sums = c.ps("ps_sums", [1, 512])
    ps_bc = c.ps("ps_bc", [128, 512])
    ps_tr = c.ps("ps_tr", [128, 128], BF16)

    if own:
        c.begin()
    make_ln(c, ps_sums, ps_bc)
    c.make_consts([1e-5])
    xls = [c.sem("xls0"), c.sem("xls1")]
    shl = c.sem("shl")
    sts = [c.sem("sts0"), c.sem("sts1")]
    c.ydma(idb[:], identb[:, :])
    c.ydma(mk_sb[:], maskb[:, :])
    c.ydma(gm_sb[:], gmr[:, :])
    c.ydma(bt_sb[:], btr[:, :])
    c.yc.wait(nc.tensor)
    c.yc.wait(nc.vector)

    xT_r = xT.rearrange("(ic p) t -> p ic t", p=128)
    with ExitStack() as ph1:
        wsb = [ph1.enter_context(nc.sbuf_tensor(f"wp{i}", [128, 8, D], BF16))
               for i in range(4)]
        xb = [ph1.enter_context(nc.sbuf_tensor(f"xb{i}", [128, 8, 512], F32))
              for i in range(2)]
        hh_ = ph1.enter_context(nc.sbuf_tensor("hb", [128, 8, 512], BF16))
        h32_ = ph1.enter_context(nc.sbuf_tensor("h32b", [128, 8, 512], F32))
        pstore = ph1.enter_context(nc.sbuf_tensor("pstore", [128, 512], BF16))
        ps_a = [ph1.enter_context(nc.psum_tensor(f"ps_a{i}", [128, 512], F32))
                for i in range(2)]
        for wi, W in enumerate((WqT, WkT, WvT, WkpT)):
            c.gdma(wsb[wi][:], W.rearrange("(ic p) o -> p ic o", p=128))
        c.gc.wait(nc.tensor)
        nst = TE // 512
        xl_at = {0: c.gdma(xb[0][:], xT_r[:, :, 0:512], sem=xls[0])}
        for st in range(nst):
            t0 = st * 512
            if st + 1 < nst:
                if st >= 1:
                    nc.gpsimd.wait_ge(c.vc.h, c.vc.n)
                xl_at[st + 1] = c.gdma(xb[(st + 1) % 2][:],
                                       xT_r[:, :, t0 + 512:t0 + 1024],
                                       sem=xls[(st + 1) % 2])
            xv = xb[st % 2]
            for eng in (nc.tensor, nc.vector, nc.scalar):
                eng.wait_ge(xls[st % 2].h, xl_at[st])
            if st > 0:
                nc.vector.wait_ge(c.tc.h, c.tc.n)
                nc.scalar.wait_ge(c.tc.h, c.tc.n)
                nc.vector.wait_ge(c.gc.h, c.gc.n)
            ln_fm_512(c, [xv[:, i] for i in range(8)], [hh_[:, i] for i in range(8)],
                      h32_views=[h32_[:, i] for i in range(8)])
            h_ready = c.vc.n
            nc.tensor.wait_ge(c.vc.h, h_ready)
            if t0 >= HALO:
                nc.gpsimd.wait_ge(c.vc.h, h_ready)
                c.gdma(hsc.rearrange("(ic p) t -> p ic t", p=128)
                       [:, :, t0 - HALO:t0 - HALO + 512], h32_[:])
            for wi, sc_ in enumerate((qsc, ksc, vsc, kpsc)):
                for tb in range(4):
                    for ob in range(2):
                        par = (tb * 2 + ob) % 2
                        pa = ps_a[par]
                        nc.tensor.wait_ge(c.vc.h, c.vc.n)
                        for icc in range(8):
                            i = c.T("matmul", pa[:, :],
                                    hh_[:, icc, tb * 128:tb * 128 + 128],
                                    wsb[wi][:, icc, ob * 512:ob * 512 + 512],
                                    start=(icc == 0), stop=(icc == 7))
                        pd = c.tmark(i)
                        nc.vector.wait_ge(c.tc.h, pd)
                        nc.vector.wait_ge(c.gc.h, c.gc.n)
                        c.V("tensor_copy", pstore[:, :], pa[:, :])
                        wr = c.vc.n
                        nc.gpsimd.wait_ge(c.vc.h, wr)
                        c.gdma(sc_[t0 + tb * 128:t0 + tb * 128 + 128,
                                   ob * 512:ob * 512 + 512], pstore[:])
        barrier(c)
    def mkap(src, offset_el, dims):
        return bass.AP(src.tensor, offset_el, dims)

    with ExitStack() as ph2:
        ksh = ph2.enter_context(nc.sbuf_tensor("ksh", [128, 16, D], BF16))
        kpsh = ph2.enter_context(nc.sbuf_tensor("kpsh", [128, 16, D], BF16))
        vsh = ph2.enter_context(nc.sbuf_tensor("vsh", [128, 16, D], BF16))
        qtm = ph2.enter_context(nc.sbuf_tensor("qtm", [128, D], BF16))
        up_ = ph2.enter_context(nc.sbuf_tensor("uprod", [128, 16, 64], BF16))
        p3 = ph2.enter_context(nc.sbuf_tensor("p3", [128, 16 * 16 * 64], BF16))
        arow = ph2.enter_context(nc.sbuf_tensor("arow", [128, 256], F32))
        asc = ph2.enter_context(nc.sbuf_tensor("asc", [128, 4], F32))
        pp = ph2.enter_context(nc.sbuf_tensor("pp", [128, 256], BF16))
        zt = ph2.enter_context(nc.sbuf_tensor("zt", [128, 16, 64], F32))
        zt_bf = ph2.enter_context(nc.sbuf_tensor("zt_bf", [128, 1024], BF16))
        for tb in range(T // 128):
            g0 = HALO + tb * 128
            nc.gpsimd.wait_ge(c.vc.h, c.vc.n)
            nc.gpsimd.wait_ge(c.tc.h, c.tc.n)
            c.gdma(qtm[:], qsc[g0:g0 + 128, :], sem=shl)
            for sc_, dst in ((ksc, ksh), (kpsc, kpsh), (vsc, vsh)):
                c.gdma(dst[:], mkap(sc_, (g0 - 15) * D,
                                    [[D, 128], [D, 16], [1, D]]), sem=shl)
            sh_at = shl.n
            nc.vector.wait_ge(shl.h, sh_at)
            for hh in range(16):
                hsl = slice(hh * 64, hh * 64 + 64)
                c.V("tensor_tensor", up_[:, :, :],
                    qtm[:, hsl].unsqueeze(1).broadcast_to([128, 16, 64]),
                    ksh[:, :, hsl], ALU.mult)
                c.V("tensor_tensor", p3[:, :].rearrange("p (j k e) -> p j k e", j=16, k=16),
                    up_[:, :, :].unsqueeze(2).broadcast_to([128, 16, 16, 64]),
                    mkap(kpsh.ap() if False else kpsh[:, :, hsl], 0, None)
                    if False else
                    kpsh[:, :, hsl].unsqueeze(1).broadcast_to([128, 16, 16, 64]),
                    ALU.mult)
                c.V("tensor_reduce", arow[:, :].rearrange("p (j k) -> p j k", j=16).unsqueeze(3),
                    p3[:, :].rearrange("p (j k e) -> p j k e", j=16, k=16), AX.X, ALU.add)
                if tb == 0:
                    c.V("scalar_tensor_tensor", arow[:, :], arow[:, :], 0.125,
                        mk_sb[:, :], ALU.mult, ALU.add)
                else:
                    c.V("tensor_scalar", arow[:, :], arow[:, :], 0.125, None, ALU.mult)
                c.V("reduce_max", asc[:, 0:1], arow[:, :], AX.X)
                c.V("tensor_scalar", asc[:, 1:2], asc[:, 0:1], -1.0, None, ALU.mult)
                c.vc.wait(nc.scalar)
                c.S("activation", arow[:, :], arow[:, :], AF.Exp, bias=asc[:, 1:2])
                c.sc.wait(nc.vector)
                c.V("reduce_sum", asc[:, 2:3], arow[:, :], AX.X)
                c.V("reciprocal", asc[:, 3:4], asc[:, 2:3])
                c.V("tensor_scalar", pp[:, :], arow[:, :], asc[:, 3:4], None, ALU.mult)
                c.V("tensor_tensor", p3[:, :].rearrange("p (e j k) -> p e j k", e=64, j=16),
                    mkap(vsh[0:128, 0:1, 0:1], hh * 64,
                         [[16 * D, 128], [1, 64], [D, 16], [0, 16]]),
                    mkap(vsh[0:128, 0:1, 0:1], hh * 64,
                         [[16 * D, 128], [1, 64], [0, 16], [D, 16]]), ALU.mult)
                c.V("tensor_tensor", p3[:, :].rearrange("p (e jk) -> p e jk", e=64),
                    p3[:, :].rearrange("p (e jk) -> p e jk", e=64),
                    pp[:, :].unsqueeze(1).broadcast_to([128, 64, 256]), ALU.mult)
                c.V("tensor_reduce", zt[:, hh, :].unsqueeze(2),
                    p3[:, :].rearrange("p (e jk) -> p e jk", e=64), AX.X, ALU.add)
            c.V("tensor_copy", zt_bf[:, :], zt[:, :, :].rearrange("p h e -> p (h e)"))
            z_ready = c.vc.n
            nc.tensor.wait_ge(c.vc.h, z_ready)
            for icc in range(8):
                if icc > 0:
                    nc.tensor.wait_ge(c.vc.h, c.vc.n)
                i = c.T("transpose", ps_tr[:, :], zt_bf[:, icc * 128:icc * 128 + 128],
                        idb[:])
                trd = c.tmark(i)
                nc.vector.wait_ge(c.tc.h, trd)
                c.V("tensor_copy", ctxf[:, icc, tb * 128:tb * 128 + 128], ps_tr[:, :])
        barrier(c)
    with ExitStack() as ph3:
        wo_sb = ph3.enter_context(nc.sbuf_tensor("wo_sb", [128, 8, D], BF16))
        outp = ph3.enter_context(nc.sbuf_tensor("outp", [128, 8, 512], F32))
        outp2 = ph3.enter_context(nc.sbuf_tensor("outp2", [128, 8, 512], F32))
        xb3 = [ph3.enter_context(nc.sbuf_tensor(f"xb3{i}", [128, 8, 512], F32))
               for i in range(2)]
        ps_y = [ph3.enter_context(nc.psum_tensor(f"ps_y{i}", [128, 512], F32))
                for i in range(2)]
        x3s = [c.sem("x3s0"), c.sem("x3s1")]
        h3s = [c.sem("h3s0"), c.sem("h3s1")]
        hb3 = [ph3.enter_context(nc.sbuf_tensor(f"hb3{i}", [128, 8, 512], F32))
               for i in range(2)]
        c.gdma(wo_sb[:], WoT.rearrange("(ic p) o -> p ic o", p=128))
        c.gc.wait(nc.tensor)
        x3_at = {}

        h3_at = {}

        def x3load(qb):
            if qb >= 2:
                nc.sync.wait_ge(c.vc.h, c.vc.n)
            x3_at[qb] = c.ydma(xb3[qb % 2][:],
                               xT_r[:, :, HALO + qb * 512:HALO + qb * 512 + 512],
                               sem=x3s[qb % 2])
            h3_at[qb] = c.ydma(hb3[qb % 2][:],
                               hsc.rearrange("(ic p) t -> p ic t", p=128)
                               [:, :, qb * 512:qb * 512 + 512],
                               sem=h3s[qb % 2])

        x3load(0)
        if T // 512 > 1:
            x3load(1)
        for qb in range(T // 512):
            if qb > 0:
                nc.vector.wait_ge(c.tc.h, c.tc.n)
                nc.scalar.wait_ge(c.tc.h, c.tc.n)
            for oc in range(8):
                py = ps_y[oc % 2]
                if oc >= 2:
                    nc.tensor.wait_ge(c.vc.h, c.vc.n)
                for icc in range(8):
                    i = c.T("matmul", py[:, :], wo_sb[:, icc, oc * 128:oc * 128 + 128],
                            ctxf[:, icc, qb * 512:qb * 512 + 512],
                            start=(icc == 0), stop=(icc == 7))
                yd = c.tmark(i)
                nc.vector.wait_ge(c.tc.h, yd)
                nc.vector.wait_ge(x3s[qb % 2].h, x3_at[qb])
                nc.vector.wait_ge(h3s[qb % 2].h, h3_at[qb])
                c.V("tensor_add", outp[:, oc, :], py[:, :], hb3[qb % 2][:, oc, :])
            o_ready = c.vc.n
            nc.tensor.wait_ge(c.vc.h, o_ready)
            ln_fm_512(c, [outp[:, i, :] for i in range(8)],
                      [outp2[:, i, :] for i in range(8)],
                      colmul=[gm_sb[:, i:i + 1] for i in range(8)],
                      coladd=[bt_sb[:, i:i + 1] for i in range(8)])
            for icc in range(8):
                c.V("tensor_add", outp2[:, icc, :], outp2[:, icc, :],
                    xb3[qb % 2][:, icc, :])
            y_ready = c.vc.n
            if qb + 2 < T // 512:
                x3load(qb + 2)
            nc.gpsimd.wait_ge(c.vc.h, y_ready)
            c.gdma(yT.rearrange("(ic p) t -> p ic t", p=128)[:, :, qb * 512:qb * 512 + 512],
                   outp2[:])
    c.gc.wait(nc.gpsimd)
    c.cur_stack = None
    if own:
        c.end()
        return c.nc
    barrier(c)
    blk.close()
    return None


def run_l2a(x, inp):
    nc = _get("l2a", build_l2a)
    HALO = 512
    identb = np.eye(128, dtype=BFD)
    gmr = np.ascontiguousarray(inp["L2_attn_gamma"].reshape(8, 128).T)
    btr = np.ascontiguousarray(inp["L2_attn_beta"].reshape(8, 128).T)
    t_idx = np.arange(128)[:, None, None]
    j_idx = np.arange(16)[None, :, None]
    k_idx = np.arange(16)[None, None, :]
    allowed = (j_idx >= 15 - t_idx) & (k_idx >= 15 - t_idx)
    mask0 = np.where(allowed, 0.0, -1e9).astype(np.float32).reshape(128, 256)
    maskz = np.zeros((128, 256), np.float32)
    in_maps = []
    for cid in range(N_CORES):
        t0 = cid * TPC
        if t0 % S == 0:
            xe = np.concatenate([np.zeros((D, HALO), np.float32),
                                 x[:, t0:t0 + TPC]], axis=1)
            mb = mask0
        else:
            xe = x[:, t0 - HALO:t0 + TPC]
            mb = maskz
        in_maps.append({"xT": np.ascontiguousarray(xe),
                        "WqT": np.ascontiguousarray(inp["L2_attn_Wq"].T),
                        "WkT": np.ascontiguousarray(inp["L2_attn_Wk"].T),
                        "WvT": np.ascontiguousarray(inp["L2_attn_Wv"].T),
                        "WkpT": np.ascontiguousarray(inp["L2_attn_Wkp"].T),
                        "WoT": np.ascontiguousarray(inp["L2_attn_Wo"].T),
                        "gmr": gmr, "btr": btr, "maskb": mb, "identb": identb})
    res = run_bass_kernel_spmd(nc, in_maps, core_ids=list(range(N_CORES)))
    return np.concatenate([r["yT"] for r in res.results], axis=1)



def build_dn_prep(T=TPC, c=None, io=None):
    own = c is None
    if own:
        c = Ctx("dn_prep")
    nc = c.nc
    blk = c.st if own else ExitStack()
    HALO = 512
    TE = T + HALO
    if own:
        io = NS(xT=c.din("xT", (D, TE)), Win=c.din("Win", (D, 3 * D)),
                Wab=c.din("Wab", (D, 8)), babc=c.din("babc", (8, 1)),
                Wog=c.din("Wog", (D, D)), bogc=c.din("bogc", (128, 8)),
                wconv=c.din("wconv", (128, 8, 3, 4)),
                q_out=c.dout("q_out", (D, T), BF16),
                k_out=c.dout("k_out", (D, T), BF16),
                v_out=c.dout("v_out", (D, T), BF16),
                ab_out=c.dout("ab_out", (8, T)), og_out=c.dout("og_out", (D, T)))
    xT, Win, Wab, babc, Wog, bogc, wconv = (io.xT, io.Win, io.Wab, io.babc,
                                            io.Wog, io.bogc, io.wconv)
    q_out, k_out, v_out, ab_out, og_out = (io.q_out, io.k_out, io.v_out,
                                           io.ab_out, io.og_out)
    c.cur_stack = blk

    pj_sc = nc.dram_tensor("pj_sc", [3 * D, TE], F32, kind="Internal").ap()

    win_sb = c.sb("win_sb", [128, 8, 3 * D], BF16)
    wab_sb = c.sb("wab_sb", [128, 8, 8], BF16)
    bab_sb = c.sb("bab_sb", [8, 1])
    wog_sb = c.sb("wog_sb", [128, 8, D], BF16)
    bog_sb = c.sb("bog_sb", [128, 8])
    wcv_sb = c.sb("wcv_sb", [128, 8, 3, 4])
    xb = [c.sb(f"xb{i}", [128, 8, 512]) for i in range(2)]
    h = c.sb("hb", [128, 8, 512], BF16)
    evb = c.sb("evb", [128, 512])
    onesb = c.sb("onesb", [1, 128], BF16)
    ps_a = [c.ps(f"ps_a{i}", [128, 512]) for i in range(2)]
    ps_ab = c.ps("ps_ab", [8, 512])
    ps_sums = c.ps("ps_sums", [1, 512])
    ps_bc = c.ps("ps_bc", [128, 512])

    if own:
        c.begin()
    make_ln(c, ps_sums, ps_bc)
    c.make_consts([1e-5])
    xls = [c.sem("xls0"), c.sem("xls1")]
    c.gdma(win_sb[:], Win.rearrange("(ic p) o -> p ic o", p=128))
    c.gdma(wab_sb[:], Wab.rearrange("(ic p) o -> p ic o", p=128))
    c.gdma(wog_sb[:], Wog.rearrange("(ic p) o -> p ic o", p=128))
    c.ydma(bab_sb[:], babc[:, :])
    c.ydma(bog_sb[:], bogc[:, :])
    c.ydma(wcv_sb[:], wconv[:, :, :, :])
    c.V("memset", onesb[:], 1.0)
    c.gc.wait(nc.tensor)
    c.yc.wait(nc.tensor)
    c.yc.wait(nc.vector)

    xT_r = xT.rearrange("(ic p) t -> p ic t", p=128)
    nst = TE // 512
    xl_at = {0: c.gdma(xb[0][:], xT_r[:, :, 0:512], sem=xls[0])}
    for st in range(nst):
        t0 = st * 512
        if st + 1 < nst:
            if st >= 1:
                nc.gpsimd.wait_ge(c.vc.h, c.vc.n)
            xl_at[st + 1] = c.gdma(xb[(st + 1) % 2][:], xT_r[:, :, t0 + 512:t0 + 1024],
                                   sem=xls[(st + 1) % 2])
        xv = xb[st % 2]
        for eng in (nc.tensor, nc.vector, nc.scalar):
            eng.wait_ge(xls[st % 2].h, xl_at[st])
        if st > 0:
            nc.vector.wait_ge(c.tc.h, c.tc.n)
            nc.scalar.wait_ge(c.tc.h, c.tc.n)
        ln_fm_512(c, [xv[:, i] for i in range(8)], [h[:, i] for i in range(8)])
        h_ready = c.vc.n
        nc.tensor.wait_ge(c.vc.h, h_ready)
        for oc in range(24):
            pa = ps_a[oc % 2]
            if oc >= 2:
                nc.tensor.wait_ge(c.vc.h, c.vc.n)
            for icc in range(8):
                i = c.T("matmul", pa[:, :], win_sb[:, icc, oc * 128:oc * 128 + 128],
                        h[:, icc], start=(icc == 0), stop=(icc == 7))
            pd = c.tmark(i)
            nc.vector.wait_ge(c.tc.h, pd)
            nc.vector.wait_ge(c.gc.h, c.gc.n)
            c.V("tensor_copy", evb[:, :], pa[:, :])
            wr = c.vc.n
            nc.gpsimd.wait_ge(c.vc.h, wr)
            c.gdma(pj_sc[oc * 128:oc * 128 + 128, t0:t0 + 512], evb[:])
        if t0 >= HALO:
            tg = t0 - HALO
            nc.tensor.wait_ge(c.vc.h, c.vc.n)
            nc.tensor.wait_ge(c.sc.h, c.sc.n)
            for icc in range(8):
                i = c.T("matmul", ps_ab[:, :], wab_sb[:, icc, 0:8], h[:, icc],
                        start=(icc == 0), stop=(icc == 7))
            abd = c.tmark(i)
            nc.scalar.wait_ge(c.tc.h, abd)
            nc.scalar.wait_ge(c.vc.h, c.vc.n)
            nc.scalar.wait_ge(c.gc.h, c.gc.n)
            c.S("activation", evb[0:8, :], ps_ab[:, :], AF.Sigmoid,
                bias=bab_sb[:, 0:1])
            sgd = c.sc.n
            nc.vector.wait_ge(c.sc.h, sgd)
            c.V("tensor_scalar", evb[0:4, :], evb[0:4, :], 0.98, 0.01,
                ALU.mult, ALU.add)
            abr = c.vc.n
            nc.gpsimd.wait_ge(c.vc.h, abr)
            c.gdma(ab_out[:, tg:tg + 512], evb[0:8, :])
            for oc in range(8):
                pa = ps_a[oc % 2]
                nc.tensor.wait_ge(c.vc.h, c.vc.n)
                nc.tensor.wait_ge(c.sc.h, c.sc.n)
                for icc in range(8):
                    i = c.T("matmul", pa[:, :], wog_sb[:, icc, oc * 128:oc * 128 + 128],
                            h[:, icc], start=(icc == 0), stop=(icc == 7))
                ogd = c.tmark(i)
                nc.scalar.wait_ge(c.tc.h, ogd)
                nc.scalar.wait_ge(c.gc.h, c.gc.n)
                c.S("activation", evb[:, :], pa[:, :], AF.Sigmoid,
                    bias=bog_sb[:, oc:oc + 1])
                ogr = c.sc.n
                nc.gpsimd.wait_ge(c.sc.h, ogr)
                c.gdma(og_out[oc * 128:oc * 128 + 128, tg:tg + 512], evb[:])
    barrier(c)
    with ExitStack() as ph2:
        pj = ph2.enter_context(nc.sbuf_tensor("pj", [128, 8, 515], F32))
        cv = ph2.enter_context(nc.sbuf_tensor("cv", [128, 8, 512], F32))
        cs = ph2.enter_context(nc.sbuf_tensor("cs", [128, 8, 512], F32))
        sq_ = ph2.enter_context(nc.sbuf_tensor("sq_", [128, 8, 512], F32))
        nr = ph2.enter_context(nc.sbuf_tensor("nr", [1, 4, 512], F32))
        nb = ph2.enter_context(nc.sbuf_tensor("nb", [128, 512], F32))
        st_bf = ph2.enter_context(nc.sbuf_tensor("st_bf", [128, 8, 512], BF16))
        onesf = ph2.enter_context(nc.sbuf_tensor("onesf", [128, 1], F32))
        ps_n = ph2.enter_context(nc.psum_tensor("ps_n", [1, 512], F32))
        ps_nb = ph2.enter_context(nc.psum_tensor("ps_nb", [128, 512], F32))
        pjl = c.sem("pjl")
        c.V("memset", onesf[:], 1.0)
        c.vc.wait(nc.tensor)
        pj_r = pj_sc.rearrange("(ic p) t -> p ic t", p=128)
        outs3 = {0: q_out, 1: k_out, 2: v_out}
        for st in range(T // 512):
            tg = st * 512
            for wi in range(3):
                nc.gpsimd.wait_ge(c.vc.h, c.vc.n)
                at = c.gdma(pj[:], pj_r[:, wi * 8:wi * 8 + 8,
                                        tg + HALO - 3:tg + HALO + 512], sem=pjl)
                nc.vector.wait_ge(pjl.h, at)
                nc.vector.wait_ge(c.gc.h, c.gc.n)
                nc.vector.wait_ge(c.sc.h, c.sc.n)
                nc.scalar.wait_ge(c.vc.h, c.vc.n)
                for icc in range(8):
                    c.V("tensor_scalar", cv[:, icc, :], pj[:, icc, 3:515],
                        wcv_sb[:, icc, wi, 3:4], None, ALU.mult)
                    for kk in range(3):
                        c.V("scalar_tensor_tensor", cv[:, icc, :],
                            pj[:, icc, kk:kk + 512], wcv_sb[:, icc, wi, kk:kk + 1],
                            cv[:, icc, :], ALU.mult, ALU.add)
                cv_ready = c.vc.n
                nc.scalar.wait_ge(c.vc.h, cv_ready)
                c.S("activation", cs[:, :, :], cv[:, :, :], AF.Silu)
                cs_ready = c.sc.n
                if wi < 2:
                    c.S("activation", sq_[:, :, :], cs[:, :, :], AF.Square)
                    sq_ready = c.sc.n
                    nc.tensor.wait_ge(c.sc.h, sq_ready)
                    for hh4 in range(4):
                        if hh4 > 0:
                            nc.tensor.wait_ge(c.vc.h, c.vc.n)
                        for cc2 in range(2):
                            i = c.T("matmul", ps_n[0:1, :], onesf[:, 0:1],
                                    sq_[:, hh4 * 2 + cc2, :],
                                    start=(cc2 == 0), stop=(cc2 == 1))
                        nd = c.tmark(i)
                        nc.vector.wait_ge(c.tc.h, nd)
                        c.V("tensor_copy", nr[0:1, hh4, :], ps_n[0:1, :])
                    nr_all = c.vc.n
                    nc.scalar.wait_ge(c.vc.h, nr_all)
                    c.S("activation", nr[0:1, :, :], nr[0:1, :, :], AF.Sqrt)
                    nsq = c.sc.n
                    nc.vector.wait_ge(c.sc.h, nsq)
                    c.V("tensor_scalar_max", nr[0:1, :, :], nr[0:1, :, :], 1e-6)
                    c.V("reciprocal", nr[0:1, :, :], nr[0:1, :, :])
                    rn_ready = c.vc.n
                    nc.tensor.wait_ge(c.vc.h, rn_ready)
                    for hh4 in range(4):
                        i = c.T("matmul", ps_nb[:, :], c.ln_ones_row[0:1, :],
                                nr[0:1, hh4, :], start=True, stop=True)
                        bd = c.tmark(i)
                        nc.vector.wait_ge(c.tc.h, bd)
                        c.V("tensor_mul", st_bf[:, hh4 * 2, :], cs[:, hh4 * 2, :],
                            ps_nb[:, :])
                        c.V("tensor_mul", st_bf[:, hh4 * 2 + 1, :],
                            cs[:, hh4 * 2 + 1, :], ps_nb[:, :])
                        nc.tensor.wait_ge(c.vc.h, c.vc.n)
                else:
                    nc.vector.wait_ge(c.sc.h, cs_ready)
                    c.V("tensor_copy", st_bf[:, :, :], cs[:, :, :])
                wdone = c.vc.n
                nc.gpsimd.wait_ge(c.vc.h, wdone)
                c.gdma(outs3[wi].rearrange("(ic p) t -> p ic t", p=128)
                       [:, :, tg:tg + 512], st_bf[:])
    c.gc.wait(nc.gpsimd)
    c.cur_stack = None
    if own:
        c.end()
        return c.nc
    barrier(c)
    blk.close()
    return None


def build_dn_scan(SQ=S, c=None, io=None, gathered=False, TLOC=TPC):
    own = c is None
    if own:
        c = Ctx("dn_scan")
    nc = c.nc
    blk = c.st if own else ExitStack()
    NCH = SQ // 128
    DK = 256
    if own:
        io = NS(q_in=c.din("q_in", (DK, SQ), BF16),
                k_in=c.din("k_in", (DK, SQ), BF16),
                v_in=c.din("v_in", (DK, SQ), BF16),
                al_in=c.din("al_in", (SQ,)), be_in=c.din("be_in", (SQ,)),
                TLm1=c.din("TLm1", (128, 128)), TUm1=c.din("TUm1", (128, 128)),
                TU0=c.din("TU0", (128, 128)), identf=c.din("identf", (128, 128)),
                identb2=c.din("identb2", (128, 128), BF16),
                twoI=c.din("twoI", (128, 128)), Pident=c.din("Pident", (128, 2, 256)),
                r_out=c.dout("r_out", (DK, SQ), BF16))
    q_in, k_in, v_in, al_in, be_in = io.q_in, io.k_in, io.v_in, io.al_in, io.be_in
    TLm1, TUm1, TU0, identf, identb2 = io.TLm1, io.TUm1, io.TU0, io.identf, io.identb2
    twoI, Pident, r_out = io.twoI, io.Pident, io.r_out
    c.cur_stack = blk

    qs = c.sb("qs", [128, 2, 128], BF16)
    ks = c.sb("ks", [128, 2, 128], BF16)
    vs = c.sb("vs", [128, 2, 128], BF16)
    alr = c.sb("alr", [1, SQ])
    ber = c.sb("ber", [1, SQ])
    zrow = c.sb("zrow", [1, 128])
    onr_ = c.sb("onr_", [1, 128])
    arow_ = c.sb("arow_", [1, 128])
    lgr = c.sb("lgr", [1, 128])
    lgc_ = c.sb("lgc_", [1, 128])
    nlgc = c.sb("nlgc", [1, 128])
    acr_ = c.sb("acr_", [1, 128])
    acsc = c.sb("acsc", [1, 1])
    acol = c.sb("acol", [128, 4])
    bacol = c.sb("bacol", [128, 2])
    accol = c.sb("accol", [128, 1])
    Ab = c.sb("Ab", [128, 128])
    lgb = c.sb("lgb", [128, 128])
    bb = c.sb("bb", [128, 128])
    DD = c.sb("DD", [128, 128])
    DDT = c.sb("DDT", [128, 128])
    G_sb = c.sb("G_sb", [128, 128])
    KQ_sb = c.sb("KQ_sb", [128, 128])
    t1 = c.sb("t1", [128, 128])
    tl_sb = c.sb("tl_sb", [128, 128])
    tu_sb = c.sb("tu_sb", [128, 128])
    tu0_sb = c.sb("tu0_sb", [128, 128])
    idf_sb = c.sb("idf_sb", [128, 128])
    idb_sb = c.sb("idb_sb", [128, 128], BF16)
    twoI_sb = c.sb("twoI_sb", [128, 128])
    pid_sb = c.sb("pid_sb", [128, 2, 256])
    amat_bf = c.sb("amat_bf", [128, 128], BF16)
    X_bf = c.sb("X_bf", [128, 128], BF16)
    XT_bf = c.sb("XT_bf", [128, 128], BF16)
    Z_bf = c.sb("Z_bf", [128, 128], BF16)
    rhs_bf = c.sb("rhs_bf", [128, 512], BF16)
    ktm = c.sb("ktm", [128, 256], BF16)
    k2tm = c.sb("k2tm", [128, 256], BF16)
    wv_sb = c.sb("wv_sb", [128, 256], BF16)
    wk_sb = c.sb("wk_sb", [128, 256], BF16)
    tp_bf = c.sb("tp_bf", [128, 128], BF16)
    qeff = c.sb("qeff", [128, 2, 128], BF16)
    P_bf = c.sb("P_bf", [128, 2, 256], BF16)
    aci = c.sb("aci", [128, 256])
    ST32 = c.sb("ST32", [128, 2, 256])
    ST_bf = c.sb("ST_bf", [128, 2, 256], BF16)
    r32 = c.sb("r32", [128, 2, 128])
    rsq = c.sb("rsq", [128, 2, 128])
    rrow = c.sb("rrow", [1, 128])
    rn_bf = c.sb("rn_bf", [128, 2, 128], BF16)

    ones_r = c.sb("ones_r", [1, 128])
    ones_c = c.sb("ones_c", [128, 1])
    psN1 = c.ps("psN1", [128, 128])
    psN2 = c.ps("psN2", [128, 128])
    psW = c.ps("psW", [128, 512])
    psP = c.ps("psP", [128, 256])
    psR = c.ps("psR", [128, 128])
    psB = c.ps("psB", [128, 128])
    psT = c.ps("psT", [128, 128], BF16)

    if own:
        c.begin()
    c.make_consts([1e-6, 1.0 / 256])
    qkl = c.sem("qkl")
    if not gathered:
        c.ydma(alr[:], al_in.unsqueeze(0))
        c.ydma(ber[:], be_in.unsqueeze(0))
    c.ydma(tl_sb[:], TLm1[:, :])
    c.ydma(tu_sb[:], TUm1[:, :])
    c.ydma(tu0_sb[:], TU0[:, :])
    c.ydma(idf_sb[:], identf[:, :])
    c.ydma(idb_sb[:], identb2[:, :])
    c.ydma(twoI_sb[:], twoI[:, :])
    c.ydma(pid_sb[:], Pident[:, :, :])
    c.V("memset", zrow[:], 0.0)
    c.V("memset", onr_[:], 1.0)
    c.V("memset", ones_r[:], 1.0)
    c.V("memset", ones_c[:], 1.0)
    c.V("memset", ST32[:], 0.0)
    c.V("tensor_copy", ST_bf[:], ST32[:])
    c.yc.wait(nc.vector)
    c.yc.wait(nc.tensor)
    c.yc.wait(nc.scalar)

    if not gathered:
        q_r = q_in.rearrange("(dc p) t -> p dc t", p=128)
        k_r = k_in.rearrange("(dc p) t -> p dc t", p=128)
        v_r = v_in.rearrange("(dc p) t -> p dc t", p=128)
    r_w = r_out.rearrange("(dc p) t -> p dc t", p=128)

    for ch in range(NCH):
        t0 = ch * 128
        tsl = slice(t0, t0 + 128)
        nc.gpsimd.wait_ge(c.tc.h, c.tc.n)
        if not gathered:
            at = c.gdma(qs[:], q_r[:, :, tsl], sem=qkl)
            at = c.gdma(ks[:], k_r[:, :, tsl], sem=qkl)
            at = c.gdma(vs[:], v_r[:, :, tsl], sem=qkl)
        else:
            nc.sync.wait_ge(c.tc.h, c.tc.n)
            nc.sync.wait_ge(c.vc.h, c.vc.n)
            nc.sync.wait_ge(c.sc.h, c.sc.n)
            CPR = TLOC // 128
            pid = c.pid()
            for src, dst in ((q_in, qs), (k_in, ks), (v_in, vs)):
                with nc.sync.register(_rn("qko")) as rg:
                    nc.sync.reg_div(rg, pid, 4)
                    nc.sync.reg_mul(rg, rg, 4)
                    nc.sync.reg_add(rg, rg, ch // CPR)
                    nc.sync.reg_mul(rg, rg, D * TLOC)
                    with nc.sync.register(_rn("qko2")) as rg2:
                        nc.sync.reg_mod(rg2, pid, 4)
                        nc.sync.reg_mul(rg2, rg2, 256 * TLOC)
                        nc.sync.reg_add(rg, rg, rg2)
                    nc.sync.reg_add(rg, rg, (ch % CPR) * 128)
                    dyn = bass.AP(src.tensor, rg,
                                  [[TLOC, 128], [128 * TLOC, 2], [1, 128]])
                    at = c.ydma(dst[:], dyn, sem=qkl)
            for rowoff, dst in ((0, alr), (4, ber)):
                with nc.sync.register(_rn("abo")) as rg:
                    nc.sync.reg_div(rg, pid, 4)
                    nc.sync.reg_mul(rg, rg, 4)
                    nc.sync.reg_add(rg, rg, ch // CPR)
                    nc.sync.reg_mul(rg, rg, 8 * TLOC)
                    with nc.sync.register(_rn("abo2")) as rg2:
                        nc.sync.reg_mod(rg2, pid, 4)
                        nc.sync.reg_add(rg2, rg2, rowoff)
                        nc.sync.reg_mul(rg2, rg2, TLOC)
                        nc.sync.reg_add(rg, rg, rg2)
                    nc.sync.reg_add(rg, rg, (ch % CPR) * 128)
                    dyn = bass.AP(al_in.tensor, rg, [[TLOC, 1], [1, 128]])
                    at = c.ydma(dst[0:1, 0:128], dyn, sem=qkl)
        nc.vector.wait_ge(qkl.h, at)
        nc.tensor.wait_ge(qkl.h, at)
        nc.scalar.wait_ge(qkl.h, at)
        nc.vector.wait_ge(c.tc.h, c.tc.n)
        nc.scalar.wait_ge(c.tc.h, c.tc.n)
        nc.scalar.wait_ge(c.vc.h, c.vc.n)
        asl = slice(0, 128) if gathered else tsl
        c.S("activation", lgr[:, :], alr[0:1, asl], AF.Ln)
        c.sc.wait(nc.vector)
        c.V("tensor_tensor_scan", lgc_[:, :], onr_[:, :], lgr[:, :], 0.0,
            ALU.mult, ALU.add)
        c.V("tensor_scalar", nlgc[:, :], lgc_[:, :], -1.0, None, ALU.mult)
        lg_d = c.vc.n
        nc.scalar.wait_ge(c.vc.h, lg_d)
        c.S("activation", arow_[:, :], lgc_[:, :], AF.Exp)
        c.S("activation", acr_[:, :], nlgc[:, :], AF.Exp,
            bias=lgc_[0:1, 127:128])
        c.S("activation", acsc[:, :], lgc_[0:1, 127:128], AF.Exp)
        ar_d = c.sc.n
        nc.tensor.wait_ge(c.sc.h, ar_d)
        nc.tensor.wait_ge(c.vc.h, c.vc.n)
        ttt = [(arow_[0:1, :], 0), (lgc_[0:1, :], 1), (nlgc[0:1, :], 2)]
        for row_ap, j in ttt:
            i = c.T("transpose", psB[:, 0:1], row_ap, idf_sb[0:1, 0:1])
            m_ = c.tmark(i)
            nc.vector.wait_ge(c.tc.h, m_)
            c.V("tensor_copy", acol[:, j:j + 1], psB[:, 0:1])
            nc.tensor.wait_ge(c.vc.h, c.vc.n)
        i = c.T("transpose", psB[:, 0:1], ber[0:1, asl], idf_sb[0:1, 0:1])
        m3 = c.tmark(i)
        nc.vector.wait_ge(c.tc.h, m3)
        c.V("tensor_copy", acol[:, 3:4], psB[:, 0:1])
        c.V("tensor_mul", bacol[:, 0:1], acol[:, 3:4], acol[:, 0:1])
        rows_d2 = c.vc.n
        nc.tensor.wait_ge(c.vc.h, rows_d2)
        for (row_ap, dst) in ((arow_[0:1, :], Ab), (lgc_[0:1, :], lgb),
                              (ber[0:1, asl], bb)):
            i = c.T("matmul", psB[:, :], ones_r[0:1, :], row_ap, start=True, stop=True)
            bd = c.tmark(i)
            nc.vector.wait_ge(c.tc.h, bd)
            c.V("tensor_copy", dst[:, :], psB[:, :])
            nc.tensor.wait_ge(c.vc.h, c.vc.n)
        i = c.T("matmul", psB[:, 0:1], ones_r[0:1, :], acsc[0:1, 0:1],
                start=True, stop=True)
        acd = c.tmark(i)
        nc.vector.wait_ge(c.tc.h, acd)
        c.V("tensor_copy", accol[:, 0:1], psB[:, 0:1])
        nc.tensor.wait_ge(c.vc.h, c.vc.n)
        i = c.T("transpose", psB[:, 0:1], acr_[:, :], idf_sb[0:1, 0:1])
        acrd = c.tmark(i)
        nc.vector.wait_ge(c.tc.h, acrd)
        c.V("tensor_copy", bacol[:, 1:2], psB[:, 0:1])
        c.V("tensor_scalar", t1[:, :], lgb[:, :], acol[:, 2:3], None, ALU.add)
        c.V("scalar_tensor_tensor", t1[:, :], tl_sb[:, :], -1e30, t1[:, :],
            ALU.mult, ALU.add)
        c.vc.wait(nc.scalar)
        c.S("activation", DDT[:, :], t1[:, :], AF.Exp)
        c.sc.wait(nc.vector)
        c.V("tensor_scalar", t1[:, :], lgb[:, :], acol[:, 1:2], None, ALU.subtract)
        c.V("tensor_scalar", t1[:, :], t1[:, :], -1.0, None, ALU.mult)
        c.V("scalar_tensor_tensor", t1[:, :], tu_sb[:, :], -1e30, t1[:, :],
            ALU.mult, ALU.add)
        c.vc.wait(nc.scalar)
        c.S("activation", DD[:, :], t1[:, :], AF.Exp)
        dd_sc = c.sc.n
        nc.vector.wait_ge(c.sc.h, dd_sc)
        dd_done = c.vc.n
        for dc in range(2):
            c.T("matmul", psN1[:, :], ks[:, dc, :], ks[:, dc, :],
                start=(dc == 0), stop=(dc == 1))
        for dc in range(2):
            i = c.T("matmul", psN2[:, :], ks[:, dc, :], qs[:, dc, :],
                    start=(dc == 0), stop=(dc == 1))
        gr_done = c.tmark(i)
        nc.vector.wait_ge(c.tc.h, gr_done)
        c.V("tensor_copy", G_sb[:, :], psN1[:, :])
        c.V("tensor_copy", KQ_sb[:, :], psN2[:, :])
        c.V("tensor_mul", t1[:, :], G_sb[:, :], DD[:, :])
        c.V("tensor_mul", t1[:, :], t1[:, :], tl_sb[:, :])
        c.V("tensor_scalar", X_bf[:, :], t1[:, :], acol[:, 3:4], None, ALU.mult)
        c.V("tensor_mul", t1[:, :], G_sb[:, :], DDT[:, :])
        c.V("tensor_mul", t1[:, :], t1[:, :], tu_sb[:, :])
        c.V("tensor_mul", t1[:, :], t1[:, :], bb[:, :])
        c.V("tensor_add", amat_bf[:, :], t1[:, :], idb_sb[:, :])
        c.V("tensor_mul", t1[:, :], KQ_sb[:, :], DDT[:, :])
        c.V("tensor_mul", tp_bf[:, :], t1[:, :], tu0_sb[:, :])
        c.V("tensor_copy", X_bf[:, :], idb_sb[:, :])
        c.V("tensor_copy", XT_bf[:, :], idb_sb[:, :])
        nt0 = c.vc.n
        nc.tensor.wait_ge(c.vc.h, nt0)
        for it in range(7):
            i = c.T("matmul", psN1[:, :], amat_bf[:, :], X_bf[:, :],
                    start=True, stop=True)
            yd = c.tmark(i)
            nc.vector.wait_ge(c.tc.h, yd)
            c.V("scalar_tensor_tensor", Z_bf[:, :], psN1[:, :], -1.0,
                twoI_sb[:, :], ALU.mult, ALU.add)
            zd = c.vc.n
            nc.tensor.wait_ge(c.vc.h, zd)
            c.T("matmul", psN1[:, :], XT_bf[:, :], Z_bf[:, :], start=True, stop=True)
            i = c.T("matmul", psN2[:, :], Z_bf[:, :], XT_bf[:, :], start=True, stop=True)
            xd = c.tmark(i)
            nc.vector.wait_ge(c.tc.h, xd)
            c.V("tensor_copy", X_bf[:, :], psN1[:, :])
            c.V("tensor_copy", XT_bf[:, :], psN2[:, :])
            nc.tensor.wait_ge(c.vc.h, c.vc.n)
        for dc in range(2):
            i = c.T("transpose", psT[:, :], vs[:, dc, :], idb_sb[:, :])
            td = c.tmark(i)
            nc.vector.wait_ge(c.tc.h, td)
            c.V("tensor_copy", rhs_bf[:, dc * 128:dc * 128 + 128], psT[:, :])
            nc.tensor.wait_ge(c.vc.h, c.vc.n)
            i = c.T("transpose", psT[:, :], ks[:, dc, :], idb_sb[:, :])
            td = c.tmark(i)
            nc.vector.wait_ge(c.tc.h, td)
            c.V("tensor_copy", ktm[:, dc * 128:dc * 128 + 128], psT[:, :])
            nc.tensor.wait_ge(c.vc.h, c.vc.n)
        c.V("tensor_scalar", rhs_bf[:, 0:256], rhs_bf[:, 0:256], acol[:, 3:4],
            None, ALU.mult)
        c.V("tensor_scalar", rhs_bf[:, 256:512], ktm[:, :], bacol[:, 0:1],
            None, ALU.mult)
        c.V("tensor_scalar", k2tm[:, :], ktm[:, :], bacol[:, 1:2], None, ALU.mult)
        rhs_d = c.vc.n
        nc.tensor.wait_ge(c.vc.h, rhs_d)
        i = c.T("matmul", psW[:, :], XT_bf[:, :], rhs_bf[:, :], start=True, stop=True)
        wd = c.tmark(i)
        nc.vector.wait_ge(c.tc.h, wd)
        c.V("tensor_copy", wv_sb[:, :], psW[:, 0:256])
        c.V("tensor_copy", wk_sb[:, :], psW[:, 256:512])
        w_d = c.vc.n
        nc.tensor.wait_ge(c.vc.h, w_d)
        c.V("tensor_scalar", aci[:, :], pid_sb[:, 0, :], accol[:, 0:1], None, ALU.mult)
        for kc in range(2):
            if kc == 1:
                c.V("tensor_scalar", aci[:, :], pid_sb[:, 1, :], accol[:, 0:1],
                    None, ALU.mult)
            i = c.T("matmul", psP[:, :], wk_sb[:, kc * 128:kc * 128 + 128],
                    k2tm[:, :], start=True, stop=True)
            pd = c.tmark(i)
            nc.vector.wait_ge(c.tc.h, pd)
            nc.vector.wait_ge(c.vc.h, c.vc.n)
            c.V("scalar_tensor_tensor", P_bf[:, kc, :], psP[:, :], -1.0,
                aci[:, :], ALU.mult, ALU.add)
            nc.tensor.wait_ge(c.vc.h, c.vc.n)
        for kc in range(2):
            i = c.T("matmul", psP[:, 0:128], wk_sb[:, kc * 128:kc * 128 + 128],
                    tp_bf[:, :], start=True, stop=True)
            qd_ = c.tmark(i)
            nc.vector.wait_ge(c.tc.h, qd_)
            c.V("tensor_mul", t1[:, :], qs[:, kc, :], Ab[:, :])
            c.V("tensor_sub", qeff[:, kc, :], t1[:, :], psP[:, 0:128])
            nc.tensor.wait_ge(c.vc.h, c.vc.n)
        for vh in range(2):
            for kc in range(2):
                c.T("matmul", psR[:, :], ST_bf[:, kc, vh * 128:vh * 128 + 128],
                    qeff[:, kc, :], start=(kc == 0), stop=False)
            i = c.T("matmul", psR[:, :], wv_sb[:, vh * 128:vh * 128 + 128],
                    tp_bf[:, :], start=False, stop=True)
            rd = c.tmark(i)
            nc.vector.wait_ge(c.tc.h, rd)
            c.V("tensor_copy", r32[:, vh, :], psR[:, :])
            nc.tensor.wait_ge(c.vc.h, c.vc.n)
        r_d = c.vc.n
        nc.scalar.wait_ge(c.vc.h, r_d)
        c.S("activation", rsq[:, :, :], r32[:, :, :], AF.Square)
        sq_d = c.sc.n
        nc.tensor.wait_ge(c.sc.h, sq_d)
        for vh in range(2):
            i = c.T("matmul", psB[0:1, :], ones_c[:, 0:1], rsq[:, vh, :],
                    start=(vh == 0), stop=(vh == 1))
        sd = c.tmark(i)
        nc.vector.wait_ge(c.tc.h, sd)
        c.V("tensor_copy", rrow[:, :], psB[0:1, :])
        c.vc.wait(nc.scalar)
        c.S("activation", rrow[:, :], rrow[:, :], AF.Sqrt, scale=1.0 / 256,
            bias=1e-6)
        c.sc.wait(nc.vector)
        c.V("reciprocal", rrow[:, :], rrow[:, :])
        rr_d = c.vc.n
        nc.tensor.wait_ge(c.vc.h, rr_d)
        i = c.T("matmul", psB[:, :], ones_r[0:1, :], rrow[0:1, :],
                start=True, stop=True)
        rbd = c.tmark(i)
        nc.vector.wait_ge(c.tc.h, rbd)
        nc.vector.wait_ge(c.gc.h, c.gc.n)
        c.V("tensor_mul", rn_bf[:, 0, :], r32[:, 0, :], psB[:, :])
        c.V("tensor_mul", rn_bf[:, 1, :], r32[:, 1, :], psB[:, :])
        rn_d = c.vc.n
        nc.gpsimd.wait_ge(c.vc.h, rn_d)
        c.gdma(r_w[:, :, tsl], rn_bf[:])
        for kh in range(2):
            c.T("matmul", psP[:, :], k2tm[:, kh * 128:kh * 128 + 128], wv_sb[:, :],
                start=True, stop=False)
            for kc in range(2):
                i = c.T("matmul", psP[:, :], P_bf[:, kc, kh * 128:kh * 128 + 128],
                        ST_bf[:, kc, :], start=False, stop=(kc == 1))
            su = c.tmark(i)
            nc.vector.wait_ge(c.tc.h, su)
            c.V("tensor_copy", ST32[:, kh, :], psP[:, :])
            nc.tensor.wait_ge(c.vc.h, c.vc.n)
        st_d = c.vc.n
        c.V("tensor_copy", ST_bf[:, :, :], ST32[:, :, :])
    c.gc.wait(nc.gpsimd)
    c.cur_stack = None
    if own:
        c.end()
        return c.nc
    barrier(c)
    blk.close()
    return None


def build_dn_out(T=TPC, c=None, io=None, gathered=False):
    own = c is None
    if own:
        c = Ctx("dn_out")
    nc = c.nc
    blk = c.st if own else ExitStack()
    if own:
        io = NS(xT=c.din("xT", (D, T)), r_in=c.din("r_in", (D, T), BF16),
                og_in=c.din("og_in", (D, T)), Wout=c.din("Wout", (D, D)),
                boutc=c.din("boutc", (128, 8)), rlc=c.din("rlc", (128, 8)),
                yT=c.dout("yT", (D, T)))
    xT, r_in, og_in, Wout, boutc, rlc, yT = (io.xT, io.r_in, io.og_in, io.Wout,
                                             io.boutc, io.rlc, io.yT)
    c.cur_stack = blk

    rb = c.sb("rb", [128, 8, T], BF16)
    og = [c.sb(f"og{i}", [128, 512]) for i in range(2)]
    xc = [c.sb(f"xc{i}", [128, 512]) for i in range(2)]
    wo_sb = c.sb("wo_sb", [128, 8, D], BF16)
    bo_sb = c.sb("bo_sb", [128, 8])
    rl_sb = c.sb("rl_sb", [128, 8])
    rmp = c.sb("rmp", [128, 8])
    etmp = c.sb("etmp", [128, 512])
    ps_y = [c.ps(f"ps_y{i}", [128, 512]) for i in range(2)]

    if own:
        c.begin()
    xls = [c.sem("xls0"), c.sem("xls1")]
    ogl = [c.sem("ogl0"), c.sem("ogl1")]
    sts = [c.sem("sts0"), c.sem("sts1")]
    c.gdma(wo_sb[:], Wout.rearrange("(ic p) o -> p ic o", p=128))
    if not gathered:
        for i in range(8):
            c.ydma(rb[:, i, :], r_in[i * 128:(i + 1) * 128, :])
    else:
        spid = c.pide("scalar")
        rbl = c.sem("rbl")
        for h in range(4):
            with nc.scalar.register(_rn("rro")) as rg:
                nc.scalar.reg_div(rg, spid, 4)
                nc.scalar.reg_mul(rg, rg, 4)
                nc.scalar.reg_add(rg, rg, h)
                nc.scalar.reg_mul(rg, rg, 256 * S)
                with nc.scalar.register(_rn("rro2")) as rg2:
                    nc.scalar.reg_mod(rg2, spid, 4)
                    nc.scalar.reg_mul(rg2, rg2, T)
                    nc.scalar.reg_add(rg, rg, rg2)
                dyn = bass.AP(r_in.tensor, rg, [[S, 128], [128 * S, 2], [1, T]])
                i_ = nc.scalar.dma_start(out=rb[:, 2 * h:2 * h + 2, :], in_=dyn)
                rbl.inc(i_, 16)
        for eng in (nc.tensor, nc.vector, nc.scalar, nc.gpsimd):
            rbl.wait(eng)
    c.ydma(bo_sb[:], boutc[:, :])
    c.ydma(rl_sb[:], rlc[:, :])
    c.yc.wait(nc.scalar)
    c.S("activation", rmp[:, :], rl_sb[:, :], AF.Sigmoid)
    rmp_d = c.sc.n
    nc.vector.wait_ge(c.sc.h, rmp_d)
    c.yc.wait(nc.tensor)
    c.gc.wait(nc.tensor)

    NQB = T // 512
    done_ev, st_at, xl_at, og_at = {}, {}, {}, {}

    def xload(k_):
        oc, qb = divmod(k_, NQB)
        if k_ >= 2:
            nc.sync.wait_ge(sts[k_ % 2].h, st_at.get(k_ - 2, 0))
        xl_at[k_] = c.ydma(xc[k_ % 2][:],
                           xT[oc * 128:(oc + 1) * 128, qb * 512:qb * 512 + 512],
                           sem=xls[k_ % 2])
        og_at[k_] = c.ydma(og[k_ % 2][:],
                           og_in[oc * 128:(oc + 1) * 128, qb * 512:qb * 512 + 512],
                           sem=ogl[k_ % 2])

    xload(0)
    xload(1)
    for k_ in range(8 * NQB):
        oc, qb = divmod(k_, NQB)
        py = ps_y[k_ % 2]
        if k_ >= 2:
            nc.tensor.wait_ge(c.vc.h, done_ev[k_ - 2])
        for icc in range(8):
            i = c.T("matmul", py[:, :], wo_sb[:, icc, oc * 128:oc * 128 + 128],
                    rb[:, icc, qb * 512:qb * 512 + 512],
                    start=(icc == 0), stop=(icc == 7))
        yd = c.tmark(i)
        nc.vector.wait_ge(c.tc.h, yd)
        nc.vector.wait_ge(xls[k_ % 2].h, xl_at[k_])
        nc.vector.wait_ge(ogl[k_ % 2].h, og_at[k_])
        c.V("tensor_scalar", etmp[:, :], py[:, :], bo_sb[:, oc:oc + 1], None, ALU.add)
        c.V("tensor_scalar", etmp[:, :], etmp[:, :], rmp[:, oc:oc + 1], None, ALU.mult)
        c.V("tensor_mul", etmp[:, :], etmp[:, :], og[k_ % 2][:, :])
        c.V("tensor_add", xc[k_ % 2][:, :], xc[k_ % 2][:, :], etmp[:, :])
        done_ev[k_] = c.vc.n
        nc.gpsimd.wait_ge(c.vc.h, done_ev[k_])
        st_at[k_] = c.gdma(yT[oc * 128:(oc + 1) * 128, qb * 512:qb * 512 + 512],
                           xc[k_ % 2][:], sem=sts[k_ % 2])
        if k_ + 2 < 8 * NQB:
            xload(k_ + 2)
    c.gc.wait(nc.gpsimd)
    c.cur_stack = None
    if own:
        c.end()
        return c.nc
    barrier(c)
    blk.close()
    return None


def run_deltanet(x, inp):
    nc_a = _get("dn_prep", build_dn_prep)
    nc_b = _get("dn_scan", lambda: build_dn_scan(S))
    nc_c = _get("dn_out", build_dn_out)
    HALO = 512
    Wc = inp["L0_attn_W_ctrl"]
    bc = inp["L0_attn_b_ctrl"]
    wcv = np.stack([inp["L0_attn_w_q"][:, 0, :], inp["L0_attn_w_k"][:, 0, :],
                    inp["L0_attn_w_v"][:, 0, :]], axis=1)
    wconv = np.ascontiguousarray(wcv.reshape(8, 128, 3, 4).transpose(1, 0, 2, 3))
    in_maps = []
    for cid in range(N_CORES):
        t0 = cid * TPC
        if t0 % S == 0:
            xe = np.concatenate([np.zeros((D, HALO), np.float32),
                                 x[:, t0:t0 + TPC]], axis=1)
        else:
            xe = x[:, t0 - HALO:t0 + TPC]
        in_maps.append({"xT": np.ascontiguousarray(xe),
                        "Win": inp["L0_attn_W_in"],
                        "Wab": np.ascontiguousarray(Wc[:, 0:8]),
                        "babc": np.ascontiguousarray(bc[0:8].reshape(8, 1)),
                        "Wog": np.ascontiguousarray(Wc[:, 8:]),
                        "bogc": np.ascontiguousarray(bc[8:].reshape(8, 128).T),
                        "wconv": wconv})
    res = run_bass_kernel_spmd(nc_a, in_maps, core_ids=list(range(N_CORES)))
    r = res.results
    q_all = np.concatenate([rr["q_out"] for rr in r], axis=1)
    k_all = np.concatenate([rr["k_out"] for rr in r], axis=1)
    v_all = np.concatenate([rr["v_out"] for rr in r], axis=1)
    ab_all = np.concatenate([rr["ab_out"] for rr in r], axis=1)
    og_all = np.concatenate([rr["og_out"] for rr in r], axis=1)
    tl = np.tril(np.ones((128, 128), np.float32), -1)
    tu = np.triu(np.ones((128, 128), np.float32), 1)
    tu0 = np.triu(np.ones((128, 128), np.float32), 0)
    idf = np.eye(128, dtype=np.float32)
    pid = np.zeros((128, 2, 256), np.float32)
    for rr_ in range(128):
        pid[rr_, 0, rr_] = 1.0
        pid[rr_, 1, 128 + rr_] = 1.0
    in_maps = []
    for cid in range(N_CORES):
        b, hh = divmod(cid, 4)
        rs = slice(hh * 256, hh * 256 + 256)
        cs_ = slice(b * S, b * S + S)
        in_maps.append({"q_in": np.ascontiguousarray(q_all[rs, cs_]),
                        "k_in": np.ascontiguousarray(k_all[rs, cs_]),
                        "v_in": np.ascontiguousarray(v_all[rs, cs_]),
                        "al_in": np.ascontiguousarray(ab_all[hh, cs_]),
                        "be_in": np.ascontiguousarray(ab_all[4 + hh, cs_]),
                        "TLm1": tl, "TUm1": tu, "TU0": tu0, "identf": idf,
                        "identb2": np.eye(128, dtype=BFD), "twoI": 2 * idf,
                        "Pident": pid})
    res = run_bass_kernel_spmd(nc_b, in_maps, core_ids=list(range(N_CORES)))
    r2 = res.results
    R = np.empty((D, NTOK), dtype=BFD)
    for cid in range(N_CORES):
        b, hh = divmod(cid, 4)
        R[hh * 256:hh * 256 + 256, b * S:(b + 1) * S] = r2[cid]["r_out"]
    in_maps = []
    for cid in range(N_CORES):
        t0 = cid * TPC
        in_maps.append({"xT": np.ascontiguousarray(x[:, t0:t0 + TPC]),
                        "r_in": np.ascontiguousarray(R[:, t0:t0 + TPC]),
                        "og_in": np.ascontiguousarray(og_all[:, t0:t0 + TPC]),
                        "Wout": inp["L0_attn_W_out"],
                        "boutc": np.ascontiguousarray(inp["L0_attn_b_out"].reshape(8, 128).T),
                        "rlc": np.ascontiguousarray(inp["L0_attn_ramp_logit"].reshape(8, 128).T)})
    res = run_bass_kernel_spmd(nc_c, in_maps, core_ids=list(range(N_CORES)))
    return np.concatenate([rr["yT"] for rr in res.results], axis=1)



def _allgather(c, contrib, gath, tag):
    nc = c.nc
    barrier(c)
    ccs = c.sem("ccs")
    n = 1
    for d_ in contrib.tensor.shape:
        n *= int(d_)
    cin = bass.AP(contrib.tensor, 0, [[1, n]])
    cout = bass.AP(gath.tensor, 0, [[1, 8 * n]])
    i = nc.gpsimd.collective_compute(
        "AllGather", ALU.bypass, replica_groups=[list(range(8))],
        ins=[cin.opt()], outs=[cout.opt()])
    ccs.inc(i, 1)
    for eng in (nc.tensor, nc.vector, nc.scalar, nc.gpsimd, nc.sync):
        ccs.wait(eng)


def _halo_asm(c, src, dst, halo_g, hz, tag):
    nc = c.nc
    T = src.shape[1]
    hcs = c.sem("hcs")
    c.gdma(dst[:, 512:512 + T], src[:, :])
    with ExitStack() as hb:
        tile = c.sb(f"hb_{tag}", [128, 8, 512], stack=hb)
        hzc = c.sb(f"hz_{tag}", [128, 1], stack=hb)
        c.ydma(hzc[:], hz[:, :])
        pid = c.pid()
        with nc.sync.register(_rn("hro")) as rg:
            nc.sync.reg_add(rg, pid, 7)
            nc.sync.reg_mod(rg, rg, 8)
            nc.sync.reg_mul(rg, rg, D * 512)
            dyn = bass.AP(halo_g.tensor, rg, [[512, 128], [128 * 512, 8], [1, 512]])
            at2 = c.ydma(tile[:], dyn, sem=hcs)
        nc.vector.wait_ge(hcs.h, at2)
        c.vc.wait(nc.vector)
        c.yc.wait(nc.vector)
        c.V("tensor_scalar", tile[:, :, :], tile[:, :, :], hzc[:, 0:1], None,
            ALU.mult)
        hd = c.vc.n
        nc.gpsimd.wait_ge(c.vc.h, hd)
        c.gdma(dst.rearrange("(ic p) t -> p ic t", p=128)[:, :, 0:512], tile[:])
        barrier(c)


def build_fused(T=TPC):
    c = Ctx("fused")
    nc = c.nc
    din, dout = c.din, c.dout
    HALO = 512

    def dsc(name, shape, dtype=F32):
        return nc.dram_tensor(name, list(shape), dtype, kind="Internal").ap()

    x0 = din("x0", (D, T))
    yout = dout("yout", (D, T))
    identf = din("identf", (128, 128))
    identb = din("identb", (128, 128), BF16)
    g_wgT = din("g_wgT", (D, 3072), I32)
    g_wuT = din("g_wuT", (D, 3072), I32)
    g_wdT = din("g_wdT", (3072, D), I32)
    dn_Win = din("dn_Win", (D, 3 * D))
    dn_Wab = din("dn_Wab", (D, 8))
    dn_babc = din("dn_babc", (8, 1))
    dn_Wog = din("dn_Wog", (D, D))
    dn_bogc = din("dn_bogc", (128, 8))
    dn_wconv = din("dn_wconv", (128, 8, 3, 4))
    dn_TLm1 = din("dn_TLm1", (128, 128))
    dn_TUm1 = din("dn_TUm1", (128, 128))
    dn_TU0 = din("dn_TU0", (128, 128))
    dn_twoI = din("dn_twoI", (128, 128))
    dn_Pident = din("dn_Pident", (128, 2, 256))
    dn_Wout = din("dn_Wout", (D, D))
    dn_boutc = din("dn_boutc", (128, 8))
    dn_rlc = din("dn_rlc", (128, 8))
    hz_in = din("hz_in", (128, 1))
    fp4_win = din("fp4_win", (N_UNITS, 128, 256))
    l1_WqT = din("l1_WqT", (D, D))
    l1_bqc = din("l1_bqc", (128, 8))
    l1_WkT = din("l1_WkT", (D, D))
    l1_bkc = din("l1_bkc", (128, 8))
    l1_WvT = din("l1_WvT", (D, D))
    l1_bvr = din("l1_bvr", (D,))
    l1_W0 = din("l1_W0", (128, 16, 64))
    l1_WoT = din("l1_WoT", (D, D))
    l1_boc = din("l1_boc", (128, 8))
    l1_bmask = din("l1_bmask", (1, 8))
    l1_bmaskc = din("l1_bmaskc", (128, 8))
    l2_nwr = din("l2_nwr", (128, 8))
    l2_rwT = din("l2_rwT", (D, 8))
    l2_sw1 = din("l2_sw1", (D, 4096))
    l2_sb1r = din("l2_sb1r", (128, 32))
    l2_sw2 = din("l2_sw2", (4096, D))
    l2_sb2r = din("l2_sb2r", (128, 8))
    l2_ew1 = din("l2_ew1", (8, D, 4096))
    l2_eb1r = din("l2_eb1r", (128, 8, 32))
    l2_ew2 = din("l2_ew2", (8, 4096, D))
    l2_eb2r = din("l2_eb2r", (128, 8, 8))
    l2_selmat = din("l2_selmat", (8, 8 * 128))
    a2_WqT = din("a2_WqT", (D, D))
    a2_WkT = din("a2_WkT", (D, D))
    a2_WvT = din("a2_WvT", (D, D))
    a2_WkpT = din("a2_WkpT", (D, D))
    a2_WoT = din("a2_WoT", (D, D))
    a2_gmr = din("a2_gmr", (128, 8))
    a2_btr = din("a2_btr", (128, 8))
    a2_maskb = din("a2_maskb", (128, 256))
    f3_Wr = din("f3_Wr", (D, 8))
    f3_brv = din("f3_brv", (8,))
    f3_wrow = din("f3_wrow", (8,))
    f3_W1 = din("f3_W1", (8, D, D))
    f3_b1r = din("f3_b1r", (128, 8, 8))
    f3_W2 = din("f3_W2", (8, D, D))
    f3_b2r = din("f3_b2r", (128, 8, 8))
    f3_selmat = din("f3_selmat", (8, 8 * 128))
    a3_Wq = din("a3_Wq", (D, D))
    a3_bqr = din("a3_bqr", (128, 8))
    a3_Wdn = din("a3_Wdn", (D, 256))
    a3_bdnr = din("a3_bdnr", (128, 2))
    a3_Wupk = din("a3_Wupk", (256, D))
    a3_bupkr = din("a3_bupkr", (128, 8))
    a3_Wupv = din("a3_Wupv", (256, D))
    a3_bupv = din("a3_bupv", (D,))
    a3_cosF = din("a3_cosF", (128, T))
    a3_sinPM = din("a3_sinPM", (128, T))
    a3_Pswap = din("a3_Pswap", (128, 128))
    a3_Wout = din("a3_Wout", (D, D))
    a3_boutr = din("a3_boutr", (128, 8))

    xA = dsc("xA", (D, T))
    xB = dsc("xB", (D, T))
    xhal = dsc("xhal", (D, T + HALO))
    halc = dsc("halc", (D, HALO))
    halg = dsc("halg", (8, D, HALO))
    dnq = dsc("dnq", (D, T), BF16)
    dnk = dsc("dnk", (D, T), BF16)
    dnv = dsc("dnv", (D, T), BF16)
    dnab = dsc("dnab", (8, T))
    dnog = dsc("dnog", (D, T))
    dnqg = dsc("dnqg", (8, D, T), BF16)
    dnkg = dsc("dnkg", (8, D, T), BF16)
    dnvg = dsc("dnvg", (8, D, T), BF16)
    dnabg = dsc("dnabg", (8, 8, T))
    dnr = dsc("dnr", (256, S), BF16)
    qL = dsc("qL", (256, S), BF16)
    kL = dsc("kL", (256, S), BF16)
    vL = dsc("vL", (256, S), BF16)
    alL = dsc("alL", (S,))
    beL = dsc("beL", (S,))
    dnrg = dsc("dnrg", (8, 256, S), BF16)
    fp4c = dsc("fp4c", (N_UNITS, 2, 128, 128), BF16)
    fp4g = dsc("fp4g", (8 * N_UNITS, 2, 128, 128), BF16)
    wgqT = dsc("wgqT_s", (D, 2816), BF16)
    wuqT = dsc("wuqT_s", (D, 2816), BF16)
    wdqT = dsc("wdqT_s", (2816, D), BF16)
    l1q = dsc("l1q", (D, T))
    l1v = dsc("l1v", (T, D), BF16)
    l1lg = dsc("l1lg", (T, D))
    l1lm = dsc("l1lm", (16,))
    l1lmg = dsc("l1lmg", (8, 16))
    l1np = dsc("l1np", (64, D))
    l1dp = dsc("l1dp", (64, 16))
    l1npg = dsc("l1npg", (8, 64, D))
    l1dpg = dsc("l1dpg", (8, 64, 16))
    f3h = dsc("f3h", (D, T), BF16)
    f3oh = dsc("f3oh", (8, T))
    f3cs = dsc("f3cs", (8, T))
    f3gt = dsc("f3gt", (T,))
    f3cn = dsc("f3cn", (8,))
    f3cng = dsc("f3cng", (8, 8))
    a3q = dsc("a3q", (D, T), BF16)
    a3k = dsc("a3k", (D, T), BF16)
    a3v = dsc("a3v", (T, D), BF16)
    a3kg = dsc("a3kg", (8, D, T), BF16)
    a3vg = dsc("a3vg", (8, T, D), BF16)
    kL3 = dsc("kL3", (D, S), BF16)
    vL3 = dsc("vL3", (S, D), BF16)

    c.begin()
    c.cur_stack = None
    c.make_consts([1e-5, 1.0 / 127.0, 1e-8, 0.125, 1e-6, 1.0 / 256])

    def blkpfx(p):
        c.prefix = p

    blkpfx("b1_")
    build_ffn0(T, c, NS(xT=x0, wgT=g_wgT, wuT=g_wuT, wdT=g_wdT, yT=xA))
    blkpfx("b2h_")
    c.gdma(halc[:, :], xA[:, T - HALO:T])
    _allgather(c, halc, halg, "dnh")
    _halo_asm(c, xA, xhal, halg, hz_in, "dn")
    blkpfx("b2p_")
    build_dn_prep(T, c, NS(xT=xhal, Win=dn_Win, Wab=dn_Wab, babc=dn_babc,
                           Wog=dn_Wog, bogc=dn_bogc, wconv=dn_wconv,
                           q_out=dnq, k_out=dnk, v_out=dnv, ab_out=dnab,
                           og_out=dnog))
    _allgather(c, dnq, dnqg, "dq")
    _allgather(c, dnk, dnkg, "dk")
    _allgather(c, dnv, dnvg, "dv")
    _allgather(c, dnab, dnabg, "da")
    blkpfx("b2g_")
    pgs = c.sem("pgs")
    pga = c.sem("pga")
    pid = c.pid()
    spid = c.pide("scalar")
    for r in range(4):
        for srcg, dst in ((dnqg, qL), (dnkg, kL), (dnvg, vL)):
            with nc.sync.register(_rn("pg")) as rg:
                nc.sync.reg_div(rg, pid, 4)
                nc.sync.reg_mul(rg, rg, 4)
                nc.sync.reg_add(rg, rg, r)
                nc.sync.reg_mul(rg, rg, D * T)
                with nc.sync.register(_rn("pg2")) as rg2:
                    nc.sync.reg_mod(rg2, pid, 4)
                    nc.sync.reg_mul(rg2, rg2, 256 * T)
                    nc.sync.reg_add(rg, rg, rg2)
                dyn = bass.AP(srcg.tensor, rg, [[T, 256], [1, T]])
                c.ydma(dst[:, r * T:(r + 1) * T], dyn, sem=pgs)
        for rowoff, dst in ((0, alL), (4, beL)):
            with nc.scalar.register(_rn("pga")) as rg:
                nc.scalar.reg_div(rg, spid, 4)
                nc.scalar.reg_mul(rg, rg, 4)
                nc.scalar.reg_add(rg, rg, r)
                nc.scalar.reg_mul(rg, rg, 8 * T)
                with nc.scalar.register(_rn("pga2")) as rg2:
                    nc.scalar.reg_mod(rg2, spid, 4)
                    nc.scalar.reg_add(rg2, rg2, rowoff)
                    nc.scalar.reg_mul(rg2, rg2, T)
                    nc.scalar.reg_add(rg, rg, rg2)
                dyn = bass.AP(dnabg.tensor, rg, [[1, T]])
                i_ = nc.scalar.dma_start(out=dst[r * T:(r + 1) * T], in_=dyn)
                pga.inc(i_, 16)
    for eng in (nc.tensor, nc.vector, nc.scalar, nc.gpsimd, nc.sync):
        pgs.wait(eng)
        pga.wait(eng)
    blkpfx("b2s_")
    build_dn_scan(S, c, NS(q_in=qL, k_in=kL, v_in=vL, al_in=alL,
                           be_in=beL, TLm1=dn_TLm1, TUm1=dn_TUm1, TU0=dn_TU0,
                           identf=identf, identb2=identb, twoI=dn_twoI,
                           Pident=dn_Pident, r_out=dnr))
    _allgather(c, dnr, dnrg, "dr")
    blkpfx("b2o_")
    build_dn_out(T, c, NS(xT=xA, r_in=dnrg, og_in=dnog, Wout=dn_Wout,
                          boutc=dn_boutc, rlc=dn_rlc, yT=xB), gathered=True)
    blkpfx("b3_")
    build_ffn0(T, c, NS(xT=xB, wgT=g_wgT, wuT=g_wuT, wdT=g_wdT, yT=xA))

    blkpfx("b4p_")
    build_fp4_prep(c, NS(win=fp4_win, identb=identb, wout=fp4c))
    _allgather(c, fp4c, fp4g, "fq")
    asq = c.sem("asq")
    n1 = (2816 // 128) * (1024 // 256)
    for wi, wdst in ((0, wgqT), (1, wuqT), (2, wdqT)):
        r_, cdim = ((2816, 1024) if wi < 2 else (1024, 2816))
        for rt in range(r_ // 128):
            for cu in range(cdim // 256):
                u = wi * n1 + rt * (cdim // 256) + cu
                for b2 in range(2):
                    c.gdma(wdst[cu * 256 + b2 * 128:cu * 256 + b2 * 128 + 128,
                                rt * 128:rt * 128 + 128],
                           fp4g[u, b2], sem=asq)
    for eng in (nc.tensor, nc.vector, nc.scalar, nc.gpsimd, nc.sync):
        asq.wait(eng)
    barrier(c)
    blkpfx("b4_")
    build_fp4_app(T, c, NS(x_in=xA, wgqT=wgqT, wuqT=wuqT, wdqT=wdqT,
                           identb=identb, identf32=identf, y_out=xB), fm_io=True)
    blkpfx("b5q_")
    build_l1a_qkv(T, c, NS(xT=xB, WqT=l1_WqT, bqc=l1_bqc, WkT=l1_WkT,
                           bkc=l1_bkc, WvT=l1_WvT, bvr=l1_bvr, W0in=l1_W0,
                           identf=identf, q_out=l1q, v_out=l1v, lg_out=l1lg,
                           lmax_out=l1lm))
    _allgather(c, l1lm, l1lmg, "lm")
    blkpfx("b5n_")
    build_l1a_nd(T, c, NS(lg_in=l1lg, v_in=l1v, lmax8=l1lmg,
                          np_out=l1np, dp_out=l1dp, bmask=l1_bmask))
    _allgather(c, l1np, l1npg, "np")
    _allgather(c, l1dp, l1dpg, "dp")
    blkpfx("b5o_")
    build_l1a_out(T, c, NS(xT=xB, q_in=l1q, np_in=l1npg, dp_in=l1dpg,
                           W0in=l1_W0, WoT=l1_WoT, boc=l1_boc, yT=xA,
                           bmaskc=l1_bmaskc))
    blkpfx("b6_")
    build_fp4_app(T, c, NS(x_in=xA, wgqT=wgqT, wuqT=wuqT, wdqT=wdqT,
                           identb=identb, identf32=identf, y_out=xB), fm_io=True)

    blkpfx("b7_")
    build_l2f(T, c, NS(xT=xB, nwr=l2_nwr, rwT=l2_rwT, sw1=l2_sw1, sb1r=l2_sb1r,
                       sw2=l2_sw2, sb2r=l2_sb2r, ew1=l2_ew1, eb1r=l2_eb1r,
                       ew2=l2_ew2, eb2r=l2_eb2r, identf=identf,
                       selmat=l2_selmat, yT=xA))
    blkpfx("b8h_")
    c.gdma(halc[:, :], xA[:, T - HALO:T])
    _allgather(c, halc, halg, "l2h")
    _halo_asm(c, xA, xhal, halg, hz_in, "l2")
    blkpfx("b8_")
    build_l2a(T, c, NS(xT=xhal, WqT=a2_WqT, WkT=a2_WkT, WvT=a2_WvT,
                       WkpT=a2_WkpT, WoT=a2_WoT, gmr=a2_gmr, btr=a2_btr,
                       maskb=a2_maskb, identb=identb, yT=xB))
    blkpfx("b9_")
    build_l2f(T, c, NS(xT=xB, nwr=l2_nwr, rwT=l2_rwT, sw1=l2_sw1, sb1r=l2_sb1r,
                       sw2=l2_sw2, sb2r=l2_sb2r, ew1=l2_ew1, eb1r=l2_eb1r,
                       ew2=l2_ew2, eb2r=l2_eb2r, identf=identf,
                       selmat=l2_selmat, yT=xA))

    blkpfx("b10r_")
    build_l3f_route(T, c, NS(xT=xA, Wr=f3_Wr, brv=f3_brv, identf=identf,
                             h_out=f3h, oh_out=f3oh, csum_out=f3cs,
                             gate_out=f3gt, counts_out=f3cn))
    _allgather(c, f3cn, f3cng, "c1")
    blkpfx("b10e_")
    build_l3f_experts(T, c, NS(xT=xA, hT=f3h, oh_in=f3oh, csum_in=f3cs,
                               gate_in=f3gt, counts_ce=f3cng, wrow=f3_wrow,
                               W1=f3_W1, b1r=f3_b1r, W2=f3_W2, b2r=f3_b2r,
                               selmat=f3_selmat, yT=xB))
    blkpfx("b11q_")
    build_l3a_qkv(T, c, NS(xT=xB, Wq=a3_Wq, bqr=a3_bqr, Wdn=a3_Wdn,
                           bdnr=a3_bdnr, Wupk=a3_Wupk, bupkr=a3_bupkr,
                           Wupv=a3_Wupv, bupv=a3_bupv, cosF=a3_cosF,
                           sinPM=a3_sinPM, Pswap=a3_Pswap,
                           q_out=a3q, k_out=a3k, v_out=a3v))
    _allgather(c, a3k, a3kg, "k3")
    _allgather(c, a3v, a3vg, "v3")
    blkpfx("b11g_")
    kvp = c.sem("kvp")
    gpid = c.pide("gpsimd")
    for r in range(4):
        with nc.gpsimd.register(_rn("kv")) as rg:
            nc.gpsimd.reg_div(rg, gpid, 4)
            nc.gpsimd.reg_mul(rg, rg, 4)
            nc.gpsimd.reg_add(rg, rg, r)
            nc.gpsimd.reg_mul(rg, rg, D * T)
            dyn = bass.AP(a3kg.tensor, rg, [[T, D], [1, T]])
            i_ = nc.gpsimd.dma_start(out=kL3[:, r * T:(r + 1) * T], in_=dyn)
            kvp.inc(i_, 16)
        with nc.gpsimd.register(_rn("kv2")) as rg:
            nc.gpsimd.reg_div(rg, gpid, 4)
            nc.gpsimd.reg_mul(rg, rg, 4)
            nc.gpsimd.reg_add(rg, rg, r)
            nc.gpsimd.reg_mul(rg, rg, T * D)
            dyn = bass.AP(a3vg.tensor, rg, [[1, T * D]])
            i_ = nc.gpsimd.dma_start(
                out=bass.AP(vL3.tensor, r * T * D, [[1, T * D]]), in_=dyn)
            kvp.inc(i_, 16)
    for eng in (nc.tensor, nc.vector, nc.scalar, nc.gpsimd, nc.sync):
        kvp.wait(eng)
    blkpfx("b11_")
    build_l3a_attn(T, S, c, NS(xT=xB, q_in=a3q, k_in=kL3, v_in=vL3,
                               Wout=a3_Wout, boutr=a3_boutr, yT=xA))
    blkpfx("b12r_")
    build_l3f_route(T, c, NS(xT=xA, Wr=f3_Wr, brv=f3_brv, identf=identf,
                             h_out=f3h, oh_out=f3oh, csum_out=f3cs,
                             gate_out=f3gt, counts_out=f3cn))
    _allgather(c, f3cn, f3cng, "c2")
    blkpfx("b12e_")
    build_l3f_experts(T, c, NS(xT=xA, hT=f3h, oh_in=f3oh, csum_in=f3cs,
                               gate_in=f3gt, counts_ce=f3cng, wrow=f3_wrow,
                               W1=f3_W1, b1r=f3_b1r, W2=f3_W2, b2r=f3_b2r,
                               selmat=f3_selmat, yT=yout))
    c.prefix = ""
    c.end()
    return c.nc


def run_fused(inp):
    nc = _get("fused", build_fused)
    x = np.ascontiguousarray(inp["L0_attn_x"].reshape(NTOK, D).T)
    identf = np.eye(128, dtype=np.float32)
    identb = np.eye(128, dtype=BFD)

    g_wgT = np.ascontiguousarray(inp["L0_ffn_w_gate"].T)
    g_wuT = np.ascontiguousarray(inp["L0_ffn_w_up"].T)
    g_wdT = np.ascontiguousarray(inp["L0_ffn_w_down"].T)
    Wc = inp["L0_attn_W_ctrl"]
    bc = inp["L0_attn_b_ctrl"]
    wcv = np.stack([inp["L0_attn_w_q"][:, 0, :], inp["L0_attn_w_k"][:, 0, :],
                    inp["L0_attn_w_v"][:, 0, :]], axis=1)
    wconv = np.ascontiguousarray(wcv.reshape(8, 128, 3, 4).transpose(1, 0, 2, 3))
    tl = np.tril(np.ones((128, 128), np.float32), -1)
    tu = np.triu(np.ones((128, 128), np.float32), 1)
    tu0 = np.triu(np.ones((128, 128), np.float32), 0)
    pidm = np.zeros((128, 2, 256), np.float32)
    for rr_ in range(128):
        pidm[rr_, 0, rr_] = 1.0
        pidm[rr_, 1, 128 + rr_] = 1.0
    units = []
    for w in (inp["L1_ffn_w_gate"], inp["L1_ffn_w_up"], inp["L1_ffn_w_down"]):
        r, cdim = w.shape
        u = w.reshape(r // 128, 128, cdim // 256, 256).transpose(0, 2, 1, 3)
        units.append(u.reshape(-1, 128, 256))
    units = np.concatenate(units, axis=0)
    W0h = inp["L1_attn_Wrf"][0].transpose(1, 0, 2)
    W0 = np.ascontiguousarray(np.concatenate([W0h, W0h], axis=0))
    nwr = np.ascontiguousarray(inp["L2_ffn_norm_weight"].reshape(8, 128).T)
    rwT = np.ascontiguousarray(inp["L2_ffn_router_w"].T)
    sb1r = np.ascontiguousarray(inp["L2_ffn_sb1"].reshape(32, 128).T)
    sb2r = np.ascontiguousarray(inp["L2_ffn_sb2"].reshape(8, 128).T)
    eb1r = np.ascontiguousarray(inp["L2_ffn_eb1"].reshape(8, 32, 128).transpose(2, 0, 1))
    eb2r = np.ascontiguousarray(inp["L2_ffn_eb2"].reshape(8, 8, 128).transpose(2, 0, 1))
    selmat = np.repeat(np.eye(8, dtype=np.float32), 128, axis=1)
    gmr = np.ascontiguousarray(inp["L2_attn_gamma"].reshape(8, 128).T)
    btr = np.ascontiguousarray(inp["L2_attn_beta"].reshape(8, 128).T)
    t_idx = np.arange(128)[:, None, None]
    j_idx = np.arange(16)[None, :, None]
    k_idx = np.arange(16)[None, None, :]
    allowed = (j_idx >= 15 - t_idx) & (k_idx >= 15 - t_idx)
    mask0 = np.where(allowed, 0.0, -1e9).astype(np.float32).reshape(128, 256)
    maskz = np.zeros((128, 256), np.float32)
    b1r = np.ascontiguousarray(inp["L3_ffn_b1"].reshape(8, 8, 128).transpose(2, 0, 1))
    b2r = np.ascontiguousarray(inp["L3_ffn_b2"].reshape(8, 8, 128).transpose(2, 0, 1))
    bqr = np.ascontiguousarray(inp["L3_attn_bq"].reshape(8, 128).T)
    bdnr = np.ascontiguousarray(inp["L3_attn_bdown"].reshape(2, 128).T)
    Wup = inp["L3_attn_Wup"].reshape(256, 16, 2, 64)
    Wupk = np.ascontiguousarray(Wup[:, :, 0, :].reshape(256, 1024))
    Wupv = np.ascontiguousarray(Wup[:, :, 1, :].reshape(256, 1024))
    bup = inp["L3_attn_bup"].reshape(16, 2, 64)
    bupk = np.ascontiguousarray(bup[:, 0, :].reshape(1024))
    bupv = np.ascontiguousarray(bup[:, 1, :].reshape(1024))
    bupkr = np.ascontiguousarray(bupk.reshape(8, 128).T)
    inv_freq = 1.0 / 10000.0 ** (np.arange(0, 64, 2, dtype=np.float64) / 64.0)
    Pswap = np.zeros((128, 128), np.float32)
    for hh in range(2):
        for j in range(32):
            Pswap[hh * 64 + 32 + j, hh * 64 + j] = 1.0
            Pswap[hh * 64 + j, hh * 64 + 32 + j] = 1.0
    boutr = np.ascontiguousarray(inp["L3_attn_bout"].reshape(8, 128).T)

    shared = {
        "identf": identf, "identb": identb,
        "g_wgT": g_wgT, "g_wuT": g_wuT, "g_wdT": g_wdT,
        "dn_Win": inp["L0_attn_W_in"],
        "dn_Wab": np.ascontiguousarray(Wc[:, 0:8]),
        "dn_babc": np.ascontiguousarray(bc[0:8].reshape(8, 1)),
        "dn_Wog": np.ascontiguousarray(Wc[:, 8:]),
        "dn_bogc": np.ascontiguousarray(bc[8:].reshape(8, 128).T),
        "dn_wconv": wconv, "dn_TLm1": tl, "dn_TUm1": tu, "dn_TU0": tu0,
        "dn_twoI": 2 * identf, "dn_Pident": pidm,
        "dn_Wout": inp["L0_attn_W_out"],
        "dn_boutc": np.ascontiguousarray(inp["L0_attn_b_out"].reshape(8, 128).T),
        "dn_rlc": np.ascontiguousarray(inp["L0_attn_ramp_logit"].reshape(8, 128).T),
        "l1_WqT": np.ascontiguousarray(inp["L1_attn_Wq"].T),
        "l1_bqc": np.ascontiguousarray(inp["L1_attn_bq"].reshape(8, 128).T),
        "l1_WkT": np.ascontiguousarray(inp["L1_attn_Wk"].T),
        "l1_bkc": np.ascontiguousarray(inp["L1_attn_bk"].reshape(8, 128).T),
        "l1_WvT": np.ascontiguousarray(inp["L1_attn_Wv"].T),
        "l1_bvr": inp["L1_attn_bv"], "l1_W0": W0,
        "l1_WoT": np.ascontiguousarray(inp["L1_attn_Wo"].T),
        "l1_boc": np.ascontiguousarray(inp["L1_attn_bo"].reshape(8, 128).T),
        "l2_nwr": nwr, "l2_rwT": rwT,
        "l2_sw1": inp["L2_ffn_sw1"], "l2_sb1r": sb1r,
        "l2_sw2": inp["L2_ffn_sw2"], "l2_sb2r": sb2r,
        "l2_ew1": inp["L2_ffn_ew1"], "l2_eb1r": eb1r,
        "l2_ew2": inp["L2_ffn_ew2"], "l2_eb2r": eb2r,
        "l2_selmat": selmat,
        "a2_WqT": np.ascontiguousarray(inp["L2_attn_Wq"].T),
        "a2_WkT": np.ascontiguousarray(inp["L2_attn_Wk"].T),
        "a2_WvT": np.ascontiguousarray(inp["L2_attn_Wv"].T),
        "a2_WkpT": np.ascontiguousarray(inp["L2_attn_Wkp"].T),
        "a2_WoT": np.ascontiguousarray(inp["L2_attn_Wo"].T),
        "a2_gmr": gmr, "a2_btr": btr,
        "f3_Wr": np.ascontiguousarray(inp["L3_ffn_Wr"]),
        "f3_brv": np.ascontiguousarray(inp["L3_ffn_br"]),
        "f3_W1": inp["L3_ffn_W1"], "f3_b1r": b1r,
        "f3_W2": inp["L3_ffn_W2"], "f3_b2r": b2r,
        "f3_selmat": selmat,
        "a3_Wq": inp["L3_attn_Wq"], "a3_bqr": bqr,
        "a3_Wdn": inp["L3_attn_Wdown"], "a3_bdnr": bdnr,
        "a3_Wupk": Wupk, "a3_bupkr": bupkr,
        "a3_Wupv": Wupv, "a3_bupv": bupv, "a3_Pswap": Pswap,
        "a3_Wout": inp["L3_attn_Wout"], "a3_boutr": boutr,
    }

    in_maps = []
    for cid in range(N_CORES):
        t0 = cid * TPC
        b = cid // 4
        pos = np.arange(TPC, dtype=np.float64) + (t0 % S)
        fr = pos[:, None] * inv_freq[None, :]
        cos32 = np.cos(fr).astype(np.float32).T
        sin32 = np.sin(fr).astype(np.float32).T
        cosF = np.ascontiguousarray(np.tile(np.concatenate([cos32, cos32], 0), (2, 1)))
        sinPM = np.ascontiguousarray(np.tile(np.concatenate([-sin32, sin32], 0), (2, 1)))
        bmask = np.zeros((1, 8), np.float32)
        bmask[0, b * 4:b * 4 + 4] = 1.0
        bmaskc = np.ascontiguousarray(np.broadcast_to(bmask, (128, 8)))
        hz = np.zeros((128, 1), np.float32) if t0 % S == 0 \
            else np.ones((128, 1), np.float32)
        m = dict(shared)
        m.update({
            "x0": np.ascontiguousarray(x[:, t0:t0 + TPC]),
            "hz_in": hz,
            "fp4_win": np.ascontiguousarray(units[cid * N_UNITS:(cid + 1) * N_UNITS]),
            "l1_bmask": bmask, "l1_bmaskc": bmaskc,
            "a2_maskb": mask0 if t0 % S == 0 else maskz,
            "f3_wrow": (np.arange(8) < cid).astype(np.float32),
            "a3_cosF": cosF, "a3_sinPM": sinPM,
        })
        in_maps.append(m)
    res = run_bass_kernel_spmd(nc, in_maps, core_ids=list(range(N_CORES)))
    y = np.concatenate([r["yout"] for r in res.results], axis=1)
    return y


def kernel_fused(**inp):
    y = run_fused(inp)
    return np.ascontiguousarray(y.T).reshape(B, S, D).astype(np.float32)


# version 04 (final)
# speedup vs baseline: 1.2369x; 1.2369x over previous
import math
from contextlib import ExitStack
from types import SimpleNamespace as NS

import numpy as np
import ml_dtypes
import concourse.bass as bass
import concourse.mybir as mybir
from concourse.bass_utils import run_bass_kernel_spmd

F32 = mybir.dt.float32
BF16 = mybir.dt.bfloat16
I32 = mybir.dt.int32
AF = mybir.ActivationFunctionType
ALU = mybir.AluOpType
AX = mybir.AxisListType

N_CORES = 8
D = 1024
B = 2
S = 8192
NTOK = B * S
TPC = NTOK // N_CORES

BFD = np.dtype(ml_dtypes.bfloat16)


class Sem:

    def __init__(self, h):
        self.h = h
        self.n = 0

    def inc(self, instr, k=1):
        instr.then_inc(self.h, k)
        self.n += k
        return instr

    def wait(self, eng, n=None):
        eng.wait_ge(self.h, self.n if n is None else n)


_REGC = [0]
def _rn(tag):
    _REGC[0] += 1
    return f"{tag}{_REGC[0]}"


class Ctx:

    def __init__(self, name):
        self.nc = bass.Bass("TRN2", target_bir_lowering=False, debug=False)
        self.st = ExitStack()
        self.name = name

    def sb(self, name, shape, dtype=F32, stack=None):
        name = getattr(self, "prefix", "") + name
        st = stack or getattr(self, "cur_stack", None) or self.st
        return st.enter_context(self.nc.sbuf_tensor(name, shape, dtype))

    def ps(self, name, shape, dtype=F32, stack=None):
        name = getattr(self, "prefix", "") + name
        st = stack or getattr(self, "cur_stack", None) or self.st
        return st.enter_context(self.nc.psum_tensor(name, shape, dtype))

    def sem(self, name):
        reg = getattr(self, "_semreg", None)
        if reg is None:
            reg = self._semreg = {}
        if name not in reg:
            reg[name] = Sem(self.st.enter_context(self.nc.semaphore(name)))
        return reg[name]

    def din(self, name, shape, dtype=F32):
        return self.nc.dram_tensor(name, list(shape), dtype, kind="ExternalInput").ap()

    def dout(self, name, shape, dtype=F32):
        return self.nc.dram_tensor(name, list(shape), dtype, kind="ExternalOutput").ap()

    def begin(self):
        self.block = self.st.enter_context(self.nc.Block())
        self.tc = self.sem("ch_t")
        self.vc = self.sem("ch_v")
        self.sc = self.sem("ch_s")
        self.gc = self.sem("ch_g")
        self.yc = self.sem("ch_y")

    def end(self):
        self.st.close()

    def V(self, method, *a, **k):
        self.nc.vector.wait_ge(self.vc.h, self.vc.n)
        i = getattr(self.nc.vector, method)(*a, **k)
        self.vc.inc(i)
        return i

    def S(self, method, *a, **k):
        self.nc.scalar.wait_ge(self.sc.h, self.sc.n)
        i = getattr(self.nc.scalar, method)(*a, **k)
        self.sc.inc(i)
        return i

    def T(self, method, *a, **k):
        return getattr(self.nc.tensor, method)(*a, **k)

    def tmark(self, instr=None):
        self.tc.inc(instr if instr is not None else self.nc.tensor.nop())
        return self.tc.n

    def gdma(self, out, in_, sem=None, **k):
        i = self.nc.gpsimd.dma_start(out=out, in_=in_, **k)
        sem = sem if sem is not None else self.gc
        sem.inc(i, 16)
        return sem.n

    def ydma(self, out, in_, sem=None, **k):
        i = self.nc.sync.dma_start(out=out, in_=in_, **k)
        sem = sem if sem is not None else self.yc
        sem.inc(i, 16)
        return sem.n

    def pid(self):
        return self.pide("sync")

    def pide(self, engname):
        d = getattr(self, "_pids", None)
        if d is None:
            d = self._pids = {}
        if engname not in d:
            d[engname] = getattr(self.nc, engname).partition_id()
        return d[engname]

    def make_consts(self, values):
        done = getattr(self, "_consts_done", set())
        self._consts_done = done
        vals = sorted(set(float(v) for v in values) - done)
        if not vals:
            return
        done.update(vals)
        tile = self.sb(f"c_constpool{len(done)}", [128, max(1, len(vals))])
        for j, v in enumerate(vals):
            self.V("memset", tile[:, j:j + 1], v)
            self.nc.const_aps.register(F32, v, tile[:, j:j + 1])
        n = self.vc.n
        for eng in (self.nc.scalar, self.nc.tensor, self.nc.gpsimd, self.nc.sync):
            eng.wait_ge(self.vc.h, n)



def make_ln(c, ps_sums, ps_bc):
    key = getattr(c, "prefix", "")
    if getattr(c, "_ln_key", None) != key:
        c._ln_key = key
        c.ln_sq = [c.sb("ln_sq0", [128, 512]), c.sb("ln_sq1", [128, 512])]
        c.ln_rows = c.sb("ln_rows", [1, 4 * 512])
        c.ln_ones_col = c.sb("ln_ones_col", [128, 1])
        c.ln_ones_row = c.sb("ln_ones_row", [1, 128])
        c.ln_br = c.sb("ln_br", [128, 512])
        c.ln_bm = c.sb("ln_bm", [128, 512])
        c.ln_init_done = False
    c.ln_ps_sums = ps_sums
    c.ln_ps_bc = ps_bc


def ln_fm_512(c, x_views, h_views, eps=1e-5, h32_views=None, rms2_eps=None,
              colmul=None, coladd=None):
    nc = c.nc
    nch = len(x_views)
    ndim = nch * 128
    if not c.ln_init_done:
        c.V("memset", c.ln_ones_col[:], 1.0)
        c.V("memset", c.ln_ones_row[:], 1.0)
        c.ln_init_done = True
    c.vc.wait(nc.tensor)
    for icc in range(nch):
        c.T("matmul", c.ln_ps_sums[0:1, :], c.ln_ones_col[:, 0:1],
            x_views[icc], start=(icc == 0), stop=(icc == nch - 1))
    c.vc.wait(nc.scalar)
    sqmm_at = {}
    for icc in range(nch):
        if icc >= 2:
            nc.scalar.wait_ge(c.tc.h, sqmm_at[icc - 2])
        c.S("activation", c.ln_sq[icc % 2][:, :], x_views[icc], AF.Square)
        c.sc.wait(nc.tensor)
        i = c.T("matmul", c.ln_ps_bc[0:1, :], c.ln_ones_col[:, 0:1],
                c.ln_sq[icc % 2][:, :], start=(icc == 0), stop=(icc == nch - 1))
        sqmm_at[icc] = c.tmark(i)
    sums_done = c.tc.n
    r = c.ln_rows
    rm, rv, rs = r[0:1, 0:512], r[0:1, 512:1024], r[0:1, 1024:1536]
    nc.vector.wait_ge(c.tc.h, sums_done)
    c.V("tensor_scalar", rm, c.ln_ps_sums[0:1, :], 1.0 / ndim, None, ALU.mult)
    c.V("tensor_scalar", rv, c.ln_ps_bc[0:1, :], 1.0 / ndim, None, ALU.mult)
    c.V("tensor_mul", rs, rm, rm)
    c.V("tensor_sub", rv, rv, rs)
    rw = r[0:1, 1536:2048]
    if rms2_eps is not None:
        c.V("tensor_copy", rw, rv)
    c.vc.wait(nc.scalar)
    c.S("activation", rs, rv, AF.Sqrt, bias=eps)
    c.sc.wait(nc.vector)
    c.V("reciprocal", rv, rs)
    if rms2_eps is not None:
        c.V("tensor_mul", rs, rv, rv)
        c.V("tensor_mul", rs, rs, rw)
        c.vc.wait(nc.scalar)
        c.S("activation", rs, rs, AF.Sqrt, bias=rms2_eps)
        c.sc.wait(nc.vector)
        c.V("reciprocal", rw, rs)
        c.V("tensor_mul", rv, rv, rw)
    c.V("scalar_tensor_tensor", rm, rm, -1.0, rv, ALU.mult, ALU.mult)
    rows_done = c.vc.n
    nc.tensor.wait_ge(c.vc.h, rows_done)
    i = c.T("matmul", c.ln_ps_bc[:, :], c.ln_ones_row[0:1, :], rv, start=True, stop=True)
    bc1 = c.tmark(i)
    nc.vector.wait_ge(c.tc.h, bc1)
    c.V("tensor_copy", c.ln_br[:, :], c.ln_ps_bc[:, :])
    nc.tensor.wait_ge(c.vc.h, c.vc.n)
    i = c.T("matmul", c.ln_ps_bc[:, :], c.ln_ones_row[0:1, :], rm, start=True, stop=True)
    bc2 = c.tmark(i)
    nc.vector.wait_ge(c.tc.h, bc2)
    c.V("tensor_copy", c.ln_bm[:, :], c.ln_ps_bc[:, :])
    for icc in range(nch):
        c.V("tensor_mul", c.ln_sq[icc % 2][:, :], x_views[icc], c.ln_br[:, :])
        if colmul is not None:
            c.V("tensor_add", c.ln_sq[icc % 2][:, :], c.ln_sq[icc % 2][:, :],
                c.ln_bm[:, :])
            if coladd is not None:
                c.V("tensor_scalar", h_views[icc], c.ln_sq[icc % 2][:, :],
                    colmul[icc], coladd[icc], ALU.mult, ALU.add)
            elif h32_views is not None:
                c.V("tensor_scalar", h32_views[icc], c.ln_sq[icc % 2][:, :],
                    colmul[icc], None, ALU.mult)
                c.V("tensor_copy", h_views[icc], h32_views[icc])
            else:
                c.V("tensor_scalar", h_views[icc], c.ln_sq[icc % 2][:, :],
                    colmul[icc], None, ALU.mult)
        elif h32_views is not None:
            c.V("tensor_add", h32_views[icc], c.ln_sq[icc % 2][:, :], c.ln_bm[:, :])
            c.V("tensor_copy", h_views[icc], h32_views[icc])
        else:
            c.V("tensor_add", h_views[icc], c.ln_sq[icc % 2][:, :], c.ln_bm[:, :])



def build_ffn0(T=TPC, c=None, io=None):
    own = c is None
    if own:
        c = Ctx("ffn0")
    nc = c.nc
    blk = c.st if own else ExitStack()
    if own:
        io = NS(xT=c.din("xT", (D, T)), wgT=c.din("wgT", (D, 3072), I32),
                wuT=c.din("wuT", (D, 3072), I32), wdT=c.din("wdT", (3072, D), I32),
                yT=c.dout("yT", (D, T)))
    xT, wgT, wuT, wdT, yT = io.xT, io.wgT, io.wuT, io.wdT, io.yT
    H = 3072
    NOC = H // 128
    TT = 1024
    NTT = T // TT
    s127 = 1.0 / 127.0
    c.cur_stack = blk

    x = [c.sb(f"x{i}", [128, T]) for i in range(8)]
    h = [c.sb(f"hln{i}", [128, TT], BF16) for i in range(8)]
    h2 = c.sb("hid2", [128, NOC, TT], BF16)
    silu_t = c.sb("silu", [128, 512])
    wbuf = [c.sb(f"wb{i}", [128, 2, 8, 128], BF16) for i in range(2)]
    dbuf = [c.sb(f"db{i}", [128, 24, 128], BF16) for i in range(2)]

    ps_g = [c.ps(f"psg{i}", [128, 512]) for i in range(2)]
    ps_u = [c.ps(f"psu{i}", [128, 512]) for i in range(2)]
    ps_o = [c.ps(f"pso{i}", [128, 512]) for i in range(2)]
    ps_sums = c.ps("ps_sums", [1, 512])
    ps_bc = c.ps("ps_bc", [128, 512])

    wls = None
    dls = None
    if own:
        c.begin()
    wls = [c.sem("wls0"), c.sem("wls1")]
    dls = [c.sem("dls0"), c.sem("dls1")]
    make_ln(c, ps_sums, ps_bc)
    c.make_consts([1e-5, s127])

    for i in range(8):
        c.ydma(x[i][:], xT[i * 128:(i + 1) * 128, :])
    c.yc.wait(nc.scalar)
    c.yc.wait(nc.vector)
    c.yc.wait(nc.tensor)

    wgT_r = wgT.rearrange("(ic p) o -> p ic o", p=128)
    wuT_r = wuT.rearrange("(ic p) o -> p ic o", p=128)
    wdT_r = wdT.rearrange("(hc p) d -> p hc d", p=128)

    for tt in range(NTT):
        t0 = tt * TT
        if tt > 0:
            nc.vector.wait_ge(c.tc.h, c.tc.n)
            nc.scalar.wait_ge(c.tc.h, c.tc.n)
        for hh in range(2):
            a, b_ = t0 + hh * 512, t0 + hh * 512 + 512
            ln_fm_512(c, [xi[:, a:b_] for xi in x],
                      [hi[:, hh * 512:hh * 512 + 512] for hi in h])
        h_ready = c.vc.n
        nc.tensor.wait_ge(c.vc.h, h_ready)
        nc.gpsimd.wait_ge(c.vc.h, h_ready)

        wl_at, mm_at, evs_at, evv_at = {}, {}, {}, {}

        def prefetch(oc):
            if oc >= 2:
                nc.gpsimd.wait_ge(c.tc.h, mm_at[oc - 2])
            o0 = oc * 128
            nbuf = wbuf[oc % 2]
            sm = wls[oc % 2]
            c.gdma(nbuf[:, 0], wgT_r[:, :, o0:o0 + 128], sem=sm)
            wl_at[oc] = (sm, c.gdma(nbuf[:, 1], wuT_r[:, :, o0:o0 + 128], sem=sm))

        prefetch(0)
        prefetch(1)
        for oc in range(NOC):
            buf = wbuf[oc % 2]
            nc.tensor.wait_ge(wl_at[oc][0].h, wl_at[oc][1])
            pg, pu = ps_g[oc % 2], ps_u[oc % 2]
            if oc >= 2:
                nc.tensor.wait_ge(c.sc.h, evs_at[oc - 2])
                nc.tensor.wait_ge(c.vc.h, evv_at[oc - 2])
            for hh in range(2):
                sl = slice(hh * 512, hh * 512 + 512)
                if hh == 1:
                    nc.tensor.wait_ge(c.sc.h, c.sc.n)
                    nc.tensor.wait_ge(c.vc.h, c.vc.n)
                for icc in range(8):
                    i = c.T("matmul", pg[:, :], buf[:, 0, icc], h[icc][:, sl],
                            start=(icc == 0), stop=(icc == 7))
                g_done = c.tmark(i)
                for icc in range(8):
                    i = c.T("matmul", pu[:, :], buf[:, 1, icc], h[icc][:, sl],
                            start=(icc == 0), stop=(icc == 7))
                u_done = c.tmark(i)
                nc.scalar.wait_ge(c.tc.h, g_done)
                nc.scalar.wait_ge(c.vc.h, c.vc.n)
                c.S("activation", silu_t[:, :], pg[:, :], AF.Silu, scale=s127)
                silu_done = c.sc.n
                nc.vector.wait_ge(c.tc.h, u_done)
                nc.vector.wait_ge(c.sc.h, silu_done)
                c.V("scalar_tensor_tensor", h2[:, oc, sl], pu[:, :], s127,
                    silu_t[:, :], ALU.mult, ALU.mult)
            mm_at[oc] = c.tc.n
            evs_at[oc] = c.sc.n
            evv_at[oc] = c.vc.n
            if oc + 2 < NOC:
                prefetch(oc + 2)

        h2_all = c.vc.n
        dwl_at, dmm_at, devv_at = {}, {}, {}

        def dprefetch(dc):
            if dc >= 2:
                nc.gpsimd.wait_ge(c.tc.h, dmm_at[dc - 2])
            d0 = dc * 128
            sm = dls[dc % 2]
            dwl_at[dc] = (sm, c.gdma(dbuf[dc % 2][:], wdT_r[:, :, d0:d0 + 128], sem=sm))

        dprefetch(0)
        dprefetch(1)
        nc.tensor.wait_ge(c.vc.h, h2_all)
        for dc in range(8):
            buf = dbuf[dc % 2]
            nc.tensor.wait_ge(dwl_at[dc][0].h, dwl_at[dc][1])
            po = ps_o[dc % 2]
            if dc >= 2:
                nc.tensor.wait_ge(c.vc.h, devv_at[dc - 2])
            for hh in range(2):
                sl = slice(hh * 512, hh * 512 + 512)
                if hh == 1:
                    nc.tensor.wait_ge(c.vc.h, c.vc.n)
                for hc in range(NOC):
                    i = c.T("matmul", po[:, :], buf[:, hc], h2[:, hc, sl],
                            start=(hc == 0), stop=(hc == NOC - 1))
                od = c.tmark(i)
                nc.vector.wait_ge(c.tc.h, od)
                c.V("scalar_tensor_tensor",
                    x[dc][:, t0 + hh * 512:t0 + hh * 512 + 512], po[:, :], 0.5 * s127,
                    x[dc][:, t0 + hh * 512:t0 + hh * 512 + 512], ALU.mult, ALU.add)
            dmm_at[dc] = c.tc.n
            devv_at[dc] = c.vc.n
            if dc + 2 < 8:
                dprefetch(dc + 2)
        c.vc.wait(nc.gpsimd)
        for i in range(8):
            c.gdma(yT[i * 128:(i + 1) * 128, t0:t0 + TT], x[i][:, t0:t0 + TT])
    c.gc.wait(nc.gpsimd)
    c.cur_stack = None
    if own:
        c.end()
        return c.nc
    barrier(c)
    blk.close()
    return None



_cache = {}


def _get(name, builder):
    if name not in _cache:
        _cache[name] = builder()
    return _cache[name]


def run_ffn0(x, wg, wu, wd):
    nc = _get("ffn0", build_ffn0)
    wgT = np.ascontiguousarray(wg.T)
    wuT = np.ascontiguousarray(wu.T)
    wdT = np.ascontiguousarray(wd.T)
    in_maps = []
    for cid in range(N_CORES):
        t0 = cid * TPC
        in_maps.append({"xT": np.ascontiguousarray(x[:, t0:t0 + TPC]),
                        "wgT": wgT, "wuT": wuT, "wdT": wdT})
    res = run_bass_kernel_spmd(nc, in_maps, core_ids=list(range(N_CORES)))
    return np.concatenate([r["yT"] for r in res.results], axis=1)


def kernel(**inp):
    return kernel_fused(**inp)


def kernel_multilaunch(**inp):
    _fp4_cache.clear()
    x = np.ascontiguousarray(inp["L0_attn_x"].reshape(NTOK, D).T)

    x = run_ffn0(x, inp["L0_ffn_w_gate"], inp["L0_ffn_w_up"], inp["L0_ffn_w_down"])
    x = run_deltanet(x, inp)
    x = run_ffn0(x, inp["L0_ffn_w_gate"], inp["L0_ffn_w_up"], inp["L0_ffn_w_down"])
    x = run_fp4_app(x, inp)
    x = run_l1a(x, inp)
    x = run_fp4_app(x, inp)
    x = run_l2f(x, inp)
    x = run_l2a(x, inp)
    x = run_l2f(x, inp)
    x = run_l3f(x, inp)
    x = run_l3a(x, inp)
    x = run_l3f(x, inp)

    return np.ascontiguousarray(x.T).reshape(B, S, D).astype(np.float32)



def build_l3f_route(T=TPC, c=None, io=None):
    own = c is None
    if own:
        c = Ctx("l3f_route")
    nc = c.nc
    blk = c.st if own else ExitStack()
    if own:
        io = NS(xT=c.din("xT", (D, T)), Wr=c.din("Wr", (D, 8)),
                brv=c.din("brv", (8,)), identf=c.din("identf", (128, 128)),
                h_out=c.dout("h_out", (D, T), BF16), oh_out=c.dout("oh_out", (8, T)),
                csum_out=c.dout("csum_out", (8, T)), gate_out=c.dout("gate_out", (T,)),
                counts_out=c.dout("counts_out", (8,)))
    xT, Wr, brv, identf = io.xT, io.Wr, io.brv, io.identf
    h_out, oh_out, csum_out = io.h_out, io.oh_out, io.csum_out
    gate_out, counts_out = io.gate_out, io.counts_out
    c.cur_stack = blk

    xb = [c.sb(f"xb{i}", [128, 8, 512]) for i in range(2)]
    h = c.sb("hb", [128, 8, 512], BF16)
    h32 = c.sb("hb32", [128, 8, 512])
    idf = c.sb("idf", [128, 128])
    wr_sb = c.sb("wr_sb", [128, 8, 8])
    br_row = c.sb("br_row", [1, 8])
    lg_tm = c.sb("lg_tm", [128, 4, 8])
    rowsc = c.sb("rowsc", [128, 8])
    mx = c.sb("mx", [128, 2])
    zs = c.sb("zs", [128, 2])
    oh_fm = c.sb("oh_fm", [8, T])
    cs_fm = c.sb("cs_fm", [8, T])
    ones8 = c.sb("ones8", [8, T])
    cnt = c.sb("cnt", [8, 1])

    ps_tr = c.ps("ps_tr", [128, 8])
    ps_tr2 = c.ps("ps_tr2", [8, 128])
    ps_sums = c.ps("ps_sums", [1, 512])
    ps_bc = c.ps("ps_bc", [128, 512])

    if own:
        c.begin()
    make_ln(c, ps_sums, ps_bc)
    c.make_consts([1e-5])
    xls = [c.sem("xl0"), c.sem("xl1")]
    xl_at = {}

    c.ydma(idf[:], identf[:, :])
    c.ydma(wr_sb[:], Wr.rearrange("(ic p) e -> p ic e", p=128))
    c.ydma(br_row[:], brv.unsqueeze(0))
    c.yc.wait(nc.tensor)
    c.V("memset", ones8[:], 1.0)
    nst = T // 512
    xl_at[0] = c.gdma(xb[0][:], xT.rearrange("(ic p) t -> p ic t", p=128)[:, :, 0:512],
                      sem=xls[0])
    for st in range(nst):
        t0 = st * 512
        if st + 1 < nst:
            if st >= 1:
                nc.gpsimd.wait_ge(c.vc.h, c.vc.n)
            xl_at[st + 1] = c.gdma(xb[(st + 1) % 2][:],
                                   xT.rearrange("(ic p) t -> p ic t", p=128)
                                   [:, :, t0 + 512:t0 + 1024], sem=xls[(st + 1) % 2])
        xv = xb[st % 2]
        nc.tensor.wait_ge(xls[st % 2].h, xl_at[st])
        nc.scalar.wait_ge(xls[st % 2].h, xl_at[st])
        nc.vector.wait_ge(xls[st % 2].h, xl_at[st])
        if st > 0:
            nc.vector.wait_ge(c.tc.h, c.tc.n)
            nc.vector.wait_ge(c.gc.h, c.gc.n)
        ln_fm_512(c, [xv[:, i] for i in range(8)], [h[:, i] for i in range(8)],
                  h32_views=[h32[:, i] for i in range(8)])
        h_ready = c.vc.n
        nc.gpsimd.wait_ge(c.vc.h, h_ready)
        c.gdma(h_out.rearrange("(ic p) t -> p ic t", p=128)[:, :, t0:t0 + 512], h[:])
        nc.tensor.wait_ge(c.vc.h, h_ready)
        for bb in range(4):
            sl = slice(bb * 128, bb * 128 + 128)
            nc.vector.wait_ge(c.gc.h, c.gc.n)
            nc.vector.wait_ge(c.tc.h, c.tc.n)
            nc.scalar.wait_ge(c.tc.h, c.tc.n)
            for icc in range(8):
                c.T("matmul", ps_tr[:, 0:8], h32[:, icc, sl], wr_sb[:, icc],
                    start=(icc == 0), stop=False)
            i = c.T("matmul", ps_tr[:, 0:8], c.ln_ones_row[0:1, sl.start:sl.start + 128]
                    if False else c.ln_ones_row[0:1, 0:128], br_row[0:1, :],
                    start=False, stop=True)
            lgd = c.tmark(i)
            nc.vector.wait_ge(c.tc.h, lgd)
            c.V("tensor_copy", lg_tm[:, bb], ps_tr[:, 0:8])
            c.V("reduce_max", mx[:, 0:1], lg_tm[:, bb], AX.X)
            c.V("tensor_scalar", mx[:, 1:2], mx[:, 0:1], -1.0, None, ALU.mult)
            c.vc.wait(nc.scalar)
            c.S("activation", rowsc[:, :], lg_tm[:, bb], AF.Exp, bias=mx[:, 1:2])
            c.sc.wait(nc.vector)
            c.V("reduce_sum", zs[:, 0:1], rowsc[:, :], AX.X)
            c.V("reciprocal", zs[:, 1:2], zs[:, 0:1])
            c.V("tensor_scalar", rowsc[:, 0:8], lg_tm[:, bb], mx[:, 0:1], None,
                ALU.is_equal)
            oh_ready = c.vc.n
            nc.gpsimd.wait_ge(c.vc.h, oh_ready)
            c.gdma(gate_out.unsqueeze(1)[t0 + bb * 128:t0 + bb * 128 + 128, :],
                   zs[:, 1:2])
            nc.tensor.wait_ge(c.vc.h, oh_ready)
            i = c.T("transpose", ps_tr2[0:8, :], rowsc[:, 0:8], idf[:])
            trd2 = c.tmark(i)
            nc.vector.wait_ge(c.tc.h, trd2)
            c.V("tensor_copy", oh_fm[0:8, t0 + bb * 128:t0 + bb * 128 + 128],
                ps_tr2[0:8, 0:128])
    c.V("tensor_tensor_scan", cs_fm[:, :], ones8[:, :], oh_fm[:, :], 0.0,
        ALU.mult, ALU.add)
    c.V("reduce_sum", cnt[:, :], oh_fm[:, :], AX.X)
    done = c.vc.n
    nc.gpsimd.wait_ge(c.vc.h, done)
    c.gdma(oh_out[:, :], oh_fm[:, :])
    c.gdma(csum_out[:, :], cs_fm[:, :])
    c.gdma(counts_out.unsqueeze(1), cnt[:, :])
    c.gc.wait(nc.gpsimd)
    c.cur_stack = None
    if own:
        c.end()
        return c.nc
    barrier(c)
    blk.close()
    return None


def build_l3f_experts(T=TPC, c=None, io=None):
    own = c is None
    if own:
        c = Ctx("l3f_experts")
    nc = c.nc
    blk = c.st if own else ExitStack()
    if own:
        io = NS(xT=c.din("xT", (D, T)), hT=c.din("hT", (D, T), BF16),
                oh_in=c.din("oh_in", (8, T)), csum_in=c.din("csum_in", (8, T)),
                gate_in=c.din("gate_in", (T,)), counts_ce=c.din("counts_ce", (8, 8)),
                wrow=c.din("wrow", (8,)), W1=c.din("W1", (8, D, D)),
                b1r=c.din("b1r", (128, 8, 8)), W2=c.din("W2", (8, D, D)),
                b2r=c.din("b2r", (128, 8, 8)), selmat=c.din("selmat", (8, 8 * 128)),
                yT=c.dout("yT", (D, T)))
    xT, hT, oh_in, csum_in, gate_in = io.xT, io.hT, io.oh_in, io.csum_in, io.gate_in
    counts_ce, wrow, W1, b1r, W2, b2r = (io.counts_ce, io.wrow, io.W1, io.b1r,
                                         io.W2, io.b2r)
    selmat, yT = io.selmat, io.yT
    c.cur_stack = blk

    TT = 1024
    NTT = T // TT
    x = [c.sb(f"x{i}", [128, T]) for i in range(8)]
    h = [c.sb(f"h{i}", [128, T], BF16) for i in range(8)]
    mid = c.sb("mid", [128, 8, TT], BF16)
    cw = c.sb("cw", [8, T])
    scr8 = c.sb("scr8", [8, T])
    offc = c.sb("offc", [8, 1])
    cnt_sb = c.sb("cnt_sb", [8, 8])
    wrow_sb = c.sb("wrow_sb", [8, 1])
    gate_sb = c.sb("gate_sb", [1, T])
    b1_sb = c.sb("b1_sb", [128, 8, 8])
    b2_sb = c.sb("b2_sb", [128, 8, 8])
    cwb = c.sb("cwb", [128, TT])
    etmp = c.sb("etmp", [128, 512])
    wbuf = [c.sb(f"wb{i}", [128, 8, 128], BF16) for i in range(2)]
    dbuf = [c.sb(f"db{i}", [128, 8, 128], BF16) for i in range(2)]

    ps_m = [c.ps(f"psm{i}", [128, 512]) for i in range(2)]
    ps_y = [c.ps(f"psy{i}", [128, 512]) for i in range(2)]
    ps_off = c.ps("ps_off", [8, 1])
    ps_cwb = c.ps("ps_cwb", [128, 512])
    ps_g8 = c.ps("ps_g8", [8, 512])

    if own:
        c.begin()
    wls = [c.sem("wls0"), c.sem("wls1")]
    dls = [c.sem("dls0"), c.sem("dls1")]
    sel_sb = c.sb("sel_sb", [8, 8 * 128])
    ones18 = c.sb("ones18", [1, 8])
    gate8 = c.sb("gate8", [8, T])

    c.ydma(sel_sb[:], selmat[:, :])
    for i in range(8):
        c.ydma(x[i][:], xT[i * 128:(i + 1) * 128, :])
        c.ydma(h[i][:], hT[i * 128:(i + 1) * 128, :])
    c.ydma(cw[:, :], oh_in[:, :])
    c.ydma(scr8[:, :], csum_in[:, :])
    c.ydma(cnt_sb[:], counts_ce[:, :])
    c.ydma(wrow_sb[:], wrow.unsqueeze(1))
    c.ydma(gate_sb[:], gate_in.unsqueeze(0))
    c.ydma(b1_sb[:], b1r[:, :, :])
    c.ydma(b2_sb[:], b2r[:, :, :])
    c.yc.wait(nc.tensor)
    c.yc.wait(nc.vector)
    c.yc.wait(nc.scalar)

    i = c.T("matmul", ps_off[:, :], cnt_sb[:, 0:8], wrow_sb[:, 0:1],
            start=True, stop=True)
    offd = c.tmark(i)
    nc.vector.wait_ge(c.tc.h, offd)
    c.V("tensor_copy", offc[:, :], ps_off[:, :])
    c.V("memset", ones18[:], 1.0)
    c.vc.wait(nc.tensor)
    for stp in range(T // 512):
        i = c.T("matmul", ps_g8[:, :], ones18[0:1, :],
                gate_sb[0:1, stp * 512:stp * 512 + 512], start=True, stop=True)
        gd = c.tmark(i)
        nc.vector.wait_ge(c.tc.h, gd)
        c.V("tensor_copy", gate8[:, stp * 512:stp * 512 + 512], ps_g8[:, :])
        nc.tensor.wait_ge(c.vc.h, c.vc.n)
    c.V("tensor_scalar", scr8[:, :], scr8[:, :], offc[:, 0:1], None, ALU.add)
    c.V("tensor_single_scalar", scr8[:, :], scr8[:, :], 2048.0, ALU.is_lt)
    c.V("tensor_mul", cw[:, :], cw[:, :], scr8[:, :])
    c.V("tensor_mul", cw[:, :], cw[:, :], gate8[:, :])
    cw_ready = c.vc.n

    W1r = [W1[e].rearrange("(ic p) o -> p ic o", p=128) for e in range(8)]
    W2r = [W2[e].rearrange("(hc p) o -> p hc o", p=128) for e in range(8)]
    first = True
    for e in range(8):
        for tt in range(NTT):
            t0 = tt * TT
            nc.tensor.wait_ge(c.vc.h, cw_ready if first else c.vc.n)
            for hhb in range(2):
                if hhb == 1:
                    nc.tensor.wait_ge(c.vc.h, c.vc.n)
                i = c.T("matmul", ps_cwb[:, :], sel_sb[:, e * 128:(e + 1) * 128],
                        cw[:, t0 + hhb * 512:t0 + hhb * 512 + 512],
                        start=True, stop=True)
                cbd = c.tmark(i)
                nc.vector.wait_ge(c.tc.h, cbd)
                c.V("tensor_copy", cwb[:, hhb * 512:hhb * 512 + 512], ps_cwb[:, :])
            cwb_ready = c.vc.n
            first = False
            wl_at, mm_at, ev_at = {}, {}, {}

            def pre1(oc):
                if oc >= 2:
                    nc.gpsimd.wait_ge(c.tc.h, mm_at[oc - 2])
                sm = wls[oc % 2]
                wl_at[oc] = (sm, c.gdma(wbuf[oc % 2][:], W1r[e][:, :, oc * 128:oc * 128 + 128], sem=sm))

            pre1(0)
            pre1(1)
            for oc in range(8):
                nc.tensor.wait_ge(wl_at[oc][0].h, wl_at[oc][1])
                pm = ps_m[oc % 2]
                if oc >= 2:
                    nc.tensor.wait_ge(c.sc.h, ev_at[oc - 2])
                for hhb in range(2):
                    sl = slice(hhb * 512, hhb * 512 + 512)
                    if hhb == 1:
                        nc.tensor.wait_ge(c.sc.h, c.sc.n)
                    for icc in range(8):
                        i = c.T("matmul", pm[:, :], wbuf[oc % 2][:, icc],
                                h[icc][:, t0 + hhb * 512:t0 + hhb * 512 + 512],
                                start=(icc == 0), stop=(icc == 7))
                    md = c.tmark(i)
                    nc.scalar.wait_ge(c.tc.h, md)
                    c.S("activation", mid[:, oc, sl], pm[:, :], AF.Relu,
                        bias=b1_sb[:, e, oc:oc + 1])
                mm_at[oc] = c.tc.n
                ev_at[oc] = c.sc.n
                if oc + 2 < 8:
                    pre1(oc + 2)
            mid_all = c.sc.n
            dwl_at, dmm_at, dev_at = {}, {}, {}

            def pre2(dc):
                if dc >= 2:
                    nc.gpsimd.wait_ge(c.tc.h, dmm_at[dc - 2])
                sm = dls[dc % 2]
                dwl_at[dc] = (sm, c.gdma(dbuf[dc % 2][:], W2r[e][:, :, dc * 128:dc * 128 + 128], sem=sm))

            pre2(0)
            pre2(1)
            nc.tensor.wait_ge(c.sc.h, mid_all)
            nc.tensor.wait_ge(c.vc.h, cwb_ready)
            for dc in range(8):
                nc.tensor.wait_ge(dwl_at[dc][0].h, dwl_at[dc][1])
                py = ps_y[dc % 2]
                if dc >= 2:
                    nc.tensor.wait_ge(c.vc.h, dev_at[dc - 2])
                for hhb in range(2):
                    sl = slice(hhb * 512, hhb * 512 + 512)
                    if hhb == 1:
                        nc.tensor.wait_ge(c.vc.h, c.vc.n)
                    for hc in range(8):
                        i = c.T("matmul", py[:, :], dbuf[dc % 2][:, hc],
                                mid[:, hc, sl], start=(hc == 0), stop=(hc == 7))
                    yd = c.tmark(i)
                    nc.vector.wait_ge(c.tc.h, yd)
                    c.V("scalar_tensor_tensor", etmp[:, :], py[:, :],
                        b2_sb[:, e, dc:dc + 1], cwb[:, sl], ALU.add, ALU.mult)
                    c.V("scalar_tensor_tensor",
                        x[dc][:, t0 + hhb * 512:t0 + hhb * 512 + 512], etmp[:, :], 0.5,
                        x[dc][:, t0 + hhb * 512:t0 + hhb * 512 + 512], ALU.mult, ALU.add)
                dmm_at[dc] = c.tc.n
                dev_at[dc] = c.vc.n
                if dc + 2 < 8:
                    pre2(dc + 2)
    c.vc.wait(nc.gpsimd)
    for i in range(8):
        c.gdma(yT[i * 128:(i + 1) * 128, :], x[i][:, :])
    c.gc.wait(nc.gpsimd)
    c.cur_stack = None
    if own:
        c.end()
        return c.nc
    barrier(c)
    blk.close()
    return None


def run_l3f(x, inp):
    nc_a = _get("l3f_route", build_l3f_route)
    nc_b = _get("l3f_experts", build_l3f_experts)
    identf = np.eye(128, dtype=np.float32)
    Wr = np.ascontiguousarray(inp["L3_ffn_Wr"])
    brv = np.ascontiguousarray(inp["L3_ffn_br"])
    in_maps = []
    for cid in range(N_CORES):
        t0 = cid * TPC
        in_maps.append({"xT": np.ascontiguousarray(x[:, t0:t0 + TPC]),
                        "Wr": Wr, "brv": brv, "identf": identf})
    res = run_bass_kernel_spmd(nc_a, in_maps, core_ids=list(range(N_CORES)))
    r = res.results
    counts_ce = np.stack([r[cid]["counts_out"] for cid in range(N_CORES)])
    b1r = np.ascontiguousarray(inp["L3_ffn_b1"].reshape(8, 8, 128).transpose(2, 0, 1))
    b2r = np.ascontiguousarray(inp["L3_ffn_b2"].reshape(8, 8, 128).transpose(2, 0, 1))
    W1 = np.ascontiguousarray(inp["L3_ffn_W1"])
    W2 = np.ascontiguousarray(inp["L3_ffn_W2"])
    in_maps = []
    for cid in range(N_CORES):
        t0 = cid * TPC
        wrow = (np.arange(8) < cid).astype(np.float32)
        selmat = np.repeat(np.eye(8, dtype=np.float32), 128, axis=1)
        in_maps.append({"selmat": selmat,
                        "xT": np.ascontiguousarray(x[:, t0:t0 + TPC]),
                        "hT": r[cid]["h_out"],
                        "oh_in": r[cid]["oh_out"],
                        "csum_in": r[cid]["csum_out"],
                        "gate_in": r[cid]["gate_out"],
                        "counts_ce": counts_ce, "wrow": wrow,
                        "W1": W1, "b1r": b1r, "W2": W2, "b2r": b2r})
    res = run_bass_kernel_spmd(nc_b, in_maps, core_ids=list(range(N_CORES)))
    return np.concatenate([rr["yT"] for rr in res.results], axis=1)



def build_l2f(T=TPC, c=None, io=None):
    own = c is None
    if own:
        c = Ctx("l2f")
    nc = c.nc
    blk = c.st if own else ExitStack()
    HID = 4096
    NOC = HID // 128
    if own:
        io = NS(xT=c.din("xT", (D, T)), nwr=c.din("nwr", (128, 8)),
                rwT=c.din("rwT", (D, 8)), sw1=c.din("sw1", (D, HID)),
                sb1r=c.din("sb1r", (128, 32)), sw2=c.din("sw2", (HID, D)),
                sb2r=c.din("sb2r", (128, 8)), ew1=c.din("ew1", (8, D, HID)),
                eb1r=c.din("eb1r", (128, 8, 32)), ew2=c.din("ew2", (8, HID, D)),
                eb2r=c.din("eb2r", (128, 8, 8)), identf=c.din("identf", (128, 128)),
                selmat=c.din("selmat", (8, 8 * 128)), yT=c.dout("yT", (D, T)))
    xT, nwr, rwT, sw1, sb1r, sw2, sb2r = (io.xT, io.nwr, io.rwT, io.sw1, io.sb1r,
                                          io.sw2, io.sb2r)
    ew1, eb1r, ew2, eb2r, identf, selmat, yT = (io.ew1, io.eb1r, io.ew2, io.eb2r,
                                                io.identf, io.selmat, io.yT)
    c.cur_stack = blk

    TT = 1024
    NTT = T // TT
    xb = c.sb("xb", [128, 8, TT])
    h = c.sb("hb", [128, 8, TT], BF16)
    h32 = c.sb("h32b", [128, 8, 512])
    mid = c.sb("mid", [128, NOC, TT], BF16)
    cw = c.sb("cwf", [8, TT])
    cwb = c.sb("cwbb", [128, TT])
    etmp = c.sb("etmp", [128, 512])
    idf = c.sb("idf", [128, 128])
    nw_sb = c.sb("nw_sb", [128, 8])
    rw_sb = c.sb("rw_sb", [128, 8, 8])
    sb1_sb = c.sb("sb1_sb", [128, 32])
    sb2_sb = c.sb("sb2_sb", [128, 8])
    eb1_sb = c.sb("eb1_sb", [128, 8, 32])
    eb2_sb = c.sb("eb2_sb", [128, 8, 8])
    lg_tm = c.sb("lg_tm", [128, 8])
    l2m = c.sb("l2m", [128, 8])
    mx2 = c.sb("mx2", [128, 6])
    cw_tm = c.sb("cw_tm", [128, 8])
    sel_sb2 = c.sb("sel_sb2", [8, 8 * 128])
    wbuf = [c.sb(f"wb{i}", [128, 8, 128], BF16) for i in range(2)]
    dbuf = [c.sb(f"db{i}", [128, 32, 128], BF16) for i in range(2)]

    ps_m = [c.ps(f"psm{i}", [128, 512]) for i in range(2)]
    ps_y = [c.ps(f"psy{i}", [128, 512]) for i in range(2)]
    ps_r = c.ps("ps_r", [128, 8])
    ps_t2 = c.ps("ps_t2", [8, 128])
    ps_sums = c.ps("ps_sums", [1, 512])
    ps_bc = c.ps("ps_bc", [128, 512])

    if own:
        c.begin()
    wls = [c.sem("wls0"), c.sem("wls1")]
    dls = [c.sem("dls0"), c.sem("dls1")]
    xls = c.sem("xls")
    make_ln(c, ps_sums, ps_bc)
    c.make_consts([1e-5, 1e-8])

    c.ydma(idf[:], identf[:, :])
    c.ydma(sel_sb2[:], selmat[:, :])
    c.ydma(nw_sb[:], nwr[:, :])
    c.ydma(rw_sb[:], rwT.rearrange("(ic p) e -> p ic e", p=128))
    c.ydma(sb1_sb[:], sb1r[:, :])
    c.ydma(sb2_sb[:], sb2r[:, :])
    c.ydma(eb1_sb[:], eb1r[:, :, :])
    c.ydma(eb2_sb[:], eb2r[:, :, :])
    c.yc.wait(nc.tensor)
    c.yc.wait(nc.vector)
    c.yc.wait(nc.scalar)

    xT_r = xT.rearrange("(ic p) t -> p ic t", p=128)
    sw1_r = sw1.rearrange("(ic p) o -> p ic o", p=128)
    sw2_r = sw2.rearrange("(hc p) o -> p hc o", p=128)
    ew1_r = [ew1[e].rearrange("(ic p) o -> p ic o", p=128) for e in range(8)]
    ew2_r = [ew2[e].rearrange("(hc p) o -> p hc o", p=128) for e in range(8)]

    for tt in range(NTT):
        t0 = tt * TT
        nc.gpsimd.wait_ge(c.vc.h, c.vc.n)
        nc.gpsimd.wait_ge(c.gc.h, c.gc.n)
        xld = c.gdma(xb[:], xT_r[:, :, t0:t0 + TT], sem=xls)
        for eng in (nc.tensor, nc.vector, nc.scalar):
            eng.wait_ge(xls.h, xld)
        for st in range(2):
            s0 = st * 512
            if st > 0 or tt > 0:
                nc.vector.wait_ge(c.tc.h, c.tc.n)
                nc.scalar.wait_ge(c.tc.h, c.tc.n)
            ln_fm_512(c, [xb[:, i, s0:s0 + 512] for i in range(8)],
                      [h[:, i, s0:s0 + 512] for i in range(8)],
                      h32_views=[h32[:, i] for i in range(8)],
                      rms2_eps=1e-8, colmul=[nw_sb[:, i:i + 1] for i in range(8)])
            h_ready = c.vc.n
            nc.tensor.wait_ge(c.vc.h, h_ready)
            for bb in range(4):
                sl = slice(bb * 128, bb * 128 + 128)
                nc.vector.wait_ge(c.tc.h, c.tc.n)
                for icc in range(8):
                    i = c.T("matmul", ps_r[:, :], h32[:, icc, sl], rw_sb[:, icc],
                            start=(icc == 0), stop=(icc == 7))
                lgd = c.tmark(i)
                nc.vector.wait_ge(c.tc.h, lgd)
                c.V("tensor_copy", lg_tm[:, :], ps_r[:, :])
                c.V("reduce_max", mx2[:, 0:1], lg_tm[:, :], AX.X)
                c.V("tensor_scalar", cw_tm[:, :], lg_tm[:, :], mx2[:, 0:1], None,
                    ALU.is_equal)
                c.V("scalar_tensor_tensor", l2m[:, :], cw_tm[:, :], -1e30,
                    lg_tm[:, :], ALU.mult, ALU.add)
                c.V("reduce_max", mx2[:, 1:2], l2m[:, :], AX.X)
                c.vc.wait(nc.scalar)
                c.S("activation", mx2[:, 2:4], mx2[:, 0:2], AF.Sigmoid)
                c.sc.wait(nc.vector)
                c.V("reduce_sum", mx2[:, 4:5], mx2[:, 2:4], AX.X)
                c.V("tensor_scalar", mx2[:, 4:5], mx2[:, 4:5], 1e-6, None, ALU.add)
                c.V("reciprocal", mx2[:, 5:6], mx2[:, 4:5])
                c.V("tensor_mul", mx2[:, 2:4], mx2[:, 2:4],
                    mx2[:, 5:6].broadcast_to([128, 2]))
                c.V("tensor_scalar", cw_tm[:, :], cw_tm[:, :], mx2[:, 2:3], None,
                    ALU.mult)
                c.V("tensor_scalar", l2m[:, :], l2m[:, :], mx2[:, 1:2], None,
                    ALU.is_equal)
                c.V("tensor_scalar", l2m[:, :], l2m[:, :], mx2[:, 3:4], None,
                    ALU.mult)
                c.V("tensor_add", cw_tm[:, :], cw_tm[:, :], l2m[:, :])
                cwd = c.vc.n
                nc.tensor.wait_ge(c.vc.h, cwd)
                i = c.T("transpose", ps_t2[0:8, :], cw_tm[:, :], idf[:])
                trd = c.tmark(i)
                nc.vector.wait_ge(c.tc.h, trd)
                c.V("tensor_copy", cw[0:8, s0 + bb * 128:s0 + bb * 128 + 128],
                    ps_t2[0:8, 0:128])
        cw_ready = c.vc.n
        nc.gpsimd.wait_ge(c.vc.h, cw_ready)

        for xx in range(-1, 8):
            w1r = sw1_r if xx < 0 else ew1_r[xx]
            w2r = sw2_r if xx < 0 else ew2_r[xx]
            b1c = sb1_sb if xx < 0 else eb1_sb[:, xx]
            b2c = sb2_sb if xx < 0 else eb2_sb[:, xx]
            if xx >= 0:
                nc.tensor.wait_ge(c.vc.h, c.vc.n)
                for hhb in range(2):
                    i = c.T("matmul", ps_bc[:, :], sel_sb2[:, xx * 128:(xx + 1) * 128],
                            cw[:, hhb * 512:hhb * 512 + 512], start=True, stop=True)
                    cbd = c.tmark(i)
                    nc.vector.wait_ge(c.tc.h, cbd)
                    c.V("tensor_copy", cwb[:, hhb * 512:hhb * 512 + 512], ps_bc[:, :])
                    nc.tensor.wait_ge(c.vc.h, c.vc.n)
            cwb_ready = c.vc.n
            wl_at, mm_at, ev_at = {}, {}, {}

            def pre1(oc):
                if oc >= 2:
                    nc.gpsimd.wait_ge(c.tc.h, mm_at[oc - 2])
                sm = wls[oc % 2]
                wl_at[oc] = (sm, c.gdma(wbuf[oc % 2][:], w1r[:, :, oc * 128:oc * 128 + 128], sem=sm))

            pre1(0)
            pre1(1)
            for oc in range(NOC):
                nc.tensor.wait_ge(wl_at[oc][0].h, wl_at[oc][1])
                pm = ps_m[oc % 2]
                if oc >= 2:
                    nc.tensor.wait_ge(c.sc.h, ev_at[oc - 2])
                for hhb in range(2):
                    sl = slice(hhb * 512, hhb * 512 + 512)
                    if hhb == 1:
                        nc.tensor.wait_ge(c.sc.h, c.sc.n)
                    for icc in range(8):
                        i = c.T("matmul", pm[:, :], wbuf[oc % 2][:, icc],
                                h[:, icc, t0 % TT + sl.start:t0 % TT + sl.stop]
                                if False else h[:, icc, sl],
                                start=(icc == 0), stop=(icc == 7))
                    md = c.tmark(i)
                    nc.scalar.wait_ge(c.tc.h, md)
                    c.S("activation", mid[:, oc, sl], pm[:, :], AF.Relu,
                        bias=b1c[:, oc:oc + 1])
                mm_at[oc] = c.tc.n
                ev_at[oc] = c.sc.n
                if oc + 2 < NOC:
                    pre1(oc + 2)
            mid_all = c.sc.n
            dwl_at, dmm_at, dev_at = {}, {}, {}

            def pre2(dc):
                if dc >= 2:
                    nc.gpsimd.wait_ge(c.tc.h, dmm_at[dc - 2])
                sm = dls[dc % 2]
                dwl_at[dc] = (sm, c.gdma(dbuf[dc % 2][:], w2r[:, :, dc * 128:dc * 128 + 128], sem=sm))

            pre2(0)
            pre2(1)
            nc.tensor.wait_ge(c.sc.h, mid_all)
            nc.tensor.wait_ge(c.vc.h, cwb_ready)
            for dc in range(8):
                nc.tensor.wait_ge(dwl_at[dc][0].h, dwl_at[dc][1])
                py = ps_y[dc % 2]
                if dc >= 2:
                    nc.tensor.wait_ge(c.vc.h, dev_at[dc - 2])
                for hhb in range(2):
                    sl = slice(hhb * 512, hhb * 512 + 512)
                    if hhb == 1:
                        nc.tensor.wait_ge(c.vc.h, c.vc.n)
                    for hc in range(NOC):
                        i = c.T("matmul", py[:, :], dbuf[dc % 2][:, hc],
                                mid[:, hc, sl], start=(hc == 0), stop=(hc == NOC - 1))
                    yd = c.tmark(i)
                    nc.vector.wait_ge(c.tc.h, yd)
                    if xx < 0:
                        c.V("tensor_scalar", etmp[:, :], py[:, :], b2c[:, dc:dc + 1],
                            None, ALU.add)
                    else:
                        c.V("scalar_tensor_tensor", etmp[:, :], py[:, :],
                            b2c[:, dc:dc + 1], cwb[:, sl], ALU.add, ALU.mult)
                    c.V("scalar_tensor_tensor", xb[:, dc, sl], etmp[:, :], 0.5,
                        xb[:, dc, sl], ALU.mult, ALU.add)
                dmm_at[dc] = c.tc.n
                dev_at[dc] = c.vc.n
                if dc + 2 < 8:
                    pre2(dc + 2)
        c.vc.wait(nc.gpsimd)
        c.gdma(yT.rearrange("(ic p) t -> p ic t", p=128)[:, :, t0:t0 + TT], xb[:])
    c.gc.wait(nc.gpsimd)
    c.cur_stack = None
    if own:
        c.end()
        return c.nc
    barrier(c)
    blk.close()
    return None


def run_l2f(x, inp):
    nc = _get("l2f", build_l2f)
    identf = np.eye(128, dtype=np.float32)
    nwr = np.ascontiguousarray(inp["L2_ffn_norm_weight"].reshape(8, 128).T)
    rwT = np.ascontiguousarray(inp["L2_ffn_router_w"].T)
    sb1r = np.ascontiguousarray(inp["L2_ffn_sb1"].reshape(32, 128).T)
    sb2r = np.ascontiguousarray(inp["L2_ffn_sb2"].reshape(8, 128).T)
    eb1r = np.ascontiguousarray(inp["L2_ffn_eb1"].reshape(8, 32, 128).transpose(2, 0, 1))
    eb2r = np.ascontiguousarray(inp["L2_ffn_eb2"].reshape(8, 8, 128).transpose(2, 0, 1))
    in_maps = []
    for cid in range(N_CORES):
        t0 = cid * TPC
        in_maps.append({"xT": np.ascontiguousarray(x[:, t0:t0 + TPC]),
                        "selmat": np.repeat(np.eye(8, dtype=np.float32), 128, axis=1),
                        "nwr": nwr, "rwT": rwT,
                        "sw1": inp["L2_ffn_sw1"], "sb1r": sb1r,
                        "sw2": inp["L2_ffn_sw2"], "sb2r": sb2r,
                        "ew1": inp["L2_ffn_ew1"], "eb1r": eb1r,
                        "ew2": inp["L2_ffn_ew2"], "eb2r": eb2r,
                        "identf": identf})
    res = run_bass_kernel_spmd(nc, in_maps, core_ids=list(range(N_CORES)))
    return np.concatenate([r["yT"] for r in res.results], axis=1)



_FP4_MIDS = [0.25, 0.75, 1.25, 1.75, 2.5, 3.5, 5.0]
_FP4_STEPS = [0.5, 0.5, 0.5, 0.5, 1.0, 1.0, 2.0]


def make_quant(c, F_max):
    key = (getattr(c, "prefix", ""), F_max)
    if getattr(c, "_q_key", None) == key:
        return
    c._q_key = key
    c.q_Fmax = F_max
    c.q_xa = c.sb("q_xa", [128, F_max])
    c.q_sc = c.sb("q_sc", [128, F_max // 16])
    c.q_acc = c.sb("q_acc", [128, F_max])
    c.q_ind = c.sb("q_ind", [128, F_max])
    c.q_sgn = c.sb("q_sgn", [128, F_max])


def quant_fp4(c, out_view, in_view, F):
    nc = c.nc
    nb = F // 16
    xa, sc = c.q_xa[:, 0:F], c.q_sc[:, 0:nb]
    acc, ind, sgn = c.q_acc[:, 0:F], c.q_ind[:, 0:F], c.q_sgn[:, 0:F]
    in3 = in_view.rearrange("p (a b) -> p a b", b=16)
    c.vc.wait(nc.scalar)
    c.S("sign", sgn, in_view)
    c.V("tensor_reduce", sc.unsqueeze(2), in3, AX.X, ALU.max,
        apply_absolute_value=True)
    c.V("tensor_scalar_mul", sc, sc, 1.0 / 6.0)
    c.V("tensor_scalar_max", sc, sc, 1e-12)
    c.V("reciprocal", sc, sc)
    c.vc.wait(nc.scalar)
    c.S("activation", xa, in_view, AF.Abs)
    c.sc.wait(nc.vector)
    c.V("tensor_tensor", xa.rearrange("p (a b) -> p a b", b=16),
        xa.rearrange("p (a b) -> p a b", b=16),
        sc.unsqueeze(2).broadcast_to([128, nb, 16]), ALU.mult)
    c.V("memset", acc, 0.0)
    for mid, step in zip(_FP4_MIDS, _FP4_STEPS):
        c.V("tensor_single_scalar", ind, xa, mid, ALU.is_gt)
        c.V("scalar_tensor_tensor", acc, ind, step, acc, ALU.mult, ALU.add)
    c.V("reciprocal", sc, sc)
    c.V("tensor_tensor", acc.rearrange("p (a b) -> p a b", b=16),
        acc.rearrange("p (a b) -> p a b", b=16),
        sc.unsqueeze(2).broadcast_to([128, nb, 16]), ALU.mult)
    c.sc.wait(nc.vector)
    c.V("tensor_mul", out_view, acc, sgn)



N_UNITS = 33


def build_fp4_prep(c=None, io=None):
    own = c is None
    if own:
        c = Ctx("fp4_prep")
    nc = c.nc
    blk = c.st if own else ExitStack()
    if own:
        io = NS(win=c.din("win", (N_UNITS, 128, 256)),
                identb=c.din("identb", (128, 128), BF16),
                wout=c.dout("wout", (N_UNITS, 2, 128, 128), BF16))
    win, identb, wout = io.win, io.identb, io.wout
    c.cur_stack = blk
    u_in = [c.sb(f"u_in{i}", [128, 256]) for i in range(2)]
    u_q = c.sb("u_q", [128, 256], BF16)
    u_tr = c.sb("u_tr", [128, 2, 128], BF16)
    idb = c.sb("idb", [128, 128], BF16)
    ps_tr = c.ps("ps_tr", [128, 128], BF16)
    if own:
        c.begin()
    make_quant(c, 256)
    uls = [c.sem("uls0"), c.sem("uls1")]
    c.ydma(idb[:], identb[:, :])
    c.yc.wait(nc.tensor)
    ul_at = {}
    ul_at[0] = c.gdma(u_in[0][:], win[0], sem=uls[0])
    for u in range(N_UNITS):
        if u + 1 < N_UNITS:
            if u >= 1:
                nc.gpsimd.wait_ge(c.vc.h, c.vc.n)
            ul_at[u + 1] = c.gdma(u_in[(u + 1) % 2][:], win[u + 1],
                                  sem=uls[(u + 1) % 2])
        nc.vector.wait_ge(uls[u % 2].h, ul_at[u])
        nc.scalar.wait_ge(uls[u % 2].h, ul_at[u])
        nc.vector.wait_ge(c.tc.h, c.tc.n)
        nc.vector.wait_ge(c.gc.h, c.gc.n)
        quant_fp4(c, u_q[:, :], u_in[u % 2][:, :], 256)
        qd = c.vc.n
        nc.tensor.wait_ge(c.vc.h, qd)
        for b2 in range(2):
            if u >= 1 or b2 == 1:
                nc.tensor.wait_ge(c.vc.h, c.vc.n)
            i = c.T("transpose", ps_tr[:, :], u_q[:, b2 * 128:b2 * 128 + 128], idb[:])
            trd = c.tmark(i)
            nc.vector.wait_ge(c.tc.h, trd)
            c.V("tensor_copy", u_tr[:, b2], ps_tr[:, :])
        trc = c.vc.n
        nc.gpsimd.wait_ge(c.vc.h, trc)
        c.gdma(wout[u].rearrange("b p o -> p b o"), u_tr[:])
    c.gc.wait(nc.gpsimd)
    c.cur_stack = None
    if own:
        c.end()
        return c.nc
    barrier(c)
    blk.close()
    return None



def build_fp4_app(T=TPC, c=None, io=None, fm_io=False):
    own = c is None
    if own:
        c = Ctx("fp4_app")
    nc = c.nc
    blk = c.st if own else ExitStack()
    HID = 2816
    NHC = HID // 128
    NTB = 2
    STEP = NTB * 128
    if own:
        io = NS(x_in=c.din("x_in", (T, D)), wgqT=c.din("wgqT", (D, HID), BF16),
                wuqT=c.din("wuqT", (D, HID), BF16), wdqT=c.din("wdqT", (HID, D), BF16),
                identb=c.din("identb", (128, 128), BF16), y_out=c.dout("y_out", (T, D)))
    x_in, wgqT, wuqT, wdqT, identb, y_out = (io.x_in, io.wgqT, io.wuqT, io.wdqT,
                                             io.identb, io.y_out)
    c.cur_stack = blk

    x = c.sb("x", [128, NTB, D])
    xfm = c.sb("xfm", [128, 8, NTB * 128]) if fm_io else None
    yst = c.sb("yst", [128, 128], BF16) if fm_io else None
    hq = c.sb("hq", [128, D], BF16)
    hqT = c.sb("hqT", [128, 8, STEP], BF16)
    h2 = c.sb("h2f", [128, NTB, HID])
    h2q = c.sb("h2q", [128, NTB, HID], BF16)
    h2qT = c.sb("h2qT", [128, NHC, STEP], BF16)
    lnt = c.sb("lnt", [128, D])
    lnrows = c.sb("lnrows", [128, 4])
    silu2 = c.sb("silu2", [128, 512])
    idb = c.sb("idb", [128, 128], BF16)
    wb = [c.sb(f"wb{i}", [128, 2, 8, 512], BF16) for i in range(2)]
    db = [c.sb(f"dbz{i}", [128, NHC, 512], BF16) for i in range(2)]

    ps_tr = c.ps("ps_tr", [128, 128], BF16)
    ps_g = c.ps("ps_g", [128, 512])
    ps_u = c.ps("ps_u", [128, 512])
    ps_o = [c.ps(f"ps_o{i}", [128, 512]) for i in range(2)]

    if own:
        c.begin()
    make_quant(c, HID)
    c.make_consts([1e-5])
    xls = c.sem("xls")
    wls = [c.sem("wls0"), c.sem("wls1")]
    dls = [c.sem("dls0"), c.sem("dls1")]
    c.ydma(idb[:], identb[:, :])
    if fm_io:
        idf32 = c.sb("idf32fp4", [128, 128])
        c.ydma(idf32[:], io.identf32[:, :])
    c.yc.wait(nc.tensor)

    NT = T // STEP
    OBS = [(i * 512, min(HID, i * 512 + 512)) for i in range((HID + 511) // 512)]
    wgq_r = wgqT.rearrange("(ic p) o -> p ic o", p=128)
    wuq_r = wuqT.rearrange("(ic p) o -> p ic o", p=128)
    wdq_r = wdqT.rearrange("(hc p) o -> p hc o", p=128)
    xl_at = {}

    def xload(st):
        if not fm_io:
            xl_at[st] = c.gdma(x[:], x_in.rearrange("(n p) d -> p n d", p=128)
                               [:, st * NTB:st * NTB + NTB, :], sem=xls)
        else:
            xl_at[st] = c.gdma(xfm[:], x_in.rearrange("(ic p) t -> p ic t", p=128)
                               [:, :, st * STEP:st * STEP + STEP], sem=xls)

    for st in range(NT):
        if st > 0:
            nc.gpsimd.wait_ge(c.gc.h, c.gc.n)
        xload(st)
        for eng in (nc.tensor, nc.vector, nc.scalar):
            eng.wait_ge(xls.h, xl_at[st])
        if fm_io:
            for tb in range(NTB):
                for icc in range(8):
                    nc.tensor.wait_ge(c.vc.h, c.vc.n)
                    i = c.T("transpose", ps_g[:, 0:128].bitcast(F32)
                            if False else ps_g[:, 0:128],
                            xfm[:, icc, tb * 128:tb * 128 + 128], idf32[:])
                    trd = c.tmark(i)
                    nc.vector.wait_ge(c.tc.h, trd)
                    c.V("tensor_copy", x[:, tb, icc * 128:icc * 128 + 128],
                        ps_g[:, 0:128])
        for tb in range(NTB):
            nc.vector.wait_ge(c.tc.h, c.tc.n)
            xv = x[:, tb, :]
            c.V("reduce_sum", lnrows[:, 0:1], xv, AX.X)
            c.V("tensor_scalar", lnrows[:, 0:1], lnrows[:, 0:1], 1.0 / D, None,
                ALU.mult)
            c.V("tensor_scalar", lnt[:, :], xv, lnrows[:, 0:1], None, ALU.subtract)
            c.vc.wait(nc.scalar)
            c.S("activation", c.q_xa[:, 0:D], lnt[:, :], AF.Square)
            c.sc.wait(nc.vector)
            c.V("reduce_sum", lnrows[:, 1:2], c.q_xa[:, 0:D], AX.X)
            c.V("tensor_scalar", lnrows[:, 1:2], lnrows[:, 1:2], 1.0 / D, None,
                ALU.mult)
            c.vc.wait(nc.scalar)
            c.S("activation", lnrows[:, 2:3], lnrows[:, 1:2], AF.Sqrt, bias=1e-5)
            c.sc.wait(nc.vector)
            c.V("reciprocal", lnrows[:, 3:4], lnrows[:, 2:3])
            c.V("tensor_scalar", lnt[:, :], lnt[:, :], lnrows[:, 3:4], None,
                ALU.mult)
            quant_fp4(c, hq[:, :], lnt[:, :], D)
            qd_ = c.vc.n
            nc.tensor.wait_ge(c.vc.h, qd_)
            for icc in range(8):
                if icc > 0:
                    nc.tensor.wait_ge(c.vc.h, c.vc.n)
                i = c.T("transpose", ps_tr[:, :], hq[:, icc * 128:icc * 128 + 128],
                        idb[:])
                trd = c.tmark(i)
                nc.vector.wait_ge(c.tc.h, trd)
                c.V("tensor_copy", hqT[:, icc, tb * 128:tb * 128 + 128], ps_tr[:, :])
        hqT_all = c.vc.n
        wl_at, mm_at, ev_at = {}, {}, {}

        def pre1(ob):
            if ob >= 2:
                nc.gpsimd.wait_ge(c.tc.h, mm_at[ob - 2])
            o0, o1 = OBS[ob]
            sm = wls[ob % 2]
            c.gdma(wb[ob % 2][:, 0, :, 0:o1 - o0], wgq_r[:, :, o0:o1], sem=sm)
            wl_at[ob] = (sm, c.gdma(wb[ob % 2][:, 1, :, 0:o1 - o0],
                                    wuq_r[:, :, o0:o1], sem=sm))

        pre1(0)
        pre1(1)
        nc.tensor.wait_ge(c.vc.h, hqT_all)
        for ob in range(len(OBS)):
            o0, o1 = OBS[ob]
            ow = o1 - o0
            nc.tensor.wait_ge(wl_at[ob][0].h, wl_at[ob][1])
            for tb in range(NTB):
                if tb > 0 or ob > 0:
                    nc.tensor.wait_ge(c.vc.h, c.vc.n)
                    nc.tensor.wait_ge(c.sc.h, c.sc.n)
                for icc in range(8):
                    i = c.T("matmul", ps_g[:, 0:ow],
                            hqT[:, icc, tb * 128:tb * 128 + 128],
                            wb[ob % 2][:, 0, icc, 0:ow],
                            start=(icc == 0), stop=(icc == 7))
                gd = c.tmark(i)
                for icc in range(8):
                    i = c.T("matmul", ps_u[:, 0:ow], hqT[:, icc, tb * 128:tb * 128 + 128],
                            wb[ob % 2][:, 1, icc, 0:ow],
                            start=(icc == 0), stop=(icc == 7))
                ud = c.tmark(i)
                nc.scalar.wait_ge(c.tc.h, gd)
                nc.scalar.wait_ge(c.vc.h, c.vc.n)
                c.S("activation", silu2[:, 0:ow], ps_g[:, 0:ow], AF.Silu)
                sd = c.sc.n
                nc.vector.wait_ge(c.tc.h, ud)
                nc.vector.wait_ge(c.sc.h, sd)
                c.V("tensor_mul", h2[:, tb, o0:o1], ps_u[:, 0:ow], silu2[:, 0:ow])
            mm_at[ob] = c.tc.n
            ev_at[ob] = c.vc.n
            if ob + 2 < len(OBS):
                pre1(ob + 2)
        for tb in range(NTB):
            nc.vector.wait_ge(c.tc.h, c.tc.n)
            quant_fp4(c, h2q[:, tb, :], h2[:, tb, :], HID)
            qd_ = c.vc.n
            nc.tensor.wait_ge(c.vc.h, qd_)
            for hc in range(NHC):
                if hc > 0:
                    nc.tensor.wait_ge(c.vc.h, c.vc.n)
                i = c.T("transpose", ps_tr[:, :], h2q[:, tb, hc * 128:hc * 128 + 128],
                        idb[:])
                trd = c.tmark(i)
                nc.vector.wait_ge(c.tc.h, trd)
                c.V("tensor_copy", h2qT[:, hc, tb * 128:tb * 128 + 128], ps_tr[:, :])
        h2qT_all = c.vc.n
        dl_at, dmm_at, dev_at = {}, {}, {}

        def pre2(dbk):
            if dbk >= 2:
                nc.gpsimd.wait_ge(c.tc.h, dmm_at[dbk - 2])
            sm = dls[dbk % 2]
            dl_at[dbk] = (sm, c.gdma(db[dbk % 2][:], wdq_r[:, :, dbk * 512:dbk * 512 + 512], sem=sm))

        pre2(0)
        pre2(1)
        nc.tensor.wait_ge(c.vc.h, h2qT_all)
        for dbk in range(2):
            nc.tensor.wait_ge(dl_at[dbk][0].h, dl_at[dbk][1])
            po = ps_o[dbk % 2]
            for tb in range(NTB):
                if tb > 0 or dbk > 0:
                    nc.tensor.wait_ge(c.vc.h, c.vc.n)
                for hc in range(NHC):
                    i = c.T("matmul", po[:, :], h2qT[:, hc, tb * 128:tb * 128 + 128],
                            db[dbk % 2][:, hc, :], start=(hc == 0), stop=(hc == NHC - 1))
                od = c.tmark(i)
                nc.vector.wait_ge(c.tc.h, od)
                c.V("scalar_tensor_tensor", x[:, tb, dbk * 512:dbk * 512 + 512],
                    po[:, :], 0.5, x[:, tb, dbk * 512:dbk * 512 + 512],
                    ALU.mult, ALU.add)
            dmm_at[dbk] = c.tc.n
            dev_at[dbk] = c.vc.n
        yd2 = c.vc.n
        if not fm_io:
            nc.gpsimd.wait_ge(c.vc.h, yd2)
            c.gdma(y_out.rearrange("(n p) d -> p n d", p=128)
                   [:, st * NTB:st * NTB + NTB, :], x[:])
        else:
            nc.tensor.wait_ge(c.vc.h, yd2)
            for tb in range(NTB):
                for icc in range(8):
                    if tb > 0 or icc > 0:
                        nc.tensor.wait_ge(c.vc.h, c.vc.n)
                    i = c.T("transpose", ps_g[:, 0:128],
                            x[:, tb, icc * 128:icc * 128 + 128], idf32[:])
                    trd = c.tmark(i)
                    nc.vector.wait_ge(c.tc.h, trd)
                    c.V("tensor_copy", xfm[:, icc, tb * 128:tb * 128 + 128],
                        ps_g[:, 0:128])
            ydf = c.vc.n
            nc.gpsimd.wait_ge(c.vc.h, ydf)
            c.gdma(y_out.rearrange("(ic p) t -> p ic t", p=128)
                   [:, :, st * STEP:st * STEP + STEP], xfm[:])
    c.gc.wait(nc.gpsimd)
    c.cur_stack = None
    if own:
        c.end()
        return c.nc
    barrier(c)
    blk.close()
    return None


_fp4_cache = {}


def run_fp4_prep(inp):
    if "wq" in _fp4_cache:
        return _fp4_cache["wq"]
    nc = _get("fp4_prep", build_fp4_prep)
    wg, wu, wd = inp["L1_ffn_w_gate"], inp["L1_ffn_w_up"], inp["L1_ffn_w_down"]
    units = []
    for w in (wg, wu, wd):
        r, cdim = w.shape
        u = w.reshape(r // 128, 128, cdim // 256, 256).transpose(0, 2, 1, 3)
        units.append(u.reshape(-1, 128, 256))
    units = np.concatenate(units, axis=0)
    identb = np.eye(128, dtype=BFD)
    in_maps = []
    for cid in range(N_CORES):
        in_maps.append({"win": np.ascontiguousarray(units[cid * N_UNITS:(cid + 1) * N_UNITS]),
                        "identb": identb})
    res = run_bass_kernel_spmd(nc, in_maps, core_ids=list(range(N_CORES)))
    out = np.concatenate([r["wout"] for r in res.results], axis=0)
    def asm(block, r, cdim):
        u = block.reshape(r // 128, cdim // 256, 2, 128, 128)
        u = u.transpose(1, 2, 3, 0, 4)
        return np.ascontiguousarray(u.reshape(cdim, r))
    n1 = (2816 // 128) * (1024 // 256)
    wgqT = asm(out[0:n1], 2816, 1024)
    wuqT = asm(out[n1:2 * n1], 2816, 1024)
    wdqT = asm(out[2 * n1:3 * n1], 1024, 2816)
    _fp4_cache["wq"] = (wgqT, wuqT, wdqT)
    return _fp4_cache["wq"]


def run_fp4_app(x, inp):
    wgqT, wuqT, wdqT = run_fp4_prep(inp)
    nc = _get("fp4_app", build_fp4_app)
    x_tm = np.ascontiguousarray(x.T)
    identb = np.eye(128, dtype=BFD)
    in_maps = []
    for cid in range(N_CORES):
        t0 = cid * TPC
        in_maps.append({"x_in": np.ascontiguousarray(x_tm[t0:t0 + TPC]),
                        "wgqT": wgqT, "wuqT": wuqT, "wdqT": wdqT,
                        "identb": identb})
    res = run_bass_kernel_spmd(nc, in_maps, core_ids=list(range(N_CORES)))
    y = np.concatenate([r["y_out"] for r in res.results], axis=0)
    return np.ascontiguousarray(y.T)



def build_l3a_qkv(T=TPC, c=None, io=None):
    own = c is None
    if own:
        c = Ctx("l3a_qkv")
    nc = c.nc
    blk = c.st if own else ExitStack()
    DL = 256
    if own:
        io = NS(xT=c.din("xT", (D, T)), Wq=c.din("Wq", (D, D)),
                bqr=c.din("bqr", (128, 8)), Wdn=c.din("Wdn", (D, DL)),
                bdnr=c.din("bdnr", (128, 2)), Wupk=c.din("Wupk", (DL, D)),
                bupkr=c.din("bupkr", (128, 8)), Wupv=c.din("Wupv", (DL, D)),
                bupv=c.din("bupv", (D,)), cosF=c.din("cosF", (128, T)),
                sinPM=c.din("sinPM", (128, T)), Pswap=c.din("Pswap", (128, 128)),
                q_out=c.dout("q_out", (D, T), BF16),
                k_out=c.dout("k_out", (D, T), BF16),
                v_out=c.dout("v_out", (T, D), BF16))
    xT, Wq, bqr, Wdn, bdnr = io.xT, io.Wq, io.bqr, io.Wdn, io.bdnr
    Wupk, bupk, Wupv, bupv = io.Wupk, io.bupkr, io.Wupv, io.bupv
    cosF, sinPM, Pswap = io.cosF, io.sinPM, io.Pswap
    q_out, k_out, v_out = io.q_out, io.k_out, io.v_out
    c.cur_stack = blk

    xb = [c.sb(f"xb{i}", [128, 8, 512]) for i in range(2)]
    h = c.sb("hb", [128, 8, 512], BF16)
    wq_sb = c.sb("wq_sb", [128, 8, D], BF16)
    wdn_sb = c.sb("wdn_sb", [128, 8, DL], BF16)
    wupk_sb = c.sb("wupk_sb", [128, 2, D], BF16)
    wupv_sb = c.sb("wupv_sb", [128, 2, D], BF16)
    bq_sb = c.sb("bq_sb", [128, 8])
    bdn_sb = c.sb("bdn_sb", [128, 2])
    bupk_sb = c.sb("bupk_sb", [128, 8])
    bupv_row = c.sb("bupv_row", [1, D], BF16)
    cos_sb = c.sb("cos_sb", [128, T])
    sin_sb = c.sb("sin_sb", [128, T])
    psw_sb = c.sb("psw_sb", [128, 128])
    lat = c.sb("lat", [128, 2, 512], BF16)
    rsc1 = c.sb("rsc1", [128, 512])
    rsc2 = c.sb("rsc2", [128, 512])
    qstore = c.sb("qstore", [128, 512], BF16)
    ones1 = c.sb("ones1b", [1, 128], BF16)

    ps_a = [c.ps(f"ps_a{i}", [128, 512]) for i in range(2)]
    ps_sw = c.ps("ps_sw", [128, 512])
    ps_v = c.ps("ps_v", [128, 512])
    ps_sums = c.ps("ps_sums", [1, 512])
    ps_bc = c.ps("ps_bc", [128, 512])

    if own:
        c.begin()
    make_ln(c, ps_sums, ps_bc)
    c.make_consts([1e-5])
    xls = [c.sem("xls0"), c.sem("xls1")]

    c.gdma(wq_sb[:], Wq.rearrange("(ic p) o -> p ic o", p=128))
    c.gdma(wdn_sb[:], Wdn.rearrange("(ic p) o -> p ic o", p=128))
    c.gdma(wupk_sb[:], Wupk.rearrange("(ic p) o -> p ic o", p=128))
    c.gdma(wupv_sb[:], Wupv.rearrange("(ic p) o -> p ic o", p=128))
    c.gdma(bupv_row[:], bupv.unsqueeze(0))
    c.ydma(bq_sb[:], bqr[:, :])
    c.ydma(bdn_sb[:], bdnr[:, :])
    c.ydma(bupk_sb[:], bupk[:, :])
    c.ydma(cos_sb[:], cosF[:, :])
    c.ydma(sin_sb[:], sinPM[:, :])
    c.ydma(psw_sb[:], Pswap[:, :])
    c.V("memset", ones1[:], 1.0)
    c.gc.wait(nc.tensor)
    c.yc.wait(nc.tensor)
    c.yc.wait(nc.vector)

    xT_r = xT.rearrange("(ic p) t -> p ic t", p=128)
    nst = T // 512
    xl_at = {0: c.gdma(xb[0][:], xT_r[:, :, 0:512], sem=xls[0])}

    def rope_pair(src, dst, t0):
        vv = c.vc.n
        nc.tensor.wait_ge(c.vc.h, vv)
        i = c.T("matmul", ps_sw[:, :], psw_sb[:, :], src, start=True, stop=True)
        swd = c.tmark(i)
        c.V("tensor_mul", rsc1[:, :], src, cos_sb[:, t0:t0 + 512])
        nc.vector.wait_ge(c.tc.h, swd)
        c.V("tensor_mul", rsc2[:, :], ps_sw[:, :], sin_sb[:, t0:t0 + 512])
        c.V("tensor_add", dst, rsc1[:, :], rsc2[:, :])

    for st in range(nst):
        t0 = st * 512
        if st + 1 < nst:
            if st >= 1:
                nc.gpsimd.wait_ge(c.vc.h, c.vc.n)
            xl_at[st + 1] = c.gdma(xb[(st + 1) % 2][:], xT_r[:, :, t0 + 512:t0 + 1024],
                                   sem=xls[(st + 1) % 2])
        xv = xb[st % 2]
        for eng in (nc.tensor, nc.vector, nc.scalar):
            eng.wait_ge(xls[st % 2].h, xl_at[st])
        if st > 0:
            nc.vector.wait_ge(c.tc.h, c.tc.n)
            nc.scalar.wait_ge(c.tc.h, c.tc.n)
        ln_fm_512(c, [xv[:, i] for i in range(8)], [h[:, i] for i in range(8)])
        h_ready = c.vc.n
        nc.tensor.wait_ge(c.vc.h, h_ready)
        for oc in range(8):
            pa = ps_a[oc % 2]
            if oc >= 2:
                nc.tensor.wait_ge(c.vc.h, c.vc.n)
            for icc in range(8):
                i = c.T("matmul", pa[:, :], wq_sb[:, icc, oc * 128:oc * 128 + 128],
                        h[:, icc], start=(icc == 0), stop=(icc == 7))
            qd_ = c.tmark(i)
            nc.vector.wait_ge(c.tc.h, qd_)
            nc.vector.wait_ge(c.gc.h, c.gc.n)
            c.V("tensor_scalar", c.ln_sq[1][:, :], pa[:, :], bq_sb[:, oc:oc + 1],
                None, ALU.add)
            rope_pair(c.ln_sq[1][:, 0:512], qstore[:, 0:512], t0)
            wr = c.vc.n
            nc.gpsimd.wait_ge(c.vc.h, wr)
            c.gdma(q_out[oc * 128:oc * 128 + 128, t0:t0 + 512], qstore[:])
        for oc in range(2):
            pa = ps_a[oc % 2]
            nc.tensor.wait_ge(c.vc.h, c.vc.n)
            for icc in range(8):
                i = c.T("matmul", pa[:, :], wdn_sb[:, icc, oc * 128:oc * 128 + 128],
                        h[:, icc], start=(icc == 0), stop=(icc == 7))
            ld_ = c.tmark(i)
            nc.vector.wait_ge(c.tc.h, ld_)
            c.V("tensor_scalar", lat[:, oc, :], pa[:, :], bdn_sb[:, oc:oc + 1],
                None, ALU.add)
        lat_ready = c.vc.n
        nc.tensor.wait_ge(c.vc.h, lat_ready)
        for oc in range(8):
            pa = ps_a[oc % 2]
            if oc >= 2:
                nc.tensor.wait_ge(c.vc.h, c.vc.n)
            for icc in range(2):
                i = c.T("matmul", pa[:, :], wupk_sb[:, icc, oc * 128:oc * 128 + 128],
                        lat[:, icc], start=(icc == 0), stop=(icc == 1))
            kd = c.tmark(i)
            nc.vector.wait_ge(c.tc.h, kd)
            nc.vector.wait_ge(c.gc.h, c.gc.n)
            c.V("tensor_scalar", c.ln_sq[1][:, :], pa[:, :], bupk_sb[:, oc:oc + 1],
                None, ALU.add)
            rope_pair(c.ln_sq[1][:, 0:512], qstore[:, 0:512], t0)
            wr = c.vc.n
            nc.gpsimd.wait_ge(c.vc.h, wr)
            c.gdma(k_out[oc * 128:oc * 128 + 128, t0:t0 + 512], qstore[:])
        for tb in range(4):
            for ob in range(2):
                nc.tensor.wait_ge(c.vc.h, c.vc.n)
                for icc in range(2):
                    c.T("matmul", ps_v[:, :],
                        lat[:, icc, tb * 128:tb * 128 + 128],
                        wupv_sb[:, icc, ob * 512:ob * 512 + 512],
                        start=(icc == 0), stop=False)
                i = c.T("matmul", ps_v[:, :], ones1[0:1, 0:128],
                        bupv_row[0:1, ob * 512:ob * 512 + 512],
                        start=False, stop=True)
                vd = c.tmark(i)
                nc.vector.wait_ge(c.tc.h, vd)
                nc.vector.wait_ge(c.gc.h, c.gc.n)
                c.V("tensor_copy", qstore[:, :], ps_v[:, :])
                wr = c.vc.n
                nc.gpsimd.wait_ge(c.vc.h, wr)
                c.gdma(v_out[t0 + tb * 128:t0 + tb * 128 + 128,
                             ob * 512:ob * 512 + 512], qstore[:])
    c.gc.wait(nc.gpsimd)
    c.cur_stack = None
    if own:
        c.end()
        return c.nc
    barrier(c)
    blk.close()
    return None


def build_l3a_attn(T=TPC, SQ=S, c=None, io=None, kv_ranked=False):
    own = c is None
    if own:
        c = Ctx("l3a_attn")
    nc = c.nc
    blk = c.st if own else ExitStack()
    if own:
        io = NS(xT=c.din("xT", (D, T)), q_in=c.din("q_in", (D, T), BF16),
                k_in=c.din("k_in", (D, SQ), BF16), v_in=c.din("v_in", (SQ, D), BF16),
                Wout=c.din("Wout", (D, D)), boutr=c.din("boutr", (128, 8)),
                yT=c.dout("yT", (D, T)))
    xT, q_in, k_in, v_in, Wout, boutr, yT = (io.xT, io.q_in, io.k_in, io.v_in,
                                             io.Wout, io.boutr, io.yT)
    c.cur_stack = blk

    NQB = T // 512
    NKB = SQ // 128
    q = c.sb("q", [128, 8, T], BF16)
    kp = c.sb("kp", [128, SQ], BF16)
    vp = c.sb("vp", [128, NKB, 128], BF16)
    pt = [c.sb(f"pt{i}", [128, 512], BF16) for i in range(2)]
    ctx = c.sb("ctx", [128, 8, T], BF16)
    den = c.sb("den", [1, 512])
    deni = c.sb("deni", [1, 512])
    ctf = c.sb("ctf", [128, 512])
    wout_sb = c.sb("wout_sb", [128, 8, D], BF16)
    bout_sb = c.sb("bout_sb", [128, 8])
    xc = [c.sb(f"xc{i}", [128, 512]) for i in range(2)]
    onesb = c.sb("onesb", [128, 1], BF16)
    ones128f = c.sb("ones128f", [1, 128])

    ps_s = [c.ps(f"ps_s{i}", [128, 512]) for i in range(2)]
    ps_acc = c.ps("ps_acc", [128, 512])
    ps_den = c.ps("ps_den", [1, 512])
    ps_bc = c.ps("ps_bc", [128, 512])
    ps_y = [c.ps(f"ps_y{i}", [128, 512]) for i in range(2)]

    if own:
        c.begin()
    c.make_consts([0.125])
    kvl = c.sem("kvl")
    xls = [c.sem("xls0"), c.sem("xls1")]
    sts = [c.sem("sts0"), c.sem("sts1")]
    c.gdma(wout_sb[:], Wout.rearrange("(ic p) o -> p ic o", p=128))
    c.ydma(bout_sb[:], boutr[:, :])
    for i in range(8):
        c.ydma(q[:, i, :], q_in[i * 128:(i + 1) * 128, :])
    c.V("memset", onesb[:], 1.0)
    c.V("memset", ones128f[:], 1.0)
    c.yc.wait(nc.tensor)
    c.gc.wait(nc.tensor)
    c.vc.wait(nc.tensor)

    TLOC = T

    def kv_load(hp):
        if not kv_ranked:
            c.gdma(kp[:], k_in[hp * 128:(hp + 1) * 128, :], sem=kvl)
            return c.gdma(vp[:], v_in.rearrange("(kb p) d -> p kb d", p=128)
                          [:, :, hp * 128:(hp + 1) * 128], sem=kvl)
        for sl_ in range(4):
            pid = c.pid()
            with nc.sync.register(_rn("kvo")) as rg:
                nc.sync.reg_div(rg, pid, 4)
                nc.sync.reg_mul(rg, rg, 4)
                nc.sync.reg_add(rg, rg, sl_)
                nc.sync.reg_mul(rg, rg, D * TLOC)
                nc.sync.reg_add(rg, rg, hp * 128 * TLOC)
                dynk = bass.AP(k_in.tensor, rg, [[TLOC, 128], [1, TLOC]])
                c.ydma(kp[:, sl_ * TLOC:(sl_ + 1) * TLOC], dynk, sem=kvl)
            with nc.sync.register(_rn("kvo2")) as rg2:
                nc.sync.reg_div(rg2, pid, 4)
                nc.sync.reg_mul(rg2, rg2, 4)
                nc.sync.reg_add(rg2, rg2, sl_)
                nc.sync.reg_mul(rg2, rg2, D * TLOC)
                nc.sync.reg_add(rg2, rg2, hp * 128)
                dynv = bass.AP(v_in.tensor, rg2, [[D, 128], [128 * D, TLOC // 128],
                                                  [1, 128]])
                last = c.ydma(vp[:, sl_ * (TLOC // 128):(sl_ + 1) * (TLOC // 128), :],
                              dynv, sem=kvl)
        return last

    kv_at = {}
    kv_at[0] = kv_load(0)

    for hp in range(8):
        nc.tensor.wait_ge(kvl.h, kv_at[hp])
        for h2 in range(2):
            hb = h2 * 64
            for qb in range(NQB):
                q0 = qb * 512
                nc.tensor.wait_ge(c.vc.h, c.vc.n)
                for kvb in range(NKB):
                    pss = ps_s[kvb % 2]
                    if kvb >= 2:
                        nc.tensor.wait_ge(c.sc.h, c.sc.n)
                    i = c.T("matmul", pss[:, :],
                            kp[hb:hb + 64, kvb * 128:kvb * 128 + 128],
                            q[hb:hb + 64, hp, q0:q0 + 512],
                            start=True, stop=True)
                    sd = c.tmark(i)
                    nc.scalar.wait_ge(c.tc.h, sd)
                    c.S("activation", pt[kvb % 2][:, :], pss[:, :], AF.Exp,
                        scale=0.125)
                    pd = c.sc.n
                    nc.tensor.wait_ge(c.sc.h, pd)
                    c.T("matmul", ps_den[:, :], onesb[:, 0:1], pt[kvb % 2][:, :],
                        start=(kvb == 0), stop=(kvb == NKB - 1))
                    i = c.T("matmul", ps_acc[hb:hb + 64, :],
                            vp[:, kvb, hb:hb + 64], pt[kvb % 2][:, :],
                            start=(kvb == 0), stop=(kvb == NKB - 1))
                ad = c.tmark(i)
                nc.vector.wait_ge(c.tc.h, ad)
                c.V("reciprocal", deni[:, :], ps_den[:, :])
                rd = c.vc.n
                nc.tensor.wait_ge(c.vc.h, rd)
                i = c.T("matmul", ps_bc[:, :], ones128f[0:1, :], deni[0:1, :],
                        start=True, stop=True)
                bd = c.tmark(i)
                nc.vector.wait_ge(c.tc.h, bd)
                c.V("tensor_copy", ctf[:, :], ps_bc[:, :])
                c.V("tensor_mul", ctx[hb:hb + 64, hp, q0:q0 + 512],
                    ps_acc[hb:hb + 64, :], ctf[hb:hb + 64, :])
        if hp + 1 < 8:
            nc.gpsimd.wait_ge(c.tc.h, c.tc.n)
            nc.sync.wait_ge(c.tc.h, c.tc.n)
            kv_at[hp + 1] = kv_load(hp + 1)
    ctx_all = c.vc.n
    nc.tensor.wait_ge(c.vc.h, ctx_all)
    done_ev = {}
    st_at = {}
    xl_at2 = {}

    def xload(k_):
        oc, qb = divmod(k_, NQB)
        if k_ >= 2:
            nc.sync.wait_ge(sts[k_ % 2].h, st_at.get(k_ - 2, 0))
        xl_at2[k_] = c.ydma(xc[k_ % 2][:],
                            xT[oc * 128:(oc + 1) * 128, qb * 512:qb * 512 + 512],
                            sem=xls[k_ % 2])

    xload(0)
    xload(1)
    for k_ in range(8 * NQB):
        oc, qb = divmod(k_, NQB)
        py = ps_y[k_ % 2]
        if k_ >= 2:
            nc.tensor.wait_ge(c.vc.h, done_ev[k_ - 2])
        for icc in range(8):
            i = c.T("matmul", py[:, :], wout_sb[:, icc, oc * 128:oc * 128 + 128],
                    ctx[:, icc, qb * 512:qb * 512 + 512],
                    start=(icc == 0), stop=(icc == 7))
        yd = c.tmark(i)
        nc.vector.wait_ge(c.tc.h, yd)
        nc.vector.wait_ge(xls[k_ % 2].h, xl_at2[k_])
        c.V("scalar_tensor_tensor", xc[k_ % 2][:, :], py[:, :],
            bout_sb[:, oc:oc + 1], xc[k_ % 2][:, :], ALU.add, ALU.add)
        done_ev[k_] = c.vc.n
        nc.gpsimd.wait_ge(c.vc.h, done_ev[k_])
        st_at[k_] = c.gdma(yT[oc * 128:(oc + 1) * 128, qb * 512:qb * 512 + 512],
                           xc[k_ % 2][:], sem=sts[k_ % 2])
        if k_ + 2 < 8 * NQB:
            xload(k_ + 2)
    c.gc.wait(nc.gpsimd)
    c.cur_stack = None
    if own:
        c.end()
        return c.nc
    barrier(c)
    blk.close()
    return None


def run_l3a(x, inp):
    nc_a = _get("l3a_qkv", build_l3a_qkv)
    nc_b = _get("l3a_attn", build_l3a_attn)
    bqr = np.ascontiguousarray(inp["L3_attn_bq"].reshape(8, 128).T)
    bdnr = np.ascontiguousarray(inp["L3_attn_bdown"].reshape(2, 128).T)
    Wup = inp["L3_attn_Wup"].reshape(256, 16, 2, 64)
    Wupk = np.ascontiguousarray(Wup[:, :, 0, :].reshape(256, 1024))
    Wupv = np.ascontiguousarray(Wup[:, :, 1, :].reshape(256, 1024))
    bup = inp["L3_attn_bup"].reshape(16, 2, 64)
    bupk = np.ascontiguousarray(bup[:, 0, :].reshape(1024))
    bupv = np.ascontiguousarray(bup[:, 1, :].reshape(1024))
    inv_freq = 1.0 / 10000.0 ** (np.arange(0, 64, 2, dtype=np.float64) / 64.0)
    Pswap = np.zeros((128, 128), np.float32)
    for hh in range(2):
        for j in range(32):
            Pswap[hh * 64 + 32 + j, hh * 64 + j] = 1.0
            Pswap[hh * 64 + j, hh * 64 + 32 + j] = 1.0
    in_maps = []
    for cid in range(N_CORES):
        t0 = cid * TPC
        pos = np.arange(TPC, dtype=np.float64) + (t0 % S)
        fr = pos[:, None] * inv_freq[None, :]
        cos32 = np.cos(fr).astype(np.float32).T
        sin32 = np.sin(fr).astype(np.float32).T
        cosF = np.tile(np.concatenate([cos32, cos32], 0), (2, 1))
        sinPM = np.tile(np.concatenate([-sin32, sin32], 0), (2, 1))
        in_maps.append({"xT": np.ascontiguousarray(x[:, t0:t0 + TPC]),
                        "Wq": inp["L3_attn_Wq"], "bqr": bqr,
                        "Wdn": inp["L3_attn_Wdown"], "bdnr": bdnr,
                        "Wupk": Wupk, "bupkr": np.ascontiguousarray(bupk.reshape(8, 128).T),
                        "Wupv": Wupv, "bupv": bupv,
                        "cosF": np.ascontiguousarray(cosF),
                        "sinPM": np.ascontiguousarray(sinPM), "Pswap": Pswap})
    res = run_bass_kernel_spmd(nc_a, in_maps, core_ids=list(range(N_CORES)))
    r = res.results
    q_all = np.concatenate([rr["q_out"] for rr in r], axis=1)
    k_all = np.concatenate([rr["k_out"] for rr in r], axis=1)
    v_all = np.concatenate([rr["v_out"] for rr in r], axis=0)
    boutr = np.ascontiguousarray(inp["L3_attn_bout"].reshape(8, 128).T)
    in_maps = []
    for cid in range(N_CORES):
        t0 = cid * TPC
        b = t0 // S
        in_maps.append({"xT": np.ascontiguousarray(x[:, t0:t0 + TPC]),
                        "q_in": np.ascontiguousarray(q_all[:, t0:t0 + TPC]),
                        "k_in": np.ascontiguousarray(k_all[:, b * S:(b + 1) * S]),
                        "v_in": np.ascontiguousarray(v_all[b * S:(b + 1) * S]),
                        "Wout": inp["L3_attn_Wout"], "boutr": boutr})
    res = run_bass_kernel_spmd(nc_b, in_maps, core_ids=list(range(N_CORES)))
    return np.concatenate([rr["yT"] for rr in res.results], axis=1)



def build_l1a_qkv(T=TPC, c=None, io=None):
    own = c is None
    if own:
        c = Ctx("l1a_qkv")
    nc = c.nc
    blk = c.st if own else ExitStack()
    if own:
        io = NS(xT=c.din("xT", (D, T)), WqT=c.din("WqT", (D, D)),
                bqc=c.din("bqc", (128, 8)), WkT=c.din("WkT", (D, D)),
                bkc=c.din("bkc", (128, 8)), WvT=c.din("WvT", (D, D)),
                bvr=c.din("bvr", (D,)), W0in=c.din("W0in", (128, 16, 64)),
                identf=c.din("identf", (128, 128)),
                q_out=c.dout("q_out", (D, T)), v_out=c.dout("v_out", (T, D), BF16),
                lg_out=c.dout("lg_out", (T, D)), lmax_out=c.dout("lmax_out", (16,)))
    xT, WqT, bqc, WkT, bkc, WvT, bvr, W0in = (io.xT, io.WqT, io.bqc, io.WkT,
                                              io.bkc, io.WvT, io.bvr, io.W0in)
    q_out, v_out, lg_out, lmax_out = io.q_out, io.v_out, io.lg_out, io.lmax_out
    identf = io.identf
    c.cur_stack = blk

    xb = [c.sb(f"xb{i}", [128, 8, 512]) for i in range(2)]
    h = c.sb("hb", [128, 8, 512], BF16)
    wq_sb = c.sb("wq_sb", [128, 8, D], BF16)
    wk_sb = c.sb("wk_sb", [128, 8, D], BF16)
    wv_sb = c.sb("wv_sb", [128, 8, D], BF16)
    bq_sb = c.sb("bq_sb", [128, 8])
    bk_sb = c.sb("bk_sb", [128, 8])
    bv_row = c.sb("bv_row", [1, D], BF16)
    w0_sb = c.sb("w0_sb", [128, 16, 64])
    nhalf = c.sb("nhalf", [128, 64])
    kf = c.sb("kf", [128, 8, 512])
    k2 = c.sb("k2", [128, 8, 512])
    lgt = c.sb("lgt", [128, 1024])
    lmax = c.sb("lmax", [128, 16])
    lrow = c.sb("lrow", [16, 128])
    evb = c.sb("evb", [128, 512])
    ones1 = c.sb("ones1b", [1, 128], BF16)
    idf = c.sb("idf2", [128, 128])

    ps_a = [c.ps(f"ps_a{i}", [128, 512]) for i in range(2)]
    ps_l = c.ps("ps_l", [128, 64])
    ps_t = c.ps("ps_t", [16, 128])
    ps_sums = c.ps("ps_sums", [1, 512])
    ps_bc = c.ps("ps_bc", [128, 512])

    if own:
        c.begin()
    make_ln(c, ps_sums, ps_bc)
    c.make_consts([1e-5])
    xls = [c.sem("xls0"), c.sem("xls1")]

    c.gdma(wq_sb[:], WqT.rearrange("(ic p) o -> p ic o", p=128))
    c.gdma(wk_sb[:], WkT.rearrange("(ic p) o -> p ic o", p=128))
    c.gdma(wv_sb[:], WvT.rearrange("(ic p) o -> p ic o", p=128))
    c.gdma(bv_row[:], bvr.unsqueeze(0))
    c.ydma(bq_sb[:], bqc[:, :])
    c.ydma(bk_sb[:], bkc[:, :])
    c.ydma(w0_sb[:], W0in[:, :, :])
    c.ydma(idf[:], identf[:, :])
    c.V("memset", ones1[:], 1.0)
    c.V("memset", nhalf[:], -0.5)
    c.V("memset", lmax[:], -1e30)
    c.gc.wait(nc.tensor)
    c.yc.wait(nc.tensor)
    c.yc.wait(nc.vector)

    xT_r = xT.rearrange("(ic p) t -> p ic t", p=128)
    nst = T // 512
    xl_at = {0: c.gdma(xb[0][:], xT_r[:, :, 0:512], sem=xls[0])}
    for st in range(nst):
        t0 = st * 512
        if st + 1 < nst:
            if st >= 1:
                nc.gpsimd.wait_ge(c.vc.h, c.vc.n)
            xl_at[st + 1] = c.gdma(xb[(st + 1) % 2][:], xT_r[:, :, t0 + 512:t0 + 1024],
                                   sem=xls[(st + 1) % 2])
        xv = xb[st % 2]
        for eng in (nc.tensor, nc.vector, nc.scalar):
            eng.wait_ge(xls[st % 2].h, xl_at[st])
        if st > 0:
            nc.vector.wait_ge(c.tc.h, c.tc.n)
            nc.scalar.wait_ge(c.tc.h, c.tc.n)
        ln_fm_512(c, [xv[:, i] for i in range(8)], [h[:, i] for i in range(8)])
        h_ready = c.vc.n
        nc.tensor.wait_ge(c.vc.h, h_ready)
        for oc in range(8):
            pa = ps_a[oc % 2]
            if oc >= 2:
                nc.tensor.wait_ge(c.vc.h, c.vc.n)
            for icc in range(8):
                i = c.T("matmul", pa[:, :], wq_sb[:, icc, oc * 128:oc * 128 + 128],
                        h[:, icc], start=(icc == 0), stop=(icc == 7))
            qd_ = c.tmark(i)
            nc.vector.wait_ge(c.tc.h, qd_)
            nc.vector.wait_ge(c.gc.h, c.gc.n)
            c.V("tensor_scalar", evb[:, :], pa[:, :], bq_sb[:, oc:oc + 1], None, ALU.add)
            wr = c.vc.n
            nc.gpsimd.wait_ge(c.vc.h, wr)
            c.gdma(q_out[oc * 128:oc * 128 + 128, t0:t0 + 512], evb[:])
        for tb in range(4):
            for ob in range(2):
                nc.tensor.wait_ge(c.vc.h, c.vc.n)
                for icc in range(8):
                    c.T("matmul", ps_a[0][:, :], h[:, icc, tb * 128:tb * 128 + 128],
                        wv_sb[:, icc, ob * 512:ob * 512 + 512],
                        start=(icc == 0), stop=False)
                i = c.T("matmul", ps_a[0][:, :], ones1[0:1, 0:128],
                        bv_row[0:1, ob * 512:ob * 512 + 512], start=False, stop=True)
                vd = c.tmark(i)
                nc.vector.wait_ge(c.tc.h, vd)
                nc.vector.wait_ge(c.gc.h, c.gc.n)
                c.V("tensor_copy", evb[:, :].bitcast(BF16)[:, 0:512], ps_a[0][:, :])
                wr = c.vc.n
                nc.gpsimd.wait_ge(c.vc.h, wr)
                c.gdma(v_out[t0 + tb * 128:t0 + tb * 128 + 128,
                             ob * 512:ob * 512 + 512], evb[:, :].bitcast(BF16)[:, 0:512])
        for oc in range(8):
            pa = ps_a[oc % 2]
            nc.tensor.wait_ge(c.vc.h, c.vc.n)
            nc.tensor.wait_ge(c.sc.h, c.sc.n)
            for icc in range(8):
                i = c.T("matmul", pa[:, :], wk_sb[:, icc, oc * 128:oc * 128 + 128],
                        h[:, icc], start=(icc == 0), stop=(icc == 7))
            kd = c.tmark(i)
            nc.vector.wait_ge(c.tc.h, kd)
            c.V("tensor_scalar", kf[:, oc, :], pa[:, :], bk_sb[:, oc:oc + 1], None, ALU.add)
            c.vc.wait(nc.scalar)
            c.S("activation", k2[:, oc, :], kf[:, oc, :], AF.Square)
        kk_ready = c.sc.n
        nc.tensor.wait_ge(c.sc.h, kk_ready)
        for tb in range(4):
            nc.vector.wait_ge(c.gc.h, c.gc.n)
            for hh in range(16):
                ch, hb = hh // 2, (hh % 2) * 64
                sl = slice(tb * 128, tb * 128 + 128)
                if tb > 0 or hh > 0:
                    nc.tensor.wait_ge(c.vc.h, c.vc.n)
                c.T("matmul", ps_l[:, :],
                    kf[hb:hb + 64, ch, sl], w0_sb[hb:hb + 64, hh, :],
                    start=True, stop=False)
                i = c.T("matmul", ps_l[:, :],
                        k2[hb:hb + 64, ch, sl], nhalf[hb:hb + 64, :],
                        start=False, stop=True)
                lgd = c.tmark(i)
                nc.vector.wait_ge(c.tc.h, lgd)
                c.V("tensor_copy", lgt[:, hh * 64:hh * 64 + 64], ps_l[:, :])
            c.V("tensor_reduce", evb[:, 0:16].unsqueeze(2),
                lgt[:, :].rearrange("p (hh f) -> p hh f", f=64), AX.X, ALU.max)
            c.V("tensor_max", lmax[:, :], lmax[:, :], evb[:, 0:16])
            lg_ready = c.vc.n
            nc.gpsimd.wait_ge(c.vc.h, lg_ready)
            c.gdma(lg_out[t0 + tb * 128:t0 + tb * 128 + 128, :], lgt[:])
    lm_ready = c.vc.n
    nc.tensor.wait_ge(c.vc.h, lm_ready)
    i = c.T("transpose", ps_t[:, :], lmax[:, :], idf[:])
    td = c.tmark(i)
    nc.vector.wait_ge(c.tc.h, td)
    c.V("tensor_copy", lrow[:, :], ps_t[:, :])
    c.V("tensor_reduce", lmax[0:16, 0:1], lrow[:, :], AX.X, ALU.max)
    fm_ = c.vc.n
    nc.gpsimd.wait_ge(c.vc.h, fm_)
    c.gdma(lmax_out.unsqueeze(1), lmax[0:16, 0:1])
    c.gc.wait(nc.gpsimd)
    c.cur_stack = None
    if own:
        c.end()
        return c.nc
    barrier(c)
    blk.close()
    return None


def build_l1a_nd(T=TPC, c=None, io=None):
    own = c is None
    if own:
        c = Ctx("l1a_nd")
    nc = c.nc
    blk = c.st if own else ExitStack()
    if own:
        io = NS(lg_in=c.din("lg_in", (T, D)), v_in=c.din("v_in", (T, D), BF16),
                lmax8=c.din("lmax8", (16, 4)), np_out=c.dout("np_out", (64, D)),
                dp_out=c.dout("dp_out", (64, 16)), bmask=None)
    lg_in, v_in, lmax8 = io.lg_in, io.v_in, io.lmax8
    np_out, dp_out = io.np_out, io.dp_out
    bmask = io.bmask
    c.cur_stack = blk

    NTB = T // 128
    alpha = c.sb("alpha", [128, NTB, D], BF16)
    vt = c.sb("vt", [128, NTB, D], BF16)
    lgt = [c.sb(f"lgt{i}", [128, D]) for i in range(2)]
    NRK = 4 if bmask is None else 8
    lm_sb = c.sb("lm_sb", [1, NRK, 16] if bmask is not None else [1, 16, 4])
    bm_sb = c.sb("bm_sb", [1, 8])
    gmx = c.sb("gmx", [1, 16])
    gmaxr = c.sb("gmaxr", [1, D])
    gmb = c.sb("gmb", [128, D])
    sub = c.sb("sub", [128, D])
    n_sb = c.sb("n_sb", [64, D])
    d_sb = c.sb("d_sb", [64, 16])
    onesc = c.sb("onesc", [128, 1], BF16)
    ones1f = c.sb("ones1f", [1, 128])

    ps_g = c.ps("ps_g", [128, 512])
    ps_n = c.ps("ps_n", [64, 64])
    ps_d = c.ps("ps_d", [64, 1])

    if own:
        c.begin()
    lls = [c.sem("lls0"), c.sem("lls1")]
    vl = c.sem("vl")
    c.ydma(lm_sb[:], lmax8.unsqueeze(0))
    if bmask is not None:
        c.ydma(bm_sb[:], bmask)
    c.V("memset", onesc[:], 1.0)
    c.V("memset", ones1f[:], 1.0)
    c.yc.wait(nc.vector)
    if bmask is None:
        c.V("tensor_reduce", gmx[0:1, :].unsqueeze(2), lm_sb[:, :, :], AX.X, ALU.max)
    else:
        c.V("tensor_scalar", bm_sb[0:1, :], bm_sb[0:1, :], 1e30, -1e30,
            ALU.mult, ALU.add)
        c.V("tensor_tensor", lm_sb[0:1, :, :], lm_sb[0:1, :, :],
            bm_sb[0:1, :].unsqueeze(2).broadcast_to([1, 8, 16]), ALU.add)
        c.V("tensor_reduce", gmx[0:1, :].unsqueeze(2),
            lm_sb[0:1, 0:8, :].transpose([0, 2, 1]), AX.X, ALU.max)
    c.V("tensor_copy", gmaxr[0:1, :].rearrange("a (h f) -> a h f", f=64),
        gmx[0:1, :].unsqueeze(2).broadcast_to([1, 16, 64]))
    gm_ready = c.vc.n
    nc.tensor.wait_ge(c.vc.h, gm_ready)
    for bb in range(2):
        if bb == 1:
            nc.tensor.wait_ge(c.vc.h, c.vc.n)
        i = c.T("matmul", ps_g[:, :], ones1f[0:1, :],
                gmaxr[0:1, bb * 512:bb * 512 + 512], start=True, stop=True)
        gd = c.tmark(i)
        nc.vector.wait_ge(c.tc.h, gd)
        c.V("tensor_copy", gmb[:, bb * 512:bb * 512 + 512], ps_g[:, :])
    lg_r = lg_in.rearrange("(n p) d -> p n d", p=128)
    vld = c.gdma(vt[:], v_in.rearrange("(n p) d -> p n d", p=128), sem=vl)
    ll_at = {0: c.gdma(lgt[0][:], lg_r[:, 0, :], sem=lls[0])}
    for tb in range(NTB):
        if tb + 1 < NTB:
            if tb >= 1:
                nc.gpsimd.wait_ge(c.sc.h, c.sc.n)
            ll_at[tb + 1] = c.gdma(lgt[(tb + 1) % 2][:], lg_r[:, tb + 1, :],
                                   sem=lls[(tb + 1) % 2])
        nc.vector.wait_ge(lls[tb % 2].h, ll_at[tb])
        nc.vector.wait_ge(c.sc.h, c.sc.n)
        c.V("tensor_sub", sub[:, :], lgt[tb % 2][:, :], gmb[:, :])
        c.vc.wait(nc.scalar)
        c.S("activation", alpha[:, tb, :], sub[:, :], AF.Exp)
    al_ready = c.sc.n
    nc.tensor.wait_ge(c.sc.h, al_ready)
    nc.tensor.wait_ge(vl.h, vld)
    for hh in range(16):
        for tb in range(NTB):
            c.T("matmul", ps_n[:, 0:64],
                alpha[:, tb, hh * 64:hh * 64 + 64], vt[:, tb, hh * 64:hh * 64 + 64],
                start=(tb == 0), stop=(tb == NTB - 1))
        for tb in range(NTB):
            i = c.T("matmul", ps_d[:, 0:1],
                    alpha[:, tb, hh * 64:hh * 64 + 64], onesc[:, 0:1],
                    start=(tb == 0), stop=(tb == NTB - 1))
        nd = c.tmark(i)
        nc.vector.wait_ge(c.tc.h, nd)
        c.V("tensor_copy", n_sb[:, hh * 64:hh * 64 + 64], ps_n[:, 0:64])
        c.V("tensor_copy", d_sb[:, hh:hh + 1], ps_d[:, 0:1])
        nc.tensor.wait_ge(c.vc.h, c.vc.n)
    done = c.vc.n
    nc.gpsimd.wait_ge(c.vc.h, done)
    c.gdma(np_out[:, :], n_sb[:, :])
    c.gdma(dp_out[:, :], d_sb[:, :])
    c.gc.wait(nc.gpsimd)
    c.cur_stack = None
    if own:
        c.end()
        return c.nc
    barrier(c)
    blk.close()
    return None


def build_l1a_out(T=TPC, c=None, io=None):
    own = c is None
    if own:
        c = Ctx("l1a_out")
    nc = c.nc
    blk = c.st if own else ExitStack()
    if own:
        io = NS(xT=c.din("xT", (D, T)), q_in=c.din("q_in", (D, T)),
                np_in=c.din("np_in", (4, 64, D)), dp_in=c.din("dp_in", (4, 64, 16)),
                W0in=c.din("W0in", (128, 16, 64)), WoT=c.din("WoT", (D, D)),
                boc=c.din("boc", (128, 8)), yT=c.dout("yT", (D, T)), bmaskc=None)
    xT, q_in, np_in, dp_in, W0in, WoT, boc, yT = (io.xT, io.q_in, io.np_in,
                                                  io.dp_in, io.W0in, io.WoT,
                                                  io.boc, io.yT)
    bmaskc = io.bmaskc
    NRK = 4 if bmaskc is None else 8
    c.cur_stack = blk

    q = c.sb("q", [128, 8, T])
    ctx = c.sb("ctx", [128, 8, T], BF16)
    ng = c.sb("ng", [64, NRK, D])
    dg = c.sb("dg", [64, NRK, 16])
    bmc_sb = c.sb("bmc_sb", [128, 8])
    ngb = c.sb("ngb", [64, D], BF16)
    dgb = c.sb("dgb", [64, 16], BF16)
    rz = c.sb("rz", [1, 16])
    rzrow = c.sb("rzrow", [1, D])
    w0_sb = c.sb("w0_sb", [128, 16, 64])
    wo_sb = c.sb("wo_sb", [128, 8, D], BF16)
    bo_sb = c.sb("bo_sb", [128, 8])
    phl = c.sb("phl", [128, 512])
    phm = c.sb("phm", [1, 512])
    phb = c.sb("phb", [128, 512])
    phi = c.sb("phi", [128, 512], BF16)
    ones64b = c.sb("ones64b", [64, 1], BF16)
    sphi = c.sb("sphi", [1, 512])
    dnr = c.sb("dnr", [1, 512])
    dnb = c.sb("dnb", [128, 512])
    xc = [c.sb(f"xc{i}", [128, 512]) for i in range(2)]
    ones64 = c.sb("ones64", [64, 1])
    ones1f = c.sb("ones1f", [1, 128])

    ps_ph = c.ps("ps_ph", [128, 512])
    ps_m = c.ps("ps_m", [1, 512])
    ps_b = c.ps("ps_b", [128, 512])
    ps_n2 = c.ps("ps_n2", [128, 512])
    ps_d2 = c.ps("ps_d2", [1, 512])
    ps_z = c.ps("ps_z", [1, 16])
    ps_y = [c.ps(f"ps_y{i}", [128, 512]) for i in range(2)]

    if own:
        c.begin()
    xls = [c.sem("xls0"), c.sem("xls1")]
    sts = [c.sem("sts0"), c.sem("sts1")]
    for i in range(8):
        c.ydma(q[:, i, :], q_in[i * 128:(i + 1) * 128, :])
    for cc in range(NRK):
        c.ydma(ng[:, cc, :], np_in[cc])
    for cc in range(NRK):
        c.ydma(dg[:, cc, :], dp_in[cc])
    if bmaskc is not None:
        c.ydma(bmc_sb[:], bmaskc)
    c.ydma(w0_sb[:], W0in[:, :, :])
    c.gdma(wo_sb[:], WoT.rearrange("(ic p) o -> p ic o", p=128))
    c.ydma(bo_sb[:], boc[:, :])
    c.V("memset", ones64[:], 1.0)
    c.V("memset", ones64b[:], 1.0)
    c.V("memset", ones1f[:], 1.0)
    c.yc.wait(nc.vector)
    c.gc.wait(nc.tensor)
    if bmaskc is None:
        c.V("tensor_add", ng[:, 0, :], ng[:, 0, :], ng[:, 1, :])
        c.V("tensor_add", ng[:, 2, :], ng[:, 2, :], ng[:, 3, :])
        c.V("tensor_add", ng[:, 0, :], ng[:, 0, :], ng[:, 2, :])
        c.V("tensor_add", dg[:, 0, :], dg[:, 0, :], dg[:, 1, :])
        c.V("tensor_add", dg[:, 2, :], dg[:, 2, :], dg[:, 3, :])
        c.V("tensor_add", dg[:, 0, :], dg[:, 0, :], dg[:, 2, :])
    else:
        c.V("tensor_scalar", ng[:, 0, :], ng[:, 0, :], bmc_sb[0:64, 0:1],
            None, ALU.mult)
        c.V("tensor_scalar", dg[:, 0, :], dg[:, 0, :], bmc_sb[0:64, 0:1],
            None, ALU.mult)
        for cc in range(1, 8):
            c.V("scalar_tensor_tensor", ng[:, 0, :], ng[:, cc, :],
                bmc_sb[0:64, cc:cc + 1], ng[:, 0, :], ALU.mult, ALU.add)
            c.V("scalar_tensor_tensor", dg[:, 0, :], dg[:, cc, :],
                bmc_sb[0:64, cc:cc + 1], dg[:, 0, :], ALU.mult, ALU.add)
    nd_ready = c.vc.n
    nc.tensor.wait_ge(c.vc.h, nd_ready)
    i = c.T("matmul", ps_z[0:1, :], ones64[:, 0:1], dg[:, 0, :], start=True, stop=True)
    zd = c.tmark(i)
    nc.vector.wait_ge(c.tc.h, zd)
    c.V("reciprocal", rz[0:1, :], ps_z[0:1, :])
    c.V("tensor_copy", rzrow[0:1, :].rearrange("a (h f) -> a h f", f=64),
        rz[0:1, :].unsqueeze(2).broadcast_to([1, 16, 64]))
    rz_ready = c.vc.n
    nc.tensor.wait_ge(c.vc.h, rz_ready)
    for bb in range(2):
        if bb == 1:
            nc.tensor.wait_ge(c.vc.h, c.vc.n)
        i = c.T("matmul", ps_b[:, :], ones1f[0:1, :],
                rzrow[0:1, bb * 512:bb * 512 + 512], start=True, stop=True)
        bd = c.tmark(i)
        nc.vector.wait_ge(c.tc.h, bd)
        c.V("tensor_tensor", ngb[:, bb * 512:bb * 512 + 512], ng[:, 0, bb * 512:bb * 512 + 512],
            ps_b[0:64, :], ALU.mult)
    nc.tensor.wait_ge(c.vc.h, c.vc.n)
    i = c.T("matmul", ps_b[0:64, 0:16], ones1f[0:1, 0:64], rz[0:1, 0:16],
            start=True, stop=True)
    bd = c.tmark(i)
    nc.vector.wait_ge(c.tc.h, bd)
    c.V("tensor_tensor", dgb[:, :], dg[:, 0, :], ps_b[0:64, 0:16], ALU.mult)
    nd2 = c.vc.n
    nc.tensor.wait_ge(c.vc.h, nd2)

    NTB = T // 128
    for tb in range(NTB):
        for hh in range(16):
            ch, hb = hh // 2, (hh % 2) * 64
            if not (tb == 0 and hh == 0):
                nc.tensor.wait_ge(c.vc.h, c.vc.n)
                nc.tensor.wait_ge(c.sc.h, c.sc.n)
            i = c.T("matmul", ps_ph[0:64, 0:128], w0_sb[hb:hb + 64, hh, :],
                    q[hb:hb + 64, ch, tb * 128:tb * 128 + 128], start=True, stop=True)
            pld = c.tmark(i)
            nc.vector.wait_ge(c.tc.h, pld)
            c.V("tensor_copy", phl[0:64, 0:128], ps_ph[0:64, 0:128])
            pev = c.vc.n
            nc.tensor.wait_ge(c.vc.h, pev)
            i = c.T("matmul", ps_m[0:1, 0:128], ones64[:, 0:1], phl[0:64, 0:128],
                    start=True, stop=True)
            md = c.tmark(i)
            nc.vector.wait_ge(c.tc.h, md)
            c.V("tensor_scalar", phm[0:1, 0:128], ps_m[0:1, 0:128], -1.0 / 64, None,
                ALU.mult)
            mr = c.vc.n
            nc.tensor.wait_ge(c.vc.h, mr)
            i = c.T("matmul", ps_b[0:64, 0:128], ones1f[0:1, 0:64], phm[0:1, 0:128],
                    start=True, stop=True)
            bd2 = c.tmark(i)
            nc.vector.wait_ge(c.tc.h, bd2)
            c.V("tensor_tensor", phl[0:64, 0:128], phl[0:64, 0:128], ps_b[0:64, 0:128],
                ALU.add)
            c.vc.wait(nc.scalar)
            c.S("activation", phi[0:64, 0:128], phl[0:64, 0:128], AF.Exp)
            phid = c.sc.n
            nc.tensor.wait_ge(c.sc.h, phid)
            c.T("matmul", ps_n2[hb:hb + 64, 0:128], ngb[:, hh * 64:hh * 64 + 64],
                phi[0:64, 0:128], start=True, stop=True)
            c.T("matmul", ps_d2[0:1, 0:128], dgb[:, hh:hh + 1], phi[0:64, 0:128],
                start=True, stop=True)
            i = c.T("matmul", ps_m[0:1, 0:128], ones64b[:, 0:1], phi[0:64, 0:128],
                    start=True, stop=True)
            numd = c.tmark(i)
            nc.vector.wait_ge(c.tc.h, numd)
            c.V("tensor_copy", sphi[0:1, 0:128], ps_m[0:1, 0:128])
            c.V("scalar_tensor_tensor", dnr[0:1, 0:128], sphi[0:1, 0:128], 1e-6,
                ps_d2[0:1, 0:128], ALU.mult, ALU.add)
            c.V("reciprocal", dnr[0:1, 0:128], dnr[0:1, 0:128])
            dnd = c.vc.n
            nc.tensor.wait_ge(c.vc.h, dnd)
            i = c.T("matmul", ps_b[:, 0:128], ones1f[0:1, :], dnr[0:1, 0:128],
                    start=True, stop=True)
            bd3 = c.tmark(i)
            nc.vector.wait_ge(c.tc.h, bd3)
            c.V("tensor_copy", dnb[:, 0:128], ps_b[:, 0:128])
            c.V("tensor_mul", ctx[hb:hb + 64, ch, tb * 128:tb * 128 + 128],
                ps_n2[hb:hb + 64, 0:128], dnb[hb:hb + 64, 0:128])
    ctx_all = c.vc.n
    nc.tensor.wait_ge(c.vc.h, ctx_all)
    NQB = T // 512
    done_ev, st_at, xl_at2 = {}, {}, {}

    def xload(k_):
        oc, qb = divmod(k_, NQB)
        if k_ >= 2:
            nc.sync.wait_ge(sts[k_ % 2].h, st_at.get(k_ - 2, 0))
        xl_at2[k_] = c.ydma(xc[k_ % 2][:],
                            xT[oc * 128:(oc + 1) * 128, qb * 512:qb * 512 + 512],
                            sem=xls[k_ % 2])

    xload(0)
    xload(1)
    for k_ in range(8 * NQB):
        oc, qb = divmod(k_, NQB)
        py = ps_y[k_ % 2]
        if k_ >= 2:
            nc.tensor.wait_ge(c.vc.h, done_ev[k_ - 2])
        for icc in range(8):
            i = c.T("matmul", py[:, :], wo_sb[:, icc, oc * 128:oc * 128 + 128],
                    ctx[:, icc, qb * 512:qb * 512 + 512],
                    start=(icc == 0), stop=(icc == 7))
        yd = c.tmark(i)
        nc.vector.wait_ge(c.tc.h, yd)
        nc.vector.wait_ge(xls[k_ % 2].h, xl_at2[k_])
        c.V("scalar_tensor_tensor", xc[k_ % 2][:, :], py[:, :],
            bo_sb[:, oc:oc + 1], xc[k_ % 2][:, :], ALU.add, ALU.add)
        done_ev[k_] = c.vc.n
        nc.gpsimd.wait_ge(c.vc.h, done_ev[k_])
        st_at[k_] = c.gdma(yT[oc * 128:(oc + 1) * 128, qb * 512:qb * 512 + 512],
                           xc[k_ % 2][:], sem=sts[k_ % 2])
        if k_ + 2 < 8 * NQB:
            xload(k_ + 2)
    c.gc.wait(nc.gpsimd)
    c.cur_stack = None
    if own:
        c.end()
        return c.nc
    barrier(c)
    blk.close()
    return None


def run_l1a(x, inp):
    nc_a = _get("l1a_qkv", build_l1a_qkv)
    nc_b = _get("l1a_nd", build_l1a_nd)
    nc_c = _get("l1a_out", build_l1a_out)
    identf = np.eye(128, dtype=np.float32)
    W0h = inp["L1_attn_Wrf"][0].transpose(1, 0, 2)
    W0 = np.ascontiguousarray(np.concatenate([W0h, W0h], axis=0))
    in_maps = []
    for cid in range(N_CORES):
        t0 = cid * TPC
        in_maps.append({"xT": np.ascontiguousarray(x[:, t0:t0 + TPC]),
                        "WqT": np.ascontiguousarray(inp["L1_attn_Wq"].T),
                        "bqc": np.ascontiguousarray(inp["L1_attn_bq"].reshape(8, 128).T),
                        "WkT": np.ascontiguousarray(inp["L1_attn_Wk"].T),
                        "bkc": np.ascontiguousarray(inp["L1_attn_bk"].reshape(8, 128).T),
                        "WvT": np.ascontiguousarray(inp["L1_attn_Wv"].T),
                        "bvr": inp["L1_attn_bv"],
                        "W0in": W0, "identf": identf})
    res = run_bass_kernel_spmd(nc_a, in_maps, core_ids=list(range(N_CORES)))
    r = res.results
    lmax_all = np.stack([rr["lmax_out"] for rr in r], axis=1)
    in_maps = []
    for cid in range(N_CORES):
        b = cid // 4
        in_maps.append({"lg_in": r[cid]["lg_out"], "v_in": r[cid]["v_out"],
                        "lmax8": np.ascontiguousarray(lmax_all[:, b * 4:b * 4 + 4])})
    res = run_bass_kernel_spmd(nc_b, in_maps, core_ids=list(range(N_CORES)))
    r2 = res.results
    in_maps = []
    for cid in range(N_CORES):
        t0 = cid * TPC
        b = cid // 4
        np4 = np.stack([r2[cc]["np_out"] for cc in range(b * 4, b * 4 + 4)])
        dp4 = np.stack([r2[cc]["dp_out"] for cc in range(b * 4, b * 4 + 4)])
        in_maps.append({"xT": np.ascontiguousarray(x[:, t0:t0 + TPC]),
                        "q_in": r[cid]["q_out"],
                        "np_in": np4, "dp_in": dp4, "W0in": W0,
                        "WoT": np.ascontiguousarray(inp["L1_attn_Wo"].T),
                        "boc": np.ascontiguousarray(inp["L1_attn_bo"].reshape(8, 128).T)})
    res = run_bass_kernel_spmd(nc_c, in_maps, core_ids=list(range(N_CORES)))
    return np.concatenate([rr["yT"] for rr in res.results], axis=1)



def barrier(c):
    nc = c.nc
    for eng in (nc.tensor, nc.vector, nc.scalar, nc.gpsimd, nc.sync):
        eng.wait_ge(c.tc.h, c.tc.n)
        eng.wait_ge(c.vc.h, c.vc.n)
        eng.wait_ge(c.sc.h, c.sc.n)
        eng.wait_ge(c.gc.h, c.gc.n)
        eng.wait_ge(c.yc.h, c.yc.n)


def build_l2a(T=TPC, c=None, io=None):
    own = c is None
    if own:
        c = Ctx("l2a")
    nc = c.nc
    blk = c.st if own else ExitStack()
    HALO = 512
    TE = T + HALO
    NB = T // 16
    if own:
        io = NS(xT=c.din("xT", (D, TE)), WqT=c.din("WqT", (D, D)),
                WkT=c.din("WkT", (D, D)), WvT=c.din("WvT", (D, D)),
                WkpT=c.din("WkpT", (D, D)), WoT=c.din("WoT", (D, D)),
                gmr=c.din("gmr", (128, 8)), btr=c.din("btr", (128, 8)),
                mstd=c.din("mstd", (64, 512)), mb0=c.din("mb0", (64, 512)),
                e2in=c.din("e2in", (2, 128)),
                identb=c.din("identb", (128, 128), BF16), yT=c.dout("yT", (D, T)))
    xT, WqT, WkT, WvT, WkpT, WoT = io.xT, io.WqT, io.WkT, io.WvT, io.WkpT, io.WoT
    gmr, btr, mstd, mb0, identb, yT = io.gmr, io.btr, io.mstd, io.mb0, io.identb, io.yT
    e2in = io.e2in
    c.cur_stack = blk

    qfm = c.nc.dram_tensor("qfm", [D, TE], BF16, kind="Internal").ap()
    kfm = c.nc.dram_tensor("kfm", [D, TE], BF16, kind="Internal").ap()
    kpfm = c.nc.dram_tensor("kpfm", [D, TE], BF16, kind="Internal").ap()
    vfm = c.nc.dram_tensor("vfm", [D, TE], BF16, kind="Internal").ap()
    vsc = c.nc.dram_tensor("vsc", [TE, D], BF16, kind="Internal").ap()
    hsc = c.nc.dram_tensor("hsc", [D, T], F32, kind="Internal").ap()

    idb = c.sb("idb", [128, 128], BF16)
    gm_sb = c.sb("gm_sb", [128, 8])
    bt_sb = c.sb("bt_sb", [128, 8])
    ctxf = c.sb("ctxf", [128, 8, T], BF16)
    ps_sums = c.ps("ps_sums", [1, 512])
    ps_bc = c.ps("ps_bc", [128, 512])

    if own:
        c.begin()
    make_ln(c, ps_sums, ps_bc)
    c.make_consts([1e-5, 0.125])
    xls = [c.sem("xls0"), c.sem("xls1")]
    slb = c.sem("slb")
    vws = [c.sem("vws0"), c.sem("vws1"), c.sem("vws2")]
    sts = [c.sem("sts0"), c.sem("sts1")]
    c.ydma(idb[:], identb[:, :])
    c.ydma(gm_sb[:], gmr[:, :])
    c.ydma(bt_sb[:], btr[:, :])
    c.yc.wait(nc.tensor)
    c.yc.wait(nc.vector)

    xT_r = xT.rearrange("(ic p) t -> p ic t", p=128)
    with ExitStack() as ph1:
        wsb = [ph1.enter_context(nc.sbuf_tensor(f"wp{i}", [128, 8, D], BF16))
               for i in range(4)]
        xb = [ph1.enter_context(nc.sbuf_tensor(f"xb{i}", [128, 8, 512], F32))
              for i in range(2)]
        hh_ = ph1.enter_context(nc.sbuf_tensor("hb", [128, 8, 512], BF16))
        h32_ = ph1.enter_context(nc.sbuf_tensor("h32b", [128, 8, 512], F32))
        pstore = [ph1.enter_context(nc.sbuf_tensor(f"pst{i}", [128, 512], BF16))
                  for i in range(2)]
        ps_a = [ph1.enter_context(nc.psum_tensor(f"ps_a{i}", [128, 512], F32))
                for i in range(2)]
        for wi, W in enumerate((WqT, WkT, WkpT, WvT)):
            c.gdma(wsb[wi][:], W.rearrange("(ic p) o -> p ic o", p=128))
        c.gc.wait(nc.tensor)
        nst = TE // 512
        evac = {0: 0, 1: 0}
        st_done = {0: 0, 1: 0}
        xl_at = {0: c.gdma(xb[0][:], xT_r[:, :, 0:512], sem=xls[0])}
        for st in range(nst):
            t0 = st * 512
            if st + 1 < nst:
                if st >= 1:
                    nc.gpsimd.wait_ge(c.vc.h, c.vc.n)
                xl_at[st + 1] = c.gdma(xb[(st + 1) % 2][:],
                                       xT_r[:, :, t0 + 512:t0 + 1024],
                                       sem=xls[(st + 1) % 2])
            xv = xb[st % 2]
            for eng in (nc.tensor, nc.vector, nc.scalar):
                eng.wait_ge(xls[st % 2].h, xl_at[st])
            if st > 0:
                nc.vector.wait_ge(c.tc.h, c.tc.n)
                nc.scalar.wait_ge(c.tc.h, c.tc.n)
                nc.vector.wait_ge(c.gc.h, c.gc.n)
            ln_fm_512(c, [xv[:, i] for i in range(8)], [hh_[:, i] for i in range(8)],
                      h32_views=[h32_[:, i] for i in range(8)])
            h_ready = c.vc.n
            nc.tensor.wait_ge(c.vc.h, h_ready)
            if t0 >= HALO:
                nc.gpsimd.wait_ge(c.vc.h, h_ready)
                c.gdma(hsc.rearrange("(ic p) t -> p ic t", p=128)
                       [:, :, t0 - HALO:t0 - HALO + 512], h32_[:])
            kk = 0
            for wi, dstfm in enumerate((qfm, kfm, kpfm, vfm)):
                for oc in range(8):
                    par = kk % 2
                    pa = ps_a[par]
                    if evac[par]:
                        nc.tensor.wait_ge(c.vc.h, evac[par])
                    for icc in range(8):
                        i = c.T("matmul", pa[:, :],
                                wsb[wi][:, icc, oc * 128:oc * 128 + 128],
                                hh_[:, icc, :],
                                start=(icc == 0), stop=(icc == 7))
                    pd = c.tmark(i)
                    nc.vector.wait_ge(c.tc.h, pd)
                    if st_done[par]:
                        nc.vector.wait_ge(sts[par].h, st_done[par])
                    c.V("tensor_copy", pstore[par][:, :], pa[:, :])
                    evac[par] = c.vc.n
                    nc.gpsimd.wait_ge(c.vc.h, evac[par])
                    st_done[par] = c.gdma(dstfm[oc * 128:oc * 128 + 128,
                                                t0:t0 + 512], pstore[par][:],
                                          sem=sts[par])
                    kk += 1
            for tb in range(4):
                for ob in range(2):
                    par = kk % 2
                    pa = ps_a[par]
                    if evac[par]:
                        nc.tensor.wait_ge(c.vc.h, evac[par])
                    for icc in range(8):
                        i = c.T("matmul", pa[:, :],
                                hh_[:, icc, tb * 128:tb * 128 + 128],
                                wsb[3][:, icc, ob * 512:ob * 512 + 512],
                                start=(icc == 0), stop=(icc == 7))
                    pd = c.tmark(i)
                    nc.vector.wait_ge(c.tc.h, pd)
                    if st_done[par]:
                        nc.vector.wait_ge(sts[par].h, st_done[par])
                    c.V("tensor_copy", pstore[par][:, :], pa[:, :])
                    evac[par] = c.vc.n
                    nc.gpsimd.wait_ge(c.vc.h, evac[par])
                    st_done[par] = c.gdma(vsc[t0 + tb * 128:t0 + tb * 128 + 128,
                                              ob * 512:ob * 512 + 512],
                                          pstore[par][:], sem=sts[par])
                    kk += 1
        barrier(c)
    with ExitStack() as ph2:
        kslab = ph2.enter_context(nc.sbuf_tensor("kslab", [128, 4, TE], BF16))
        kpslab = ph2.enter_context(nc.sbuf_tensor("kpslab", [128, 4, TE], BF16))
        vslab = ph2.enter_context(nc.sbuf_tensor("vslab", [128, 4, TE], BF16))
        qslab = ph2.enter_context(nc.sbuf_tensor("qslab", [128, 4, T], BF16))
        r2t = [ph2.enter_context(nc.sbuf_tensor(f"r2t{i}", [128, 512], BF16))
               for i in range(2)]
        acp = [ph2.enter_context(nc.sbuf_tensor(f"acp{i}", [64, 512], F32))
               for i in range(2)]
        pt = [ph2.enter_context(nc.sbuf_tensor(f"pt{i}", [64, 512], BF16))
              for i in range(2)]
        redt = ph2.enter_context(nc.sbuf_tensor("redt", [64, 16], F32))
        rect = ph2.enter_context(nc.sbuf_tensor("rect", [2, 16], F32))
        w2m = [ph2.enter_context(nc.sbuf_tensor(f"w2m{i}", [128, 512], BF16))
               for i in range(2)]
        ztt = ph2.enter_context(nc.sbuf_tensor("ztt", [128, 16], F32))
        vw = [ph2.enter_context(nc.sbuf_tensor(f"vw{i}", [64, 512], BF16))
              for i in range(3)]
        mstd_sb = ph2.enter_context(nc.sbuf_tensor("mstd_sb", [64, 512], F32))
        mb0_sb = ph2.enter_context(nc.sbuf_tensor("mb0_sb", [64, 512], F32))
        sel64 = ph2.enter_context(nc.sbuf_tensor("sel64", [64, 2], F32))
        e2t = ph2.enter_context(nc.sbuf_tensor("e2t", [2, 128], F32))
        psA = [ph2.enter_context(nc.psum_tensor(f"psA{i}", [64, 512], F32))
               for i in range(2)]
        psU = [ph2.enter_context(nc.psum_tensor(f"psU{i}", [128, 512], F32))
               for i in range(2)]
        psD = ph2.enter_context(nc.psum_tensor("psD", [2, 16], F32))
        psR = ph2.enter_context(nc.psum_tensor("psR", [128, 16], F32))
        c.ydma(mstd_sb[:], mstd[:, :])
        c.ydma(mb0_sb[:], mb0[:, :])
        c.V("memset", sel64[:, :], 0.0)
        c.V("memset", sel64[0:32, 0:1], 1.0)
        c.V("memset", sel64[32:64, 1:2], 1.0)
        c.ydma(e2t[:], e2in[:, :])
        c.yc.wait(nc.vector)
        c.yc.wait(nc.scalar)
        c.yc.wait(nc.tensor)

        a_mark = {}; madd = {}; expd = {}; redd = {}; denm = {}; recd = {}
        rzbm = {}; u2m = {}; zd = {}; vw_at = {}
        jj = 0

        def vwload(c_, half, Bp):
            jref = (Bp - 3) * 4 + 3 + half * NB * 4
            if jref in u2m:
                nc.gpsimd.wait_ge(c.tc.h, u2m[jref])
            w0 = 496 + 16 * Bp
            for rhalf in range(2):
                dst = bass.AP(vw[Bp % 3][:].tensor, rhalf * 32 * 512,
                              [[512, 32], [128, 4], [1, 64]])
                src = bass.AP(vsc.tensor, w0 * D + half * 512 + rhalf * 64,
                              [[D, 32], [128, 4], [1, 64]])
                vw_at[Bp] = c.gdma(dst, src, sem=vws[Bp % 3])

        def astage(j, half, B, hpi):
            w0 = 496 + 16 * B
            if j - 2 in a_mark:
                nc.vector.wait_ge(c.tc.h, a_mark[j - 2])
            qrep = bass.AP(qslab[:].tensor, hpi * T + 16 * B,
                           [[4 * T, 128], [1, 16], [0, 32]])
            kprep = bass.AP(kpslab[:].tensor, hpi * TE + w0,
                            [[4 * TE, 128], [0, 16], [1, 32]])
            c.V("tensor_tensor", r2t[j % 2][:, :], kprep, qrep, ALU.mult)
            r2d = c.vc.n
            nc.tensor.wait_ge(c.vc.h, r2d)
            if j - 2 in madd:
                nc.tensor.wait_ge(c.vc.h, madd[j - 2])
            c.T("matmul", psA[j % 2][0:32, :], kslab[0:64, hpi, w0:w0 + 32],
                r2t[j % 2][0:64, :], start=True, stop=True)
            i = c.T("matmul", psA[j % 2][32:64, :],
                    kslab[64:128, hpi, w0:w0 + 32],
                    r2t[j % 2][64:128, :], start=True, stop=True)
            a_mark[j] = c.tmark(i)

        def tail(j, half, B, hpi):
            hp_g = half * 4 + hpi
            nc.vector.wait_ge(c.tc.h, a_mark[j])
            if j - 2 in expd:
                nc.vector.wait_ge(c.sc.h, expd[j - 2])
            mt = mb0_sb if B == 0 else mstd_sb
            c.V("tensor_tensor", acp[j % 2][:, :], psA[j % 2][:, :], mt[:, :],
                ALU.add)
            madd[j] = c.vc.n
            nc.scalar.wait_ge(c.vc.h, madd[j])
            if j - 2 in u2m:
                nc.scalar.wait_ge(c.tc.h, u2m[j - 2])
            c.S("activation", pt[j % 2][:, :], acp[j % 2][:, :], AF.Exp,
                scale=0.125)
            expd[j] = c.sc.n
            nc.vector.wait_ge(c.sc.h, expd[j])
            if j - 1 in denm:
                nc.vector.wait_ge(c.tc.h, denm[j - 1])
            c.V("tensor_reduce", redt[:, :].unsqueeze(2),
                pt[j % 2][:, :].rearrange("p (s b) -> p s b", s=16), AX.X,
                ALU.add)
            redd[j] = c.vc.n
            nc.tensor.wait_ge(c.vc.h, redd[j])
            if j - 1 in recd:
                nc.tensor.wait_ge(c.vc.h, recd[j - 1])
            i = c.T("matmul", psD[:, :], sel64[:, :], redt[:, :],
                    start=True, stop=True)
            denm[j] = c.tmark(i)
            nc.vector.wait_ge(c.tc.h, denm[j])
            c.V("reciprocal", rect[:, :], psD[:, :])
            recd[j] = c.vc.n
            nc.tensor.wait_ge(c.vc.h, recd[j])
            if j - 1 in zd:
                nc.tensor.wait_ge(c.vc.h, zd[j - 1])
            i = c.T("matmul", psR[:, :], e2t[:, :], rect[:, :],
                    start=True, stop=True)
            rzbm[j] = c.tmark(i)
            nc.tensor.wait_ge(vws[B % 3].h, vw_at[B])
            if j - 2 in zd:
                nc.tensor.wait_ge(c.vc.h, zd[j - 2])
            i = c.T("matmul", psU[j % 2][:, :],
                    vw[B % 3][:, hpi * 128:hpi * 128 + 128],
                    pt[j % 2][:, :], start=True, stop=True)
            u2m[j] = c.tmark(i)
            w0 = 496 + 16 * B
            nc.vector.wait_ge(c.tc.h, u2m[j])
            vprep = bass.AP(vslab[:].tensor, hpi * TE + w0,
                            [[4 * TE, 128], [0, 16], [1, 32]])
            c.V("tensor_tensor", w2m[j % 2][:, :], psU[j % 2][:, :], vprep,
                ALU.mult)
            c.V("tensor_reduce", ztt[:, :].unsqueeze(2),
                w2m[j % 2][:, :].rearrange("p (s b) -> p s b", s=16), AX.X,
                ALU.add)
            c.V("tensor_tensor", ctxf[:, hp_g, 16 * B:16 * B + 16],
                ztt[:, :], psR[:, :], ALU.mult)
            zd[j] = c.vc.n

        for half in range(2):
            for eng in (nc.gpsimd, nc.sync, nc.vector, nc.tensor, nc.scalar):
                eng.wait_ge(c.tc.h, c.tc.n)
                eng.wait_ge(c.vc.h, c.vc.n)
                eng.wait_ge(c.sc.h, c.sc.n)
            for i4 in range(4):
                rr = (half * 4 + i4) * 128
                c.gdma(kslab[:, i4, :], kfm[rr:rr + 128, :], sem=slb)
                c.gdma(kpslab[:, i4, :], kpfm[rr:rr + 128, :], sem=slb)
                c.gdma(vslab[:, i4, :], vfm[rr:rr + 128, :], sem=slb)
                c.gdma(qslab[:, i4, :], qfm[rr:rr + 128, HALO:TE], sem=slb)
            for eng in (nc.tensor, nc.vector, nc.scalar):
                slb.wait(eng)
            for i3 in range(3):
                c.V("memset", vw[i3][:, :], 0.0)
            mz = c.vc.n
            nc.gpsimd.wait_ge(c.vc.h, mz)
            vwload(c, half, 0)
            vwload(c, half, 1)
            prev = None
            for B in range(NB):
                if B + 2 < NB:
                    vwload(c, half, B + 2)
                for hpi in range(4):
                    j = jj
                    astage(j, half, B, hpi)
                    if prev is not None:
                        tail(*prev)
                    prev = (j, half, B, hpi)
                    jj += 1
            tail(*prev)
            prev = None
        barrier(c)
    with ExitStack() as ph3:
        wo_sb = ph3.enter_context(nc.sbuf_tensor("wo_sb", [128, 8, D], BF16))
        outp = ph3.enter_context(nc.sbuf_tensor("outp", [128, 8, 512], F32))
        outp2 = ph3.enter_context(nc.sbuf_tensor("outp2", [128, 8, 512], F32))
        xb3 = [ph3.enter_context(nc.sbuf_tensor(f"xb3{i}", [128, 8, 512], F32))
               for i in range(2)]
        ps_y = [ph3.enter_context(nc.psum_tensor(f"ps_y{i}", [128, 512], F32))
                for i in range(2)]
        x3s = [c.sem("x3s0"), c.sem("x3s1")]
        h3s = [c.sem("h3s0"), c.sem("h3s1")]
        hb3 = [ph3.enter_context(nc.sbuf_tensor(f"hb3{i}", [128, 8, 512], F32))
               for i in range(2)]
        c.gdma(wo_sb[:], WoT.rearrange("(ic p) o -> p ic o", p=128))
        c.gc.wait(nc.tensor)
        x3_at = {}

        h3_at = {}

        def x3load(qb):
            if qb >= 2:
                nc.sync.wait_ge(c.vc.h, c.vc.n)
            x3_at[qb] = c.ydma(xb3[qb % 2][:],
                               xT_r[:, :, HALO + qb * 512:HALO + qb * 512 + 512],
                               sem=x3s[qb % 2])
            h3_at[qb] = c.ydma(hb3[qb % 2][:],
                               hsc.rearrange("(ic p) t -> p ic t", p=128)
                               [:, :, qb * 512:qb * 512 + 512],
                               sem=h3s[qb % 2])

        x3load(0)
        if T // 512 > 1:
            x3load(1)
        for qb in range(T // 512):
            if qb > 0:
                nc.vector.wait_ge(c.tc.h, c.tc.n)
                nc.scalar.wait_ge(c.tc.h, c.tc.n)
            for oc in range(8):
                py = ps_y[oc % 2]
                if oc >= 2:
                    nc.tensor.wait_ge(c.vc.h, c.vc.n)
                for icc in range(8):
                    i = c.T("matmul", py[:, :], wo_sb[:, icc, oc * 128:oc * 128 + 128],
                            ctxf[:, icc, qb * 512:qb * 512 + 512],
                            start=(icc == 0), stop=(icc == 7))
                yd = c.tmark(i)
                nc.vector.wait_ge(c.tc.h, yd)
                nc.vector.wait_ge(x3s[qb % 2].h, x3_at[qb])
                nc.vector.wait_ge(h3s[qb % 2].h, h3_at[qb])
                c.V("tensor_add", outp[:, oc, :], py[:, :], hb3[qb % 2][:, oc, :])
            o_ready = c.vc.n
            nc.tensor.wait_ge(c.vc.h, o_ready)
            ln_fm_512(c, [outp[:, i, :] for i in range(8)],
                      [outp2[:, i, :] for i in range(8)],
                      colmul=[gm_sb[:, i:i + 1] for i in range(8)],
                      coladd=[bt_sb[:, i:i + 1] for i in range(8)])
            for icc in range(8):
                c.V("tensor_add", outp2[:, icc, :], outp2[:, icc, :],
                    xb3[qb % 2][:, icc, :])
            y_ready = c.vc.n
            if qb + 2 < T // 512:
                x3load(qb + 2)
            nc.gpsimd.wait_ge(c.vc.h, y_ready)
            c.gdma(yT.rearrange("(ic p) t -> p ic t", p=128)[:, :, qb * 512:qb * 512 + 512],
                   outp2[:])
    c.gc.wait(nc.gpsimd)
    c.cur_stack = None
    if own:
        c.end()
        return c.nc
    barrier(c)
    blk.close()
    return None


def l2a_masks():
    r_ = np.arange(16)[:, None, None]
    a_ = np.arange(32)[None, :, None]
    b_ = np.arange(32)[None, None, :]
    vstd = (a_ >= r_ + 1) & (a_ <= r_ + 16) & (b_ >= r_ + 1) & (b_ <= r_ + 16)
    vfst = vstd & (a_ >= 16) & (b_ >= 16)
    def pack(v):
        m = np.where(v, 0.0, -1e9).astype(np.float32)
        m = m.transpose(1, 0, 2).reshape(32, 512)
        return np.ascontiguousarray(np.concatenate([m, m], axis=0))
    return pack(vstd), pack(vfst)


def l2a_e2():
    e2 = np.zeros((2, 128), np.float32)
    e2[0, 0:64] = 1.0
    e2[1, 64:128] = 1.0
    return e2


def run_l2a(x, inp):
    nc = _get("l2a", build_l2a)
    HALO = 512
    identb = np.eye(128, dtype=BFD)
    gmr = np.ascontiguousarray(inp["L2_attn_gamma"].reshape(8, 128).T)
    btr = np.ascontiguousarray(inp["L2_attn_beta"].reshape(8, 128).T)
    mstd, mfst = l2a_masks()
    in_maps = []
    for cid in range(N_CORES):
        t0 = cid * TPC
        if t0 % S == 0:
            xe = np.concatenate([np.zeros((D, HALO), np.float32),
                                 x[:, t0:t0 + TPC]], axis=1)
            mb = mfst
        else:
            xe = x[:, t0 - HALO:t0 + TPC]
            mb = mstd
        in_maps.append({"xT": np.ascontiguousarray(xe),
                        "WqT": np.ascontiguousarray(inp["L2_attn_Wq"].T),
                        "WkT": np.ascontiguousarray(inp["L2_attn_Wk"].T),
                        "WvT": np.ascontiguousarray(inp["L2_attn_Wv"].T),
                        "WkpT": np.ascontiguousarray(inp["L2_attn_Wkp"].T),
                        "WoT": np.ascontiguousarray(inp["L2_attn_Wo"].T),
                        "gmr": gmr, "btr": btr, "mstd": mstd, "mb0": mb,
                        "e2in": l2a_e2(), "identb": identb})
    res = run_bass_kernel_spmd(nc, in_maps, core_ids=list(range(N_CORES)))
    return np.concatenate([r["yT"] for r in res.results], axis=1)



def build_dn_prep(T=TPC, c=None, io=None):
    own = c is None
    if own:
        c = Ctx("dn_prep")
    nc = c.nc
    blk = c.st if own else ExitStack()
    HALO = 512
    TE = T + HALO
    if own:
        io = NS(xT=c.din("xT", (D, TE)), Win=c.din("Win", (D, 3 * D)),
                Wab=c.din("Wab", (D, 8)), babc=c.din("babc", (8, 1)),
                Wog=c.din("Wog", (D, D)), bogc=c.din("bogc", (128, 8)),
                wconv=c.din("wconv", (128, 8, 3, 4)),
                q_out=c.dout("q_out", (D, T), BF16),
                k_out=c.dout("k_out", (D, T), BF16),
                v_out=c.dout("v_out", (D, T), BF16),
                ab_out=c.dout("ab_out", (8, T)), og_out=c.dout("og_out", (D, T)))
    xT, Win, Wab, babc, Wog, bogc, wconv = (io.xT, io.Win, io.Wab, io.babc,
                                            io.Wog, io.bogc, io.wconv)
    q_out, k_out, v_out, ab_out, og_out = (io.q_out, io.k_out, io.v_out,
                                           io.ab_out, io.og_out)
    c.cur_stack = blk

    pj_sc = nc.dram_tensor("pj_sc", [3 * D, TE], F32, kind="Internal").ap()

    win_sb = c.sb("win_sb", [128, 8, 3 * D], BF16)
    wab_sb = c.sb("wab_sb", [128, 8, 8], BF16)
    bab_sb = c.sb("bab_sb", [8, 1])
    wog_sb = c.sb("wog_sb", [128, 8, D], BF16)
    bog_sb = c.sb("bog_sb", [128, 8])
    wcv_sb = c.sb("wcv_sb", [128, 8, 3, 4])
    xb = [c.sb(f"xb{i}", [128, 8, 512]) for i in range(2)]
    h = c.sb("hb", [128, 8, 512], BF16)
    evb = c.sb("evb", [128, 512])
    onesb = c.sb("onesb", [1, 128], BF16)
    ps_a = [c.ps(f"ps_a{i}", [128, 512]) for i in range(2)]
    ps_ab = c.ps("ps_ab", [8, 512])
    ps_sums = c.ps("ps_sums", [1, 512])
    ps_bc = c.ps("ps_bc", [128, 512])

    if own:
        c.begin()
    make_ln(c, ps_sums, ps_bc)
    c.make_consts([1e-5])
    xls = [c.sem("xls0"), c.sem("xls1")]
    c.gdma(win_sb[:], Win.rearrange("(ic p) o -> p ic o", p=128))
    c.gdma(wab_sb[:], Wab.rearrange("(ic p) o -> p ic o", p=128))
    c.gdma(wog_sb[:], Wog.rearrange("(ic p) o -> p ic o", p=128))
    c.ydma(bab_sb[:], babc[:, :])
    c.ydma(bog_sb[:], bogc[:, :])
    c.ydma(wcv_sb[:], wconv[:, :, :, :])
    c.V("memset", onesb[:], 1.0)
    c.gc.wait(nc.tensor)
    c.yc.wait(nc.tensor)
    c.yc.wait(nc.vector)

    xT_r = xT.rearrange("(ic p) t -> p ic t", p=128)
    nst = TE // 512
    xl_at = {0: c.gdma(xb[0][:], xT_r[:, :, 0:512], sem=xls[0])}
    for st in range(nst):
        t0 = st * 512
        if st + 1 < nst:
            if st >= 1:
                nc.gpsimd.wait_ge(c.vc.h, c.vc.n)
            xl_at[st + 1] = c.gdma(xb[(st + 1) % 2][:], xT_r[:, :, t0 + 512:t0 + 1024],
                                   sem=xls[(st + 1) % 2])
        xv = xb[st % 2]
        for eng in (nc.tensor, nc.vector, nc.scalar):
            eng.wait_ge(xls[st % 2].h, xl_at[st])
        if st > 0:
            nc.vector.wait_ge(c.tc.h, c.tc.n)
            nc.scalar.wait_ge(c.tc.h, c.tc.n)
        ln_fm_512(c, [xv[:, i] for i in range(8)], [h[:, i] for i in range(8)])
        h_ready = c.vc.n
        nc.tensor.wait_ge(c.vc.h, h_ready)
        for oc in range(24):
            pa = ps_a[oc % 2]
            if oc >= 2:
                nc.tensor.wait_ge(c.vc.h, c.vc.n)
            for icc in range(8):
                i = c.T("matmul", pa[:, :], win_sb[:, icc, oc * 128:oc * 128 + 128],
                        h[:, icc], start=(icc == 0), stop=(icc == 7))
            pd = c.tmark(i)
            nc.vector.wait_ge(c.tc.h, pd)
            nc.vector.wait_ge(c.gc.h, c.gc.n)
            c.V("tensor_copy", evb[:, :], pa[:, :])
            wr = c.vc.n
            nc.gpsimd.wait_ge(c.vc.h, wr)
            c.gdma(pj_sc[oc * 128:oc * 128 + 128, t0:t0 + 512], evb[:])
        if t0 >= HALO:
            tg = t0 - HALO
            nc.tensor.wait_ge(c.vc.h, c.vc.n)
            nc.tensor.wait_ge(c.sc.h, c.sc.n)
            for icc in range(8):
                i = c.T("matmul", ps_ab[:, :], wab_sb[:, icc, 0:8], h[:, icc],
                        start=(icc == 0), stop=(icc == 7))
            abd = c.tmark(i)
            nc.scalar.wait_ge(c.tc.h, abd)
            nc.scalar.wait_ge(c.vc.h, c.vc.n)
            nc.scalar.wait_ge(c.gc.h, c.gc.n)
            c.S("activation", evb[0:8, :], ps_ab[:, :], AF.Sigmoid,
                bias=bab_sb[:, 0:1])
            sgd = c.sc.n
            nc.vector.wait_ge(c.sc.h, sgd)
            c.V("tensor_scalar", evb[0:4, :], evb[0:4, :], 0.98, 0.01,
                ALU.mult, ALU.add)
            abr = c.vc.n
            nc.gpsimd.wait_ge(c.vc.h, abr)
            c.gdma(ab_out[:, tg:tg + 512], evb[0:8, :])
            for oc in range(8):
                pa = ps_a[oc % 2]
                nc.tensor.wait_ge(c.vc.h, c.vc.n)
                nc.tensor.wait_ge(c.sc.h, c.sc.n)
                for icc in range(8):
                    i = c.T("matmul", pa[:, :], wog_sb[:, icc, oc * 128:oc * 128 + 128],
                            h[:, icc], start=(icc == 0), stop=(icc == 7))
                ogd = c.tmark(i)
                nc.scalar.wait_ge(c.tc.h, ogd)
                nc.scalar.wait_ge(c.gc.h, c.gc.n)
                c.S("activation", evb[:, :], pa[:, :], AF.Sigmoid,
                    bias=bog_sb[:, oc:oc + 1])
                ogr = c.sc.n
                nc.gpsimd.wait_ge(c.sc.h, ogr)
                c.gdma(og_out[oc * 128:oc * 128 + 128, tg:tg + 512], evb[:])
    barrier(c)
    with ExitStack() as ph2:
        pj = ph2.enter_context(nc.sbuf_tensor("pj", [128, 8, 515], F32))
        cv = ph2.enter_context(nc.sbuf_tensor("cv", [128, 8, 512], F32))
        cs = ph2.enter_context(nc.sbuf_tensor("cs", [128, 8, 512], F32))
        sq_ = ph2.enter_context(nc.sbuf_tensor("sq_", [128, 8, 512], F32))
        nr = ph2.enter_context(nc.sbuf_tensor("nr", [1, 4, 512], F32))
        nb = ph2.enter_context(nc.sbuf_tensor("nb", [128, 512], F32))
        st_bf = ph2.enter_context(nc.sbuf_tensor("st_bf", [128, 8, 512], BF16))
        onesf = ph2.enter_context(nc.sbuf_tensor("onesf", [128, 1], F32))
        ps_n = ph2.enter_context(nc.psum_tensor("ps_n", [1, 512], F32))
        ps_nb = ph2.enter_context(nc.psum_tensor("ps_nb", [128, 512], F32))
        pjl = c.sem("pjl")
        c.V("memset", onesf[:], 1.0)
        c.vc.wait(nc.tensor)
        pj_r = pj_sc.rearrange("(ic p) t -> p ic t", p=128)
        outs3 = {0: q_out, 1: k_out, 2: v_out}
        for st in range(T // 512):
            tg = st * 512
            for wi in range(3):
                nc.gpsimd.wait_ge(c.vc.h, c.vc.n)
                at = c.gdma(pj[:], pj_r[:, wi * 8:wi * 8 + 8,
                                        tg + HALO - 3:tg + HALO + 512], sem=pjl)
                nc.vector.wait_ge(pjl.h, at)
                nc.vector.wait_ge(c.gc.h, c.gc.n)
                nc.vector.wait_ge(c.sc.h, c.sc.n)
                nc.scalar.wait_ge(c.vc.h, c.vc.n)
                for icc in range(8):
                    c.V("tensor_scalar", cv[:, icc, :], pj[:, icc, 3:515],
                        wcv_sb[:, icc, wi, 3:4], None, ALU.mult)
                    for kk in range(3):
                        c.V("scalar_tensor_tensor", cv[:, icc, :],
                            pj[:, icc, kk:kk + 512], wcv_sb[:, icc, wi, kk:kk + 1],
                            cv[:, icc, :], ALU.mult, ALU.add)
                cv_ready = c.vc.n
                nc.scalar.wait_ge(c.vc.h, cv_ready)
                c.S("activation", cs[:, :, :], cv[:, :, :], AF.Silu)
                cs_ready = c.sc.n
                if wi < 2:
                    c.S("activation", sq_[:, :, :], cs[:, :, :], AF.Square)
                    sq_ready = c.sc.n
                    nc.tensor.wait_ge(c.sc.h, sq_ready)
                    for hh4 in range(4):
                        if hh4 > 0:
                            nc.tensor.wait_ge(c.vc.h, c.vc.n)
                        for cc2 in range(2):
                            i = c.T("matmul", ps_n[0:1, :], onesf[:, 0:1],
                                    sq_[:, hh4 * 2 + cc2, :],
                                    start=(cc2 == 0), stop=(cc2 == 1))
                        nd = c.tmark(i)
                        nc.vector.wait_ge(c.tc.h, nd)
                        c.V("tensor_copy", nr[0:1, hh4, :], ps_n[0:1, :])
                    nr_all = c.vc.n
                    nc.scalar.wait_ge(c.vc.h, nr_all)
                    c.S("activation", nr[0:1, :, :], nr[0:1, :, :], AF.Sqrt)
                    nsq = c.sc.n
                    nc.vector.wait_ge(c.sc.h, nsq)
                    c.V("tensor_scalar_max", nr[0:1, :, :], nr[0:1, :, :], 1e-6)
                    c.V("reciprocal", nr[0:1, :, :], nr[0:1, :, :])
                    rn_ready = c.vc.n
                    nc.tensor.wait_ge(c.vc.h, rn_ready)
                    for hh4 in range(4):
                        i = c.T("matmul", ps_nb[:, :], c.ln_ones_row[0:1, :],
                                nr[0:1, hh4, :], start=True, stop=True)
                        bd = c.tmark(i)
                        nc.vector.wait_ge(c.tc.h, bd)
                        c.V("tensor_mul", st_bf[:, hh4 * 2, :], cs[:, hh4 * 2, :],
                            ps_nb[:, :])
                        c.V("tensor_mul", st_bf[:, hh4 * 2 + 1, :],
                            cs[:, hh4 * 2 + 1, :], ps_nb[:, :])
                        nc.tensor.wait_ge(c.vc.h, c.vc.n)
                else:
                    nc.vector.wait_ge(c.sc.h, cs_ready)
                    c.V("tensor_copy", st_bf[:, :, :], cs[:, :, :])
                wdone = c.vc.n
                nc.gpsimd.wait_ge(c.vc.h, wdone)
                c.gdma(outs3[wi].rearrange("(ic p) t -> p ic t", p=128)
                       [:, :, tg:tg + 512], st_bf[:])
    c.gc.wait(nc.gpsimd)
    c.cur_stack = None
    if own:
        c.end()
        return c.nc
    barrier(c)
    blk.close()
    return None


def build_dn_scan(SQ=S, c=None, io=None, gathered=False, TLOC=TPC):
    own = c is None
    if own:
        c = Ctx("dn_scan")
    nc = c.nc
    blk = c.st if own else ExitStack()
    NCH = SQ // 128
    DK = 256
    if own:
        io = NS(q_in=c.din("q_in", (DK, SQ), BF16),
                k_in=c.din("k_in", (DK, SQ), BF16),
                v_in=c.din("v_in", (DK, SQ), BF16),
                al_in=c.din("al_in", (SQ,)), be_in=c.din("be_in", (SQ,)),
                TLm1=c.din("TLm1", (128, 128)), TUm1=c.din("TUm1", (128, 128)),
                TU0=c.din("TU0", (128, 128)), identf=c.din("identf", (128, 128)),
                identb2=c.din("identb2", (128, 128), BF16),
                twoI=c.din("twoI", (128, 128)), Pident=c.din("Pident", (128, 2, 256)),
                r_out=c.dout("r_out", (DK, SQ), BF16))
    q_in, k_in, v_in, al_in, be_in = io.q_in, io.k_in, io.v_in, io.al_in, io.be_in
    TLm1, TUm1, TU0, identf, identb2 = io.TLm1, io.TUm1, io.TU0, io.identf, io.identb2
    twoI, Pident, r_out = io.twoI, io.Pident, io.r_out
    c.cur_stack = blk

    qs = c.sb("qs", [128, 2, 128], BF16)
    ks = c.sb("ks", [128, 2, 128], BF16)
    vs = c.sb("vs", [128, 2, 128], BF16)
    alr = c.sb("alr", [1, SQ])
    ber = c.sb("ber", [1, SQ])
    zrow = c.sb("zrow", [1, 128])
    onr_ = c.sb("onr_", [1, 128])
    arow_ = c.sb("arow_", [1, 128])
    lgr = c.sb("lgr", [1, 128])
    lgc_ = c.sb("lgc_", [1, 128])
    nlgc = c.sb("nlgc", [1, 128])
    acr_ = c.sb("acr_", [1, 128])
    acsc = c.sb("acsc", [1, 1])
    acol = c.sb("acol", [128, 4])
    bacol = c.sb("bacol", [128, 2])
    accol = c.sb("accol", [128, 1])
    Ab = c.sb("Ab", [128, 128])
    lgb = c.sb("lgb", [128, 128])
    bb = c.sb("bb", [128, 128])
    DD = c.sb("DD", [128, 128])
    DDT = c.sb("DDT", [128, 128])
    G_sb = c.sb("G_sb", [128, 128])
    KQ_sb = c.sb("KQ_sb", [128, 128])
    t1 = c.sb("t1", [128, 128])
    tl_sb = c.sb("tl_sb", [128, 128])
    tu_sb = c.sb("tu_sb", [128, 128])
    tu0_sb = c.sb("tu0_sb", [128, 128])
    idf_sb = c.sb("idf_sb", [128, 128])
    idb_sb = c.sb("idb_sb", [128, 128], BF16)
    twoI_sb = c.sb("twoI_sb", [128, 128])
    pid_sb = c.sb("pid_sb", [128, 2, 256])
    amat_bf = c.sb("amat_bf", [128, 128], BF16)
    X_bf = c.sb("X_bf", [128, 128], BF16)
    XT_bf = c.sb("XT_bf", [128, 128], BF16)
    Z_bf = c.sb("Z_bf", [128, 128], BF16)
    rhs_bf = c.sb("rhs_bf", [128, 512], BF16)
    ktm = c.sb("ktm", [128, 256], BF16)
    k2tm = c.sb("k2tm", [128, 256], BF16)
    wv_sb = c.sb("wv_sb", [128, 256], BF16)
    wk_sb = c.sb("wk_sb", [128, 256], BF16)
    tp_bf = c.sb("tp_bf", [128, 128], BF16)
    qeff = c.sb("qeff", [128, 2, 128], BF16)
    P_bf = c.sb("P_bf", [128, 2, 256], BF16)
    aci = c.sb("aci", [128, 256])
    ST32 = c.sb("ST32", [128, 2, 256])
    ST_bf = c.sb("ST_bf", [128, 2, 256], BF16)
    r32 = c.sb("r32", [128, 2, 128])
    rsq = c.sb("rsq", [128, 2, 128])
    rrow = c.sb("rrow", [1, 128])
    rn_bf = c.sb("rn_bf", [128, 2, 128], BF16)

    ones_r = c.sb("ones_r", [1, 128])
    ones_c = c.sb("ones_c", [128, 1])
    psN1 = c.ps("psN1", [128, 128])
    psN2 = c.ps("psN2", [128, 128])
    psW = c.ps("psW", [128, 512])
    psP = c.ps("psP", [128, 256])
    psR = c.ps("psR", [128, 128])
    psB = c.ps("psB", [128, 128])
    psT = c.ps("psT", [128, 128], BF16)

    if own:
        c.begin()
    c.make_consts([1e-6, 1.0 / 256])
    qkl = c.sem("qkl")
    if not gathered:
        c.ydma(alr[:], al_in.unsqueeze(0))
        c.ydma(ber[:], be_in.unsqueeze(0))
    c.ydma(tl_sb[:], TLm1[:, :])
    c.ydma(tu_sb[:], TUm1[:, :])
    c.ydma(tu0_sb[:], TU0[:, :])
    c.ydma(idf_sb[:], identf[:, :])
    c.ydma(idb_sb[:], identb2[:, :])
    c.ydma(twoI_sb[:], twoI[:, :])
    c.ydma(pid_sb[:], Pident[:, :, :])
    c.V("memset", zrow[:], 0.0)
    c.V("memset", onr_[:], 1.0)
    c.V("memset", ones_r[:], 1.0)
    c.V("memset", ones_c[:], 1.0)
    c.V("memset", ST32[:], 0.0)
    c.V("tensor_copy", ST_bf[:], ST32[:])
    c.yc.wait(nc.vector)
    c.yc.wait(nc.tensor)
    c.yc.wait(nc.scalar)

    if not gathered:
        q_r = q_in.rearrange("(dc p) t -> p dc t", p=128)
        k_r = k_in.rearrange("(dc p) t -> p dc t", p=128)
        v_r = v_in.rearrange("(dc p) t -> p dc t", p=128)
    r_w = r_out.rearrange("(dc p) t -> p dc t", p=128)

    for ch in range(NCH):
        t0 = ch * 128
        tsl = slice(t0, t0 + 128)
        nc.gpsimd.wait_ge(c.tc.h, c.tc.n)
        if not gathered:
            at = c.gdma(qs[:], q_r[:, :, tsl], sem=qkl)
            at = c.gdma(ks[:], k_r[:, :, tsl], sem=qkl)
            at = c.gdma(vs[:], v_r[:, :, tsl], sem=qkl)
        else:
            nc.sync.wait_ge(c.tc.h, c.tc.n)
            nc.sync.wait_ge(c.vc.h, c.vc.n)
            nc.sync.wait_ge(c.sc.h, c.sc.n)
            CPR = TLOC // 128
            pid = c.pid()
            for src, dst in ((q_in, qs), (k_in, ks), (v_in, vs)):
                with nc.sync.register(_rn("qko")) as rg:
                    nc.sync.reg_div(rg, pid, 4)
                    nc.sync.reg_mul(rg, rg, 4)
                    nc.sync.reg_add(rg, rg, ch // CPR)
                    nc.sync.reg_mul(rg, rg, D * TLOC)
                    with nc.sync.register(_rn("qko2")) as rg2:
                        nc.sync.reg_mod(rg2, pid, 4)
                        nc.sync.reg_mul(rg2, rg2, 256 * TLOC)
                        nc.sync.reg_add(rg, rg, rg2)
                    nc.sync.reg_add(rg, rg, (ch % CPR) * 128)
                    dyn = bass.AP(src.tensor, rg,
                                  [[TLOC, 128], [128 * TLOC, 2], [1, 128]])
                    at = c.ydma(dst[:], dyn, sem=qkl)
            for rowoff, dst in ((0, alr), (4, ber)):
                with nc.sync.register(_rn("abo")) as rg:
                    nc.sync.reg_div(rg, pid, 4)
                    nc.sync.reg_mul(rg, rg, 4)
                    nc.sync.reg_add(rg, rg, ch // CPR)
                    nc.sync.reg_mul(rg, rg, 8 * TLOC)
                    with nc.sync.register(_rn("abo2")) as rg2:
                        nc.sync.reg_mod(rg2, pid, 4)
                        nc.sync.reg_add(rg2, rg2, rowoff)
                        nc.sync.reg_mul(rg2, rg2, TLOC)
                        nc.sync.reg_add(rg, rg, rg2)
                    nc.sync.reg_add(rg, rg, (ch % CPR) * 128)
                    dyn = bass.AP(al_in.tensor, rg, [[TLOC, 1], [1, 128]])
                    at = c.ydma(dst[0:1, 0:128], dyn, sem=qkl)
        nc.vector.wait_ge(qkl.h, at)
        nc.tensor.wait_ge(qkl.h, at)
        nc.scalar.wait_ge(qkl.h, at)
        nc.vector.wait_ge(c.tc.h, c.tc.n)
        nc.scalar.wait_ge(c.tc.h, c.tc.n)
        nc.scalar.wait_ge(c.vc.h, c.vc.n)
        asl = slice(0, 128) if gathered else tsl
        c.S("activation", lgr[:, :], alr[0:1, asl], AF.Ln)
        c.sc.wait(nc.vector)
        c.V("tensor_tensor_scan", lgc_[:, :], onr_[:, :], lgr[:, :], 0.0,
            ALU.mult, ALU.add)
        c.V("tensor_scalar", nlgc[:, :], lgc_[:, :], -1.0, None, ALU.mult)
        lg_d = c.vc.n
        nc.scalar.wait_ge(c.vc.h, lg_d)
        c.S("activation", arow_[:, :], lgc_[:, :], AF.Exp)
        c.S("activation", acr_[:, :], nlgc[:, :], AF.Exp,
            bias=lgc_[0:1, 127:128])
        c.S("activation", acsc[:, :], lgc_[0:1, 127:128], AF.Exp)
        ar_d = c.sc.n
        nc.tensor.wait_ge(c.sc.h, ar_d)
        nc.tensor.wait_ge(c.vc.h, c.vc.n)
        ttt = [(arow_[0:1, :], 0), (lgc_[0:1, :], 1), (nlgc[0:1, :], 2)]
        for row_ap, j in ttt:
            i = c.T("transpose", psB[:, 0:1], row_ap, idf_sb[0:1, 0:1])
            m_ = c.tmark(i)
            nc.vector.wait_ge(c.tc.h, m_)
            c.V("tensor_copy", acol[:, j:j + 1], psB[:, 0:1])
            nc.tensor.wait_ge(c.vc.h, c.vc.n)
        i = c.T("transpose", psB[:, 0:1], ber[0:1, asl], idf_sb[0:1, 0:1])
        m3 = c.tmark(i)
        nc.vector.wait_ge(c.tc.h, m3)
        c.V("tensor_copy", acol[:, 3:4], psB[:, 0:1])
        c.V("tensor_mul", bacol[:, 0:1], acol[:, 3:4], acol[:, 0:1])
        rows_d2 = c.vc.n
        nc.tensor.wait_ge(c.vc.h, rows_d2)
        for (row_ap, dst) in ((arow_[0:1, :], Ab), (lgc_[0:1, :], lgb),
                              (ber[0:1, asl], bb)):
            i = c.T("matmul", psB[:, :], ones_r[0:1, :], row_ap, start=True, stop=True)
            bd = c.tmark(i)
            nc.vector.wait_ge(c.tc.h, bd)
            c.V("tensor_copy", dst[:, :], psB[:, :])
            nc.tensor.wait_ge(c.vc.h, c.vc.n)
        i = c.T("matmul", psB[:, 0:1], ones_r[0:1, :], acsc[0:1, 0:1],
                start=True, stop=True)
        acd = c.tmark(i)
        nc.vector.wait_ge(c.tc.h, acd)
        c.V("tensor_copy", accol[:, 0:1], psB[:, 0:1])
        nc.tensor.wait_ge(c.vc.h, c.vc.n)
        i = c.T("transpose", psB[:, 0:1], acr_[:, :], idf_sb[0:1, 0:1])
        acrd = c.tmark(i)
        nc.vector.wait_ge(c.tc.h, acrd)
        c.V("tensor_copy", bacol[:, 1:2], psB[:, 0:1])
        c.V("tensor_scalar", t1[:, :], lgb[:, :], acol[:, 2:3], None, ALU.add)
        c.V("scalar_tensor_tensor", t1[:, :], tl_sb[:, :], -1e30, t1[:, :],
            ALU.mult, ALU.add)
        c.vc.wait(nc.scalar)
        c.S("activation", DDT[:, :], t1[:, :], AF.Exp)
        c.sc.wait(nc.vector)
        c.V("tensor_scalar", t1[:, :], lgb[:, :], acol[:, 1:2], None, ALU.subtract)
        c.V("tensor_scalar", t1[:, :], t1[:, :], -1.0, None, ALU.mult)
        c.V("scalar_tensor_tensor", t1[:, :], tu_sb[:, :], -1e30, t1[:, :],
            ALU.mult, ALU.add)
        c.vc.wait(nc.scalar)
        c.S("activation", DD[:, :], t1[:, :], AF.Exp)
        dd_sc = c.sc.n
        nc.vector.wait_ge(c.sc.h, dd_sc)
        dd_done = c.vc.n
        for dc in range(2):
            c.T("matmul", psN1[:, :], ks[:, dc, :], ks[:, dc, :],
                start=(dc == 0), stop=(dc == 1))
        for dc in range(2):
            i = c.T("matmul", psN2[:, :], ks[:, dc, :], qs[:, dc, :],
                    start=(dc == 0), stop=(dc == 1))
        gr_done = c.tmark(i)
        nc.vector.wait_ge(c.tc.h, gr_done)
        c.V("tensor_copy", G_sb[:, :], psN1[:, :])
        c.V("tensor_copy", KQ_sb[:, :], psN2[:, :])
        c.V("tensor_mul", t1[:, :], G_sb[:, :], DD[:, :])
        c.V("tensor_mul", t1[:, :], t1[:, :], tl_sb[:, :])
        c.V("tensor_scalar", X_bf[:, :], t1[:, :], acol[:, 3:4], None, ALU.mult)
        c.V("tensor_mul", t1[:, :], G_sb[:, :], DDT[:, :])
        c.V("tensor_mul", t1[:, :], t1[:, :], tu_sb[:, :])
        c.V("tensor_mul", t1[:, :], t1[:, :], bb[:, :])
        c.V("tensor_add", amat_bf[:, :], t1[:, :], idb_sb[:, :])
        c.V("tensor_mul", t1[:, :], KQ_sb[:, :], DDT[:, :])
        c.V("tensor_mul", tp_bf[:, :], t1[:, :], tu0_sb[:, :])
        c.V("tensor_copy", X_bf[:, :], idb_sb[:, :])
        c.V("tensor_copy", XT_bf[:, :], idb_sb[:, :])
        nt0 = c.vc.n
        nc.tensor.wait_ge(c.vc.h, nt0)
        for it in range(7):
            i = c.T("matmul", psN1[:, :], amat_bf[:, :], X_bf[:, :],
                    start=True, stop=True)
            yd = c.tmark(i)
            nc.vector.wait_ge(c.tc.h, yd)
            c.V("scalar_tensor_tensor", Z_bf[:, :], psN1[:, :], -1.0,
                twoI_sb[:, :], ALU.mult, ALU.add)
            zd = c.vc.n
            nc.tensor.wait_ge(c.vc.h, zd)
            c.T("matmul", psN1[:, :], XT_bf[:, :], Z_bf[:, :], start=True, stop=True)
            i = c.T("matmul", psN2[:, :], Z_bf[:, :], XT_bf[:, :], start=True, stop=True)
            xd = c.tmark(i)
            nc.vector.wait_ge(c.tc.h, xd)
            c.V("tensor_copy", X_bf[:, :], psN1[:, :])
            c.V("tensor_copy", XT_bf[:, :], psN2[:, :])
            nc.tensor.wait_ge(c.vc.h, c.vc.n)
        for dc in range(2):
            i = c.T("transpose", psT[:, :], vs[:, dc, :], idb_sb[:, :])
            td = c.tmark(i)
            nc.vector.wait_ge(c.tc.h, td)
            c.V("tensor_copy", rhs_bf[:, dc * 128:dc * 128 + 128], psT[:, :])
            nc.tensor.wait_ge(c.vc.h, c.vc.n)
            i = c.T("transpose", psT[:, :], ks[:, dc, :], idb_sb[:, :])
            td = c.tmark(i)
            nc.vector.wait_ge(c.tc.h, td)
            c.V("tensor_copy", ktm[:, dc * 128:dc * 128 + 128], psT[:, :])
            nc.tensor.wait_ge(c.vc.h, c.vc.n)
        c.V("tensor_scalar", rhs_bf[:, 0:256], rhs_bf[:, 0:256], acol[:, 3:4],
            None, ALU.mult)
        c.V("tensor_scalar", rhs_bf[:, 256:512], ktm[:, :], bacol[:, 0:1],
            None, ALU.mult)
        c.V("tensor_scalar", k2tm[:, :], ktm[:, :], bacol[:, 1:2], None, ALU.mult)
        rhs_d = c.vc.n
        nc.tensor.wait_ge(c.vc.h, rhs_d)
        i = c.T("matmul", psW[:, :], XT_bf[:, :], rhs_bf[:, :], start=True, stop=True)
        wd = c.tmark(i)
        nc.vector.wait_ge(c.tc.h, wd)
        c.V("tensor_copy", wv_sb[:, :], psW[:, 0:256])
        c.V("tensor_copy", wk_sb[:, :], psW[:, 256:512])
        w_d = c.vc.n
        nc.tensor.wait_ge(c.vc.h, w_d)
        c.V("tensor_scalar", aci[:, :], pid_sb[:, 0, :], accol[:, 0:1], None, ALU.mult)
        for kc in range(2):
            if kc == 1:
                c.V("tensor_scalar", aci[:, :], pid_sb[:, 1, :], accol[:, 0:1],
                    None, ALU.mult)
            i = c.T("matmul", psP[:, :], wk_sb[:, kc * 128:kc * 128 + 128],
                    k2tm[:, :], start=True, stop=True)
            pd = c.tmark(i)
            nc.vector.wait_ge(c.tc.h, pd)
            nc.vector.wait_ge(c.vc.h, c.vc.n)
            c.V("scalar_tensor_tensor", P_bf[:, kc, :], psP[:, :], -1.0,
                aci[:, :], ALU.mult, ALU.add)
            nc.tensor.wait_ge(c.vc.h, c.vc.n)
        for kc in range(2):
            i = c.T("matmul", psP[:, 0:128], wk_sb[:, kc * 128:kc * 128 + 128],
                    tp_bf[:, :], start=True, stop=True)
            qd_ = c.tmark(i)
            nc.vector.wait_ge(c.tc.h, qd_)
            c.V("tensor_mul", t1[:, :], qs[:, kc, :], Ab[:, :])
            c.V("tensor_sub", qeff[:, kc, :], t1[:, :], psP[:, 0:128])
            nc.tensor.wait_ge(c.vc.h, c.vc.n)
        for vh in range(2):
            for kc in range(2):
                c.T("matmul", psR[:, :], ST_bf[:, kc, vh * 128:vh * 128 + 128],
                    qeff[:, kc, :], start=(kc == 0), stop=False)
            i = c.T("matmul", psR[:, :], wv_sb[:, vh * 128:vh * 128 + 128],
                    tp_bf[:, :], start=False, stop=True)
            rd = c.tmark(i)
            nc.vector.wait_ge(c.tc.h, rd)
            c.V("tensor_copy", r32[:, vh, :], psR[:, :])
            nc.tensor.wait_ge(c.vc.h, c.vc.n)
        r_d = c.vc.n
        nc.scalar.wait_ge(c.vc.h, r_d)
        c.S("activation", rsq[:, :, :], r32[:, :, :], AF.Square)
        sq_d = c.sc.n
        nc.tensor.wait_ge(c.sc.h, sq_d)
        for vh in range(2):
            i = c.T("matmul", psB[0:1, :], ones_c[:, 0:1], rsq[:, vh, :],
                    start=(vh == 0), stop=(vh == 1))
        sd = c.tmark(i)
        nc.vector.wait_ge(c.tc.h, sd)
        c.V("tensor_copy", rrow[:, :], psB[0:1, :])
        c.vc.wait(nc.scalar)
        c.S("activation", rrow[:, :], rrow[:, :], AF.Sqrt, scale=1.0 / 256,
            bias=1e-6)
        c.sc.wait(nc.vector)
        c.V("reciprocal", rrow[:, :], rrow[:, :])
        rr_d = c.vc.n
        nc.tensor.wait_ge(c.vc.h, rr_d)
        i = c.T("matmul", psB[:, :], ones_r[0:1, :], rrow[0:1, :],
                start=True, stop=True)
        rbd = c.tmark(i)
        nc.vector.wait_ge(c.tc.h, rbd)
        nc.vector.wait_ge(c.gc.h, c.gc.n)
        c.V("tensor_mul", rn_bf[:, 0, :], r32[:, 0, :], psB[:, :])
        c.V("tensor_mul", rn_bf[:, 1, :], r32[:, 1, :], psB[:, :])
        rn_d = c.vc.n
        nc.gpsimd.wait_ge(c.vc.h, rn_d)
        c.gdma(r_w[:, :, tsl], rn_bf[:])
        for kh in range(2):
            c.T("matmul", psP[:, :], k2tm[:, kh * 128:kh * 128 + 128], wv_sb[:, :],
                start=True, stop=False)
            for kc in range(2):
                i = c.T("matmul", psP[:, :], P_bf[:, kc, kh * 128:kh * 128 + 128],
                        ST_bf[:, kc, :], start=False, stop=(kc == 1))
            su = c.tmark(i)
            nc.vector.wait_ge(c.tc.h, su)
            c.V("tensor_copy", ST32[:, kh, :], psP[:, :])
            nc.tensor.wait_ge(c.vc.h, c.vc.n)
        st_d = c.vc.n
        c.V("tensor_copy", ST_bf[:, :, :], ST32[:, :, :])
    c.gc.wait(nc.gpsimd)
    c.cur_stack = None
    if own:
        c.end()
        return c.nc
    barrier(c)
    blk.close()
    return None


def build_dn_out(T=TPC, c=None, io=None, gathered=False):
    own = c is None
    if own:
        c = Ctx("dn_out")
    nc = c.nc
    blk = c.st if own else ExitStack()
    if own:
        io = NS(xT=c.din("xT", (D, T)), r_in=c.din("r_in", (D, T), BF16),
                og_in=c.din("og_in", (D, T)), Wout=c.din("Wout", (D, D)),
                boutc=c.din("boutc", (128, 8)), rlc=c.din("rlc", (128, 8)),
                yT=c.dout("yT", (D, T)))
    xT, r_in, og_in, Wout, boutc, rlc, yT = (io.xT, io.r_in, io.og_in, io.Wout,
                                             io.boutc, io.rlc, io.yT)
    c.cur_stack = blk

    rb = c.sb("rb", [128, 8, T], BF16)
    og = [c.sb(f"og{i}", [128, 512]) for i in range(2)]
    xc = [c.sb(f"xc{i}", [128, 512]) for i in range(2)]
    wo_sb = c.sb("wo_sb", [128, 8, D], BF16)
    bo_sb = c.sb("bo_sb", [128, 8])
    rl_sb = c.sb("rl_sb", [128, 8])
    rmp = c.sb("rmp", [128, 8])
    etmp = c.sb("etmp", [128, 512])
    ps_y = [c.ps(f"ps_y{i}", [128, 512]) for i in range(2)]

    if own:
        c.begin()
    xls = [c.sem("xls0"), c.sem("xls1")]
    ogl = [c.sem("ogl0"), c.sem("ogl1")]
    sts = [c.sem("sts0"), c.sem("sts1")]
    c.gdma(wo_sb[:], Wout.rearrange("(ic p) o -> p ic o", p=128))
    if not gathered:
        for i in range(8):
            c.ydma(rb[:, i, :], r_in[i * 128:(i + 1) * 128, :])
    else:
        spid = c.pide("scalar")
        rbl = c.sem("rbl")
        for h in range(4):
            with nc.scalar.register(_rn("rro")) as rg:
                nc.scalar.reg_div(rg, spid, 4)
                nc.scalar.reg_mul(rg, rg, 4)
                nc.scalar.reg_add(rg, rg, h)
                nc.scalar.reg_mul(rg, rg, 256 * S)
                with nc.scalar.register(_rn("rro2")) as rg2:
                    nc.scalar.reg_mod(rg2, spid, 4)
                    nc.scalar.reg_mul(rg2, rg2, T)
                    nc.scalar.reg_add(rg, rg, rg2)
                dyn = bass.AP(r_in.tensor, rg, [[S, 128], [128 * S, 2], [1, T]])
                i_ = nc.scalar.dma_start(out=rb[:, 2 * h:2 * h + 2, :], in_=dyn)
                rbl.inc(i_, 16)
        for eng in (nc.tensor, nc.vector, nc.scalar, nc.gpsimd):
            rbl.wait(eng)
    c.ydma(bo_sb[:], boutc[:, :])
    c.ydma(rl_sb[:], rlc[:, :])
    c.yc.wait(nc.scalar)
    c.S("activation", rmp[:, :], rl_sb[:, :], AF.Sigmoid)
    rmp_d = c.sc.n
    nc.vector.wait_ge(c.sc.h, rmp_d)
    c.yc.wait(nc.tensor)
    c.gc.wait(nc.tensor)

    NQB = T // 512
    done_ev, st_at, xl_at, og_at = {}, {}, {}, {}

    def xload(k_):
        oc, qb = divmod(k_, NQB)
        if k_ >= 2:
            nc.sync.wait_ge(sts[k_ % 2].h, st_at.get(k_ - 2, 0))
        xl_at[k_] = c.ydma(xc[k_ % 2][:],
                           xT[oc * 128:(oc + 1) * 128, qb * 512:qb * 512 + 512],
                           sem=xls[k_ % 2])
        og_at[k_] = c.ydma(og[k_ % 2][:],
                           og_in[oc * 128:(oc + 1) * 128, qb * 512:qb * 512 + 512],
                           sem=ogl[k_ % 2])

    xload(0)
    xload(1)
    for k_ in range(8 * NQB):
        oc, qb = divmod(k_, NQB)
        py = ps_y[k_ % 2]
        if k_ >= 2:
            nc.tensor.wait_ge(c.vc.h, done_ev[k_ - 2])
        for icc in range(8):
            i = c.T("matmul", py[:, :], wo_sb[:, icc, oc * 128:oc * 128 + 128],
                    rb[:, icc, qb * 512:qb * 512 + 512],
                    start=(icc == 0), stop=(icc == 7))
        yd = c.tmark(i)
        nc.vector.wait_ge(c.tc.h, yd)
        nc.vector.wait_ge(xls[k_ % 2].h, xl_at[k_])
        nc.vector.wait_ge(ogl[k_ % 2].h, og_at[k_])
        c.V("tensor_scalar", etmp[:, :], py[:, :], bo_sb[:, oc:oc + 1], None, ALU.add)
        c.V("tensor_scalar", etmp[:, :], etmp[:, :], rmp[:, oc:oc + 1], None, ALU.mult)
        c.V("tensor_mul", etmp[:, :], etmp[:, :], og[k_ % 2][:, :])
        c.V("tensor_add", xc[k_ % 2][:, :], xc[k_ % 2][:, :], etmp[:, :])
        done_ev[k_] = c.vc.n
        nc.gpsimd.wait_ge(c.vc.h, done_ev[k_])
        st_at[k_] = c.gdma(yT[oc * 128:(oc + 1) * 128, qb * 512:qb * 512 + 512],
                           xc[k_ % 2][:], sem=sts[k_ % 2])
        if k_ + 2 < 8 * NQB:
            xload(k_ + 2)
    c.gc.wait(nc.gpsimd)
    c.cur_stack = None
    if own:
        c.end()
        return c.nc
    barrier(c)
    blk.close()
    return None


def run_deltanet(x, inp):
    nc_a = _get("dn_prep", build_dn_prep)
    nc_b = _get("dn_scan", lambda: build_dn_scan(S))
    nc_c = _get("dn_out", build_dn_out)
    HALO = 512
    Wc = inp["L0_attn_W_ctrl"]
    bc = inp["L0_attn_b_ctrl"]
    wcv = np.stack([inp["L0_attn_w_q"][:, 0, :], inp["L0_attn_w_k"][:, 0, :],
                    inp["L0_attn_w_v"][:, 0, :]], axis=1)
    wconv = np.ascontiguousarray(wcv.reshape(8, 128, 3, 4).transpose(1, 0, 2, 3))
    in_maps = []
    for cid in range(N_CORES):
        t0 = cid * TPC
        if t0 % S == 0:
            xe = np.concatenate([np.zeros((D, HALO), np.float32),
                                 x[:, t0:t0 + TPC]], axis=1)
        else:
            xe = x[:, t0 - HALO:t0 + TPC]
        in_maps.append({"xT": np.ascontiguousarray(xe),
                        "Win": inp["L0_attn_W_in"],
                        "Wab": np.ascontiguousarray(Wc[:, 0:8]),
                        "babc": np.ascontiguousarray(bc[0:8].reshape(8, 1)),
                        "Wog": np.ascontiguousarray(Wc[:, 8:]),
                        "bogc": np.ascontiguousarray(bc[8:].reshape(8, 128).T),
                        "wconv": wconv})
    res = run_bass_kernel_spmd(nc_a, in_maps, core_ids=list(range(N_CORES)))
    r = res.results
    q_all = np.concatenate([rr["q_out"] for rr in r], axis=1)
    k_all = np.concatenate([rr["k_out"] for rr in r], axis=1)
    v_all = np.concatenate([rr["v_out"] for rr in r], axis=1)
    ab_all = np.concatenate([rr["ab_out"] for rr in r], axis=1)
    og_all = np.concatenate([rr["og_out"] for rr in r], axis=1)
    tl = np.tril(np.ones((128, 128), np.float32), -1)
    tu = np.triu(np.ones((128, 128), np.float32), 1)
    tu0 = np.triu(np.ones((128, 128), np.float32), 0)
    idf = np.eye(128, dtype=np.float32)
    pid = np.zeros((128, 2, 256), np.float32)
    for rr_ in range(128):
        pid[rr_, 0, rr_] = 1.0
        pid[rr_, 1, 128 + rr_] = 1.0
    in_maps = []
    for cid in range(N_CORES):
        b, hh = divmod(cid, 4)
        rs = slice(hh * 256, hh * 256 + 256)
        cs_ = slice(b * S, b * S + S)
        in_maps.append({"q_in": np.ascontiguousarray(q_all[rs, cs_]),
                        "k_in": np.ascontiguousarray(k_all[rs, cs_]),
                        "v_in": np.ascontiguousarray(v_all[rs, cs_]),
                        "al_in": np.ascontiguousarray(ab_all[hh, cs_]),
                        "be_in": np.ascontiguousarray(ab_all[4 + hh, cs_]),
                        "TLm1": tl, "TUm1": tu, "TU0": tu0, "identf": idf,
                        "identb2": np.eye(128, dtype=BFD), "twoI": 2 * idf,
                        "Pident": pid})
    res = run_bass_kernel_spmd(nc_b, in_maps, core_ids=list(range(N_CORES)))
    r2 = res.results
    R = np.empty((D, NTOK), dtype=BFD)
    for cid in range(N_CORES):
        b, hh = divmod(cid, 4)
        R[hh * 256:hh * 256 + 256, b * S:(b + 1) * S] = r2[cid]["r_out"]
    in_maps = []
    for cid in range(N_CORES):
        t0 = cid * TPC
        in_maps.append({"xT": np.ascontiguousarray(x[:, t0:t0 + TPC]),
                        "r_in": np.ascontiguousarray(R[:, t0:t0 + TPC]),
                        "og_in": np.ascontiguousarray(og_all[:, t0:t0 + TPC]),
                        "Wout": inp["L0_attn_W_out"],
                        "boutc": np.ascontiguousarray(inp["L0_attn_b_out"].reshape(8, 128).T),
                        "rlc": np.ascontiguousarray(inp["L0_attn_ramp_logit"].reshape(8, 128).T)})
    res = run_bass_kernel_spmd(nc_c, in_maps, core_ids=list(range(N_CORES)))
    return np.concatenate([rr["yT"] for rr in res.results], axis=1)



def _allgather(c, contrib, gath, tag):
    nc = c.nc
    barrier(c)
    ccs = c.sem("ccs")
    n = 1
    for d_ in contrib.tensor.shape:
        n *= int(d_)
    cin = bass.AP(contrib.tensor, 0, [[1, n]])
    cout = bass.AP(gath.tensor, 0, [[1, 8 * n]])
    i = nc.gpsimd.collective_compute(
        "AllGather", ALU.bypass, replica_groups=[list(range(8))],
        ins=[cin.opt()], outs=[cout.opt()])
    ccs.inc(i, 1)
    for eng in (nc.tensor, nc.vector, nc.scalar, nc.gpsimd, nc.sync):
        ccs.wait(eng)


def _halo_asm(c, src, dst, halo_g, hz, tag):
    nc = c.nc
    T = src.shape[1]
    hcs = c.sem("hcs")
    c.gdma(dst[:, 512:512 + T], src[:, :])
    with ExitStack() as hb:
        tile = c.sb(f"hb_{tag}", [128, 8, 512], stack=hb)
        hzc = c.sb(f"hz_{tag}", [128, 1], stack=hb)
        c.ydma(hzc[:], hz[:, :])
        pid = c.pid()
        with nc.sync.register(_rn("hro")) as rg:
            nc.sync.reg_add(rg, pid, 7)
            nc.sync.reg_mod(rg, rg, 8)
            nc.sync.reg_mul(rg, rg, D * 512)
            dyn = bass.AP(halo_g.tensor, rg, [[512, 128], [128 * 512, 8], [1, 512]])
            at2 = c.ydma(tile[:], dyn, sem=hcs)
        nc.vector.wait_ge(hcs.h, at2)
        c.vc.wait(nc.vector)
        c.yc.wait(nc.vector)
        c.V("tensor_scalar", tile[:, :, :], tile[:, :, :], hzc[:, 0:1], None,
            ALU.mult)
        hd = c.vc.n
        nc.gpsimd.wait_ge(c.vc.h, hd)
        c.gdma(dst.rearrange("(ic p) t -> p ic t", p=128)[:, :, 0:512], tile[:])
        barrier(c)


def build_fused(T=TPC):
    c = Ctx("fused")
    nc = c.nc
    din, dout = c.din, c.dout
    HALO = 512

    def dsc(name, shape, dtype=F32):
        return nc.dram_tensor(name, list(shape), dtype, kind="Internal").ap()

    x0 = din("x0", (D, T))
    yout = dout("yout", (D, T))
    identf = din("identf", (128, 128))
    identb = din("identb", (128, 128), BF16)
    g_wgT = din("g_wgT", (D, 3072), I32)
    g_wuT = din("g_wuT", (D, 3072), I32)
    g_wdT = din("g_wdT", (3072, D), I32)
    dn_Win = din("dn_Win", (D, 3 * D))
    dn_Wab = din("dn_Wab", (D, 8))
    dn_babc = din("dn_babc", (8, 1))
    dn_Wog = din("dn_Wog", (D, D))
    dn_bogc = din("dn_bogc", (128, 8))
    dn_wconv = din("dn_wconv", (128, 8, 3, 4))
    dn_TLm1 = din("dn_TLm1", (128, 128))
    dn_TUm1 = din("dn_TUm1", (128, 128))
    dn_TU0 = din("dn_TU0", (128, 128))
    dn_twoI = din("dn_twoI", (128, 128))
    dn_Pident = din("dn_Pident", (128, 2, 256))
    dn_Wout = din("dn_Wout", (D, D))
    dn_boutc = din("dn_boutc", (128, 8))
    dn_rlc = din("dn_rlc", (128, 8))
    hz_in = din("hz_in", (128, 1))
    fp4_win = din("fp4_win", (N_UNITS, 128, 256))
    l1_WqT = din("l1_WqT", (D, D))
    l1_bqc = din("l1_bqc", (128, 8))
    l1_WkT = din("l1_WkT", (D, D))
    l1_bkc = din("l1_bkc", (128, 8))
    l1_WvT = din("l1_WvT", (D, D))
    l1_bvr = din("l1_bvr", (D,))
    l1_W0 = din("l1_W0", (128, 16, 64))
    l1_WoT = din("l1_WoT", (D, D))
    l1_boc = din("l1_boc", (128, 8))
    l1_bmask = din("l1_bmask", (1, 8))
    l1_bmaskc = din("l1_bmaskc", (128, 8))
    l2_nwr = din("l2_nwr", (128, 8))
    l2_rwT = din("l2_rwT", (D, 8))
    l2_sw1 = din("l2_sw1", (D, 4096))
    l2_sb1r = din("l2_sb1r", (128, 32))
    l2_sw2 = din("l2_sw2", (4096, D))
    l2_sb2r = din("l2_sb2r", (128, 8))
    l2_ew1 = din("l2_ew1", (8, D, 4096))
    l2_eb1r = din("l2_eb1r", (128, 8, 32))
    l2_ew2 = din("l2_ew2", (8, 4096, D))
    l2_eb2r = din("l2_eb2r", (128, 8, 8))
    l2_selmat = din("l2_selmat", (8, 8 * 128))
    a2_WqT = din("a2_WqT", (D, D))
    a2_WkT = din("a2_WkT", (D, D))
    a2_WvT = din("a2_WvT", (D, D))
    a2_WkpT = din("a2_WkpT", (D, D))
    a2_WoT = din("a2_WoT", (D, D))
    a2_gmr = din("a2_gmr", (128, 8))
    a2_btr = din("a2_btr", (128, 8))
    a2_mstd = din("a2_mstd", (64, 512))
    a2_mb0 = din("a2_mb0", (64, 512))
    a2_e2 = din("a2_e2", (2, 128))
    f3_Wr = din("f3_Wr", (D, 8))
    f3_brv = din("f3_brv", (8,))
    f3_wrow = din("f3_wrow", (8,))
    f3_W1 = din("f3_W1", (8, D, D))
    f3_b1r = din("f3_b1r", (128, 8, 8))
    f3_W2 = din("f3_W2", (8, D, D))
    f3_b2r = din("f3_b2r", (128, 8, 8))
    f3_selmat = din("f3_selmat", (8, 8 * 128))
    a3_Wq = din("a3_Wq", (D, D))
    a3_bqr = din("a3_bqr", (128, 8))
    a3_Wdn = din("a3_Wdn", (D, 256))
    a3_bdnr = din("a3_bdnr", (128, 2))
    a3_Wupk = din("a3_Wupk", (256, D))
    a3_bupkr = din("a3_bupkr", (128, 8))
    a3_Wupv = din("a3_Wupv", (256, D))
    a3_bupv = din("a3_bupv", (D,))
    a3_cosF = din("a3_cosF", (128, T))
    a3_sinPM = din("a3_sinPM", (128, T))
    a3_Pswap = din("a3_Pswap", (128, 128))
    a3_Wout = din("a3_Wout", (D, D))
    a3_boutr = din("a3_boutr", (128, 8))

    xA = dsc("xA", (D, T))
    xB = dsc("xB", (D, T))
    xhal = dsc("xhal", (D, T + HALO))
    halc = dsc("halc", (D, HALO))
    halg = dsc("halg", (8, D, HALO))
    dnq = dsc("dnq", (D, T), BF16)
    dnk = dsc("dnk", (D, T), BF16)
    dnv = dsc("dnv", (D, T), BF16)
    dnab = dsc("dnab", (8, T))
    dnog = dsc("dnog", (D, T))
    dnqg = dsc("dnqg", (8, D, T), BF16)
    dnkg = dsc("dnkg", (8, D, T), BF16)
    dnvg = dsc("dnvg", (8, D, T), BF16)
    dnabg = dsc("dnabg", (8, 8, T))
    dnr = dsc("dnr", (256, S), BF16)
    qL = dsc("qL", (256, S), BF16)
    kL = dsc("kL", (256, S), BF16)
    vL = dsc("vL", (256, S), BF16)
    alL = dsc("alL", (S,))
    beL = dsc("beL", (S,))
    dnrg = dsc("dnrg", (8, 256, S), BF16)
    fp4c = dsc("fp4c", (N_UNITS, 2, 128, 128), BF16)
    fp4g = dsc("fp4g", (8 * N_UNITS, 2, 128, 128), BF16)
    wgqT = dsc("wgqT_s", (D, 2816), BF16)
    wuqT = dsc("wuqT_s", (D, 2816), BF16)
    wdqT = dsc("wdqT_s", (2816, D), BF16)
    l1q = dsc("l1q", (D, T))
    l1v = dsc("l1v", (T, D), BF16)
    l1lg = dsc("l1lg", (T, D))
    l1lm = dsc("l1lm", (16,))
    l1lmg = dsc("l1lmg", (8, 16))
    l1np = dsc("l1np", (64, D))
    l1dp = dsc("l1dp", (64, 16))
    l1npg = dsc("l1npg", (8, 64, D))
    l1dpg = dsc("l1dpg", (8, 64, 16))
    f3h = dsc("f3h", (D, T), BF16)
    f3oh = dsc("f3oh", (8, T))
    f3cs = dsc("f3cs", (8, T))
    f3gt = dsc("f3gt", (T,))
    f3cn = dsc("f3cn", (8,))
    f3cng = dsc("f3cng", (8, 8))
    a3q = dsc("a3q", (D, T), BF16)
    a3k = dsc("a3k", (D, T), BF16)
    a3v = dsc("a3v", (T, D), BF16)
    a3kg = dsc("a3kg", (8, D, T), BF16)
    a3vg = dsc("a3vg", (8, T, D), BF16)
    kL3 = dsc("kL3", (D, S), BF16)
    vL3 = dsc("vL3", (S, D), BF16)

    c.begin()
    c.cur_stack = None
    c.make_consts([1e-5, 1.0 / 127.0, 1e-8, 0.125, 1e-6, 1.0 / 256])

    def blkpfx(p):
        c.prefix = p

    blkpfx("b1_")
    build_ffn0(T, c, NS(xT=x0, wgT=g_wgT, wuT=g_wuT, wdT=g_wdT, yT=xA))
    blkpfx("b2h_")
    c.gdma(halc[:, :], xA[:, T - HALO:T])
    _allgather(c, halc, halg, "dnh")
    _halo_asm(c, xA, xhal, halg, hz_in, "dn")
    blkpfx("b2p_")
    build_dn_prep(T, c, NS(xT=xhal, Win=dn_Win, Wab=dn_Wab, babc=dn_babc,
                           Wog=dn_Wog, bogc=dn_bogc, wconv=dn_wconv,
                           q_out=dnq, k_out=dnk, v_out=dnv, ab_out=dnab,
                           og_out=dnog))
    _allgather(c, dnq, dnqg, "dq")
    _allgather(c, dnk, dnkg, "dk")
    _allgather(c, dnv, dnvg, "dv")
    _allgather(c, dnab, dnabg, "da")
    blkpfx("b2g_")
    pgs = c.sem("pgs")
    pga = c.sem("pga")
    pid = c.pid()
    spid = c.pide("scalar")
    for r in range(4):
        for srcg, dst in ((dnqg, qL), (dnkg, kL), (dnvg, vL)):
            with nc.sync.register(_rn("pg")) as rg:
                nc.sync.reg_div(rg, pid, 4)
                nc.sync.reg_mul(rg, rg, 4)
                nc.sync.reg_add(rg, rg, r)
                nc.sync.reg_mul(rg, rg, D * T)
                with nc.sync.register(_rn("pg2")) as rg2:
                    nc.sync.reg_mod(rg2, pid, 4)
                    nc.sync.reg_mul(rg2, rg2, 256 * T)
                    nc.sync.reg_add(rg, rg, rg2)
                dyn = bass.AP(srcg.tensor, rg, [[T, 256], [1, T]])
                c.ydma(dst[:, r * T:(r + 1) * T], dyn, sem=pgs)
        for rowoff, dst in ((0, alL), (4, beL)):
            with nc.scalar.register(_rn("pga")) as rg:
                nc.scalar.reg_div(rg, spid, 4)
                nc.scalar.reg_mul(rg, rg, 4)
                nc.scalar.reg_add(rg, rg, r)
                nc.scalar.reg_mul(rg, rg, 8 * T)
                with nc.scalar.register(_rn("pga2")) as rg2:
                    nc.scalar.reg_mod(rg2, spid, 4)
                    nc.scalar.reg_add(rg2, rg2, rowoff)
                    nc.scalar.reg_mul(rg2, rg2, T)
                    nc.scalar.reg_add(rg, rg, rg2)
                dyn = bass.AP(dnabg.tensor, rg, [[1, T]])
                i_ = nc.scalar.dma_start(out=dst[r * T:(r + 1) * T], in_=dyn)
                pga.inc(i_, 16)
    for eng in (nc.tensor, nc.vector, nc.scalar, nc.gpsimd, nc.sync):
        pgs.wait(eng)
        pga.wait(eng)
    blkpfx("b2s_")
    build_dn_scan(S, c, NS(q_in=qL, k_in=kL, v_in=vL, al_in=alL,
                           be_in=beL, TLm1=dn_TLm1, TUm1=dn_TUm1, TU0=dn_TU0,
                           identf=identf, identb2=identb, twoI=dn_twoI,
                           Pident=dn_Pident, r_out=dnr))
    _allgather(c, dnr, dnrg, "dr")
    blkpfx("b2o_")
    build_dn_out(T, c, NS(xT=xA, r_in=dnrg, og_in=dnog, Wout=dn_Wout,
                          boutc=dn_boutc, rlc=dn_rlc, yT=xB), gathered=True)
    blkpfx("b3_")
    build_ffn0(T, c, NS(xT=xB, wgT=g_wgT, wuT=g_wuT, wdT=g_wdT, yT=xA))

    blkpfx("b4p_")
    build_fp4_prep(c, NS(win=fp4_win, identb=identb, wout=fp4c))
    _allgather(c, fp4c, fp4g, "fq")
    asq = c.sem("asq")
    n1 = (2816 // 128) * (1024 // 256)
    for wi, wdst in ((0, wgqT), (1, wuqT), (2, wdqT)):
        r_, cdim = ((2816, 1024) if wi < 2 else (1024, 2816))
        for rt in range(r_ // 128):
            for cu in range(cdim // 256):
                u = wi * n1 + rt * (cdim // 256) + cu
                for b2 in range(2):
                    c.gdma(wdst[cu * 256 + b2 * 128:cu * 256 + b2 * 128 + 128,
                                rt * 128:rt * 128 + 128],
                           fp4g[u, b2], sem=asq)
    for eng in (nc.tensor, nc.vector, nc.scalar, nc.gpsimd, nc.sync):
        asq.wait(eng)
    barrier(c)
    blkpfx("b4_")
    build_fp4_app(T, c, NS(x_in=xA, wgqT=wgqT, wuqT=wuqT, wdqT=wdqT,
                           identb=identb, identf32=identf, y_out=xB), fm_io=True)
    blkpfx("b5q_")
    build_l1a_qkv(T, c, NS(xT=xB, WqT=l1_WqT, bqc=l1_bqc, WkT=l1_WkT,
                           bkc=l1_bkc, WvT=l1_WvT, bvr=l1_bvr, W0in=l1_W0,
                           identf=identf, q_out=l1q, v_out=l1v, lg_out=l1lg,
                           lmax_out=l1lm))
    _allgather(c, l1lm, l1lmg, "lm")
    blkpfx("b5n_")
    build_l1a_nd(T, c, NS(lg_in=l1lg, v_in=l1v, lmax8=l1lmg,
                          np_out=l1np, dp_out=l1dp, bmask=l1_bmask))
    _allgather(c, l1np, l1npg, "np")
    _allgather(c, l1dp, l1dpg, "dp")
    blkpfx("b5o_")
    build_l1a_out(T, c, NS(xT=xB, q_in=l1q, np_in=l1npg, dp_in=l1dpg,
                           W0in=l1_W0, WoT=l1_WoT, boc=l1_boc, yT=xA,
                           bmaskc=l1_bmaskc))
    blkpfx("b6_")
    build_fp4_app(T, c, NS(x_in=xA, wgqT=wgqT, wuqT=wuqT, wdqT=wdqT,
                           identb=identb, identf32=identf, y_out=xB), fm_io=True)

    blkpfx("b7_")
    build_l2f(T, c, NS(xT=xB, nwr=l2_nwr, rwT=l2_rwT, sw1=l2_sw1, sb1r=l2_sb1r,
                       sw2=l2_sw2, sb2r=l2_sb2r, ew1=l2_ew1, eb1r=l2_eb1r,
                       ew2=l2_ew2, eb2r=l2_eb2r, identf=identf,
                       selmat=l2_selmat, yT=xA))
    blkpfx("b8h_")
    c.gdma(halc[:, :], xA[:, T - HALO:T])
    _allgather(c, halc, halg, "l2h")
    _halo_asm(c, xA, xhal, halg, hz_in, "l2")
    blkpfx("b8_")
    build_l2a(T, c, NS(xT=xhal, WqT=a2_WqT, WkT=a2_WkT, WvT=a2_WvT,
                       WkpT=a2_WkpT, WoT=a2_WoT, gmr=a2_gmr, btr=a2_btr,
                       mstd=a2_mstd, mb0=a2_mb0, e2in=a2_e2,
                       identb=identb, yT=xB))
    blkpfx("b9_")
    build_l2f(T, c, NS(xT=xB, nwr=l2_nwr, rwT=l2_rwT, sw1=l2_sw1, sb1r=l2_sb1r,
                       sw2=l2_sw2, sb2r=l2_sb2r, ew1=l2_ew1, eb1r=l2_eb1r,
                       ew2=l2_ew2, eb2r=l2_eb2r, identf=identf,
                       selmat=l2_selmat, yT=xA))

    blkpfx("b10r_")
    build_l3f_route(T, c, NS(xT=xA, Wr=f3_Wr, brv=f3_brv, identf=identf,
                             h_out=f3h, oh_out=f3oh, csum_out=f3cs,
                             gate_out=f3gt, counts_out=f3cn))
    _allgather(c, f3cn, f3cng, "c1")
    blkpfx("b10e_")
    build_l3f_experts(T, c, NS(xT=xA, hT=f3h, oh_in=f3oh, csum_in=f3cs,
                               gate_in=f3gt, counts_ce=f3cng, wrow=f3_wrow,
                               W1=f3_W1, b1r=f3_b1r, W2=f3_W2, b2r=f3_b2r,
                               selmat=f3_selmat, yT=xB))
    blkpfx("b11q_")
    build_l3a_qkv(T, c, NS(xT=xB, Wq=a3_Wq, bqr=a3_bqr, Wdn=a3_Wdn,
                           bdnr=a3_bdnr, Wupk=a3_Wupk, bupkr=a3_bupkr,
                           Wupv=a3_Wupv, bupv=a3_bupv, cosF=a3_cosF,
                           sinPM=a3_sinPM, Pswap=a3_Pswap,
                           q_out=a3q, k_out=a3k, v_out=a3v))
    _allgather(c, a3k, a3kg, "k3")
    _allgather(c, a3v, a3vg, "v3")
    blkpfx("b11g_")
    kvp = c.sem("kvp")
    gpid = c.pide("gpsimd")
    for r in range(4):
        with nc.gpsimd.register(_rn("kv")) as rg:
            nc.gpsimd.reg_div(rg, gpid, 4)
            nc.gpsimd.reg_mul(rg, rg, 4)
            nc.gpsimd.reg_add(rg, rg, r)
            nc.gpsimd.reg_mul(rg, rg, D * T)
            dyn = bass.AP(a3kg.tensor, rg, [[T, D], [1, T]])
            i_ = nc.gpsimd.dma_start(out=kL3[:, r * T:(r + 1) * T], in_=dyn)
            kvp.inc(i_, 16)
        with nc.gpsimd.register(_rn("kv2")) as rg:
            nc.gpsimd.reg_div(rg, gpid, 4)
            nc.gpsimd.reg_mul(rg, rg, 4)
            nc.gpsimd.reg_add(rg, rg, r)
            nc.gpsimd.reg_mul(rg, rg, T * D)
            dyn = bass.AP(a3vg.tensor, rg, [[1, T * D]])
            i_ = nc.gpsimd.dma_start(
                out=bass.AP(vL3.tensor, r * T * D, [[1, T * D]]), in_=dyn)
            kvp.inc(i_, 16)
    for eng in (nc.tensor, nc.vector, nc.scalar, nc.gpsimd, nc.sync):
        kvp.wait(eng)
    blkpfx("b11_")
    build_l3a_attn(T, S, c, NS(xT=xB, q_in=a3q, k_in=kL3, v_in=vL3,
                               Wout=a3_Wout, boutr=a3_boutr, yT=xA))
    blkpfx("b12r_")
    build_l3f_route(T, c, NS(xT=xA, Wr=f3_Wr, brv=f3_brv, identf=identf,
                             h_out=f3h, oh_out=f3oh, csum_out=f3cs,
                             gate_out=f3gt, counts_out=f3cn))
    _allgather(c, f3cn, f3cng, "c2")
    blkpfx("b12e_")
    build_l3f_experts(T, c, NS(xT=xA, hT=f3h, oh_in=f3oh, csum_in=f3cs,
                               gate_in=f3gt, counts_ce=f3cng, wrow=f3_wrow,
                               W1=f3_W1, b1r=f3_b1r, W2=f3_W2, b2r=f3_b2r,
                               selmat=f3_selmat, yT=yout))
    c.prefix = ""
    c.end()
    return c.nc


def run_fused(inp):
    nc = _get("fused", build_fused)
    x = np.ascontiguousarray(inp["L0_attn_x"].reshape(NTOK, D).T)
    identf = np.eye(128, dtype=np.float32)
    identb = np.eye(128, dtype=BFD)

    g_wgT = np.ascontiguousarray(inp["L0_ffn_w_gate"].T)
    g_wuT = np.ascontiguousarray(inp["L0_ffn_w_up"].T)
    g_wdT = np.ascontiguousarray(inp["L0_ffn_w_down"].T)
    Wc = inp["L0_attn_W_ctrl"]
    bc = inp["L0_attn_b_ctrl"]
    wcv = np.stack([inp["L0_attn_w_q"][:, 0, :], inp["L0_attn_w_k"][:, 0, :],
                    inp["L0_attn_w_v"][:, 0, :]], axis=1)
    wconv = np.ascontiguousarray(wcv.reshape(8, 128, 3, 4).transpose(1, 0, 2, 3))
    tl = np.tril(np.ones((128, 128), np.float32), -1)
    tu = np.triu(np.ones((128, 128), np.float32), 1)
    tu0 = np.triu(np.ones((128, 128), np.float32), 0)
    pidm = np.zeros((128, 2, 256), np.float32)
    for rr_ in range(128):
        pidm[rr_, 0, rr_] = 1.0
        pidm[rr_, 1, 128 + rr_] = 1.0
    units = []
    for w in (inp["L1_ffn_w_gate"], inp["L1_ffn_w_up"], inp["L1_ffn_w_down"]):
        r, cdim = w.shape
        u = w.reshape(r // 128, 128, cdim // 256, 256).transpose(0, 2, 1, 3)
        units.append(u.reshape(-1, 128, 256))
    units = np.concatenate(units, axis=0)
    W0h = inp["L1_attn_Wrf"][0].transpose(1, 0, 2)
    W0 = np.ascontiguousarray(np.concatenate([W0h, W0h], axis=0))
    nwr = np.ascontiguousarray(inp["L2_ffn_norm_weight"].reshape(8, 128).T)
    rwT = np.ascontiguousarray(inp["L2_ffn_router_w"].T)
    sb1r = np.ascontiguousarray(inp["L2_ffn_sb1"].reshape(32, 128).T)
    sb2r = np.ascontiguousarray(inp["L2_ffn_sb2"].reshape(8, 128).T)
    eb1r = np.ascontiguousarray(inp["L2_ffn_eb1"].reshape(8, 32, 128).transpose(2, 0, 1))
    eb2r = np.ascontiguousarray(inp["L2_ffn_eb2"].reshape(8, 8, 128).transpose(2, 0, 1))
    selmat = np.repeat(np.eye(8, dtype=np.float32), 128, axis=1)
    gmr = np.ascontiguousarray(inp["L2_attn_gamma"].reshape(8, 128).T)
    btr = np.ascontiguousarray(inp["L2_attn_beta"].reshape(8, 128).T)
    mstd_h, mfst_h = l2a_masks()
    b1r = np.ascontiguousarray(inp["L3_ffn_b1"].reshape(8, 8, 128).transpose(2, 0, 1))
    b2r = np.ascontiguousarray(inp["L3_ffn_b2"].reshape(8, 8, 128).transpose(2, 0, 1))
    bqr = np.ascontiguousarray(inp["L3_attn_bq"].reshape(8, 128).T)
    bdnr = np.ascontiguousarray(inp["L3_attn_bdown"].reshape(2, 128).T)
    Wup = inp["L3_attn_Wup"].reshape(256, 16, 2, 64)
    Wupk = np.ascontiguousarray(Wup[:, :, 0, :].reshape(256, 1024))
    Wupv = np.ascontiguousarray(Wup[:, :, 1, :].reshape(256, 1024))
    bup = inp["L3_attn_bup"].reshape(16, 2, 64)
    bupk = np.ascontiguousarray(bup[:, 0, :].reshape(1024))
    bupv = np.ascontiguousarray(bup[:, 1, :].reshape(1024))
    bupkr = np.ascontiguousarray(bupk.reshape(8, 128).T)
    inv_freq = 1.0 / 10000.0 ** (np.arange(0, 64, 2, dtype=np.float64) / 64.0)
    Pswap = np.zeros((128, 128), np.float32)
    for hh in range(2):
        for j in range(32):
            Pswap[hh * 64 + 32 + j, hh * 64 + j] = 1.0
            Pswap[hh * 64 + j, hh * 64 + 32 + j] = 1.0
    boutr = np.ascontiguousarray(inp["L3_attn_bout"].reshape(8, 128).T)

    shared = {
        "identf": identf, "identb": identb,
        "g_wgT": g_wgT, "g_wuT": g_wuT, "g_wdT": g_wdT,
        "dn_Win": inp["L0_attn_W_in"],
        "dn_Wab": np.ascontiguousarray(Wc[:, 0:8]),
        "dn_babc": np.ascontiguousarray(bc[0:8].reshape(8, 1)),
        "dn_Wog": np.ascontiguousarray(Wc[:, 8:]),
        "dn_bogc": np.ascontiguousarray(bc[8:].reshape(8, 128).T),
        "dn_wconv": wconv, "dn_TLm1": tl, "dn_TUm1": tu, "dn_TU0": tu0,
        "dn_twoI": 2 * identf, "dn_Pident": pidm,
        "dn_Wout": inp["L0_attn_W_out"],
        "dn_boutc": np.ascontiguousarray(inp["L0_attn_b_out"].reshape(8, 128).T),
        "dn_rlc": np.ascontiguousarray(inp["L0_attn_ramp_logit"].reshape(8, 128).T),
        "l1_WqT": np.ascontiguousarray(inp["L1_attn_Wq"].T),
        "l1_bqc": np.ascontiguousarray(inp["L1_attn_bq"].reshape(8, 128).T),
        "l1_WkT": np.ascontiguousarray(inp["L1_attn_Wk"].T),
        "l1_bkc": np.ascontiguousarray(inp["L1_attn_bk"].reshape(8, 128).T),
        "l1_WvT": np.ascontiguousarray(inp["L1_attn_Wv"].T),
        "l1_bvr": inp["L1_attn_bv"], "l1_W0": W0,
        "l1_WoT": np.ascontiguousarray(inp["L1_attn_Wo"].T),
        "l1_boc": np.ascontiguousarray(inp["L1_attn_bo"].reshape(8, 128).T),
        "l2_nwr": nwr, "l2_rwT": rwT,
        "l2_sw1": inp["L2_ffn_sw1"], "l2_sb1r": sb1r,
        "l2_sw2": inp["L2_ffn_sw2"], "l2_sb2r": sb2r,
        "l2_ew1": inp["L2_ffn_ew1"], "l2_eb1r": eb1r,
        "l2_ew2": inp["L2_ffn_ew2"], "l2_eb2r": eb2r,
        "l2_selmat": selmat,
        "a2_WqT": np.ascontiguousarray(inp["L2_attn_Wq"].T),
        "a2_WkT": np.ascontiguousarray(inp["L2_attn_Wk"].T),
        "a2_WvT": np.ascontiguousarray(inp["L2_attn_Wv"].T),
        "a2_WkpT": np.ascontiguousarray(inp["L2_attn_Wkp"].T),
        "a2_WoT": np.ascontiguousarray(inp["L2_attn_Wo"].T),
        "a2_gmr": gmr, "a2_btr": btr, "a2_mstd": mstd_h, "a2_e2": l2a_e2(),
        "f3_Wr": np.ascontiguousarray(inp["L3_ffn_Wr"]),
        "f3_brv": np.ascontiguousarray(inp["L3_ffn_br"]),
        "f3_W1": inp["L3_ffn_W1"], "f3_b1r": b1r,
        "f3_W2": inp["L3_ffn_W2"], "f3_b2r": b2r,
        "f3_selmat": selmat,
        "a3_Wq": inp["L3_attn_Wq"], "a3_bqr": bqr,
        "a3_Wdn": inp["L3_attn_Wdown"], "a3_bdnr": bdnr,
        "a3_Wupk": Wupk, "a3_bupkr": bupkr,
        "a3_Wupv": Wupv, "a3_bupv": bupv, "a3_Pswap": Pswap,
        "a3_Wout": inp["L3_attn_Wout"], "a3_boutr": boutr,
    }

    in_maps = []
    for cid in range(N_CORES):
        t0 = cid * TPC
        b = cid // 4
        pos = np.arange(TPC, dtype=np.float64) + (t0 % S)
        fr = pos[:, None] * inv_freq[None, :]
        cos32 = np.cos(fr).astype(np.float32).T
        sin32 = np.sin(fr).astype(np.float32).T
        cosF = np.ascontiguousarray(np.tile(np.concatenate([cos32, cos32], 0), (2, 1)))
        sinPM = np.ascontiguousarray(np.tile(np.concatenate([-sin32, sin32], 0), (2, 1)))
        bmask = np.zeros((1, 8), np.float32)
        bmask[0, b * 4:b * 4 + 4] = 1.0
        bmaskc = np.ascontiguousarray(np.broadcast_to(bmask, (128, 8)))
        hz = np.zeros((128, 1), np.float32) if t0 % S == 0 \
            else np.ones((128, 1), np.float32)
        m = dict(shared)
        m.update({
            "x0": np.ascontiguousarray(x[:, t0:t0 + TPC]),
            "hz_in": hz,
            "fp4_win": np.ascontiguousarray(units[cid * N_UNITS:(cid + 1) * N_UNITS]),
            "l1_bmask": bmask, "l1_bmaskc": bmaskc,
            "a2_mb0": mfst_h if t0 % S == 0 else mstd_h,
            "f3_wrow": (np.arange(8) < cid).astype(np.float32),
            "a3_cosF": cosF, "a3_sinPM": sinPM,
        })
        in_maps.append(m)
    res = run_bass_kernel_spmd(nc, in_maps, core_ids=list(range(N_CORES)))
    y = np.concatenate([r["yout"] for r in res.results], axis=1)
    return y


def kernel_fused(**inp):
    y = run_fused(inp)
    return np.ascontiguousarray(y.T).reshape(B, S, D).astype(np.float32)
